# version 06
# baseline (speedup 1.0000x reference)
import numpy as np
import ml_dtypes
from contextlib import ExitStack

import concourse.bass as bass
import concourse.mybir as mybir
from concourse.bass_utils import run_bass_kernel_spmd

F32 = mybir.dt.float32
BF16 = mybir.dt.bfloat16
AF = mybir.ActivationFunctionType
ALU = mybir.AluOpType
AX = mybir.AxisListType

BF = ml_dtypes.bfloat16

NCORE = 8
LSEQ = 4096
DM = 1024
DI = 2048
NST = 16
DTR = 64
NPT = 8
AC = 512
HC = 256
NCH = LSEQ // AC
HEADS = 8
DH = 64
INNER = 512
HID = 2730
HIDP = 2816
NHT = 22
NEG = -1e30

NCHRUN = NCH
STAGE = 4


def _pp(v):
    return np.ascontiguousarray(np.asarray(v, np.float32).reshape(-1, 128).T)


def _pp3(a):
    a = np.asarray(a, np.float32)
    F, K = a.shape
    return np.ascontiguousarray(a.reshape(F // 128, 128, K).transpose(1, 0, 2))


def _host_prep(inp):
    g = {}
    g["w_inT"] = np.ascontiguousarray(inp["L0_attn_in_proj_w"].T)
    g["w_xpT"] = np.ascontiguousarray(inp["L0_attn_x_proj_w"].T)
    g["w_dtT"] = np.ascontiguousarray(inp["L0_attn_dt_proj_w"].T)
    g["w_outT"] = np.ascontiguousarray(inp["L0_attn_out_proj_w"].T)
    g["convw"] = _pp3(inp["L0_attn_conv_w"][:, 0, :])
    cw = _pp3(inp["L0_attn_conv_w"][:, 0, :])
    cd = np.zeros((16, 4, 128, 128), np.float32)
    ii128 = np.arange(128)
    for t in range(16):
        for k in range(4):
            cd[t, k, ii128, ii128] = cw[:, t, k]
    g["convdiag"] = np.ascontiguousarray(cd.reshape(16 * 4 * 128, 128))
    g["convb"] = _pp(inp["L0_attn_conv_b"])
    g["dtb"] = _pp(inp["L0_attn_dt_proj_b"])
    g["Alog"] = _pp3(inp["L0_attn_A_log"])
    g["Dp"] = _pp(inp["L0_attn_D_param"])
    for l in (0, 1):
        g[f"wqkv{l}"] = np.ascontiguousarray(inp["L0_ffn_wqkv"][l])
        g[f"wo{l}"] = np.ascontiguousarray(inp["L0_ffn_wo"][l])
        w1 = inp["L0_ffn_w1"][l]
        a = np.zeros((DM, HIDP), np.float32); a[:, :HID] = w1[:, :HID]
        gg = np.zeros((DM, HIDP), np.float32); gg[:, :HID] = w1[:, HID:]
        at = a.reshape(DM, NHT, 128); gt = gg.reshape(DM, NHT, 128)
        g[f"w1p{l}"] = np.ascontiguousarray(
            np.stack([at, gt], axis=2).reshape(DM, 2 * NHT * 128))
        w2 = np.zeros((HIDP, DM), np.float32); w2[:HID] = inp["L0_ffn_w2"][l]
        g[f"w2p{l}"] = w2
        b1 = inp["L0_ffn_b1"][l]
        b1a = np.zeros(HIDP, np.float32); b1a[:HID] = b1[:HID]
        b1g = np.zeros(HIDP, np.float32); b1g[:HID] = b1[HID:]
        g[f"b1a{l}"] = _pp(b1a); g[f"b1g{l}"] = _pp(b1g)
    g["qs"] = np.ascontiguousarray(np.stack(
        [np.tile(np.asarray(inp["L0_ffn_q_scale"][l], np.float32), 2) for l in (0, 1)], 1))
    g["ks"] = np.ascontiguousarray(np.stack(
        [np.tile(np.asarray(inp["L0_ffn_k_scale"][l], np.float32), 2) for l in (0, 1)], 1))
    g["lng"] = np.ascontiguousarray(np.stack([_pp(inp["L0_ffn_ln_g"][l]) for l in (0, 1)], 1))
    g["lnb"] = np.ascontiguousarray(np.stack([_pp(inp["L0_ffn_ln_b"][l]) for l in (0, 1)], 1))
    g["b2"] = np.ascontiguousarray(np.stack([_pp(inp["L0_ffn_b2"][l]) for l in (0, 1)], 1))

    ii = np.arange(128)
    g["m0"] = np.where(ii[None, :] < ii[:, None], NEG, 0.0).astype(BF)
    g["m4"] = np.where(ii[None, :] > ii[:, None], NEG, 0.0).astype(BF)
    g["ident"] = np.eye(128, dtype=BF)
    identD = np.zeros((128, 64), np.float32)
    for p_ in range(128):
        identD[p_, p_ % 64] = 1.0
    g["identD"] = identD.astype(BF)
    g["onescol"] = np.ones((1, 128), dtype=BF)
    g["onek"] = np.ones((128, 1), dtype=BF)
    g["cvals"] = np.tile(np.array([[1e-5, -0.5, 1.0 / DM, 8.0, -1.0, 1.0]], np.float32), (128, 1))
    bd = np.zeros((128, 4, 8), np.float32)
    for t in range(4):
        bd[:64, t, 2 * t] = 1.0; bd[64:, t, 2 * t + 1] = 1.0
    g["bdones"] = bd.astype(BF)
    sel8 = np.zeros((8, 4, 128), np.float32)
    for t in range(4):
        for p_ in range(128):
            sel8[2 * t + (p_ >= 64), t, p_] = 1.0
    g["sel8"] = np.ascontiguousarray(sel8.reshape(8, 512)).astype(BF)
    sel16 = np.zeros((16, 16, 128), np.float32)
    for n in range(16):
        sel16[n, n, :] = 1.0
    g["sel16"] = np.ascontiguousarray(sel16.reshape(16, 2048)).astype(BF)
    P = np.zeros((128, 128), np.float32)
    for di in range(128):
        h0 = (di // 64) * 64; d = di % 64
        if d < 32:
            P[h0 + d + 32, di] = -1.0
        else:
            P[h0 + d - 32, di] = 1.0
    g["rotP"] = P.astype(BF)

    dih = np.arange(64); j = dih % 32
    invfreq = 1.0 / 10000.0 ** ((2 * j) / DH)
    t = np.arange(LSEQ, dtype=np.float64)
    freqs = t[None, :] * invfreq[:, None]
    base = ((2 * j) + 0.4 * DH) / (1.4 * DH)
    power = (t - LSEQ // 2) / float(AC // 2)
    scale = base[:, None] ** power[None, :]
    cos = np.cos(freqs); sin = np.sin(freqs)
    rows2 = lambda m: np.ascontiguousarray(np.concatenate([m, m], 0).astype(BF))
    g["rcq"] = rows2(cos * scale); g["rsq"] = rows2(sin * scale)
    g["rck"] = rows2(cos / scale); g["rsk"] = rows2(sin / scale)
    return g


ENGINES = ("sy", "gq", "pe", "ac", "dv")
SEMS = ("sy", "gq", "pe", "ac", "dv", "gc", "wl0", "wl1", "wg0", "wg1", "wb0", "wb1")


class Prog:
    def __init__(self, nc):
        self.nc = nc
        self.prog = {e: [] for e in ENGINES}
        self.iswait = {e: [] for e in ENGINES}
        self.cnt = {e: 0 for e in SEMS}
        self.dirty = {e: False for e in SEMS}
        self.snap = dict(self.cnt)
        self.lastwait = {e: {q: 0 for q in SEMS} for e in ENGINES}
        self.pending = {e: False for e in ENGINES}
        self.sems = None

    def _emit(self, e, fn, compute=False):
        if self.pending[e]:
            self.pending[e] = False
            for q in SEMS:
                self.wait(e, q, self.snap[q])
        self.prog[e].append([fn, []])
        self.iswait[e].append(False)
        if e == "sy" or (e == "gq" and not compute):
            self.prog[e][-1][1] = [(e, 16)]
            self.cnt[e] += 16
        elif e == "gq":
            self.dirty["gc"] = True
        else:
            self.dirty[e] = True

    def sy(self, fn): self._emit("sy", fn)
    def gq(self, fn): self._emit("gq", fn)
    def gc(self, fn): self._emit("gq", fn, compute=True)

    def dma(self, queue, fn, sem):
        if self.pending[queue]:
            self.pending[queue] = False
            for q in SEMS:
                self.wait(queue, q, self.snap[q])
        self.prog[queue].append([fn, [(sem, 16)]])
        self.iswait[queue].append(False)
        self.cnt[sem] += 16
    def pe(self, fn): self._emit("pe", fn)
    def ac(self, fn): self._emit("ac", fn)
    def dv(self, fn): self._emit("dv", fn)

    def inc(self, e):
        stream = "gq" if e == "gc" else e
        assert self.prog[stream] and not self.iswait[stream][-1], e
        self.prog[stream][-1][1].append((e, 1))
        self.cnt[e] += 1
        self.dirty[e] = False
        return self.cnt[e]

    def wait(self, e, q, val):
        if val <= 0 or self.lastwait[e][q] >= val:
            return
        self.lastwait[e][q] = val
        self.prog[e].append(
            [lambda eng, q=q, val=val: eng.wait_ge(self.sems[q], val), []])
        self.iswait[e].append(True)

    def chain(self, e):
        self.inc(e)
        st = "gq" if e == "gc" else e
        self.wait(st, e, self.cnt[e])

    def bar(self):
        for e in ("pe", "ac", "dv", "gc"):
            if self.dirty[e]:
                self.inc(e)
        self.snap = dict(self.cnt)
        for e in ENGINES:
            self.pending[e] = True

    def materialize(self, block):
        sems = self.sems

        def runner(progl):
            def body(eng):
                for fn, incs in progl:
                    inst = fn(eng)
                    for s, n in incs:
                        inst.then_inc(sems[s], n)
            return body

        block.sync(runner(self.prog["sy"]))
        block.gpsimd(runner(self.prog["gq"]))
        block.tensor(runner(self.prog["pe"]))
        block.scalar(runner(self.prog["ac"]))
        block.vector(runner(self.prog["dv"]))


def build(debug=False):
    nc = bass.Bass("TRN2", target_bir_lowering=False)
    ctx = ExitStack()

    D = {}
    def din(name, shape, dt=F32):
        D[name] = nc.dram_tensor(name, list(shape), dt, kind="ExternalInput").ap()
    din("xT", (DM, LSEQ))
    din("w_inT", (DM, 2 * DI)); din("w_xpT", (DI, 96)); din("w_dtT", (DTR, DI))
    din("w_outT", (DI, DM))
    din("convw", (128, 16, 4)); din("convdiag", (16 * 4 * 128, 128)); din("convb", (128, 16)); din("dtb", (128, 16))
    din("Alog", (128, 16, 16)); din("Dp", (128, 16))
    for l in (0, 1):
        din(f"wqkv{l}", (DM, 3 * INNER)); din(f"wo{l}", (INNER, DM))
        din(f"w1p{l}", (DM, 2 * HIDP)); din(f"w2p{l}", (HIDP, DM))
        din(f"b1a{l}", (128, NHT)); din(f"b1g{l}", (128, NHT))
    din("qs", (128, 2)); din("ks", (128, 2))
    din("lng", (128, 2, 8)); din("lnb", (128, 2, 8)); din("b2", (128, 2, 8))
    din("m0", (128, 128), BF16); din("m4", (128, 128), BF16)
    din("ident", (128, 128), BF16); din("identD", (128, 64), BF16)
    din("onescol", (1, 128), BF16); din("onek", (128, 1), BF16)
    din("cvals", (128, 6))
    din("sel8", (8, 512), BF16); din("sel16", (16, 2048), BF16)
    din("bdones", (128, 4, 8), BF16)
    din("rotP", (128, 128), BF16)
    for nm in ("rcq", "rsq", "rck", "rsk"):
        din(nm, (128, LSEQ), BF16)
    out = nc.dram_tensor("out", [DM, LSEQ], F32, kind="ExternalOutput").ap()
    out_r = out.rearrange("(t p) l -> p t l", p=128)
    xT_r = D["xT"].rearrange("(t p) l -> p t l", p=128)

    _nm = [0]
    def sb(shape, dt=F32):
        _nm[0] += 1
        return ctx.enter_context(nc.sbuf_tensor(f"t{_nm[0]}", list(shape), dt))

    px = sb((128, NPT, AC))
    hcarry = sb((128, 16, 16))
    ccarry = sb((128, 16, 4))
    scarry = sb((128, 4, NPT), BF16)
    kringT = [sb((128, 4, 2 * AC), BF16) for _ in (0, 1)]
    vtringT = [sb((128, HEADS, 8, DH), BF16) for _ in (0, 1)]
    m0 = sb((128, 128), BF16); m4 = sb((128, 128), BF16)
    ident = sb((128, 128), BF16); identD = sb((128, 64), BF16)
    onescol = sb((1, 128), BF16); onek = sb((128, 1), BF16)
    cvals = sb((128, 6))
    sel8 = sb((8, 512), BF16); sel16 = sb((16, 2048), BF16)
    bdones = sb((128, 4, 8), BF16)
    rotP = sb((128, 128), BF16)
    Aneg = sb((128, 16, 16)); convw = sb((128, 16, 4))
    convb = sb((128, 16)); dtb = sb((128, 16)); Dp = sb((128, 16))
    qs8 = sb((128, 2)); ks1 = sb((128, 2))
    lng = sb((128, 2, NPT)); lnb = sb((128, 2, NPT)); b2s = sb((128, 2, NPT))
    b1a = sb((128, 2, NHT)); b1g = sb((128, 2, NHT))
    dtW = sb((DTR, DI), BF16)
    xpW = sb((128, 16, 96), BF16)
    dlr = sb((DTR, HC), BF16)
    bcrowB = sb((16, HC), BF16)
    bcrowC = sb((16, HC), BF16)
    rows1 = sb((1, 5 * AC))
    rowsb1 = sb((1, 2 * AC), BF16)
    nrm = sb((8, 2, AC))
    nrmb = sb((8, AC), BF16)
    sAB = sb((128, 8))
    wbuf = sb((128, 2, 2048), BF16)
    cbuf = sb((128, 2, 4, 128), BF16)

    A16N = 29184
    A32N = 13056
    ar16 = sb((128, A16N), BF16)
    ar32 = sb((128, A32N))

    def v16(off, shape):
        n = int(np.prod(shape[1:]))
        ap = ar16[:, off:off + n]
        if len(shape) == 3:
            ap = ap.rearrange("p (a b) -> p a b", a=shape[1])
        return ap, off + n

    def v32(off, shape):
        n = int(np.prod(shape[1:]))
        ap = ar32[:, off:off + n]
        if len(shape) == 3:
            ap = ap.rearrange("p (a b) -> p a b", a=shape[1])
        return ap, off + n

    o = 0
    lnout, o = v16(o, (128, NPT, HC))
    xinner, o = v16(o, (128, 16, HC + 4))
    zsil, o = v16(o, (128, 16, HC))
    xconv, o = v16(o, (128, 16, HC))
    Bb, o = v16(o, (128, 16, HC))
    Cb, o = v16(o, (128, 16, HC))
    wt, o = v16(o, (128, HC))
    ya, o = v16(o, (128, 4, HC))
    yb, o = v16(o, (128, 2, HC))
    yg, o = v16(o, (128, 16, HC))
    assert o <= A16N, o
    dBxA = ar16[:, 2048:2048 + 16 * HC].rearrange("p (a b) -> p a b", a=16)
    dBxP = (dBxA, dBxA)
    yah = ar16[:, 0:8 * HC].rearrange("p (a b) -> p a b", a=8)
    tt1 = ya
    tt2 = yb
    o = 0
    pxib, o = v16(o, (128, NPT, AC + 1))
    rott, o = v16(o, (128, 4, AC))
    q2b, o = v16(o, (128, 4, AC))
    q3b, o = v16(o, (128, 4, AC))
    vbf, o = v16(o, (128, 4, AC))
    sqb, o = v16(o, (128, 2, AC))
    atno, o = v16(o, (128, 4, AC))
    gated, o = v16(o, (128, NHT, AC))
    sqb2, o = v16(o, (128, 2, AC))
    assert o <= A16N, o
    _vb0 = vbf[:, 0:4, :].rearrange("p a b -> p (a b)")
    attnP = (_vb0[:, 0:640], _vb0[:, 640:1280])
    _qb0 = q2b[:, 0:4, :].rearrange("p a b -> p (a b)")
    attnTP = (_qb0[:, 0:640], _qb0[:, 640:1280])
    _q2off = NPT * (AC + 1) + 4 * AC
    ln2o = ar16[:, _q2off:_q2off + NPT * AC].rearrange("p (a b) -> p a b", a=NPT)
    gel = sqb

    o = 0
    dAb0, o = v32(o, (128, 16, HC))
    dAb1, o = v32(o, (128, 16, HC))
    dtt, o = v32(o, (128, 2, HC))
    yt, o = v32(o, (128, HC))
    assert o <= A32N, o
    dABUF = (dAb0, dAb1)
    o = 0
    pxi, o = v32(o, (128, NPT, AC + 1))
    qraw, o = v32(o, (128, 4, AC))
    tf, o = v32(o, (128, 2, AC))
    assert o <= A32N, o

    psA0 = ctx.enter_context(nc.psum_tensor([128, 1024], F32))
    psA1 = ctx.enter_context(nc.psum_tensor([128, 1024], F32))
    pb0 = ctx.enter_context(nc.psum_tensor([128, 512], F32))
    pb1 = ctx.enter_context(nc.psum_tensor([128, 512], F32))
    pb2 = ctx.enter_context(nc.psum_tensor([128, 512], F32))
    pbt = ctx.enter_context(nc.psum_tensor([128, 1024], BF16))
    BANKS = [psA0[:, 0:512], psA0[:, 512:1024],
             psA1[:, 0:512], psA1[:, 512:1024], pb0[:], pb1[:]]

    p = Prog(nc)
    p.sems = {e: ctx.enter_context(nc.semaphore(f"s_{e}")) for e in SEMS}

    def mm(bank_ap, lhsT_ap, rhs_ap, start, stop=None, tile_position=None):
        if stop is None:
            stop = start
        p.pe(lambda e, o_=bank_ap, l_=lhsT_ap, r_=rhs_ap, s_=start, t_=stop,
             tp=tile_position:
             e.matmul(out=o_, lhsT=l_, rhs=r_, start=s_, stop=t_, tile_position=tp))

    SCR16 = {}
    STORED = set()

    LASTWB = {}

    def wload(buf_ap, name, rows_, col0, cols, rt0=0, rtn=None, par=0):
        dram2d = D[name]
        if name not in SCR16:
            SCR16[name] = nc.dram_tensor(
                f"scr_{name}", [dram2d.shape[0], dram2d.shape[1]], BF16,
                kind="Internal").ap()
        scr = SCR16[name].rearrange("(t p) c -> p t c", p=128)
        src = dram2d.rearrange("(t p) c -> p t c", p=128)
        nt = rtn if rtn is not None else rows_ // 128
        step = 4
        key = (name, col0, rt0)
        first = key not in STORED
        STORED.add(key)
        wsem = f"wg{par}" if first else f"wl{par}"
        p.wait("gq" if first else "sy", f"wb{par}", LASTWB.get(par, 0))
        for tt in range(0, nt, step):
            n = min(step, nt - tt)
            if first:
                p.dma("gq", lambda e, d=buf_ap[:, tt:tt + n, :],
                      s=src[:, rt0 + tt:rt0 + tt + n, col0:col0 + cols]:
                      e.dma_start(out=d, in_=s), wsem)
            else:
                p.dma("sy", lambda e, d=buf_ap[:, tt:tt + n, :],
                      s=scr[:, rt0 + tt:rt0 + tt + n, col0:col0 + cols]:
                      e.dma_start(out=d, in_=s), wsem)
        if first:
            p.wait("sy", wsem, p.cnt[wsem])
            for tt in range(0, nt, step):
                n = min(step, nt - tt)
                p.dma("sy", lambda e, d=scr[:, rt0 + tt:rt0 + tt + n, col0:col0 + cols],
                      s=buf_ap[:, tt:tt + n, :]:
                      e.dma_start(out=d, in_=s), f"wb{par}")
            LASTWB[par] = p.cnt[f"wb{par}"]
        return (wsem, p.cnt[wsem])

    def loadc(dst_ap, src_ap, cast=False):
        (p.gq if cast else p.sy)(
            lambda e, d=dst_ap, s=src_ap: e.dma_start(out=d, in_=s))

    loadc(m0[:], D["m0"]); loadc(m4[:], D["m4"])
    loadc(ident[:], D["ident"]); loadc(identD[:], D["identD"])
    loadc(onescol[:], D["onescol"]); loadc(onek[:], D["onek"])
    loadc(cvals[:], D["cvals"])
    loadc(sel8[:], D["sel8"]); loadc(sel16[:], D["sel16"])
    loadc(bdones[:], D["bdones"]); loadc(rotP[:], D["rotP"])
    loadc(convw[:], D["convw"]); loadc(convb[:], D["convb"])
    loadc(dtb[:], D["dtb"]); loadc(Dp[:], D["Dp"])
    loadc(Aneg[:], D["Alog"])
    loadc(qs8[:], D["qs"]); loadc(ks1[:], D["ks"])
    loadc(lng[:], D["lng"]); loadc(lnb[:], D["lnb"]); loadc(b2s[:], D["b2"])
    for l in (0, 1):
        loadc(b1a[:, l, :], D[f"b1a{l}"]); loadc(b1g[:, l, :], D[f"b1g{l}"])
    loadc(dtW[:], D["w_dtT"], cast=True)
    loadc(xpW[:], D["w_xpT"].rearrange("(t p) c -> p t c", p=128), cast=True)
    p.bar()
    AnegF = Aneg[:].rearrange("p a b -> p (a b)")
    p.ac(lambda e: e.activation(out=AnegF, in_=AnegF, func=AF.Exp))
    p.ac(lambda e: e.activation(out=qs8[:], in_=qs8[:], func=AF.Copy, scale=cvals[:, 3:4]))
    p.dv(lambda e: e.memset(hcarry[:].rearrange("p a b -> p (a b)"), 0.0))
    p.dv(lambda e: e.memset(ccarry[:].rearrange("p a b -> p (a b)"), 0.0))
    p.dv(lambda e: e.memset(scarry[:].rearrange("p a b -> p (a b)"), 0.0))
    for l in (0, 1):
        p.dv(lambda e, l=l: e.memset(
            vtringT[l][:].rearrange("p a b c -> p (a b c)"), 0.0))
    p.bar()
    p.dv(lambda e: e.tensor_scalar_mul(out=AnegF, in0=AnegF, scalar1=-1.0))
    p.bar()

    def seg(k, n):
        return rows1[0:1, k * AC:k * AC + n]

    def ln_post(n):
        p.ac(lambda e, n=n: e.activation(out=seg(0, n), in_=pb0[0:1, 0:n],
                                         func=AF.Copy, scale=cvals[0:1, 2:3]))
        ac0 = p.inc("ac")
        p.wait("dv", "ac", ac0)
        p.dv(lambda e, n=n: e.tensor_mul(out=seg(1, n), in0=seg(0, n), in1=seg(0, n)))
        p.chain("dv")
        p.dv(lambda e, n=n: e.scalar_tensor_tensor(
            out=seg(2, n), in0=pb1[0:1, 0:n], scalar=1.0 / DM,
            in1=seg(1, n), op0=ALU.mult, op1=ALU.subtract))
        dv0 = p.inc("dv")
        p.wait("ac", "dv", dv0)
        p.ac(lambda e, n=n: e.activation(out=seg(3, n), in_=seg(2, n),
                                         func=AF.Ln, bias=cvals[0:1, 0:1]))
        p.chain("ac")
        p.ac(lambda e, n=n: e.activation(out=seg(4, n), in_=seg(3, n),
                                         func=AF.Exp, scale=cvals[0:1, 1:2]))
        p.chain("ac")
        p.ac(lambda e, n=n: e.copy(out=rowsb1[0:1, 0:n], in_=seg(0, n)))
        p.chain("ac")
        p.ac(lambda e, n=n: e.copy(out=rowsb1[0:1, AC:AC + n], in_=seg(4, n)))
        ac1 = p.inc("ac")
        p.wait("pe", "ac", ac1)
        mm(pb0[:, 0:n], onescol[:], rowsb1[0:1, 0:n], start=True)
        mm(pb1[:, 0:n], onescol[:], rowsb1[0:1, AC:AC + n], start=True)
        p.bar()

    def ln_stats(src_ap_fn, n):
        lnac = {}
        for t in range(NPT):
            p.ac(lambda e, t=t, n=n: e.copy(out=sqb[:, 0, 0:n], in_=src_ap_fn(t))
                 if t % 2 == 0 else
                 e.copy(out=sqb[:, 1, 0:n], in_=src_ap_fn(t)))
            p.chain("ac")
            p.ac(lambda e, t=t, n=n: e.activation(out=sqb2[:, 0, 0:n], in_=src_ap_fn(t),
                                                  func=AF.Square)
                 if t % 2 == 0 else
                 e.activation(out=sqb2[:, 1, 0:n], in_=src_ap_fn(t), func=AF.Square))
            lnac[t] = p.inc("ac")
            p.wait("pe", "ac", lnac[t])
            half = t % 2
            mm(pb0[0:1, 0:n], onek[:], sqb[:, half, 0:n], start=(t == 0),
               stop=(t == NPT - 1))
            mm(pb1[0:1, 0:n], onek[:], sqb2[:, half, 0:n], start=(t == 0),
               stop=(t == NPT - 1))
            if t < NPT - 1:
                pef = p.inc("pe")
                p.wait("ac", "pe", pef)
        p.bar()
        ln_post(n)

    for c in range(NCHRUN):
        c0 = c * AC
        for t in range(NPT):
            p.sy(lambda e, t=t, c0=c0: e.dma_start(out=px[:, t, :],
                                                   in_=xT_r[:, t, c0:c0 + AC]))
        p.bar()

        for hh in range(2):
            h0 = hh * HC
            ln_stats(lambda t, h0=h0: px[:, t, h0:h0 + HC], HC)
            for t in range(NPT):
                p.dv(lambda e, t=t, h0=h0: e.tensor_sub(
                    out=tf[:, 0, 0:HC], in0=px[:, t, h0:h0 + HC], in1=pb0[:, 0:HC]))
                p.bar()
                p.dv(lambda e, t=t: e.tensor_mul(out=lnout[:, t, :],
                                                 in0=tf[:, 0, 0:HC], in1=pb1[:, 0:HC]))
                p.bar()

            p.dv(lambda e: e.tensor_copy(xinner[:, :, 1:4], ccarry[:, :, 1:4]))
            p.bar()

            def ip_slab(w):
                return wbuf[:, w % 2, 0:NPT * 256].rearrange("p (k c) -> p k c", k=NPT)

            ipld = {}
            ippe = {}
            ipac = {-2: 0, -1: 0}
            def ip_load(w):
                p.wait("sy", "pe", ippe.get(w - 2, 0))
                p.wait("gq", "pe", ippe.get(w - 2, 0))
                ipld[w] = wload(ip_slab(w), "w_inT", DM, 256 * w, 256, par=w % 2)
            ip_load(0)
            ip_load(1)
            for w in range(16):
                p.wait("pe", ipld[w][0], ipld[w][1])
                p.wait("pe", "ac", ipac[w - 2])
                slab = ip_slab(w)
                for kt in range(NPT):
                    for oi in range(2):
                        mm(BANKS[2 * (w % 2) + oi][:, 0:HC],
                           slab[:, kt, 128 * oi:128 * (oi + 1)],
                           lnout[:, kt, :], start=(kt == 0),
                           stop=(kt == NPT - 1))
                ippe[w] = p.inc("pe")
                if w + 2 < 16:
                    ip_load(w + 2)
                p.wait("ac", "pe", ippe[w])
                for oi in range(2):
                    ot = 2 * w + oi
                    bk = BANKS[2 * (w % 2) + oi]
                    if ot < 16:
                        p.ac(lambda e, b=bk, d=ot: e.copy(
                            out=xinner[:, d, 4:4 + HC], in_=b[:, 0:HC]))
                    else:
                        p.ac(lambda e, b=bk, d=ot - 16: e.activation(
                            out=zsil[:, d, :], in_=b[:, 0:HC], func=AF.Silu))
                ipac[w] = p.inc("ac")
            p.bar()
            p.dv(lambda e: e.tensor_copy(ccarry[:, :, 1:4],
                                         xinner[:, :, 1 + HC:4 + HC]))
            p.bar()

            cvld = {}
            cvpe = {}
            cvac = {-2: 0, -1: 0}
            def cv_load(t):
                p.wait("sy", "pe", cvpe.get(t - 2, 0))
                p.wait("gq", "pe", cvpe.get(t - 2, 0))
                cvld[t] = wload(cbuf[:, t % 2, :, :], "convdiag", 16 * 4 * 128, 0,
                                128, rt0=4 * t, rtn=4, par=t % 2)
            cv_load(0)
            cv_load(1)
            for t in range(16):
                p.wait("pe", cvld[t][0], cvld[t][1])
                p.wait("pe", "ac", cvac[t - 2])
                for k in range(4):
                    mm(BANKS[t % 2][:, 0:HC], cbuf[:, t % 2, k, :],
                       xinner[:, t, 1 + k:1 + k + HC], start=(k == 0), stop=(k == 3))
                cvpe[t] = p.inc("pe")
                if t + 2 < 16:
                    cv_load(t + 2)
                p.wait("ac", "pe", cvpe[t])
                p.ac(lambda e, t=t, b=BANKS[t % 2]: e.activation(
                    out=xconv[:, t, :], in_=b[:, 0:HC], func=AF.Silu,
                    bias=convb[:, t:t + 1]))
                cvac[t] = p.inc("ac")
            p.bar()

            for kt in range(16):
                mm(pb0[0:64, 0:HC], xpW[:, kt, 0:64], xconv[:, kt, :], start=(kt == 0), stop=(kt == 15))
                mm(pb1[0:16, 0:HC], xpW[:, kt, 64:80], xconv[:, kt, :], start=(kt == 0), stop=(kt == 15))
                mm(pb2[0:16, 0:HC], xpW[:, kt, 80:96], xconv[:, kt, :], start=(kt == 0), stop=(kt == 15))
            p.bar()
            p.ac(lambda e: e.copy(out=dlr[:, :], in_=pb0[0:DTR, 0:HC]))
            p.ac(lambda e: e.copy(out=bcrowB[:, :], in_=pb1[0:16, 0:HC]))
            p.ac(lambda e: e.copy(out=bcrowC[:, :], in_=pb2[0:16, 0:HC]))
            p.bar()
            for src_, dst in ((bcrowB, Bb), (bcrowC, Cb)):
                for n4 in range(4):
                    for k4 in range(4):
                        n = 4 * n4 + k4
                        mm(psA0[:, 256 * k4:256 * k4 + HC],
                           sel16[:, 128 * n:128 * (n + 1)], src_[:, :], start=True)
                    p.bar()
                    p.ac(lambda e, n4=n4, dst=dst: e.copy(
                        out=dst[:, 4 * n4:4 * n4 + 4, :],
                        in_=psA0[:].rearrange("p (a b) -> p a b", a=4)))
                    p.bar()

            mm(pb0[:, 0:HC], dtW[:, 0:128], dlr[:, :], start=True)
            p.bar()
            acdtt = {}
            acfin = {}
            dvtts = {}
            dvfin = {}
            pefin = {0: p.cnt["pe"]}
            for t in range(16):
                dAb = dABUF[t % 2]
                dBx = dBxP[t % 2]
                pbank = (pb0, pb1)[t % 2]
                if t < 15:
                    if t >= 1:
                        p.wait("pe", "ac", acdtt[t - 1])
                    mm((pb0, pb1)[(t + 1) % 2][:, 0:HC],
                       dtW[:, 128 * (t + 1):128 * (t + 2)], dlr[:, :], start=True)
                    pefin[t + 1] = p.inc("pe")
                p.wait("ac", "pe", pefin[t])
                if t >= 2:
                    p.wait("ac", "dv", dvtts[t - 2])
                if t >= 1:
                    p.wait("ac", "ac", acfin[t - 1])
                p.ac(lambda e, t=t, b=pbank, dAb=dAb: e.activation(
                    out=dAb[:, 0, :], in_=b[:, 0:HC], func=AF.Exp,
                    bias=dtb[:, t:t + 1]))
                p.chain("ac")
                p.ac(lambda e, t=t, dAb=dAb: e.activation(
                    out=dtt[:, t % 2, :], in_=dAb[:, 0, :], func=AF.Ln,
                    bias=cvals[:, 5:6]))
                acdtt[t] = p.inc("ac")
                p.wait("ac", "ac", acdtt[t])
                for n in range(16):
                    p.ac(lambda e, t=t, n=n, dAb=dAb: e.activation(
                        out=dAb[:, n, :], in_=dtt[:, t % 2, :], func=AF.Exp,
                        scale=Aneg[:, t, n:n + 1]))
                acfin[t] = p.inc("ac")
                p.wait("dv", "ac", acdtt[t])
                if t >= 1:
                    p.wait("dv", "dv", dvfin[t - 1])
                p.dv(lambda e, t=t: e.tensor_mul(out=wt[:, :], in0=dtt[:, t % 2, :],
                                                 in1=xconv[:, t, :]))
                p.chain("dv")
                p.dv(lambda e, dBx=dBx: e.tensor_tensor(
                    out=dBx[:],
                    in0=wt[:, :].unsqueeze(1).broadcast_to([128, 16, HC]),
                    in1=Bb[:], op=ALU.mult))
                p.chain("dv")
                p.wait("dv", "ac", acfin[t])
                p.dv(lambda e, t=t, dAb=dAb: e.tensor_mul(out=yt[:, 0:16], in0=dAb[:, :, 0],
                                                 in1=hcarry[:, t, :]))
                p.chain("dv")
                p.dv(lambda e, dBx=dBx: e.tensor_add(out=dBx[:, :, 0], in0=dBx[:, :, 0],
                                            in1=yt[:, 0:16]))
                p.chain("dv")
                p.dv(lambda e, dAb=dAb: e.memset(dAb[:, :, 0], 0.0))
                p.chain("dv")
                p.dv(lambda e, dAb=dAb, dBx=dBx: e.tensor_tensor_scan(
                    out=dBx[:].rearrange("p n l -> p (n l)"),
                    data0=dAb[:].rearrange("p n l -> p (n l)"),
                    data1=dBx[:].rearrange("p n l -> p (n l)"),
                    initial=0.0, op0=ALU.mult, op1=ALU.add))
                dvtts[t] = p.inc("dv")
                p.wait("dv", "dv", dvtts[t])
                p.dv(lambda e, t=t, dBx=dBx: e.tensor_copy(hcarry[:, t, :], dBx[:, :, HC - 1]))
                p.chain("dv")
                p.dv(lambda e, dBx=dBx: e.tensor_tensor(
                    out=dBx[:], in0=dBx[:], in1=Cb[:], op=ALU.mult))
                p.chain("dv")
                p.dv(lambda e, dBx=dBx: e.tensor_add(
                    out=yah[:], in0=dBx[:, 0:8, :], in1=dBx[:, 8:16, :]))
                p.chain("dv")
                p.dv(lambda e: e.tensor_add(out=tt1[:], in0=yah[:, 0:4, :],
                                            in1=yah[:, 4:8, :]))
                p.chain("dv")
                p.dv(lambda e: e.tensor_add(out=tt2[:], in0=tt1[:, 0:2, :],
                                            in1=tt1[:, 2:4, :]))
                p.chain("dv")
                p.dv(lambda e: e.tensor_add(out=yt[:, :], in0=tt2[:, 0, :],
                                            in1=tt2[:, 1, :]))
                p.chain("dv")
                p.dv(lambda e, t=t: e.scalar_tensor_tensor(
                    out=yt[:, :], in0=xconv[:, t, :], scalar=Dp[:, t:t + 1],
                    in1=yt[:, :], op0=ALU.mult, op1=ALU.add))
                p.chain("dv")
                p.dv(lambda e, t=t: e.tensor_mul(out=yg[:, t, :], in0=yt[:, :],
                                                 in1=zsil[:, t, :]))
                dvfin[t] = p.inc("dv")
            p.bar()

            def op_slab(w):
                return wbuf[:, w % 2, 0:16 * 128].rearrange("p (k c) -> p k c", k=16)

            opld = {}
            oppe = {}
            opdv = {-2: 0, -1: 0}
            def op_load(w):
                p.wait("sy", "pe", oppe.get(w - 2, 0))
                p.wait("gq", "pe", oppe.get(w - 2, 0))
                opld[w] = wload(op_slab(w), "w_outT", DI, 128 * w, 128, par=w % 2)
            op_load(0)
            op_load(1)
            for w in range(8):
                p.wait("pe", opld[w][0], opld[w][1])
                p.wait("pe", "dv", opdv[w - 2])
                slab = op_slab(w)
                for kt in range(16):
                    mm(BANKS[w % 2][:, 0:HC], slab[:, kt, 0:128],
                       yg[:, kt, :], start=(kt == 0), stop=(kt == 15))
                oppe[w] = p.inc("pe")
                if w + 2 < 8:
                    op_load(w + 2)
                p.wait("dv", "pe", oppe[w])
                p.dv(lambda e, w=w, h0=h0, b=BANKS[w % 2]: e.tensor_add(
                    out=px[:, w, h0:h0 + HC], in0=px[:, w, h0:h0 + HC],
                    in1=b[:, 0:HC]))
                opdv[w] = p.inc("dv")
            p.bar()

        if STAGE == 1:
            for t in range(NPT):
                p.sy(lambda e, t=t, c0=c0: e.dma_start(out=out_r[:, t, c0:c0 + AC],
                                                       in_=px[:, t, :]))
            p.bar()
            continue

        ln_stats(lambda t: px[:, t, :], AC)
        for t in range(NPT):
            p.dv(lambda e, t=t: e.tensor_sub(out=tf[:, 0, :], in0=px[:, t, :],
                                             in1=pb0[:, 0:AC]))
            p.bar()
            p.dv(lambda e, t=t: e.tensor_mul(out=pxi[:, t, 1:1 + AC],
                                             in0=tf[:, 0, :], in1=pb1[:, 0:AC]))
            p.bar()
            p.ac(lambda e, t=t: e.copy(out=pxib[:, t, 1:1 + AC],
                                       in_=pxi[:, t, 1:1 + AC]))
        p.bar()
        for i, nm in enumerate(("rcq", "rsq", "rck", "rsk")):
            p.sy(lambda e, i=i, nm=nm, c0=c0: e.dma_start(
                out=rott[:, i, :], in_=D[nm][:, c0:c0 + AC]))
        p.bar()

        nlayers = 1 if STAGE in (2, 3) else 2
        for l in range(nlayers):
            ring = c % 2
            prev_ring = (c + 1) % 2
            kring = kringT[l]
            vtring = vtringT[l]

            st_ = 2 * l
            p.dv(lambda e, st_=st_: e.tensor_copy(pxib[:, :, 0], scarry[:, st_, :]))
            p.chain("dv")
            p.dv(lambda e, st_=st_: e.tensor_copy(scarry[:, st_, :], pxib[:, :, AC]))
            p.bar()

            def qkv_rhs(kt):
                return pxib[:, kt, 1:1 + AC] if kt < NPT // 2 else pxib[:, kt, 0:AC]

            def l2norm_rotary(isq, l=l, kring=kring, ring=ring):
                qac = {}
                qpe = {}
                for t in range(4):
                    if t >= 2:
                        p.wait("ac", "pe", qpe[t - 2])
                    p.ac(lambda e, t=t: e.activation(out=sqb[:, t % 2, :],
                                                     in_=qraw[:, t, :], func=AF.Square))
                    qac[t] = p.inc("ac")
                    p.wait("pe", "ac", qac[t])
                    mm(pb0[0:8, 0:AC], bdones[:, t, :], sqb[:, t % 2, :],
                       start=(t == 0), stop=(t == 3))
                    qpe[t] = p.inc("pe")
                p.wait("ac", "pe", qpe[3])
                p.ac(lambda e: e.activation(out=nrm[:, 0, :], in_=pb0[0:8, 0:AC],
                                            func=AF.Ln))
                p.chain("ac")
                p.ac(lambda e: e.activation(out=nrm[:, 1, :], in_=nrm[:, 0, :],
                                            func=AF.Exp, scale=cvals[0:8, 1:2]))
                nac = p.inc("ac")
                p.wait("dv", "ac", nac)
                p.dv(lambda e: e.tensor_scalar_min(out=nrm[:, 1, :], in0=nrm[:, 1, :],
                                                   scalar1=1e12))
                ndv = p.inc("dv")
                p.wait("ac", "dv", ndv)
                p.ac(lambda e: e.copy(out=nrmb[:, :], in_=nrm[:, 1, :]))
                nac = p.inc("ac")
                sc = qs8 if isq else ks1
                ct, stt_ = (0, 1) if isq else (2, 3)
                bcd = {}
                rpd = {}
                sdv = {}
                rdv = {}
                p.wait("pe", "ac", nac)
                for t in range(4):
                    bk = (pb1, pb2)[t % 2]
                    if t >= 2:
                        p.wait("pe", "dv", rdv[t - 2])
                    mm(bk[:, 0:AC], sel8[:, 128 * t:128 * (t + 1)], nrmb[:, :],
                       start=True)
                    bcd[t] = p.inc("pe")
                    p.wait("dv", "pe", bcd[t])
                    p.dv(lambda e, t=t, sc=sc, b=bk, l=l: e.scalar_tensor_tensor(
                        out=q2b[:, t, :], in0=qraw[:, t, :], scalar=sc[:, l:l + 1],
                        in1=b[:, 0:AC], op0=ALU.mult, op1=ALU.mult))
                    sdv[t] = p.inc("dv")
                    p.wait("pe", "dv", sdv[t])
                    mm(bk[:, 0:AC], rotP[:], q2b[:, t, :], start=True)
                    rpd[t] = p.inc("pe")
                    p.dv(lambda e, t=t, ct=ct: e.tensor_mul(
                        out=tf[:, 0, :], in0=q2b[:, t, :], in1=rott[:, ct, :]))
                    p.chain("dv")
                    p.wait("dv", "pe", rpd[t])
                    p.dv(lambda e, t=t, stt_=stt_, b=bk: e.tensor_mul(
                        out=tf[:, 1, :], in0=b[:, 0:AC], in1=rott[:, stt_, :]))
                    rdv[t] = p.inc("dv")
                    p.wait("dv", "dv", rdv[t])
                    if isq:
                        p.dv(lambda e, t=t: e.tensor_add(
                            out=q3b[:, t, :], in0=tf[:, 0, :], in1=tf[:, 1, :]))
                    else:
                        p.dv(lambda e, t=t, ring=ring, kring=kring: e.tensor_add(
                            out=kring[:, t, ring * AC:ring * AC + AC],
                            in0=tf[:, 0, :], in1=tf[:, 1, :]))
                    p.chain("dv")
                p.bar()

            def wq_slab(w):
                return wbuf[:, w % 2, 0:NPT * 256].rearrange("p (k c) -> p k c", k=NPT)

            qld = {}
            qpe_ = {}
            qac_ = {-2: 0, -1: 0}
            def q_load(w):
                p.wait("sy", "pe", qpe_.get(w - 2, 0))
                p.wait("gq", "pe", qpe_.get(w - 2, 0))
                qld[w] = wload(wq_slab(w), f"wqkv{l}", DM, 256 * w, 256, par=w % 2)
            q_load(0)
            q_load(1)
            for w in range(6):
                p.wait("pe", qld[w][0], qld[w][1])
                p.wait("pe", "ac", qac_[w - 2])
                slab = wq_slab(w)
                for kt in range(NPT):
                    for oi in range(2):
                        mm(BANKS[2 * (w % 2) + oi],
                           slab[:, kt, 128 * oi:128 * (oi + 1)],
                           qkv_rhs(kt), start=(kt == 0), stop=(kt == NPT - 1))
                qpe_[w] = p.inc("pe")
                if w + 2 < 6:
                    q_load(w + 2)
                p.wait("ac", "pe", qpe_[w])
                for oi in range(2):
                    ot = 2 * w + oi
                    bk = BANKS[2 * (w % 2) + oi]
                    if ot < 8:
                        p.ac(lambda e, ot=ot, b=bk: e.copy(
                            out=qraw[:, ot % 4, :], in_=b[:, 0:AC]))
                    else:
                        p.ac(lambda e, ot=ot, b=bk: e.copy(
                            out=vbf[:, ot % 4, :], in_=b[:, 0:AC]))
                qac_[w] = p.inc("ac")
                if w == 1:
                    p.bar()
                    l2norm_rotary(True)
                elif w == 3:
                    p.bar()
                    l2norm_rotary(False)
            p.bar()

            for hd in range(HEADS):
                pt_, off = hd // 2, (hd % 2) * 64
                for jt in range(4):
                    p.pe(lambda e, pt_=pt_, off=off, jt=jt: e.transpose(
                        out=pbt[:, 64 * jt:64 * jt + 64],
                        in_=vbf[off:off + 64, pt_, 128 * jt:128 * (jt + 1)],
                        identity=identD[off:off + 64, :]))
                p.bar()
                p.ac(lambda e, hd=hd, ring=ring, vtring=vtring: e.copy(
                    out=vtring[:, hd, 4 * ring:4 * ring + 4, :].rearrange(
                        "p a b -> p (a b)"),
                    in_=pbt[:, 0:256]))
                p.bar()

            items = [(hd, ib) for hd in range(HEADS) for ib in range(4)]
            SIMR = (psA0, psA1)
            pesim = {}; dvmax = {}; acexp = {}; dvren = {}
            petr = {}; acev = {}; peav = {}; acat = {-1: 0}
            jts_of = lambda ib: (list(range(ib, ib + 5)) if c > 0
                                 else list(range(4, 5 + ib)))

            def emit_sim(i):
                hd, ib = items[i]
                pt_, off = hd // 2, (hd % 2) * 64
                sim = SIMR[i % 2]
                jts = jts_of(ib)
                if i >= 2:
                    p.wait("pe", "ac", acexp[i - 2])
                qblk = q3b[off:off + 64, pt_, 128 * ib:128 * (ib + 1)]
                for k, jt in enumerate(jts):
                    r = ring if jt >= 4 else prev_ring
                    colbase = r * AC + 128 * (jt % 4)
                    mm(sim[:, 128 * k:128 * (k + 1)], qblk,
                       kring[off:off + 64, pt_, colbase:colbase + 128], start=True)
                pesim[i] = p.inc("pe")

            emit_sim(0)
            emit_sim(1)
            nI = len(items)
            for i in range(nI + 1):
                if i < nI:
                    hd, ib = items[i]
                    pt_, off = hd // 2, (hd % 2) * 64
                    sim = SIMR[i % 2]
                    attn = attnP[i % 2]
                    sA = sAB[:, 4 * (i % 2):4 * (i % 2) + 4]
                    nj = len(jts_of(ib))
                    p.wait("dv", "pe", pesim[i])
                    if i >= 2:
                        p.wait("dv", "ac", acexp[i - 2])
                    if c > 0:
                        p.dv(lambda e, sim=sim: e.tensor_tensor(
                            out=sim[:, 0:128], in0=sim[:, 0:128], in1=m0[:],
                            op=ALU.add))
                        p.chain("dv")
                    p.dv(lambda e, sim=sim, nj=nj: e.tensor_tensor(
                        out=sim[:, 128 * (nj - 1):128 * nj],
                        in0=sim[:, 128 * (nj - 1):128 * nj], in1=m4[:], op=ALU.add))
                    p.chain("dv")
                    p.dv(lambda e, sim=sim, nj=nj, sA=sA: e.reduce_max(
                        out=sA[:, 0:1], in_=sim[:, 0:128 * nj], axis=AX.X,
                        negate=True))
                    dvmax[i] = p.inc("dv")
                if i >= 1:
                    j = i - 1
                    hdj, ibj = items[j]
                    ptj, offj = hdj // 2, (hdj % 2) * 64
                    attnj = attnP[j % 2]
                    attnTj = attnTP[j % 2]
                    sAj = sAB[:, 4 * (j % 2):4 * (j % 2) + 4]
                    njj = len(jts_of(ibj))
                    jtsj = jts_of(ibj)
                    p.wait("dv", "ac", acexp[j])
                    p.dv(lambda e, sA=sAj: e.reciprocal(out=sA[:, 2:3], in_=sA[:, 1:2]))
                    p.chain("dv")
                    p.dv(lambda e, nj=njj, attn=attnj, sA=sAj: e.tensor_scalar_mul(
                        out=attn[:, 0:128 * nj], in0=attn[:, 0:128 * nj],
                        scalar1=sA[:, 2:3]))
                    dvren[j] = p.inc("dv")
                    p.wait("pe", "dv", dvren[j])
                    if j >= 1:
                        p.wait("pe", "ac", acev[j - 1])
                    for k in range(njj):
                        p.pe(lambda e, k=k, attn=attnj: e.transpose(
                            out=pbt[:, 128 * k:128 * (k + 1)],
                            in_=attn[:, 128 * k:128 * (k + 1)], identity=ident[:]))
                    petr[j] = p.inc("pe")
                    p.wait("ac", "pe", petr[j])
                    if j >= 2:
                        p.wait("ac", "pe", peav[j - 2])
                    p.ac(lambda e, nj=njj, attnT=attnTj: e.copy(
                        out=attnT[:, 0:128 * nj], in_=pbt[:, 0:128 * nj]))
                    acev[j] = p.inc("ac")
                    p.wait("pe", "ac", acev[j])
                    p.wait("pe", "ac", acat[j - 1])
                    for k, jt in enumerate(jtsj):
                        slot = (ring if jt >= 4 else prev_ring) * 4 + (jt % 4)
                        mm(pb2[offj:offj + 64, 0:128], vtring[:, hdj, slot, :],
                           attnTj[:, 128 * k:128 * (k + 1)], start=(k == 0),
                           stop=(k == njj - 1), tile_position=(0, offj))
                    peav[j] = p.inc("pe")
                    p.wait("ac", "pe", peav[j])
                    p.ac(lambda e, off=offj, pt_=ptj, ib=ibj: e.copy(
                        out=atno[off:off + 64, pt_, 128 * ib:128 * (ib + 1)],
                        in_=pb2[off:off + 64, 0:128]))
                    acat[j] = p.inc("ac")
                if i < nI:
                    p.wait("ac", "dv", dvmax[i])
                    if i >= 2:
                        p.wait("ac", "pe", petr[i - 2])
                    p.ac(lambda e, sim=sim, nj=nj, attn=attn, sA=sA: e.activation(
                        out=attn[:, 0:128 * nj], in_=sim[:, 0:128 * nj], func=AF.Exp,
                        bias=sA[:, 0:1], accum_out=sA[:, 1:2]))
                    acexp[i] = p.inc("ac")
                    if i + 2 < nI:
                        emit_sim(i + 2)
            p.bar()

            def wo_slab(w):
                return wbuf[:, w % 2, 0:4 * 256].rearrange("p (k c) -> p k c", k=4)

            wold = {}
            wope = {}
            wodv = {-2: 0, -1: 0}
            def wo_load(w):
                p.wait("sy", "pe", wope.get(w - 2, 0))
                p.wait("gq", "pe", wope.get(w - 2, 0))
                wold[w] = wload(wo_slab(w), f"wo{l}", INNER, 256 * w, 256, par=w % 2)
            wo_load(0)
            wo_load(1)
            for w in range(4):
                p.wait("pe", wold[w][0], wold[w][1])
                p.wait("pe", "dv", wodv[w - 2])
                slab = wo_slab(w)
                for kt in range(4):
                    for oi in range(2):
                        mm(BANKS[2 * (w % 2) + oi],
                           slab[:, kt, 128 * oi:128 * (oi + 1)],
                           atno[:, kt, :], start=(kt == 0), stop=(kt == 3))
                wope[w] = p.inc("pe")
                if w + 2 < 4:
                    wo_load(w + 2)
                p.wait("dv", "pe", wope[w])
                p.dv(lambda e, b=BANKS[2 * (w % 2)], ot=2 * w: e.tensor_add(
                    out=pxi[:, ot, 1:1 + AC], in0=pxi[:, ot, 1:1 + AC],
                    in1=b[:, 0:AC]))
                p.chain("dv")
                p.dv(lambda e, b=BANKS[2 * (w % 2) + 1], ot=2 * w + 1: e.tensor_add(
                    out=pxi[:, ot, 1:1 + AC], in0=pxi[:, ot, 1:1 + AC],
                    in1=b[:, 0:AC]))
                wodv[w] = p.inc("dv")
            p.bar()
            for ot in range(NPT):
                p.ac(lambda e, ot=ot: e.copy(out=pxib[:, ot, 1:1 + AC],
                                             in_=pxi[:, ot, 1:1 + AC]))
            p.bar()
            if STAGE == 2:
                break

            st_ = 2 * l + 1
            p.dv(lambda e, st_=st_: e.tensor_copy(pxib[:, :, 0], scarry[:, st_, :]))
            p.chain("dv")
            p.dv(lambda e, st_=st_: e.tensor_copy(scarry[:, st_, :], pxib[:, :, AC]))
            p.bar()

            def sh_ap(t):
                return pxib[:, t, 1:1 + AC] if t < NPT // 2 else pxib[:, t, 0:AC]

            ln_stats(sh_ap, AC)
            for t in range(NPT):
                p.dv(lambda e, t=t: e.tensor_sub(out=tf[:, 0, :], in0=sh_ap(t),
                                                 in1=pb0[:, 0:AC]))
                p.bar()
                p.dv(lambda e, t=t: e.tensor_mul(out=tf[:, 1, :], in0=tf[:, 0, :],
                                                 in1=pb1[:, 0:AC]))
                p.bar()
                p.dv(lambda e, t=t, l=l: e.tensor_scalar(
                    out=ln2o[:, t, :], in0=tf[:, 1, :], scalar1=lng[:, l, t:t + 1],
                    scalar2=lnb[:, l, t:t + 1], op0=ALU.mult, op1=ALU.add))
                p.bar()

            def w1_slab(w):
                return wbuf[:, w % 2, 0:NPT * 256].rearrange("p (k c) -> p k c", k=NPT)

            w1ld = {}
            w1pe = {}
            w1ac = {}
            w1dv = {-2: 0, -1: 0}
            def w1_load(w):
                p.wait("sy", "pe", w1pe.get(w - 2, 0))
                p.wait("gq", "pe", w1pe.get(w - 2, 0))
                w1ld[w] = wload(w1_slab(w), f"w1p{l}", DM, 256 * w, 256, par=w % 2)
            w1_load(0)
            w1_load(1)
            for w in range(NHT):
                p.wait("pe", w1ld[w][0], w1ld[w][1])
                p.wait("pe", "dv", w1dv[w - 2])
                slab = w1_slab(w)
                for kt in range(NPT):
                    for oi in range(2):
                        mm(BANKS[2 * (w % 2) + oi],
                           slab[:, kt, 128 * oi:128 * (oi + 1)],
                           ln2o[:, kt, :], start=(kt == 0), stop=(kt == NPT - 1))
                w1pe[w] = p.inc("pe")
                if w + 2 < NHT:
                    w1_load(w + 2)
                p.wait("ac", "pe", w1pe[w])
                gb_ = BANKS[2 * (w % 2) + 1]
                ab_ = BANKS[2 * (w % 2)]
                p.ac(lambda e, g_=gb_, w=w, l=l: e.activation(
                    out=gel[:, w % 2, :], in_=g_[:, 0:AC], func=AF.Gelu,
                    bias=b1g[:, l, w:w + 1]))
                w1ac[w] = p.inc("ac")
                p.wait("dv", "ac", w1ac[w])
                p.dv(lambda e, a_=ab_, w=w, l=l: e.scalar_tensor_tensor(
                    out=gated[:, w, :], in0=a_[:, 0:AC], scalar=b1a[:, l, w:w + 1],
                    in1=gel[:, w % 2, :], op0=ALU.add, op1=ALU.mult))
                w1dv[w] = p.inc("dv")
            p.bar()

            def w2_slab(i):
                return wbuf[:, i % 2, 0:11 * 128].rearrange("p (k c) -> p k c", k=11)

            seq2 = [(w, gpr) for w in range(8) for gpr in range(2)]
            w2ld = {}
            w2pe = {}
            w2dv = {-2: 0, -1: 0, 0: 0, 1: 0}
            def w2_load(i):
                w_, g_ = seq2[i]
                p.wait("sy", "pe", w2pe.get(i - 2, 0))
                p.wait("gq", "pe", w2pe.get(i - 2, 0))
                w2ld[i] = wload(w2_slab(i), f"w2p{l}", HIPD_ if False else HIDP,
                                128 * w_, 128, rt0=11 * g_, rtn=11, par=i % 2)
            w2_load(0)
            w2_load(1)
            for i in range(16):
                w, gpr = seq2[i]
                p.wait("pe", w2ld[i][0], w2ld[i][1])
                p.wait("pe", "dv", w2dv[i - 2])
                slab = w2_slab(i)
                for kk in range(11):
                    kt = 11 * gpr + kk
                    mm(BANKS[4 + w % 2], slab[:, kk, 0:128], gated[:, kt, :],
                       start=(kt == 0), stop=(kt == NHT - 1))
                w2pe[i] = p.inc("pe")
                if i + 2 < 16:
                    w2_load(i + 2)
                if gpr == 1:
                    p.wait("dv", "pe", w2pe[i])
                    p.dv(lambda e, w=w, l=l, b=BANKS[4 + w % 2]:
                         e.scalar_tensor_tensor(
                             out=pxi[:, w, 1:1 + AC], in0=b[:, 0:AC],
                             scalar=b2s[:, l, w:w + 1], in1=pxi[:, w, 1:1 + AC],
                             op0=ALU.add, op1=ALU.add))
                    w2dv[i] = p.inc("dv")
                else:
                    w2dv[i] = w2dv[i - 1]
            p.bar()
            if l == 0 and nlayers > 1:
                for w in range(NPT):
                    p.ac(lambda e, w=w: e.copy(out=pxib[:, w, 1:1 + AC],
                                               in_=pxi[:, w, 1:1 + AC]))
                p.bar()

        for t in range(NPT):
            p.dv(lambda e, t=t: e.tensor_add(out=px[:, t, :], in0=px[:, t, :],
                                             in1=pxi[:, t, 1:1 + AC]))
        p.bar()
        for t in range(NPT):
            p.sy(lambda e, t=t, c0=c0: e.dma_start(out=out_r[:, t, c0:c0 + AC],
                                                   in_=px[:, t, :]))
        p.bar()

    p.wait("sy", "sy", p.cnt["sy"])

    with nc.Block() as block:
        p.materialize(block)
    nc._keepalive_ctx = ctx
    return nc


def kernel(_ncores=NCORE, **inputs):
    g = _host_prep(inputs)
    x = inputs["L0_attn_x"]
    nc = build()
    in_maps = []
    for cix in range(_ncores):
        m = {"xT": np.ascontiguousarray(np.asarray(x[cix]).T)}
        m.update(g)
        in_maps.append(m)
    res = run_bass_kernel_spmd(nc, in_maps, core_ids=list(range(_ncores)))
    outs = [np.asarray(r["out"]).T for r in res.results]
    return np.ascontiguousarray(np.stack(outs, 0))


# version 07
# speedup vs baseline: 1.0274x; 1.0274x over previous
import numpy as np
import ml_dtypes
from contextlib import ExitStack

import concourse.bass as bass
import concourse.mybir as mybir
from concourse.bass_utils import run_bass_kernel_spmd

F32 = mybir.dt.float32
BF16 = mybir.dt.bfloat16
AF = mybir.ActivationFunctionType
ALU = mybir.AluOpType
AX = mybir.AxisListType

BF = ml_dtypes.bfloat16

NCORE = 8
LSEQ = 4096
DM = 1024
DI = 2048
NST = 16
DTR = 64
NPT = 8
AC = 512
HC = 256
NCH = LSEQ // AC
HEADS = 8
DH = 64
INNER = 512
HID = 2730
HIDP = 2816
NHT = 22
NEG = -1e30

NCHRUN = NCH
STAGE = 4


def _pp(v):
    return np.ascontiguousarray(np.asarray(v, np.float32).reshape(-1, 128).T)


def _pp3(a):
    a = np.asarray(a, np.float32)
    F, K = a.shape
    return np.ascontiguousarray(a.reshape(F // 128, 128, K).transpose(1, 0, 2))


def _host_prep(inp):
    g = {}
    g["w_inT"] = np.ascontiguousarray(inp["L0_attn_in_proj_w"].T)
    g["w_xpT"] = np.ascontiguousarray(inp["L0_attn_x_proj_w"].T)
    g["w_dtT"] = np.ascontiguousarray(inp["L0_attn_dt_proj_w"].T)
    g["w_outT"] = np.ascontiguousarray(inp["L0_attn_out_proj_w"].T)
    g["convw"] = _pp3(inp["L0_attn_conv_w"][:, 0, :])
    cw = _pp3(inp["L0_attn_conv_w"][:, 0, :])
    cd = np.zeros((16, 4, 128, 128), np.float32)
    ii128 = np.arange(128)
    for t in range(16):
        for k in range(4):
            cd[t, k, ii128, ii128] = cw[:, t, k]
    g["convdiag"] = np.ascontiguousarray(cd.reshape(16 * 4 * 128, 128))
    g["convb"] = _pp(inp["L0_attn_conv_b"])
    g["dtb"] = _pp(inp["L0_attn_dt_proj_b"])
    g["Alog"] = _pp3(inp["L0_attn_A_log"])
    g["Dp"] = _pp(inp["L0_attn_D_param"])
    for l in (0, 1):
        g[f"wqkv{l}"] = np.ascontiguousarray(inp["L0_ffn_wqkv"][l])
        g[f"wo{l}"] = np.ascontiguousarray(inp["L0_ffn_wo"][l])
        w1 = inp["L0_ffn_w1"][l]
        a = np.zeros((DM, HIDP), np.float32); a[:, :HID] = w1[:, :HID]
        gg = np.zeros((DM, HIDP), np.float32); gg[:, :HID] = w1[:, HID:]
        at = a.reshape(DM, NHT, 128); gt = gg.reshape(DM, NHT, 128)
        g[f"w1p{l}"] = np.ascontiguousarray(
            np.stack([at, gt], axis=2).reshape(DM, 2 * NHT * 128))
        w2 = np.zeros((HIDP, DM), np.float32); w2[:HID] = inp["L0_ffn_w2"][l]
        g[f"w2p{l}"] = w2
        b1 = inp["L0_ffn_b1"][l]
        b1a = np.zeros(HIDP, np.float32); b1a[:HID] = b1[:HID]
        b1g = np.zeros(HIDP, np.float32); b1g[:HID] = b1[HID:]
        g[f"b1a{l}"] = _pp(b1a); g[f"b1g{l}"] = _pp(b1g)
    g["qs"] = np.ascontiguousarray(np.stack(
        [np.tile(np.asarray(inp["L0_ffn_q_scale"][l], np.float32), 2) for l in (0, 1)], 1))
    g["ks"] = np.ascontiguousarray(np.stack(
        [np.tile(np.asarray(inp["L0_ffn_k_scale"][l], np.float32), 2) for l in (0, 1)], 1))
    g["lng"] = np.ascontiguousarray(np.stack([_pp(inp["L0_ffn_ln_g"][l]) for l in (0, 1)], 1))
    g["lnb"] = np.ascontiguousarray(np.stack([_pp(inp["L0_ffn_ln_b"][l]) for l in (0, 1)], 1))
    g["b2"] = np.ascontiguousarray(np.stack([_pp(inp["L0_ffn_b2"][l]) for l in (0, 1)], 1))

    ii = np.arange(128)
    g["m0"] = np.where(ii[None, :] < ii[:, None], NEG, 0.0).astype(BF)
    g["m4"] = np.where(ii[None, :] > ii[:, None], NEG, 0.0).astype(BF)
    g["ident"] = np.eye(128, dtype=BF)
    identD = np.zeros((128, 64), np.float32)
    for p_ in range(128):
        identD[p_, p_ % 64] = 1.0
    g["identD"] = identD.astype(BF)
    g["onescol"] = np.ones((1, 128), dtype=BF)
    g["onek"] = np.ones((128, 1), dtype=BF)
    g["cvals"] = np.tile(np.array([[1e-5, -0.5, 1.0 / DM, 8.0, -1.0, 1.0]], np.float32), (128, 1))
    bd = np.zeros((128, 4, 8), np.float32)
    for t in range(4):
        bd[:64, t, 2 * t] = 1.0; bd[64:, t, 2 * t + 1] = 1.0
    g["bdones"] = bd.astype(BF)
    sel8 = np.zeros((8, 4, 128), np.float32)
    for t in range(4):
        for p_ in range(128):
            sel8[2 * t + (p_ >= 64), t, p_] = 1.0
    g["sel8"] = np.ascontiguousarray(sel8.reshape(8, 512)).astype(BF)
    sel16 = np.zeros((16, 16, 128), np.float32)
    for n in range(16):
        sel16[n, n, :] = 1.0
    g["sel16"] = np.ascontiguousarray(sel16.reshape(16, 2048)).astype(BF)
    P = np.zeros((128, 128), np.float32)
    for di in range(128):
        h0 = (di // 64) * 64; d = di % 64
        if d < 32:
            P[h0 + d + 32, di] = -1.0
        else:
            P[h0 + d - 32, di] = 1.0
    g["rotP"] = P.astype(BF)

    dih = np.arange(64); j = dih % 32
    invfreq = 1.0 / 10000.0 ** ((2 * j) / DH)
    t = np.arange(LSEQ, dtype=np.float64)
    freqs = t[None, :] * invfreq[:, None]
    base = ((2 * j) + 0.4 * DH) / (1.4 * DH)
    power = (t - LSEQ // 2) / float(AC // 2)
    scale = base[:, None] ** power[None, :]
    cos = np.cos(freqs); sin = np.sin(freqs)
    rows2 = lambda m: np.ascontiguousarray(np.concatenate([m, m], 0).astype(BF))
    g["rcq"] = rows2(cos * scale); g["rsq"] = rows2(sin * scale)
    g["rck"] = rows2(cos / scale); g["rsk"] = rows2(sin / scale)
    return g


ENGINES = ("sy", "gq", "pe", "ac", "dv")
SEMS = ("sy", "gq", "pe", "ac", "dv", "gc", "wl0", "wl1", "wg0", "wg1", "wb0", "wb1")


class Prog:
    def __init__(self, nc):
        self.nc = nc
        self.prog = {e: [] for e in ENGINES}
        self.iswait = {e: [] for e in ENGINES}
        self.cnt = {e: 0 for e in SEMS}
        self.dirty = {e: False for e in SEMS}
        self.snap = dict(self.cnt)
        self.lastwait = {e: {q: 0 for q in SEMS} for e in ENGINES}
        self.pending = {e: False for e in ENGINES}
        self.sems = None

    def _emit(self, e, fn, compute=False):
        if self.pending[e]:
            self.pending[e] = False
            for q in SEMS:
                self.wait(e, q, self.snap[q])
        self.prog[e].append([fn, []])
        self.iswait[e].append(False)
        if e == "sy" or (e == "gq" and not compute):
            self.prog[e][-1][1] = [(e, 16)]
            self.cnt[e] += 16
        elif e == "gq":
            self.dirty["gc"] = True
        else:
            self.dirty[e] = True

    def sy(self, fn): self._emit("sy", fn)
    def gq(self, fn): self._emit("gq", fn)
    def gc(self, fn): self._emit("gq", fn, compute=True)

    def dma(self, queue, fn, sem):
        if self.pending[queue]:
            self.pending[queue] = False
            for q in SEMS:
                self.wait(queue, q, self.snap[q])
        self.prog[queue].append([fn, [(sem, 16)]])
        self.iswait[queue].append(False)
        self.cnt[sem] += 16
    def pe(self, fn): self._emit("pe", fn)
    def ac(self, fn): self._emit("ac", fn)
    def dv(self, fn): self._emit("dv", fn)

    def inc(self, e):
        stream = "gq" if e == "gc" else e
        assert self.prog[stream] and not self.iswait[stream][-1], e
        self.prog[stream][-1][1].append((e, 1))
        self.cnt[e] += 1
        self.dirty[e] = False
        return self.cnt[e]

    def wait(self, e, q, val):
        if val <= 0 or self.lastwait[e][q] >= val:
            return
        self.lastwait[e][q] = val
        self.prog[e].append(
            [lambda eng, q=q, val=val: eng.wait_ge(self.sems[q], val), []])
        self.iswait[e].append(True)

    def chain(self, e):
        self.inc(e)
        st = "gq" if e == "gc" else e
        self.wait(st, e, self.cnt[e])

    def bar(self):
        for e in ("pe", "ac", "dv", "gc"):
            if self.dirty[e]:
                self.inc(e)
        self.snap = dict(self.cnt)
        for e in ENGINES:
            self.pending[e] = True

    def materialize(self, block):
        sems = self.sems

        def runner(progl):
            def body(eng):
                for fn, incs in progl:
                    inst = fn(eng)
                    for s, n in incs:
                        inst.then_inc(sems[s], n)
            return body

        block.sync(runner(self.prog["sy"]))
        block.gpsimd(runner(self.prog["gq"]))
        block.tensor(runner(self.prog["pe"]))
        block.scalar(runner(self.prog["ac"]))
        block.vector(runner(self.prog["dv"]))


def build(debug=False):
    nc = bass.Bass("TRN2", target_bir_lowering=False)
    ctx = ExitStack()

    D = {}
    def din(name, shape, dt=F32):
        D[name] = nc.dram_tensor(name, list(shape), dt, kind="ExternalInput").ap()
    din("xT", (DM, LSEQ))
    din("w_inT", (DM, 2 * DI)); din("w_xpT", (DI, 96)); din("w_dtT", (DTR, DI))
    din("w_outT", (DI, DM))
    din("convw", (128, 16, 4)); din("convdiag", (16 * 4 * 128, 128)); din("convb", (128, 16)); din("dtb", (128, 16))
    din("Alog", (128, 16, 16)); din("Dp", (128, 16))
    for l in (0, 1):
        din(f"wqkv{l}", (DM, 3 * INNER)); din(f"wo{l}", (INNER, DM))
        din(f"w1p{l}", (DM, 2 * HIDP)); din(f"w2p{l}", (HIDP, DM))
        din(f"b1a{l}", (128, NHT)); din(f"b1g{l}", (128, NHT))
    din("qs", (128, 2)); din("ks", (128, 2))
    din("lng", (128, 2, 8)); din("lnb", (128, 2, 8)); din("b2", (128, 2, 8))
    din("m0", (128, 128), BF16); din("m4", (128, 128), BF16)
    din("ident", (128, 128), BF16); din("identD", (128, 64), BF16)
    din("onescol", (1, 128), BF16); din("onek", (128, 1), BF16)
    din("cvals", (128, 6))
    din("sel8", (8, 512), BF16); din("sel16", (16, 2048), BF16)
    din("bdones", (128, 4, 8), BF16)
    din("rotP", (128, 128), BF16)
    for nm in ("rcq", "rsq", "rck", "rsk"):
        din(nm, (128, LSEQ), BF16)
    out = nc.dram_tensor("out", [DM, LSEQ], F32, kind="ExternalOutput").ap()
    out_r = out.rearrange("(t p) l -> p t l", p=128)
    xT_r = D["xT"].rearrange("(t p) l -> p t l", p=128)

    _nm = [0]
    def sb(shape, dt=F32):
        _nm[0] += 1
        return ctx.enter_context(nc.sbuf_tensor(f"t{_nm[0]}", list(shape), dt))

    px = sb((128, NPT, AC))
    hcarry = sb((128, 16, 16))
    ccarry = sb((128, 16, 4))
    scarry = sb((128, 4, NPT), BF16)
    kringT = [sb((128, 4, 2 * AC), BF16) for _ in (0, 1)]
    vtringT = [sb((128, HEADS, 8, DH), BF16) for _ in (0, 1)]
    m0 = sb((128, 128), BF16); m4 = sb((128, 128), BF16)
    ident = sb((128, 128), BF16); identD = sb((128, 64), BF16)
    onescol = sb((1, 128), BF16); onek = sb((128, 1), BF16)
    cvals = sb((128, 6))
    sel8 = sb((8, 512), BF16); sel16 = sb((16, 2048), BF16)
    bdones = sb((128, 4, 8), BF16)
    rotP = sb((128, 128), BF16)
    Aneg = sb((128, 16, 16)); convw = sb((128, 16, 4))
    convb = sb((128, 16)); dtb = sb((128, 16)); Dp = sb((128, 16))
    qs8 = sb((128, 2)); ks1 = sb((128, 2))
    lng = sb((128, 2, NPT)); lnb = sb((128, 2, NPT)); b2s = sb((128, 2, NPT))
    b1a = sb((128, 2, NHT)); b1g = sb((128, 2, NHT))
    dtW = sb((DTR, DI), BF16)
    xpW = sb((128, 16, 96), BF16)
    dlr = sb((DTR, HC), BF16)
    bcrowB = sb((16, HC), BF16)
    bcrowC = sb((16, HC), BF16)
    rows1 = sb((1, 5 * AC))
    rowsb1 = sb((1, 2 * AC), BF16)
    nrm = sb((8, 2, AC))
    nrmb = sb((8, AC), BF16)
    sAB = sb((128, 8))
    wbuf = sb((128, 2, 2048), BF16)
    cbuf = sb((128, 2, 4, 128), BF16)

    A16N = 29184
    A32N = 13056
    ar16 = sb((128, A16N), BF16)
    ar32 = sb((128, A32N))

    def v16(off, shape):
        n = int(np.prod(shape[1:]))
        ap = ar16[:, off:off + n]
        if len(shape) == 3:
            ap = ap.rearrange("p (a b) -> p a b", a=shape[1])
        return ap, off + n

    def v32(off, shape):
        n = int(np.prod(shape[1:]))
        ap = ar32[:, off:off + n]
        if len(shape) == 3:
            ap = ap.rearrange("p (a b) -> p a b", a=shape[1])
        return ap, off + n

    o = 0
    lnout, o = v16(o, (128, NPT, HC))
    xinner, o = v16(o, (128, 16, HC + 4))
    zsil, o = v16(o, (128, 16, HC))
    xconv, o = v16(o, (128, 16, HC))
    Bb, o = v16(o, (128, 16, HC))
    Cb, o = v16(o, (128, 16, HC))
    wt, o = v16(o, (128, HC))
    ya, o = v16(o, (128, 4, HC))
    yb, o = v16(o, (128, 2, HC))
    yg, o = v16(o, (128, 16, HC))
    assert o <= A16N, o
    dBxA = ar16[:, 2048:2048 + 16 * HC].rearrange("p (a b) -> p a b", a=16)
    dBxP = (dBxA, dBxA)
    yah = ar16[:, 0:8 * HC].rearrange("p (a b) -> p a b", a=8)
    tt1 = ya
    tt2 = yb
    o = 0
    pxib, o = v16(o, (128, NPT, AC + 1))
    rott, o = v16(o, (128, 4, AC))
    q2b, o = v16(o, (128, 4, AC))
    q3b, o = v16(o, (128, 4, AC))
    vbf, o = v16(o, (128, 4, AC))
    sqb, o = v16(o, (128, 2, AC))
    atno, o = v16(o, (128, 4, AC))
    gated, o = v16(o, (128, NHT, AC))
    sqb2, o = v16(o, (128, 2, AC))
    assert o <= A16N, o
    _vb0 = vbf[:, 0:4, :].rearrange("p a b -> p (a b)")
    attnP = (_vb0[:, 0:640], _vb0[:, 640:1280])
    _qb0 = q2b[:, 0:4, :].rearrange("p a b -> p (a b)")
    attnTP = (_qb0[:, 0:640], _qb0[:, 640:1280])
    _q2off = NPT * (AC + 1) + 4 * AC
    ln2o = ar16[:, _q2off:_q2off + NPT * AC].rearrange("p (a b) -> p a b", a=NPT)
    gel = sqb

    o = 0
    dAb0, o = v32(o, (128, 16, HC))
    dAb1, o = v32(o, (128, 16, HC))
    dtt, o = v32(o, (128, 2, HC))
    yt, o = v32(o, (128, HC))
    assert o <= A32N, o
    dABUF = (dAb0, dAb1)
    o = 0
    pxi, o = v32(o, (128, NPT, AC + 1))
    qraw, o = v32(o, (128, 4, AC))
    tf, o = v32(o, (128, 2, AC))
    assert o <= A32N, o

    psA0 = ctx.enter_context(nc.psum_tensor([128, 1024], F32))
    psA1 = ctx.enter_context(nc.psum_tensor([128, 1024], F32))
    pb0 = ctx.enter_context(nc.psum_tensor([128, 512], F32))
    pb1 = ctx.enter_context(nc.psum_tensor([128, 512], F32))
    pb2 = ctx.enter_context(nc.psum_tensor([128, 512], F32))
    pbt = ctx.enter_context(nc.psum_tensor([128, 1024], BF16))
    BANKS = [psA0[:, 0:512], psA0[:, 512:1024],
             psA1[:, 0:512], psA1[:, 512:1024], pb0[:], pb1[:]]

    p = Prog(nc)
    p.sems = {e: ctx.enter_context(nc.semaphore(f"s_{e}")) for e in SEMS}

    def mm(bank_ap, lhsT_ap, rhs_ap, start, stop=None, tile_position=None):
        if stop is None:
            stop = start
        p.pe(lambda e, o_=bank_ap, l_=lhsT_ap, r_=rhs_ap, s_=start, t_=stop,
             tp=tile_position:
             e.matmul(out=o_, lhsT=l_, rhs=r_, start=s_, stop=t_, tile_position=tp))

    SCR16 = {}
    STORED = set()

    LASTWB = {}

    def wload(buf_ap, name, rows_, col0, cols, rt0=0, rtn=None, par=0):
        dram2d = D[name]
        if name not in SCR16:
            SCR16[name] = nc.dram_tensor(
                f"scr_{name}", [dram2d.shape[0], dram2d.shape[1]], BF16,
                kind="Internal").ap()
        scr = SCR16[name].rearrange("(t p) c -> p t c", p=128)
        src = dram2d.rearrange("(t p) c -> p t c", p=128)
        nt = rtn if rtn is not None else rows_ // 128
        step = 4
        key = (name, col0, rt0)
        first = key not in STORED
        STORED.add(key)
        wsem = f"wg{par}" if first else f"wl{par}"
        p.wait("gq" if first else "sy", f"wb{par}", LASTWB.get(par, 0))
        for tt in range(0, nt, step):
            n = min(step, nt - tt)
            if first:
                p.dma("gq", lambda e, d=buf_ap[:, tt:tt + n, :],
                      s=src[:, rt0 + tt:rt0 + tt + n, col0:col0 + cols]:
                      e.dma_start(out=d, in_=s), wsem)
            else:
                p.dma("sy", lambda e, d=buf_ap[:, tt:tt + n, :],
                      s=scr[:, rt0 + tt:rt0 + tt + n, col0:col0 + cols]:
                      e.dma_start(out=d, in_=s), wsem)
        if first:
            p.wait("sy", wsem, p.cnt[wsem])
            for tt in range(0, nt, step):
                n = min(step, nt - tt)
                p.dma("sy", lambda e, d=scr[:, rt0 + tt:rt0 + tt + n, col0:col0 + cols],
                      s=buf_ap[:, tt:tt + n, :]:
                      e.dma_start(out=d, in_=s), f"wb{par}")
            LASTWB[par] = p.cnt[f"wb{par}"]
        return (wsem, p.cnt[wsem])

    def loadc(dst_ap, src_ap, cast=False):
        (p.gq if cast else p.sy)(
            lambda e, d=dst_ap, s=src_ap: e.dma_start(out=d, in_=s))

    loadc(m0[:], D["m0"]); loadc(m4[:], D["m4"])
    loadc(ident[:], D["ident"]); loadc(identD[:], D["identD"])
    loadc(onescol[:], D["onescol"]); loadc(onek[:], D["onek"])
    loadc(cvals[:], D["cvals"])
    loadc(sel8[:], D["sel8"]); loadc(sel16[:], D["sel16"])
    loadc(bdones[:], D["bdones"]); loadc(rotP[:], D["rotP"])
    loadc(convw[:], D["convw"]); loadc(convb[:], D["convb"])
    loadc(dtb[:], D["dtb"]); loadc(Dp[:], D["Dp"])
    loadc(Aneg[:], D["Alog"])
    loadc(qs8[:], D["qs"]); loadc(ks1[:], D["ks"])
    loadc(lng[:], D["lng"]); loadc(lnb[:], D["lnb"]); loadc(b2s[:], D["b2"])
    for l in (0, 1):
        loadc(b1a[:, l, :], D[f"b1a{l}"]); loadc(b1g[:, l, :], D[f"b1g{l}"])
    loadc(dtW[:], D["w_dtT"], cast=True)
    loadc(xpW[:], D["w_xpT"].rearrange("(t p) c -> p t c", p=128), cast=True)
    p.bar()
    AnegF = Aneg[:].rearrange("p a b -> p (a b)")
    p.ac(lambda e: e.activation(out=AnegF, in_=AnegF, func=AF.Exp))
    p.ac(lambda e: e.activation(out=qs8[:], in_=qs8[:], func=AF.Copy, scale=cvals[:, 3:4]))
    p.dv(lambda e: e.memset(hcarry[:].rearrange("p a b -> p (a b)"), 0.0))
    p.dv(lambda e: e.memset(ccarry[:].rearrange("p a b -> p (a b)"), 0.0))
    p.dv(lambda e: e.memset(scarry[:].rearrange("p a b -> p (a b)"), 0.0))
    for l in (0, 1):
        p.dv(lambda e, l=l: e.memset(
            vtringT[l][:].rearrange("p a b c -> p (a b c)"), 0.0))
    p.bar()
    p.dv(lambda e: e.tensor_scalar_mul(out=AnegF, in0=AnegF, scalar1=-1.0))
    p.bar()

    def seg(k, n):
        return rows1[0:1, k * AC:k * AC + n]

    def ln_post(n):
        p.ac(lambda e, n=n: e.activation(out=seg(0, n), in_=pb0[0:1, 0:n],
                                         func=AF.Copy, scale=cvals[0:1, 2:3]))
        ac0 = p.inc("ac")
        p.wait("dv", "ac", ac0)
        p.dv(lambda e, n=n: e.tensor_mul(out=seg(1, n), in0=seg(0, n), in1=seg(0, n)))
        p.chain("dv")
        p.dv(lambda e, n=n: e.scalar_tensor_tensor(
            out=seg(2, n), in0=pb1[0:1, 0:n], scalar=1.0 / DM,
            in1=seg(1, n), op0=ALU.mult, op1=ALU.subtract))
        dv0 = p.inc("dv")
        p.wait("ac", "dv", dv0)
        p.ac(lambda e, n=n: e.activation(out=seg(3, n), in_=seg(2, n),
                                         func=AF.Ln, bias=cvals[0:1, 0:1]))
        p.chain("ac")
        p.ac(lambda e, n=n: e.activation(out=seg(4, n), in_=seg(3, n),
                                         func=AF.Exp, scale=cvals[0:1, 1:2]))
        p.chain("ac")
        p.ac(lambda e, n=n: e.copy(out=rowsb1[0:1, 0:n], in_=seg(0, n)))
        p.chain("ac")
        p.ac(lambda e, n=n: e.copy(out=rowsb1[0:1, AC:AC + n], in_=seg(4, n)))
        ac1 = p.inc("ac")
        p.wait("pe", "ac", ac1)
        mm(pb0[:, 0:n], onescol[:], rowsb1[0:1, 0:n], start=True)
        mm(pb1[:, 0:n], onescol[:], rowsb1[0:1, AC:AC + n], start=True)
        p.bar()

    def ln_stats(src_ap_fn, n):
        lnac = {}
        for t in range(NPT):
            p.ac(lambda e, t=t, n=n: e.copy(out=sqb[:, 0, 0:n], in_=src_ap_fn(t))
                 if t % 2 == 0 else
                 e.copy(out=sqb[:, 1, 0:n], in_=src_ap_fn(t)))
            p.chain("ac")
            p.ac(lambda e, t=t, n=n: e.activation(out=sqb2[:, 0, 0:n], in_=src_ap_fn(t),
                                                  func=AF.Square)
                 if t % 2 == 0 else
                 e.activation(out=sqb2[:, 1, 0:n], in_=src_ap_fn(t), func=AF.Square))
            lnac[t] = p.inc("ac")
            p.wait("pe", "ac", lnac[t])
            half = t % 2
            mm(pb0[0:1, 0:n], onek[:], sqb[:, half, 0:n], start=(t == 0),
               stop=(t == NPT - 1))
            mm(pb1[0:1, 0:n], onek[:], sqb2[:, half, 0:n], start=(t == 0),
               stop=(t == NPT - 1))
            if t < NPT - 1:
                pef = p.inc("pe")
                p.wait("ac", "pe", pef)
        p.bar()
        ln_post(n)

    for c in range(NCHRUN):
        c0 = c * AC
        for t in range(NPT):
            p.sy(lambda e, t=t, c0=c0: e.dma_start(out=px[:, t, :],
                                                   in_=xT_r[:, t, c0:c0 + AC]))
        p.bar()

        for hh in range(2):
            h0 = hh * HC
            ln_stats(lambda t, h0=h0: px[:, t, h0:h0 + HC], HC)
            for t in range(NPT):
                p.dv(lambda e, t=t, h0=h0: e.tensor_sub(
                    out=tf[:, 0, 0:HC], in0=px[:, t, h0:h0 + HC], in1=pb0[:, 0:HC]))
                p.bar()
                p.dv(lambda e, t=t: e.tensor_mul(out=lnout[:, t, :],
                                                 in0=tf[:, 0, 0:HC], in1=pb1[:, 0:HC]))
                p.bar()

            p.dv(lambda e: e.tensor_copy(xinner[:, :, 1:4], ccarry[:, :, 1:4]))
            p.bar()

            def ip_slab(w):
                return wbuf[:, w % 2, 0:NPT * 256].rearrange("p (k c) -> p k c", k=NPT)

            ipld = {}
            ippe = {}
            ipac = {-2: 0, -1: 0}
            def ip_load(w):
                p.wait("sy", "pe", ippe.get(w - 2, 0))
                p.wait("gq", "pe", ippe.get(w - 2, 0))
                ipld[w] = wload(ip_slab(w), "w_inT", DM, 256 * w, 256, par=w % 2)
            ip_load(0)
            ip_load(1)
            for w in range(16):
                p.wait("pe", ipld[w][0], ipld[w][1])
                p.wait("pe", "ac", ipac[w - 2])
                slab = ip_slab(w)
                for kt in range(NPT):
                    for oi in range(2):
                        mm(BANKS[2 * (w % 2) + oi][:, 0:HC],
                           slab[:, kt, 128 * oi:128 * (oi + 1)],
                           lnout[:, kt, :], start=(kt == 0),
                           stop=(kt == NPT - 1))
                ippe[w] = p.inc("pe")
                if w + 2 < 16:
                    ip_load(w + 2)
                p.wait("ac", "pe", ippe[w])
                for oi in range(2):
                    ot = 2 * w + oi
                    bk = BANKS[2 * (w % 2) + oi]
                    if ot < 16:
                        p.ac(lambda e, b=bk, d=ot: e.copy(
                            out=xinner[:, d, 4:4 + HC], in_=b[:, 0:HC]))
                    else:
                        p.ac(lambda e, b=bk, d=ot - 16: e.activation(
                            out=zsil[:, d, :], in_=b[:, 0:HC], func=AF.Silu))
                ipac[w] = p.inc("ac")
            p.bar()
            p.dv(lambda e: e.tensor_copy(ccarry[:, :, 1:4],
                                         xinner[:, :, 1 + HC:4 + HC]))
            p.bar()

            def op_slab(w):
                return wbuf[:, w % 2, 0:16 * 128].rearrange("p (k c) -> p k c", k=16)
            p.wait("sy", "pe", p.cnt["pe"])
            p.wait("gq", "pe", p.cnt["pe"])
            opld = {0: wload(op_slab(0), "w_outT", DI, 0, 128, par=0),
                    1: wload(op_slab(1), "w_outT", DI, 128, 128, par=1)}

            cvld = {}
            cvpe = {}
            cvac = {-2: 0, -1: 0}
            def cv_load(t):
                p.wait("sy", "pe", cvpe.get(t - 2, 0))
                p.wait("gq", "pe", cvpe.get(t - 2, 0))
                cvld[t] = wload(cbuf[:, t % 2, :, :], "convdiag", 16 * 4 * 128, 0,
                                128, rt0=4 * t, rtn=4, par=t % 2)
            cv_load(0)
            cv_load(1)
            for t in range(16):
                p.wait("pe", cvld[t][0], cvld[t][1])
                p.wait("pe", "ac", cvac[t - 2])
                for k in range(4):
                    mm(BANKS[t % 2][:, 0:HC], cbuf[:, t % 2, k, :],
                       xinner[:, t, 1 + k:1 + k + HC], start=(k == 0), stop=(k == 3))
                cvpe[t] = p.inc("pe")
                if t + 2 < 16:
                    cv_load(t + 2)
                p.wait("ac", "pe", cvpe[t])
                p.ac(lambda e, t=t, b=BANKS[t % 2]: e.activation(
                    out=xconv[:, t, :], in_=b[:, 0:HC], func=AF.Silu,
                    bias=convb[:, t:t + 1]))
                cvac[t] = p.inc("ac")
            p.bar()

            for kt in range(16):
                mm(pb0[0:64, 0:HC], xpW[:, kt, 0:64], xconv[:, kt, :], start=(kt == 0), stop=(kt == 15))
                mm(pb1[0:16, 0:HC], xpW[:, kt, 64:80], xconv[:, kt, :], start=(kt == 0), stop=(kt == 15))
                mm(pb2[0:16, 0:HC], xpW[:, kt, 80:96], xconv[:, kt, :], start=(kt == 0), stop=(kt == 15))
            p.bar()
            p.ac(lambda e: e.copy(out=dlr[:, :], in_=pb0[0:DTR, 0:HC]))
            p.ac(lambda e: e.copy(out=bcrowB[:, :], in_=pb1[0:16, 0:HC]))
            p.ac(lambda e: e.copy(out=bcrowC[:, :], in_=pb2[0:16, 0:HC]))
            p.bar()
            for src_, dst in ((bcrowB, Bb), (bcrowC, Cb)):
                for n4 in range(4):
                    for k4 in range(4):
                        n = 4 * n4 + k4
                        mm(psA0[:, 256 * k4:256 * k4 + HC],
                           sel16[:, 128 * n:128 * (n + 1)], src_[:, :], start=True)
                    p.bar()
                    p.ac(lambda e, n4=n4, dst=dst: e.copy(
                        out=dst[:, 4 * n4:4 * n4 + 4, :],
                        in_=psA0[:].rearrange("p (a b) -> p a b", a=4)))
                    p.bar()

            mm(pb0[:, 0:HC], dtW[:, 0:128], dlr[:, :], start=True)
            p.bar()
            acdtt = {}
            acfin = {}
            dvtts = {}
            dvfin = {}
            pefin = {0: p.cnt["pe"]}
            for t in range(16):
                dAb = dABUF[t % 2]
                dBx = dBxP[t % 2]
                pbank = (pb0, pb1)[t % 2]
                if t < 15:
                    if t >= 1:
                        p.wait("pe", "ac", acdtt[t - 1])
                    mm((pb0, pb1)[(t + 1) % 2][:, 0:HC],
                       dtW[:, 128 * (t + 1):128 * (t + 2)], dlr[:, :], start=True)
                    pefin[t + 1] = p.inc("pe")
                p.wait("ac", "pe", pefin[t])
                if t >= 2:
                    p.wait("ac", "dv", dvtts[t - 2])
                if t >= 1:
                    p.wait("ac", "ac", acfin[t - 1])
                p.ac(lambda e, t=t, b=pbank, dAb=dAb: e.activation(
                    out=dAb[:, 0, :], in_=b[:, 0:HC], func=AF.Exp,
                    bias=dtb[:, t:t + 1]))
                p.chain("ac")
                p.ac(lambda e, t=t, dAb=dAb: e.activation(
                    out=dtt[:, t % 2, :], in_=dAb[:, 0, :], func=AF.Ln,
                    bias=cvals[:, 5:6]))
                acdtt[t] = p.inc("ac")
                p.wait("ac", "ac", acdtt[t])
                for n in range(16):
                    p.ac(lambda e, t=t, n=n, dAb=dAb: e.activation(
                        out=dAb[:, n, :], in_=dtt[:, t % 2, :], func=AF.Exp,
                        scale=Aneg[:, t, n:n + 1]))
                acfin[t] = p.inc("ac")
                p.wait("dv", "ac", acdtt[t])
                if t >= 1:
                    p.wait("dv", "dv", dvfin[t - 1])
                p.dv(lambda e, t=t: e.tensor_mul(out=wt[:, :], in0=dtt[:, t % 2, :],
                                                 in1=xconv[:, t, :]))
                p.chain("dv")
                p.dv(lambda e, dBx=dBx: e.tensor_tensor(
                    out=dBx[:],
                    in0=wt[:, :].unsqueeze(1).broadcast_to([128, 16, HC]),
                    in1=Bb[:], op=ALU.mult))
                p.chain("dv")
                p.wait("dv", "ac", acfin[t])
                p.dv(lambda e, t=t, dAb=dAb: e.tensor_mul(out=yt[:, 0:16], in0=dAb[:, :, 0],
                                                 in1=hcarry[:, t, :]))
                p.chain("dv")
                p.dv(lambda e, dBx=dBx: e.tensor_add(out=dBx[:, :, 0], in0=dBx[:, :, 0],
                                            in1=yt[:, 0:16]))
                p.chain("dv")
                p.dv(lambda e, dAb=dAb: e.memset(dAb[:, :, 0], 0.0))
                p.chain("dv")
                p.dv(lambda e, dAb=dAb, dBx=dBx: e.tensor_tensor_scan(
                    out=dBx[:].rearrange("p n l -> p (n l)"),
                    data0=dAb[:].rearrange("p n l -> p (n l)"),
                    data1=dBx[:].rearrange("p n l -> p (n l)"),
                    initial=0.0, op0=ALU.mult, op1=ALU.add))
                dvtts[t] = p.inc("dv")
                p.wait("dv", "dv", dvtts[t])
                p.dv(lambda e, t=t, dBx=dBx: e.tensor_copy(hcarry[:, t, :], dBx[:, :, HC - 1]))
                p.chain("dv")
                p.dv(lambda e, dBx=dBx: e.tensor_tensor(
                    out=dBx[:], in0=dBx[:], in1=Cb[:], op=ALU.mult))
                p.chain("dv")
                p.dv(lambda e, dBx=dBx: e.tensor_add(
                    out=yah[:], in0=dBx[:, 0:8, :], in1=dBx[:, 8:16, :]))
                p.chain("dv")
                p.dv(lambda e: e.tensor_add(out=tt1[:], in0=yah[:, 0:4, :],
                                            in1=yah[:, 4:8, :]))
                p.chain("dv")
                p.dv(lambda e: e.tensor_add(out=tt2[:], in0=tt1[:, 0:2, :],
                                            in1=tt1[:, 2:4, :]))
                p.chain("dv")
                p.dv(lambda e: e.tensor_add(out=yt[:, :], in0=tt2[:, 0, :],
                                            in1=tt2[:, 1, :]))
                p.chain("dv")
                p.dv(lambda e, t=t: e.scalar_tensor_tensor(
                    out=yt[:, :], in0=xconv[:, t, :], scalar=Dp[:, t:t + 1],
                    in1=yt[:, :], op0=ALU.mult, op1=ALU.add))
                p.chain("dv")
                p.dv(lambda e, t=t: e.tensor_mul(out=yg[:, t, :], in0=yt[:, :],
                                                 in1=zsil[:, t, :]))
                dvfin[t] = p.inc("dv")
            p.bar()

            oppe = {}
            opdv = {-2: 0, -1: 0}
            def op_load(w):
                p.wait("sy", "pe", oppe.get(w - 2, 0))
                p.wait("gq", "pe", oppe.get(w - 2, 0))
                opld[w] = wload(op_slab(w), "w_outT", DI, 128 * w, 128, par=w % 2)
            for w in range(8):
                p.wait("pe", opld[w][0], opld[w][1])
                p.wait("pe", "dv", opdv[w - 2])
                slab = op_slab(w)
                for kt in range(16):
                    mm(BANKS[w % 2][:, 0:HC], slab[:, kt, 0:128],
                       yg[:, kt, :], start=(kt == 0), stop=(kt == 15))
                oppe[w] = p.inc("pe")
                if w + 2 < 8:
                    op_load(w + 2)
                p.wait("dv", "pe", oppe[w])
                p.dv(lambda e, w=w, h0=h0, b=BANKS[w % 2]: e.tensor_add(
                    out=px[:, w, h0:h0 + HC], in0=px[:, w, h0:h0 + HC],
                    in1=b[:, 0:HC]))
                opdv[w] = p.inc("dv")
            p.bar()

        if STAGE == 1:
            for t in range(NPT):
                p.sy(lambda e, t=t, c0=c0: e.dma_start(out=out_r[:, t, c0:c0 + AC],
                                                       in_=px[:, t, :]))
            p.bar()
            continue

        def wq_slab(w):
            return wbuf[:, w % 2, 0:NPT * 256].rearrange("p (k c) -> p k c", k=NPT)
        p.wait("sy", "pe", p.cnt["pe"])
        p.wait("gq", "pe", p.cnt["pe"])
        qld_h = {0: wload(wq_slab(0), "wqkv0", DM, 0, 256, par=0),
                 1: wload(wq_slab(1), "wqkv0", DM, 256, 256, par=1)}

        ln_stats(lambda t: px[:, t, :], AC)
        for t in range(NPT):
            p.dv(lambda e, t=t: e.tensor_sub(out=tf[:, 0, :], in0=px[:, t, :],
                                             in1=pb0[:, 0:AC]))
            p.bar()
            p.dv(lambda e, t=t: e.tensor_mul(out=pxi[:, t, 1:1 + AC],
                                             in0=tf[:, 0, :], in1=pb1[:, 0:AC]))
            p.bar()
            p.ac(lambda e, t=t: e.copy(out=pxib[:, t, 1:1 + AC],
                                       in_=pxi[:, t, 1:1 + AC]))
        p.bar()
        for i, nm in enumerate(("rcq", "rsq", "rck", "rsk")):
            p.sy(lambda e, i=i, nm=nm, c0=c0: e.dma_start(
                out=rott[:, i, :], in_=D[nm][:, c0:c0 + AC]))
        p.bar()

        nlayers = 1 if STAGE in (2, 3) else 2
        for l in range(nlayers):
            ring = c % 2
            prev_ring = (c + 1) % 2
            kring = kringT[l]
            vtring = vtringT[l]

            st_ = 2 * l
            p.dv(lambda e, st_=st_: e.tensor_copy(pxib[:, :, 0], scarry[:, st_, :]))
            p.chain("dv")
            p.dv(lambda e, st_=st_: e.tensor_copy(scarry[:, st_, :], pxib[:, :, AC]))
            p.bar()

            def qkv_rhs(kt):
                return pxib[:, kt, 1:1 + AC] if kt < NPT // 2 else pxib[:, kt, 0:AC]

            def l2norm_rotary(isq, l=l, kring=kring, ring=ring):
                qac = {}
                qpe = {}
                for t in range(4):
                    if t >= 2:
                        p.wait("ac", "pe", qpe[t - 2])
                    p.ac(lambda e, t=t: e.activation(out=sqb[:, t % 2, :],
                                                     in_=qraw[:, t, :], func=AF.Square))
                    qac[t] = p.inc("ac")
                    p.wait("pe", "ac", qac[t])
                    mm(pb0[0:8, 0:AC], bdones[:, t, :], sqb[:, t % 2, :],
                       start=(t == 0), stop=(t == 3))
                    qpe[t] = p.inc("pe")
                p.wait("ac", "pe", qpe[3])
                p.ac(lambda e: e.activation(out=nrm[:, 0, :], in_=pb0[0:8, 0:AC],
                                            func=AF.Ln))
                p.chain("ac")
                p.ac(lambda e: e.activation(out=nrm[:, 1, :], in_=nrm[:, 0, :],
                                            func=AF.Exp, scale=cvals[0:8, 1:2]))
                nac = p.inc("ac")
                p.wait("dv", "ac", nac)
                p.dv(lambda e: e.tensor_scalar_min(out=nrm[:, 1, :], in0=nrm[:, 1, :],
                                                   scalar1=1e12))
                ndv = p.inc("dv")
                p.wait("ac", "dv", ndv)
                p.ac(lambda e: e.copy(out=nrmb[:, :], in_=nrm[:, 1, :]))
                nac = p.inc("ac")
                sc = qs8 if isq else ks1
                ct, stt_ = (0, 1) if isq else (2, 3)
                bcd = {}
                rpd = {}
                sdv = {}
                rdv = {}
                p.wait("pe", "ac", nac)
                for t in range(4):
                    bk = (pb1, pb2)[t % 2]
                    if t >= 2:
                        p.wait("pe", "dv", rdv[t - 2])
                    mm(bk[:, 0:AC], sel8[:, 128 * t:128 * (t + 1)], nrmb[:, :],
                       start=True)
                    bcd[t] = p.inc("pe")
                    p.wait("dv", "pe", bcd[t])
                    p.dv(lambda e, t=t, sc=sc, b=bk, l=l: e.scalar_tensor_tensor(
                        out=q2b[:, t, :], in0=qraw[:, t, :], scalar=sc[:, l:l + 1],
                        in1=b[:, 0:AC], op0=ALU.mult, op1=ALU.mult))
                    sdv[t] = p.inc("dv")
                    p.wait("pe", "dv", sdv[t])
                    mm(bk[:, 0:AC], rotP[:], q2b[:, t, :], start=True)
                    rpd[t] = p.inc("pe")
                    p.dv(lambda e, t=t, ct=ct: e.tensor_mul(
                        out=tf[:, 0, :], in0=q2b[:, t, :], in1=rott[:, ct, :]))
                    p.chain("dv")
                    p.wait("dv", "pe", rpd[t])
                    p.dv(lambda e, t=t, stt_=stt_, b=bk: e.tensor_mul(
                        out=tf[:, 1, :], in0=b[:, 0:AC], in1=rott[:, stt_, :]))
                    rdv[t] = p.inc("dv")
                    p.wait("dv", "dv", rdv[t])
                    if isq:
                        p.dv(lambda e, t=t: e.tensor_add(
                            out=q3b[:, t, :], in0=tf[:, 0, :], in1=tf[:, 1, :]))
                    else:
                        p.dv(lambda e, t=t, ring=ring, kring=kring: e.tensor_add(
                            out=kring[:, t, ring * AC:ring * AC + AC],
                            in0=tf[:, 0, :], in1=tf[:, 1, :]))
                    p.chain("dv")
                p.bar()

            qld = {}
            qpe_ = {}
            qac_ = {-2: 0, -1: 0}
            def q_load(w):
                p.wait("sy", "pe", qpe_.get(w - 2, 0))
                p.wait("gq", "pe", qpe_.get(w - 2, 0))
                qld[w] = wload(wq_slab(w), f"wqkv{l}", DM, 256 * w, 256, par=w % 2)
            if l == 0:
                qld[0], qld[1] = qld_h[0], qld_h[1]
            else:
                q_load(0)
                q_load(1)
            for w in range(6):
                p.wait("pe", qld[w][0], qld[w][1])
                p.wait("pe", "ac", qac_[w - 2])
                slab = wq_slab(w)
                for kt in range(NPT):
                    for oi in range(2):
                        mm(BANKS[2 * (w % 2) + oi],
                           slab[:, kt, 128 * oi:128 * (oi + 1)],
                           qkv_rhs(kt), start=(kt == 0), stop=(kt == NPT - 1))
                qpe_[w] = p.inc("pe")
                if w + 2 < 6:
                    q_load(w + 2)
                p.wait("ac", "pe", qpe_[w])
                for oi in range(2):
                    ot = 2 * w + oi
                    bk = BANKS[2 * (w % 2) + oi]
                    if ot < 8:
                        p.ac(lambda e, ot=ot, b=bk: e.copy(
                            out=qraw[:, ot % 4, :], in_=b[:, 0:AC]))
                    else:
                        p.ac(lambda e, ot=ot, b=bk: e.copy(
                            out=vbf[:, ot % 4, :], in_=b[:, 0:AC]))
                qac_[w] = p.inc("ac")
                if w == 1:
                    p.bar()
                    l2norm_rotary(True)
                elif w == 3:
                    p.bar()
                    l2norm_rotary(False)
            p.bar()

            for hd in range(HEADS):
                pt_, off = hd // 2, (hd % 2) * 64
                for jt in range(4):
                    p.pe(lambda e, pt_=pt_, off=off, jt=jt: e.transpose(
                        out=pbt[:, 64 * jt:64 * jt + 64],
                        in_=vbf[off:off + 64, pt_, 128 * jt:128 * (jt + 1)],
                        identity=identD[off:off + 64, :]))
                p.bar()
                p.ac(lambda e, hd=hd, ring=ring, vtring=vtring: e.copy(
                    out=vtring[:, hd, 4 * ring:4 * ring + 4, :].rearrange(
                        "p a b -> p (a b)"),
                    in_=pbt[:, 0:256]))
                p.bar()

            items = [(hd, ib) for hd in range(HEADS) for ib in range(4)]
            SIMR = (psA0, psA1)
            pesim = {}; dvmax = {}; acexp = {}; dvren = {}
            petr = {}; acev = {}; peav = {}; acat = {-1: 0}
            jts_of = lambda ib: (list(range(ib, ib + 5)) if c > 0
                                 else list(range(4, 5 + ib)))

            def emit_sim(i):
                hd, ib = items[i]
                pt_, off = hd // 2, (hd % 2) * 64
                sim = SIMR[i % 2]
                jts = jts_of(ib)
                if i >= 2:
                    p.wait("pe", "ac", acexp[i - 2])
                qblk = q3b[off:off + 64, pt_, 128 * ib:128 * (ib + 1)]
                for k, jt in enumerate(jts):
                    r = ring if jt >= 4 else prev_ring
                    colbase = r * AC + 128 * (jt % 4)
                    mm(sim[:, 128 * k:128 * (k + 1)], qblk,
                       kring[off:off + 64, pt_, colbase:colbase + 128], start=True)
                pesim[i] = p.inc("pe")

            emit_sim(0)
            emit_sim(1)
            nI = len(items)
            for i in range(nI + 1):
                if i < nI:
                    hd, ib = items[i]
                    pt_, off = hd // 2, (hd % 2) * 64
                    sim = SIMR[i % 2]
                    attn = attnP[i % 2]
                    sA = sAB[:, 4 * (i % 2):4 * (i % 2) + 4]
                    nj = len(jts_of(ib))
                    p.wait("dv", "pe", pesim[i])
                    if i >= 2:
                        p.wait("dv", "ac", acexp[i - 2])
                    if c > 0:
                        p.dv(lambda e, sim=sim: e.tensor_tensor(
                            out=sim[:, 0:128], in0=sim[:, 0:128], in1=m0[:],
                            op=ALU.add))
                        p.chain("dv")
                    p.dv(lambda e, sim=sim, nj=nj: e.tensor_tensor(
                        out=sim[:, 128 * (nj - 1):128 * nj],
                        in0=sim[:, 128 * (nj - 1):128 * nj], in1=m4[:], op=ALU.add))
                    p.chain("dv")
                    p.dv(lambda e, sim=sim, nj=nj, sA=sA: e.reduce_max(
                        out=sA[:, 0:1], in_=sim[:, 0:128 * nj], axis=AX.X,
                        negate=True))
                    dvmax[i] = p.inc("dv")
                if i >= 1:
                    j = i - 1
                    hdj, ibj = items[j]
                    ptj, offj = hdj // 2, (hdj % 2) * 64
                    attnj = attnP[j % 2]
                    attnTj = attnTP[j % 2]
                    sAj = sAB[:, 4 * (j % 2):4 * (j % 2) + 4]
                    njj = len(jts_of(ibj))
                    jtsj = jts_of(ibj)
                    p.wait("dv", "ac", acexp[j])
                    p.dv(lambda e, sA=sAj: e.reciprocal(out=sA[:, 2:3], in_=sA[:, 1:2]))
                    p.chain("dv")
                    p.dv(lambda e, nj=njj, attn=attnj, sA=sAj: e.tensor_scalar_mul(
                        out=attn[:, 0:128 * nj], in0=attn[:, 0:128 * nj],
                        scalar1=sA[:, 2:3]))
                    dvren[j] = p.inc("dv")
                    p.wait("pe", "dv", dvren[j])
                    if j >= 1:
                        p.wait("pe", "ac", acev[j - 1])
                    for k in range(njj):
                        p.pe(lambda e, k=k, attn=attnj: e.transpose(
                            out=pbt[:, 128 * k:128 * (k + 1)],
                            in_=attn[:, 128 * k:128 * (k + 1)], identity=ident[:]))
                    petr[j] = p.inc("pe")
                    p.wait("ac", "pe", petr[j])
                    if j >= 2:
                        p.wait("ac", "pe", peav[j - 2])
                    p.ac(lambda e, nj=njj, attnT=attnTj: e.copy(
                        out=attnT[:, 0:128 * nj], in_=pbt[:, 0:128 * nj]))
                    acev[j] = p.inc("ac")
                    p.wait("pe", "ac", acev[j])
                    p.wait("pe", "ac", acat[j - 1])
                    for k, jt in enumerate(jtsj):
                        slot = (ring if jt >= 4 else prev_ring) * 4 + (jt % 4)
                        mm(pb2[offj:offj + 64, 0:128], vtring[:, hdj, slot, :],
                           attnTj[:, 128 * k:128 * (k + 1)], start=(k == 0),
                           stop=(k == njj - 1), tile_position=(0, offj))
                    peav[j] = p.inc("pe")
                    p.wait("ac", "pe", peav[j])
                    p.ac(lambda e, off=offj, pt_=ptj, ib=ibj: e.copy(
                        out=atno[off:off + 64, pt_, 128 * ib:128 * (ib + 1)],
                        in_=pb2[off:off + 64, 0:128]))
                    acat[j] = p.inc("ac")
                if i < nI:
                    p.wait("ac", "dv", dvmax[i])
                    if i >= 2:
                        p.wait("ac", "pe", petr[i - 2])
                    p.ac(lambda e, sim=sim, nj=nj, attn=attn, sA=sA: e.activation(
                        out=attn[:, 0:128 * nj], in_=sim[:, 0:128 * nj], func=AF.Exp,
                        bias=sA[:, 0:1], accum_out=sA[:, 1:2]))
                    acexp[i] = p.inc("ac")
                    if i + 2 < nI:
                        emit_sim(i + 2)
            p.bar()

            def wo_slab(w):
                return wbuf[:, w % 2, 0:4 * 256].rearrange("p (k c) -> p k c", k=4)

            wold = {}
            wope = {}
            wodv = {-2: 0, -1: 0}
            def wo_load(w):
                p.wait("sy", "pe", wope.get(w - 2, 0))
                p.wait("gq", "pe", wope.get(w - 2, 0))
                wold[w] = wload(wo_slab(w), f"wo{l}", INNER, 256 * w, 256, par=w % 2)
            wo_load(0)
            wo_load(1)
            for w in range(4):
                p.wait("pe", wold[w][0], wold[w][1])
                p.wait("pe", "dv", wodv[w - 2])
                slab = wo_slab(w)
                for kt in range(4):
                    for oi in range(2):
                        mm(BANKS[2 * (w % 2) + oi],
                           slab[:, kt, 128 * oi:128 * (oi + 1)],
                           atno[:, kt, :], start=(kt == 0), stop=(kt == 3))
                wope[w] = p.inc("pe")
                if w + 2 < 4:
                    wo_load(w + 2)
                p.wait("dv", "pe", wope[w])
                p.dv(lambda e, b=BANKS[2 * (w % 2)], ot=2 * w: e.tensor_add(
                    out=pxi[:, ot, 1:1 + AC], in0=pxi[:, ot, 1:1 + AC],
                    in1=b[:, 0:AC]))
                p.chain("dv")
                p.dv(lambda e, b=BANKS[2 * (w % 2) + 1], ot=2 * w + 1: e.tensor_add(
                    out=pxi[:, ot, 1:1 + AC], in0=pxi[:, ot, 1:1 + AC],
                    in1=b[:, 0:AC]))
                wodv[w] = p.inc("dv")
            p.bar()
            for ot in range(NPT):
                p.ac(lambda e, ot=ot: e.copy(out=pxib[:, ot, 1:1 + AC],
                                             in_=pxi[:, ot, 1:1 + AC]))
            p.bar()
            if STAGE == 2:
                break

            def w1_slab(w):
                return wbuf[:, w % 2, 0:NPT * 256].rearrange("p (k c) -> p k c", k=NPT)
            p.wait("sy", "pe", p.cnt["pe"])
            p.wait("gq", "pe", p.cnt["pe"])
            w1ld = {0: wload(w1_slab(0), f"w1p{l}", DM, 0, 256, par=0),
                    1: wload(w1_slab(1), f"w1p{l}", DM, 256, 256, par=1)}

            st_ = 2 * l + 1
            p.dv(lambda e, st_=st_: e.tensor_copy(pxib[:, :, 0], scarry[:, st_, :]))
            p.chain("dv")
            p.dv(lambda e, st_=st_: e.tensor_copy(scarry[:, st_, :], pxib[:, :, AC]))
            p.bar()

            def sh_ap(t):
                return pxib[:, t, 1:1 + AC] if t < NPT // 2 else pxib[:, t, 0:AC]

            ln_stats(sh_ap, AC)
            for t in range(NPT):
                p.dv(lambda e, t=t: e.tensor_sub(out=tf[:, 0, :], in0=sh_ap(t),
                                                 in1=pb0[:, 0:AC]))
                p.bar()
                p.dv(lambda e, t=t: e.tensor_mul(out=tf[:, 1, :], in0=tf[:, 0, :],
                                                 in1=pb1[:, 0:AC]))
                p.bar()
                p.dv(lambda e, t=t, l=l: e.tensor_scalar(
                    out=ln2o[:, t, :], in0=tf[:, 1, :], scalar1=lng[:, l, t:t + 1],
                    scalar2=lnb[:, l, t:t + 1], op0=ALU.mult, op1=ALU.add))
                p.bar()

            w1pe = {}
            w1ac = {}
            w1dv = {-2: 0, -1: 0}
            def w1_load(w):
                p.wait("sy", "pe", w1pe.get(w - 2, 0))
                p.wait("gq", "pe", w1pe.get(w - 2, 0))
                w1ld[w] = wload(w1_slab(w), f"w1p{l}", DM, 256 * w, 256, par=w % 2)
            for w in range(NHT):
                p.wait("pe", w1ld[w][0], w1ld[w][1])
                p.wait("pe", "dv", w1dv[w - 2])
                slab = w1_slab(w)
                for kt in range(NPT):
                    for oi in range(2):
                        mm(BANKS[2 * (w % 2) + oi],
                           slab[:, kt, 128 * oi:128 * (oi + 1)],
                           ln2o[:, kt, :], start=(kt == 0), stop=(kt == NPT - 1))
                w1pe[w] = p.inc("pe")
                if w + 2 < NHT:
                    w1_load(w + 2)
                p.wait("ac", "pe", w1pe[w])
                gb_ = BANKS[2 * (w % 2) + 1]
                ab_ = BANKS[2 * (w % 2)]
                p.ac(lambda e, g_=gb_, w=w, l=l: e.activation(
                    out=gel[:, w % 2, :], in_=g_[:, 0:AC], func=AF.Gelu,
                    bias=b1g[:, l, w:w + 1]))
                w1ac[w] = p.inc("ac")
                p.wait("dv", "ac", w1ac[w])
                p.dv(lambda e, a_=ab_, w=w, l=l: e.scalar_tensor_tensor(
                    out=gated[:, w, :], in0=a_[:, 0:AC], scalar=b1a[:, l, w:w + 1],
                    in1=gel[:, w % 2, :], op0=ALU.add, op1=ALU.mult))
                w1dv[w] = p.inc("dv")
            p.bar()

            def w2_slab(i):
                return wbuf[:, i % 2, 0:11 * 128].rearrange("p (k c) -> p k c", k=11)

            seq2 = [(w, gpr) for w in range(8) for gpr in range(2)]
            w2ld = {}
            w2pe = {}
            w2dv = {-2: 0, -1: 0, 0: 0, 1: 0}
            def w2_load(i):
                w_, g_ = seq2[i]
                p.wait("sy", "pe", w2pe.get(i - 2, 0))
                p.wait("gq", "pe", w2pe.get(i - 2, 0))
                w2ld[i] = wload(w2_slab(i), f"w2p{l}", HIPD_ if False else HIDP,
                                128 * w_, 128, rt0=11 * g_, rtn=11, par=i % 2)
            w2_load(0)
            w2_load(1)
            for i in range(16):
                w, gpr = seq2[i]
                p.wait("pe", w2ld[i][0], w2ld[i][1])
                p.wait("pe", "dv", w2dv[i - 2])
                slab = w2_slab(i)
                for kk in range(11):
                    kt = 11 * gpr + kk
                    mm(BANKS[4 + w % 2], slab[:, kk, 0:128], gated[:, kt, :],
                       start=(kt == 0), stop=(kt == NHT - 1))
                w2pe[i] = p.inc("pe")
                if i + 2 < 16:
                    w2_load(i + 2)
                if gpr == 1:
                    p.wait("dv", "pe", w2pe[i])
                    p.dv(lambda e, w=w, l=l, b=BANKS[4 + w % 2]:
                         e.scalar_tensor_tensor(
                             out=pxi[:, w, 1:1 + AC], in0=b[:, 0:AC],
                             scalar=b2s[:, l, w:w + 1], in1=pxi[:, w, 1:1 + AC],
                             op0=ALU.add, op1=ALU.add))
                    w2dv[i] = p.inc("dv")
                else:
                    w2dv[i] = w2dv[i - 1]
            p.bar()
            if l == 0 and nlayers > 1:
                for w in range(NPT):
                    p.ac(lambda e, w=w: e.copy(out=pxib[:, w, 1:1 + AC],
                                               in_=pxi[:, w, 1:1 + AC]))
                p.bar()

        for t in range(NPT):
            p.dv(lambda e, t=t: e.tensor_add(out=px[:, t, :], in0=px[:, t, :],
                                             in1=pxi[:, t, 1:1 + AC]))
        p.bar()
        for t in range(NPT):
            p.sy(lambda e, t=t, c0=c0: e.dma_start(out=out_r[:, t, c0:c0 + AC],
                                                   in_=px[:, t, :]))
        p.bar()

    p.wait("sy", "sy", p.cnt["sy"])

    with nc.Block() as block:
        p.materialize(block)
    nc._keepalive_ctx = ctx
    return nc


def kernel(_ncores=NCORE, **inputs):
    g = _host_prep(inputs)
    x = inputs["L0_attn_x"]
    nc = build()
    in_maps = []
    for cix in range(_ncores):
        m = {"xT": np.ascontiguousarray(np.asarray(x[cix]).T)}
        m.update(g)
        in_maps.append(m)
    res = run_bass_kernel_spmd(nc, in_maps, core_ids=list(range(_ncores)))
    outs = [np.asarray(r["out"]).T for r in res.results]
    return np.ascontiguousarray(np.stack(outs, 0))


# version 08
# speedup vs baseline: 1.0283x; 1.0009x over previous
import numpy as np
import ml_dtypes
from contextlib import ExitStack

import concourse.bass as bass
import concourse.mybir as mybir
from concourse.bass_utils import run_bass_kernel_spmd

F32 = mybir.dt.float32
BF16 = mybir.dt.bfloat16
AF = mybir.ActivationFunctionType
ALU = mybir.AluOpType
AX = mybir.AxisListType

BF = ml_dtypes.bfloat16

NCORE = 8
LSEQ = 4096
DM = 1024
DI = 2048
NST = 16
DTR = 64
NPT = 8
AC = 512
HC = 256
NCH = LSEQ // AC
HEADS = 8
DH = 64
INNER = 512
HID = 2730
HIDP = 2816
NHT = 22
NEG = -1e30

NCHRUN = NCH
STAGE = 4


def _pp(v):
    return np.ascontiguousarray(np.asarray(v, np.float32).reshape(-1, 128).T)


def _pp3(a):
    a = np.asarray(a, np.float32)
    F, K = a.shape
    return np.ascontiguousarray(a.reshape(F // 128, 128, K).transpose(1, 0, 2))


def _host_prep(inp):
    g = {}
    g["w_inT"] = np.ascontiguousarray(inp["L0_attn_in_proj_w"].T)
    g["w_xpT"] = np.ascontiguousarray(inp["L0_attn_x_proj_w"].T)
    g["w_dtT"] = np.ascontiguousarray(inp["L0_attn_dt_proj_w"].T)
    g["w_outT"] = np.ascontiguousarray(inp["L0_attn_out_proj_w"].T)
    g["convw"] = _pp3(inp["L0_attn_conv_w"][:, 0, :])
    cw = _pp3(inp["L0_attn_conv_w"][:, 0, :])
    cd = np.zeros((16, 4, 128, 128), np.float32)
    ii128 = np.arange(128)
    for t in range(16):
        for k in range(4):
            cd[t, k, ii128, ii128] = cw[:, t, k]
    g["convdiag"] = np.ascontiguousarray(cd.reshape(16 * 4 * 128, 128))
    g["convb"] = _pp(inp["L0_attn_conv_b"])
    g["dtb"] = _pp(inp["L0_attn_dt_proj_b"])
    g["Alog"] = _pp3(inp["L0_attn_A_log"])
    g["Dp"] = _pp(inp["L0_attn_D_param"])
    for l in (0, 1):
        g[f"wqkv{l}"] = np.ascontiguousarray(inp["L0_ffn_wqkv"][l])
        g[f"wo{l}"] = np.ascontiguousarray(inp["L0_ffn_wo"][l])
        w1 = inp["L0_ffn_w1"][l]
        a = np.zeros((DM, HIDP), np.float32); a[:, :HID] = w1[:, :HID]
        gg = np.zeros((DM, HIDP), np.float32); gg[:, :HID] = w1[:, HID:]
        at = a.reshape(DM, NHT, 128); gt = gg.reshape(DM, NHT, 128)
        g[f"w1p{l}"] = np.ascontiguousarray(
            np.stack([at, gt], axis=2).reshape(DM, 2 * NHT * 128))
        w2 = np.zeros((HIDP, DM), np.float32); w2[:HID] = inp["L0_ffn_w2"][l]
        g[f"w2p{l}"] = w2
        b1 = inp["L0_ffn_b1"][l]
        b1a = np.zeros(HIDP, np.float32); b1a[:HID] = b1[:HID]
        b1g = np.zeros(HIDP, np.float32); b1g[:HID] = b1[HID:]
        g[f"b1a{l}"] = _pp(b1a); g[f"b1g{l}"] = _pp(b1g)
    g["qs"] = np.ascontiguousarray(np.stack(
        [np.tile(np.asarray(inp["L0_ffn_q_scale"][l], np.float32), 2) for l in (0, 1)], 1))
    g["ks"] = np.ascontiguousarray(np.stack(
        [np.tile(np.asarray(inp["L0_ffn_k_scale"][l], np.float32), 2) for l in (0, 1)], 1))
    g["lng"] = np.ascontiguousarray(np.stack([_pp(inp["L0_ffn_ln_g"][l]) for l in (0, 1)], 1))
    g["lnb"] = np.ascontiguousarray(np.stack([_pp(inp["L0_ffn_ln_b"][l]) for l in (0, 1)], 1))
    g["b2"] = np.ascontiguousarray(np.stack([_pp(inp["L0_ffn_b2"][l]) for l in (0, 1)], 1))

    ii = np.arange(128)
    g["m0"] = np.where(ii[None, :] < ii[:, None], NEG, 0.0).astype(BF)
    g["m4"] = np.where(ii[None, :] > ii[:, None], NEG, 0.0).astype(BF)
    g["ident"] = np.eye(128, dtype=BF)
    identD = np.zeros((128, 64), np.float32)
    for p_ in range(128):
        identD[p_, p_ % 64] = 1.0
    g["identD"] = identD.astype(BF)
    g["onescol"] = np.ones((1, 128), dtype=BF)
    g["onek"] = np.ones((128, 1), dtype=BF)
    g["cvals"] = np.tile(np.array([[1e-5, -0.5, 1.0 / DM, 8.0, -1.0, 1.0]], np.float32), (128, 1))
    bd = np.zeros((128, 4, 8), np.float32)
    for t in range(4):
        bd[:64, t, 2 * t] = 1.0; bd[64:, t, 2 * t + 1] = 1.0
    g["bdones"] = bd.astype(BF)
    sel8 = np.zeros((8, 4, 128), np.float32)
    for t in range(4):
        for p_ in range(128):
            sel8[2 * t + (p_ >= 64), t, p_] = 1.0
    g["sel8"] = np.ascontiguousarray(sel8.reshape(8, 512)).astype(BF)
    sel16 = np.zeros((16, 16, 128), np.float32)
    for n in range(16):
        sel16[n, n, :] = 1.0
    g["sel16"] = np.ascontiguousarray(sel16.reshape(16, 2048)).astype(BF)
    P = np.zeros((128, 128), np.float32)
    for di in range(128):
        h0 = (di // 64) * 64; d = di % 64
        if d < 32:
            P[h0 + d + 32, di] = -1.0
        else:
            P[h0 + d - 32, di] = 1.0
    g["rotP"] = P.astype(BF)

    dih = np.arange(64); j = dih % 32
    invfreq = 1.0 / 10000.0 ** ((2 * j) / DH)
    t = np.arange(LSEQ, dtype=np.float64)
    freqs = t[None, :] * invfreq[:, None]
    base = ((2 * j) + 0.4 * DH) / (1.4 * DH)
    power = (t - LSEQ // 2) / float(AC // 2)
    scale = base[:, None] ** power[None, :]
    cos = np.cos(freqs); sin = np.sin(freqs)
    rows2 = lambda m: np.ascontiguousarray(np.concatenate([m, m], 0).astype(BF))
    g["rcq"] = rows2(cos * scale); g["rsq"] = rows2(sin * scale)
    g["rck"] = rows2(cos / scale); g["rsk"] = rows2(sin / scale)
    return g


ENGINES = ("sy", "gq", "pe", "ac", "dv")
SEMS = ("sy", "gq", "pe", "ac", "dv", "gc", "wl0", "wl1", "wg0", "wg1", "wb0", "wb1")


class Prog:
    def __init__(self, nc):
        self.nc = nc
        self.prog = {e: [] for e in ENGINES}
        self.iswait = {e: [] for e in ENGINES}
        self.cnt = {e: 0 for e in SEMS}
        self.dirty = {e: False for e in SEMS}
        self.snap = dict(self.cnt)
        self.lastwait = {e: {q: 0 for q in SEMS} for e in ENGINES}
        self.pending = {e: False for e in ENGINES}
        self.sems = None

    def _emit(self, e, fn, compute=False):
        if self.pending[e]:
            self.pending[e] = False
            for q in SEMS:
                self.wait(e, q, self.snap[q])
        self.prog[e].append([fn, []])
        self.iswait[e].append(False)
        if e == "sy" or (e == "gq" and not compute):
            self.prog[e][-1][1] = [(e, 16)]
            self.cnt[e] += 16
        elif e == "gq":
            self.dirty["gc"] = True
        else:
            self.dirty[e] = True

    def sy(self, fn): self._emit("sy", fn)
    def gq(self, fn): self._emit("gq", fn)
    def gc(self, fn): self._emit("gq", fn, compute=True)

    def dma(self, queue, fn, sem):
        if self.pending[queue]:
            self.pending[queue] = False
            for q in SEMS:
                self.wait(queue, q, self.snap[q])
        self.prog[queue].append([fn, [(sem, 16)]])
        self.iswait[queue].append(False)
        self.cnt[sem] += 16
    def pe(self, fn): self._emit("pe", fn)
    def ac(self, fn): self._emit("ac", fn)
    def dv(self, fn): self._emit("dv", fn)

    def inc(self, e):
        stream = "gq" if e == "gc" else e
        assert self.prog[stream] and not self.iswait[stream][-1], e
        self.prog[stream][-1][1].append((e, 1))
        self.cnt[e] += 1
        self.dirty[e] = False
        return self.cnt[e]

    def wait(self, e, q, val):
        if val <= 0 or self.lastwait[e][q] >= val:
            return
        self.lastwait[e][q] = val
        self.prog[e].append(
            [lambda eng, q=q, val=val: eng.wait_ge(self.sems[q], val), []])
        self.iswait[e].append(True)

    def chain(self, e):
        self.inc(e)
        st = "gq" if e == "gc" else e
        self.wait(st, e, self.cnt[e])

    def bar(self):
        for e in ("pe", "ac", "dv", "gc"):
            if self.dirty[e]:
                self.inc(e)
        self.snap = dict(self.cnt)
        for e in ENGINES:
            self.pending[e] = True

    def materialize(self, block):
        sems = self.sems

        def runner(progl):
            def body(eng):
                for fn, incs in progl:
                    inst = fn(eng)
                    for s, n in incs:
                        inst.then_inc(sems[s], n)
            return body

        block.sync(runner(self.prog["sy"]))
        block.gpsimd(runner(self.prog["gq"]))
        block.tensor(runner(self.prog["pe"]))
        block.scalar(runner(self.prog["ac"]))
        block.vector(runner(self.prog["dv"]))


def build(debug=False):
    nc = bass.Bass("TRN2", target_bir_lowering=False)
    ctx = ExitStack()

    D = {}
    def din(name, shape, dt=F32):
        D[name] = nc.dram_tensor(name, list(shape), dt, kind="ExternalInput").ap()
    din("xT", (DM, LSEQ))
    din("w_inT", (DM, 2 * DI)); din("w_xpT", (DI, 96)); din("w_dtT", (DTR, DI))
    din("w_outT", (DI, DM))
    din("convw", (128, 16, 4)); din("convdiag", (16 * 4 * 128, 128)); din("convb", (128, 16)); din("dtb", (128, 16))
    din("Alog", (128, 16, 16)); din("Dp", (128, 16))
    for l in (0, 1):
        din(f"wqkv{l}", (DM, 3 * INNER)); din(f"wo{l}", (INNER, DM))
        din(f"w1p{l}", (DM, 2 * HIDP)); din(f"w2p{l}", (HIDP, DM))
        din(f"b1a{l}", (128, NHT)); din(f"b1g{l}", (128, NHT))
    din("qs", (128, 2)); din("ks", (128, 2))
    din("lng", (128, 2, 8)); din("lnb", (128, 2, 8)); din("b2", (128, 2, 8))
    din("m0", (128, 128), BF16); din("m4", (128, 128), BF16)
    din("ident", (128, 128), BF16); din("identD", (128, 64), BF16)
    din("onescol", (1, 128), BF16); din("onek", (128, 1), BF16)
    din("cvals", (128, 6))
    din("sel8", (8, 512), BF16); din("sel16", (16, 2048), BF16)
    din("bdones", (128, 4, 8), BF16)
    din("rotP", (128, 128), BF16)
    for nm in ("rcq", "rsq", "rck", "rsk"):
        din(nm, (128, LSEQ), BF16)
    out = nc.dram_tensor("out", [DM, LSEQ], F32, kind="ExternalOutput").ap()
    out_r = out.rearrange("(t p) l -> p t l", p=128)
    xT_r = D["xT"].rearrange("(t p) l -> p t l", p=128)

    _nm = [0]
    def sb(shape, dt=F32):
        _nm[0] += 1
        return ctx.enter_context(nc.sbuf_tensor(f"t{_nm[0]}", list(shape), dt))

    px = sb((128, NPT, AC))
    hcarry = sb((128, 16, 16))
    ccarry = sb((128, 16, 4))
    scarry = sb((128, 4, NPT), BF16)
    kringT = [sb((128, 4, 2 * AC), BF16) for _ in (0, 1)]
    vtringT = [sb((128, HEADS, 8, DH), BF16) for _ in (0, 1)]
    m0 = sb((128, 128), BF16); m4 = sb((128, 128), BF16)
    ident = sb((128, 128), BF16); identD = sb((128, 64), BF16)
    onescol = sb((1, 128), BF16); onek = sb((128, 1), BF16)
    cvals = sb((128, 6))
    sel8 = sb((8, 512), BF16); sel16 = sb((16, 2048), BF16)
    bdones = sb((128, 4, 8), BF16)
    rotP = sb((128, 128), BF16)
    Aneg = sb((128, 16, 16)); convw = sb((128, 16, 4))
    convb = sb((128, 16)); dtb = sb((128, 16)); Dp = sb((128, 16))
    qs8 = sb((128, 2)); ks1 = sb((128, 2))
    lng = sb((128, 2, NPT)); lnb = sb((128, 2, NPT)); b2s = sb((128, 2, NPT))
    b1a = sb((128, 2, NHT)); b1g = sb((128, 2, NHT))
    dtW = sb((DTR, DI), BF16)
    xpW = sb((128, 16, 96), BF16)
    dlr = sb((DTR, HC), BF16)
    bcrowB = sb((16, HC), BF16)
    bcrowC = sb((16, HC), BF16)
    rows1 = sb((1, 5 * AC))
    rowsb1 = sb((1, 2 * AC), BF16)
    nrm = sb((8, 2, AC))
    nrmb = sb((8, AC), BF16)
    sAB = sb((128, 8))
    wbuf = sb((128, 2, 2048), BF16)
    cbuf = sb((128, 2, 4, 128), BF16)

    A16N = 29184
    A32N = 13056
    ar16 = sb((128, A16N), BF16)
    ar32 = sb((128, A32N))

    def v16(off, shape):
        n = int(np.prod(shape[1:]))
        ap = ar16[:, off:off + n]
        if len(shape) == 3:
            ap = ap.rearrange("p (a b) -> p a b", a=shape[1])
        return ap, off + n

    def v32(off, shape):
        n = int(np.prod(shape[1:]))
        ap = ar32[:, off:off + n]
        if len(shape) == 3:
            ap = ap.rearrange("p (a b) -> p a b", a=shape[1])
        return ap, off + n

    o = 0
    lnout, o = v16(o, (128, NPT, HC))
    xinner, o = v16(o, (128, 16, HC + 4))
    zsil, o = v16(o, (128, 16, HC))
    xconv, o = v16(o, (128, 16, HC))
    Bb, o = v16(o, (128, 16, HC))
    Cb, o = v16(o, (128, 16, HC))
    wt, o = v16(o, (128, HC))
    ya, o = v16(o, (128, 4, HC))
    yb, o = v16(o, (128, 2, HC))
    yg, o = v16(o, (128, 16, HC))
    assert o <= A16N, o
    dBxA = ar16[:, 2048:2048 + 16 * HC].rearrange("p (a b) -> p a b", a=16)
    dBxP = (dBxA, dBxA)
    yah = ar16[:, 0:8 * HC].rearrange("p (a b) -> p a b", a=8)
    tt1 = ya
    tt2 = yb
    o = 0
    pxib, o = v16(o, (128, NPT, AC + 1))
    rott, o = v16(o, (128, 4, AC))
    q2b, o = v16(o, (128, 4, AC))
    q3b, o = v16(o, (128, 4, AC))
    vbf, o = v16(o, (128, 4, AC))
    sqb, o = v16(o, (128, 2, AC))
    atno, o = v16(o, (128, 4, AC))
    gated, o = v16(o, (128, NHT, AC))
    sqb2, o = v16(o, (128, 2, AC))
    assert o <= A16N, o
    _vb0 = vbf[:, 0:4, :].rearrange("p a b -> p (a b)")
    attnP = (_vb0[:, 0:640], _vb0[:, 640:1280])
    _qb0 = q2b[:, 0:4, :].rearrange("p a b -> p (a b)")
    attnTP = (_qb0[:, 0:640], _qb0[:, 640:1280])
    _q2off = NPT * (AC + 1) + 4 * AC
    ln2o = ar16[:, _q2off:_q2off + NPT * AC].rearrange("p (a b) -> p a b", a=NPT)
    gel = sqb

    o = 0
    dAb0, o = v32(o, (128, 16, HC))
    dAb1, o = v32(o, (128, 16, HC))
    dtt, o = v32(o, (128, 2, HC))
    yt, o = v32(o, (128, HC))
    assert o <= A32N, o
    dABUF = (dAb0, dAb1)
    o = 0
    pxi, o = v32(o, (128, NPT, AC + 1))
    qraw, o = v32(o, (128, 4, AC))
    tf, o = v32(o, (128, 2, AC))
    assert o <= A32N, o

    psA0 = ctx.enter_context(nc.psum_tensor([128, 1024], F32))
    psA1 = ctx.enter_context(nc.psum_tensor([128, 1024], F32))
    pb0 = ctx.enter_context(nc.psum_tensor([128, 512], F32))
    pb1 = ctx.enter_context(nc.psum_tensor([128, 512], F32))
    pb2 = ctx.enter_context(nc.psum_tensor([128, 512], F32))
    pbt = ctx.enter_context(nc.psum_tensor([128, 1024], BF16))
    BANKS = [psA0[:, 0:512], psA0[:, 512:1024],
             psA1[:, 0:512], psA1[:, 512:1024], pb0[:], pb1[:]]

    p = Prog(nc)
    p.sems = {e: ctx.enter_context(nc.semaphore(f"s_{e}")) for e in SEMS}

    def mm(bank_ap, lhsT_ap, rhs_ap, start, stop=None, tile_position=None):
        if stop is None:
            stop = start
        p.pe(lambda e, o_=bank_ap, l_=lhsT_ap, r_=rhs_ap, s_=start, t_=stop,
             tp=tile_position:
             e.matmul(out=o_, lhsT=l_, rhs=r_, start=s_, stop=t_, tile_position=tp))

    SCR16 = {}
    STORED = set()

    LASTWB = {}

    def wload(buf_ap, name, rows_, col0, cols, rt0=0, rtn=None, par=0):
        dram2d = D[name]
        if name not in SCR16:
            SCR16[name] = nc.dram_tensor(
                f"scr_{name}", [dram2d.shape[0], dram2d.shape[1]], BF16,
                kind="Internal").ap()
        scr = SCR16[name].rearrange("(t p) c -> p t c", p=128)
        src = dram2d.rearrange("(t p) c -> p t c", p=128)
        nt = rtn if rtn is not None else rows_ // 128
        step = 4
        key = (name, col0, rt0)
        first = key not in STORED
        STORED.add(key)
        wsem = f"wg{par}" if first else f"wl{par}"
        p.wait("gq" if first else "sy", f"wb{par}", LASTWB.get(par, 0))
        for tt in range(0, nt, step):
            n = min(step, nt - tt)
            if first:
                p.dma("gq", lambda e, d=buf_ap[:, tt:tt + n, :],
                      s=src[:, rt0 + tt:rt0 + tt + n, col0:col0 + cols]:
                      e.dma_start(out=d, in_=s), wsem)
            else:
                p.dma("sy", lambda e, d=buf_ap[:, tt:tt + n, :],
                      s=scr[:, rt0 + tt:rt0 + tt + n, col0:col0 + cols]:
                      e.dma_start(out=d, in_=s), wsem)
        if first:
            p.wait("sy", wsem, p.cnt[wsem])
            for tt in range(0, nt, step):
                n = min(step, nt - tt)
                p.dma("sy", lambda e, d=scr[:, rt0 + tt:rt0 + tt + n, col0:col0 + cols],
                      s=buf_ap[:, tt:tt + n, :]:
                      e.dma_start(out=d, in_=s), f"wb{par}")
            LASTWB[par] = p.cnt[f"wb{par}"]
        return (wsem, p.cnt[wsem])

    def loadc(dst_ap, src_ap, cast=False):
        (p.gq if cast else p.sy)(
            lambda e, d=dst_ap, s=src_ap: e.dma_start(out=d, in_=s))

    loadc(m0[:], D["m0"]); loadc(m4[:], D["m4"])
    loadc(ident[:], D["ident"]); loadc(identD[:], D["identD"])
    loadc(onescol[:], D["onescol"]); loadc(onek[:], D["onek"])
    loadc(cvals[:], D["cvals"])
    loadc(sel8[:], D["sel8"]); loadc(sel16[:], D["sel16"])
    loadc(bdones[:], D["bdones"]); loadc(rotP[:], D["rotP"])
    loadc(convw[:], D["convw"]); loadc(convb[:], D["convb"])
    loadc(dtb[:], D["dtb"]); loadc(Dp[:], D["Dp"])
    loadc(Aneg[:], D["Alog"])
    loadc(qs8[:], D["qs"]); loadc(ks1[:], D["ks"])
    loadc(lng[:], D["lng"]); loadc(lnb[:], D["lnb"]); loadc(b2s[:], D["b2"])
    for l in (0, 1):
        loadc(b1a[:, l, :], D[f"b1a{l}"]); loadc(b1g[:, l, :], D[f"b1g{l}"])
    loadc(dtW[:], D["w_dtT"], cast=True)
    loadc(xpW[:], D["w_xpT"].rearrange("(t p) c -> p t c", p=128), cast=True)
    p.bar()
    AnegF = Aneg[:].rearrange("p a b -> p (a b)")
    p.ac(lambda e: e.activation(out=AnegF, in_=AnegF, func=AF.Exp))
    p.ac(lambda e: e.activation(out=qs8[:], in_=qs8[:], func=AF.Copy, scale=cvals[:, 3:4]))
    p.dv(lambda e: e.memset(hcarry[:].rearrange("p a b -> p (a b)"), 0.0))
    p.dv(lambda e: e.memset(ccarry[:].rearrange("p a b -> p (a b)"), 0.0))
    p.dv(lambda e: e.memset(scarry[:].rearrange("p a b -> p (a b)"), 0.0))
    for l in (0, 1):
        p.dv(lambda e, l=l: e.memset(
            vtringT[l][:].rearrange("p a b c -> p (a b c)"), 0.0))
    p.bar()
    p.dv(lambda e: e.tensor_scalar_mul(out=AnegF, in0=AnegF, scalar1=-1.0))
    p.bar()

    def seg(k, n):
        return rows1[0:1, k * AC:k * AC + n]

    def ln_post(n, bank):
        p.ac(lambda e, n=n, bank=bank: e.activation(out=seg(0, n), in_=bank[0:1, 0:n],
                                         func=AF.Copy, scale=cvals[0:1, 2:3]))
        ac0 = p.inc("ac")
        p.wait("dv", "ac", ac0)
        p.dv(lambda e, n=n: e.tensor_mul(out=seg(1, n), in0=seg(0, n), in1=seg(0, n)))
        p.chain("dv")
        p.dv(lambda e, n=n, bank=bank: e.scalar_tensor_tensor(
            out=seg(2, n), in0=bank[0:1, n:2 * n], scalar=1.0 / DM,
            in1=seg(1, n), op0=ALU.mult, op1=ALU.subtract))
        dv0 = p.inc("dv")
        p.wait("ac", "dv", dv0)
        p.ac(lambda e, n=n: e.activation(out=seg(3, n), in_=seg(2, n),
                                         func=AF.Ln, bias=cvals[0:1, 0:1]))
        p.chain("ac")
        p.ac(lambda e, n=n: e.activation(out=seg(4, n), in_=seg(3, n),
                                         func=AF.Exp, scale=cvals[0:1, 1:2]))
        p.chain("ac")
        p.ac(lambda e, n=n: e.copy(out=rowsb1[0:1, 0:n], in_=seg(0, n)))
        p.chain("ac")
        p.ac(lambda e, n=n: e.copy(out=rowsb1[0:1, n:2 * n], in_=seg(4, n)))
        ac1 = p.inc("ac")
        p.wait("pe", "ac", ac1)
        if n >= 512:
            mm(bank[:, 0:n], onescol[:], rowsb1[0:1, 0:n], start=True)
            mm(bank[:, n:2 * n], onescol[:], rowsb1[0:1, n:2 * n], start=True)
        else:
            mm(bank[:, 0:2 * n], onescol[:], rowsb1[0:1, 0:2 * n], start=True)
        p.bar()

    def ln_stats(src_ap_fn, n, bank):
        for t in range(NPT):
            half = t % 2
            p.ac(lambda e, t=t, n=n, half=half: e.copy(
                out=sqb[:, half, 0:n], in_=src_ap_fn(t)))
            p.chain("ac")
            p.ac(lambda e, t=t, n=n, half=half: e.activation(
                out=sqb2[:, half, 0:n], in_=src_ap_fn(t), func=AF.Square))
            lnac = p.inc("ac")
            p.wait("pe", "ac", lnac)
            mm(bank[0:1, 0:n], onek[:], sqb[:, half, 0:n], start=(t == 0),
               stop=(t == NPT - 1 and n >= 512))
            mm(bank[0:1, n:2 * n], onek[:], sqb2[:, half, 0:n],
               start=(t == 0 and n >= 512), stop=(t == NPT - 1))
            if t < NPT - 1:
                pef = p.inc("pe")
                p.wait("ac", "pe", pef)
        p.bar()
        ln_post(n, bank)

    for c in range(NCHRUN):
        c0 = c * AC
        for t in range(NPT):
            p.sy(lambda e, t=t, c0=c0: e.dma_start(out=px[:, t, :],
                                                   in_=xT_r[:, t, c0:c0 + AC]))
        p.bar()

        for hh in range(2):
            h0 = hh * HC
            ln_stats(lambda t, h0=h0: px[:, t, h0:h0 + HC], HC, pb2)
            for t in range(NPT):
                p.dv(lambda e, t=t, h0=h0: e.tensor_sub(
                    out=tf[:, 0, 0:HC], in0=px[:, t, h0:h0 + HC], in1=pb2[:, 0:HC]))
                p.chain("dv")
                p.dv(lambda e, t=t: e.tensor_mul(out=lnout[:, t, :],
                                                 in0=tf[:, 0, 0:HC],
                                                 in1=pb2[:, HC:2 * HC]))
                p.chain("dv")
            p.bar()

            p.dv(lambda e: e.tensor_copy(xinner[:, :, 1:4], ccarry[:, :, 1:4]))
            p.bar()

            def ip_slab(w):
                return wbuf[:, w % 2, 0:NPT * 256].rearrange("p (k c) -> p k c", k=NPT)

            ipld = {}
            ippe = {}
            ipac = {-3: 0, -2: 0, -1: 0}
            def ip_load(w):
                p.wait("sy", "pe", ippe.get(w - 2, 0))
                p.wait("gq", "pe", ippe.get(w - 2, 0))
                ipld[w] = wload(ip_slab(w), "w_inT", DM, 256 * w, 256, par=w % 2)
            ip_load(0)
            ip_load(1)
            for w in range(16):
                p.wait("pe", ipld[w][0], ipld[w][1])
                p.wait("pe", "ac", ipac.get(w - 3, 0))
                slab = ip_slab(w)
                for kt in range(NPT):
                    for oi in range(2):
                        mm(BANKS[2 * (w % 3) + oi][:, 0:HC],
                           slab[:, kt, 128 * oi:128 * (oi + 1)],
                           lnout[:, kt, :], start=(kt == 0),
                           stop=(kt == NPT - 1))
                ippe[w] = p.inc("pe")
                if w + 2 < 16:
                    ip_load(w + 2)
                p.wait("ac", "pe", ippe[w])
                for oi in range(2):
                    ot = 2 * w + oi
                    bk = BANKS[2 * (w % 3) + oi]
                    if ot < 16:
                        p.ac(lambda e, b=bk, d=ot: e.copy(
                            out=xinner[:, d, 4:4 + HC], in_=b[:, 0:HC]))
                    else:
                        p.ac(lambda e, b=bk, d=ot - 16: e.activation(
                            out=zsil[:, d, :], in_=b[:, 0:HC], func=AF.Silu))
                ipac[w] = p.inc("ac")
            p.bar()
            p.dv(lambda e: e.tensor_copy(ccarry[:, :, 1:4],
                                         xinner[:, :, 1 + HC:4 + HC]))
            p.bar()

            def op_slab(w):
                return wbuf[:, w % 2, 0:16 * 128].rearrange("p (k c) -> p k c", k=16)
            p.wait("sy", "pe", p.cnt["pe"])
            p.wait("gq", "pe", p.cnt["pe"])
            opld = {0: wload(op_slab(0), "w_outT", DI, 0, 128, par=0),
                    1: wload(op_slab(1), "w_outT", DI, 128, 128, par=1)}

            cvld = {}
            cvpe = {}
            cvac = {-2: 0, -1: 0}
            def cv_load(t):
                p.wait("sy", "pe", cvpe.get(t - 2, 0))
                p.wait("gq", "pe", cvpe.get(t - 2, 0))
                cvld[t] = wload(cbuf[:, t % 2, :, :], "convdiag", 16 * 4 * 128, 0,
                                128, rt0=4 * t, rtn=4, par=t % 2)
            cv_load(0)
            cv_load(1)
            for t in range(16):
                p.wait("pe", cvld[t][0], cvld[t][1])
                p.wait("pe", "ac", cvac[t - 2])
                for k in range(4):
                    mm(BANKS[t % 2][:, 0:HC], cbuf[:, t % 2, k, :],
                       xinner[:, t, 1 + k:1 + k + HC], start=(k == 0), stop=(k == 3))
                cvpe[t] = p.inc("pe")
                if t + 2 < 16:
                    cv_load(t + 2)
                p.wait("ac", "pe", cvpe[t])
                p.ac(lambda e, t=t, b=BANKS[t % 2]: e.activation(
                    out=xconv[:, t, :], in_=b[:, 0:HC], func=AF.Silu,
                    bias=convb[:, t:t + 1]))
                cvac[t] = p.inc("ac")
            p.bar()

            for kt in range(16):
                mm(pb0[0:64, 0:HC], xpW[:, kt, 0:64], xconv[:, kt, :], start=(kt == 0), stop=(kt == 15))
                mm(pb1[0:16, 0:HC], xpW[:, kt, 64:80], xconv[:, kt, :], start=(kt == 0), stop=(kt == 15))
                mm(pb2[0:16, 0:HC], xpW[:, kt, 80:96], xconv[:, kt, :], start=(kt == 0), stop=(kt == 15))
            p.bar()
            p.ac(lambda e: e.copy(out=dlr[:, :], in_=pb0[0:DTR, 0:HC]))
            p.ac(lambda e: e.copy(out=bcrowB[:, :], in_=pb1[0:16, 0:HC]))
            p.ac(lambda e: e.copy(out=bcrowC[:, :], in_=pb2[0:16, 0:HC]))
            p.bar()
            for src_, dst in ((bcrowB, Bb), (bcrowC, Cb)):
                for n4 in range(4):
                    for k4 in range(4):
                        n = 4 * n4 + k4
                        mm(psA0[:, 256 * k4:256 * k4 + HC],
                           sel16[:, 128 * n:128 * (n + 1)], src_[:, :], start=True)
                    p.bar()
                    p.ac(lambda e, n4=n4, dst=dst: e.copy(
                        out=dst[:, 4 * n4:4 * n4 + 4, :],
                        in_=psA0[:].rearrange("p (a b) -> p a b", a=4)))
                    p.bar()

            mm(pb0[:, 0:HC], dtW[:, 0:128], dlr[:, :], start=True)
            p.bar()
            acdtt = {}
            acfin = {}
            dvtts = {}
            dvfin = {}
            pefin = {0: p.cnt["pe"]}
            for t in range(16):
                dAb = dABUF[t % 2]
                dBx = dBxP[t % 2]
                pbank = (pb0, pb1)[t % 2]
                if t < 15:
                    if t >= 1:
                        p.wait("pe", "ac", acdtt[t - 1])
                    mm((pb0, pb1)[(t + 1) % 2][:, 0:HC],
                       dtW[:, 128 * (t + 1):128 * (t + 2)], dlr[:, :], start=True)
                    pefin[t + 1] = p.inc("pe")
                p.wait("ac", "pe", pefin[t])
                if t >= 2:
                    p.wait("ac", "dv", dvtts[t - 2])
                if t >= 1:
                    p.wait("ac", "ac", acfin[t - 1])
                p.ac(lambda e, t=t, b=pbank, dAb=dAb: e.activation(
                    out=dAb[:, 0, :], in_=b[:, 0:HC], func=AF.Exp,
                    bias=dtb[:, t:t + 1]))
                p.chain("ac")
                p.ac(lambda e, t=t, dAb=dAb: e.activation(
                    out=dtt[:, t % 2, :], in_=dAb[:, 0, :], func=AF.Ln,
                    bias=cvals[:, 5:6]))
                acdtt[t] = p.inc("ac")
                p.wait("ac", "ac", acdtt[t])
                for n in range(16):
                    p.ac(lambda e, t=t, n=n, dAb=dAb: e.activation(
                        out=dAb[:, n, :], in_=dtt[:, t % 2, :], func=AF.Exp,
                        scale=Aneg[:, t, n:n + 1]))
                acfin[t] = p.inc("ac")
                p.wait("dv", "ac", acdtt[t])
                if t >= 1:
                    p.wait("dv", "dv", dvfin[t - 1])
                p.dv(lambda e, t=t: e.tensor_mul(out=wt[:, :], in0=dtt[:, t % 2, :],
                                                 in1=xconv[:, t, :]))
                p.chain("dv")
                p.dv(lambda e, dBx=dBx: e.tensor_tensor(
                    out=dBx[:],
                    in0=wt[:, :].unsqueeze(1).broadcast_to([128, 16, HC]),
                    in1=Bb[:], op=ALU.mult))
                p.chain("dv")
                p.wait("dv", "ac", acfin[t])
                p.dv(lambda e, t=t, dAb=dAb: e.tensor_mul(out=yt[:, 0:16], in0=dAb[:, :, 0],
                                                 in1=hcarry[:, t, :]))
                p.chain("dv")
                p.dv(lambda e, dBx=dBx: e.tensor_add(out=dBx[:, :, 0], in0=dBx[:, :, 0],
                                            in1=yt[:, 0:16]))
                p.chain("dv")
                p.dv(lambda e, dAb=dAb: e.memset(dAb[:, :, 0], 0.0))
                p.chain("dv")
                p.dv(lambda e, dAb=dAb, dBx=dBx: e.tensor_tensor_scan(
                    out=dBx[:].rearrange("p n l -> p (n l)"),
                    data0=dAb[:].rearrange("p n l -> p (n l)"),
                    data1=dBx[:].rearrange("p n l -> p (n l)"),
                    initial=0.0, op0=ALU.mult, op1=ALU.add))
                dvtts[t] = p.inc("dv")
                p.wait("dv", "dv", dvtts[t])
                p.dv(lambda e, t=t, dBx=dBx: e.tensor_copy(hcarry[:, t, :], dBx[:, :, HC - 1]))
                p.chain("dv")
                p.dv(lambda e, dBx=dBx: e.tensor_tensor(
                    out=dBx[:], in0=dBx[:], in1=Cb[:], op=ALU.mult))
                p.chain("dv")
                p.dv(lambda e, dBx=dBx: e.tensor_add(
                    out=yah[:], in0=dBx[:, 0:8, :], in1=dBx[:, 8:16, :]))
                p.chain("dv")
                p.dv(lambda e: e.tensor_add(out=tt1[:], in0=yah[:, 0:4, :],
                                            in1=yah[:, 4:8, :]))
                p.chain("dv")
                p.dv(lambda e: e.tensor_add(out=tt2[:], in0=tt1[:, 0:2, :],
                                            in1=tt1[:, 2:4, :]))
                p.chain("dv")
                p.dv(lambda e: e.tensor_add(out=yt[:, :], in0=tt2[:, 0, :],
                                            in1=tt2[:, 1, :]))
                p.chain("dv")
                p.dv(lambda e, t=t: e.scalar_tensor_tensor(
                    out=yt[:, :], in0=xconv[:, t, :], scalar=Dp[:, t:t + 1],
                    in1=yt[:, :], op0=ALU.mult, op1=ALU.add))
                p.chain("dv")
                p.dv(lambda e, t=t: e.tensor_mul(out=yg[:, t, :], in0=yt[:, :],
                                                 in1=zsil[:, t, :]))
                dvfin[t] = p.inc("dv")
            p.bar()

            oppe = {}
            opdv = {-2: 0, -1: 0}
            def op_load(w):
                p.wait("sy", "pe", oppe.get(w - 2, 0))
                p.wait("gq", "pe", oppe.get(w - 2, 0))
                opld[w] = wload(op_slab(w), "w_outT", DI, 128 * w, 128, par=w % 2)
            for w in range(8):
                p.wait("pe", opld[w][0], opld[w][1])
                p.wait("pe", "dv", opdv[w - 2])
                slab = op_slab(w)
                for kt in range(16):
                    mm(BANKS[w % 2][:, 0:HC], slab[:, kt, 0:128],
                       yg[:, kt, :], start=(kt == 0), stop=(kt == 15))
                oppe[w] = p.inc("pe")
                if w + 2 < 8:
                    op_load(w + 2)
                p.wait("dv", "pe", oppe[w])
                p.dv(lambda e, w=w, h0=h0, b=BANKS[w % 2]: e.tensor_add(
                    out=px[:, w, h0:h0 + HC], in0=px[:, w, h0:h0 + HC],
                    in1=b[:, 0:HC]))
                opdv[w] = p.inc("dv")
            p.bar()

        if STAGE == 1:
            for t in range(NPT):
                p.sy(lambda e, t=t, c0=c0: e.dma_start(out=out_r[:, t, c0:c0 + AC],
                                                       in_=px[:, t, :]))
            p.bar()
            continue

        def wq_slab(w):
            return wbuf[:, w % 2, 0:NPT * 256].rearrange("p (k c) -> p k c", k=NPT)
        p.wait("sy", "pe", p.cnt["pe"])
        p.wait("gq", "pe", p.cnt["pe"])
        qld_h = {0: wload(wq_slab(0), "wqkv0", DM, 0, 256, par=0),
                 1: wload(wq_slab(1), "wqkv0", DM, 256, 256, par=1)}

        ln_stats(lambda t: px[:, t, :], AC, psA0)
        dvn = 0
        for t in range(NPT):
            p.dv(lambda e, t=t: e.tensor_sub(out=tf[:, 0, :], in0=px[:, t, :],
                                             in1=psA0[:, 0:AC]))
            p.chain("dv")
            p.dv(lambda e, t=t: e.tensor_mul(out=pxi[:, t, 1:1 + AC],
                                             in0=tf[:, 0, :], in1=psA0[:, AC:2 * AC]))
            dvn = p.inc("dv")
            p.wait("dv", "dv", dvn)
            p.wait("ac", "dv", dvn)
            p.ac(lambda e, t=t: e.copy(out=pxib[:, t, 1:1 + AC],
                                       in_=pxi[:, t, 1:1 + AC]))
        p.bar()
        for i, nm in enumerate(("rcq", "rsq", "rck", "rsk")):
            p.sy(lambda e, i=i, nm=nm, c0=c0: e.dma_start(
                out=rott[:, i, :], in_=D[nm][:, c0:c0 + AC]))
        p.bar()

        nlayers = 1 if STAGE in (2, 3) else 2
        for l in range(nlayers):
            ring = c % 2
            prev_ring = (c + 1) % 2
            kring = kringT[l]
            vtring = vtringT[l]

            st_ = 2 * l
            p.dv(lambda e, st_=st_: e.tensor_copy(pxib[:, :, 0], scarry[:, st_, :]))
            p.chain("dv")
            p.dv(lambda e, st_=st_: e.tensor_copy(scarry[:, st_, :], pxib[:, :, AC]))
            p.bar()

            def qkv_rhs(kt):
                return pxib[:, kt, 1:1 + AC] if kt < NPT // 2 else pxib[:, kt, 0:AC]

            def l2norm_rotary(isq, l=l, kring=kring, ring=ring):
                qac = {}
                qpe = {}
                for t in range(4):
                    if t >= 2:
                        p.wait("ac", "pe", qpe[t - 2])
                    p.ac(lambda e, t=t: e.activation(out=sqb[:, t % 2, :],
                                                     in_=qraw[:, t, :], func=AF.Square))
                    qac[t] = p.inc("ac")
                    p.wait("pe", "ac", qac[t])
                    mm(pb0[0:8, 0:AC], bdones[:, t, :], sqb[:, t % 2, :],
                       start=(t == 0), stop=(t == 3))
                    qpe[t] = p.inc("pe")
                p.wait("ac", "pe", qpe[3])
                p.ac(lambda e: e.activation(out=nrm[:, 0, :], in_=pb0[0:8, 0:AC],
                                            func=AF.Ln))
                p.chain("ac")
                p.ac(lambda e: e.activation(out=nrm[:, 1, :], in_=nrm[:, 0, :],
                                            func=AF.Exp, scale=cvals[0:8, 1:2]))
                nac = p.inc("ac")
                p.wait("dv", "ac", nac)
                p.dv(lambda e: e.tensor_scalar_min(out=nrm[:, 1, :], in0=nrm[:, 1, :],
                                                   scalar1=1e12))
                ndv = p.inc("dv")
                p.wait("ac", "dv", ndv)
                p.ac(lambda e: e.copy(out=nrmb[:, :], in_=nrm[:, 1, :]))
                nac = p.inc("ac")
                sc = qs8 if isq else ks1
                ct, stt_ = (0, 1) if isq else (2, 3)
                bcd = {}
                rpd = {}
                sdv = {}
                rdv = {}
                p.wait("pe", "ac", nac)
                for t in range(4):
                    bk = (pb1, pb2)[t % 2]
                    if t >= 2:
                        p.wait("pe", "dv", rdv[t - 2])
                    mm(bk[:, 0:AC], sel8[:, 128 * t:128 * (t + 1)], nrmb[:, :],
                       start=True)
                    bcd[t] = p.inc("pe")
                    p.wait("dv", "pe", bcd[t])
                    p.dv(lambda e, t=t, sc=sc, b=bk, l=l: e.scalar_tensor_tensor(
                        out=q2b[:, t, :], in0=qraw[:, t, :], scalar=sc[:, l:l + 1],
                        in1=b[:, 0:AC], op0=ALU.mult, op1=ALU.mult))
                    sdv[t] = p.inc("dv")
                    p.wait("pe", "dv", sdv[t])
                    mm(bk[:, 0:AC], rotP[:], q2b[:, t, :], start=True)
                    rpd[t] = p.inc("pe")
                    p.dv(lambda e, t=t, ct=ct: e.tensor_mul(
                        out=tf[:, 0, :], in0=q2b[:, t, :], in1=rott[:, ct, :]))
                    p.chain("dv")
                    p.wait("dv", "pe", rpd[t])
                    p.dv(lambda e, t=t, stt_=stt_, b=bk: e.tensor_mul(
                        out=tf[:, 1, :], in0=b[:, 0:AC], in1=rott[:, stt_, :]))
                    rdv[t] = p.inc("dv")
                    p.wait("dv", "dv", rdv[t])
                    if isq:
                        p.dv(lambda e, t=t: e.tensor_add(
                            out=q3b[:, t, :], in0=tf[:, 0, :], in1=tf[:, 1, :]))
                    else:
                        p.dv(lambda e, t=t, ring=ring, kring=kring: e.tensor_add(
                            out=kring[:, t, ring * AC:ring * AC + AC],
                            in0=tf[:, 0, :], in1=tf[:, 1, :]))
                    p.chain("dv")
                p.bar()

            qld = {}
            qpe_ = {}
            qac_ = {-2: 0, -1: 0}
            def q_load(w):
                p.wait("sy", "pe", qpe_.get(w - 2, 0))
                p.wait("gq", "pe", qpe_.get(w - 2, 0))
                qld[w] = wload(wq_slab(w), f"wqkv{l}", DM, 256 * w, 256, par=w % 2)
            if l == 0:
                qld[0], qld[1] = qld_h[0], qld_h[1]
            else:
                q_load(0)
                q_load(1)
            for w in range(6):
                p.wait("pe", qld[w][0], qld[w][1])
                p.wait("pe", "ac", qac_.get(w - 3, 0))
                slab = wq_slab(w)
                for kt in range(NPT):
                    for oi in range(2):
                        mm(BANKS[2 * (w % 3) + oi],
                           slab[:, kt, 128 * oi:128 * (oi + 1)],
                           qkv_rhs(kt), start=(kt == 0), stop=(kt == NPT - 1))
                qpe_[w] = p.inc("pe")
                if w + 2 < 6:
                    q_load(w + 2)
                p.wait("ac", "pe", qpe_[w])
                for oi in range(2):
                    ot = 2 * w + oi
                    bk = BANKS[2 * (w % 3) + oi]
                    if ot < 8:
                        p.ac(lambda e, ot=ot, b=bk: e.copy(
                            out=qraw[:, ot % 4, :], in_=b[:, 0:AC]))
                    else:
                        p.ac(lambda e, ot=ot, b=bk: e.copy(
                            out=vbf[:, ot % 4, :], in_=b[:, 0:AC]))
                qac_[w] = p.inc("ac")
                if w == 1:
                    p.bar()
                    l2norm_rotary(True)
                elif w == 3:
                    p.bar()
                    l2norm_rotary(False)
            p.bar()

            for hd in range(HEADS):
                pt_, off = hd // 2, (hd % 2) * 64
                for jt in range(4):
                    p.pe(lambda e, pt_=pt_, off=off, jt=jt: e.transpose(
                        out=pbt[:, 64 * jt:64 * jt + 64],
                        in_=vbf[off:off + 64, pt_, 128 * jt:128 * (jt + 1)],
                        identity=identD[off:off + 64, :]))
                p.bar()
                p.ac(lambda e, hd=hd, ring=ring, vtring=vtring: e.copy(
                    out=vtring[:, hd, 4 * ring:4 * ring + 4, :].rearrange(
                        "p a b -> p (a b)"),
                    in_=pbt[:, 0:256]))
                p.bar()

            items = [(hd, ib) for hd in range(HEADS) for ib in range(4)]
            SIMR = (psA0, psA1)
            pesim = {}; dvmax = {}; acexp = {}; dvren = {}
            petr = {}; acev = {}; peav = {}; acat = {-1: 0}
            jts_of = lambda ib: (list(range(ib, ib + 5)) if c > 0
                                 else list(range(4, 5 + ib)))

            def emit_sim(i):
                hd, ib = items[i]
                pt_, off = hd // 2, (hd % 2) * 64
                sim = SIMR[i % 2]
                jts = jts_of(ib)
                if i >= 2:
                    p.wait("pe", "ac", acexp[i - 2])
                qblk = q3b[off:off + 64, pt_, 128 * ib:128 * (ib + 1)]
                for k, jt in enumerate(jts):
                    r = ring if jt >= 4 else prev_ring
                    colbase = r * AC + 128 * (jt % 4)
                    mm(sim[:, 128 * k:128 * (k + 1)], qblk,
                       kring[off:off + 64, pt_, colbase:colbase + 128], start=True)
                pesim[i] = p.inc("pe")

            emit_sim(0)
            emit_sim(1)
            nI = len(items)
            for i in range(nI + 1):
                if i < nI:
                    hd, ib = items[i]
                    pt_, off = hd // 2, (hd % 2) * 64
                    sim = SIMR[i % 2]
                    attn = attnP[i % 2]
                    sA = sAB[:, 4 * (i % 2):4 * (i % 2) + 4]
                    nj = len(jts_of(ib))
                    p.wait("dv", "pe", pesim[i])
                    if i >= 2:
                        p.wait("dv", "ac", acexp[i - 2])
                    if c > 0:
                        p.dv(lambda e, sim=sim: e.tensor_tensor(
                            out=sim[:, 0:128], in0=sim[:, 0:128], in1=m0[:],
                            op=ALU.add))
                        p.chain("dv")
                    p.dv(lambda e, sim=sim, nj=nj: e.tensor_tensor(
                        out=sim[:, 128 * (nj - 1):128 * nj],
                        in0=sim[:, 128 * (nj - 1):128 * nj], in1=m4[:], op=ALU.add))
                    p.chain("dv")
                    p.dv(lambda e, sim=sim, nj=nj, sA=sA: e.reduce_max(
                        out=sA[:, 0:1], in_=sim[:, 0:128 * nj], axis=AX.X,
                        negate=True))
                    dvmax[i] = p.inc("dv")
                if i >= 1:
                    j = i - 1
                    hdj, ibj = items[j]
                    ptj, offj = hdj // 2, (hdj % 2) * 64
                    attnj = attnP[j % 2]
                    attnTj = attnTP[j % 2]
                    sAj = sAB[:, 4 * (j % 2):4 * (j % 2) + 4]
                    njj = len(jts_of(ibj))
                    jtsj = jts_of(ibj)
                    p.wait("dv", "ac", acexp[j])
                    p.dv(lambda e, sA=sAj: e.reciprocal(out=sA[:, 2:3], in_=sA[:, 1:2]))
                    p.chain("dv")
                    p.dv(lambda e, nj=njj, attn=attnj, sA=sAj: e.tensor_scalar_mul(
                        out=attn[:, 0:128 * nj], in0=attn[:, 0:128 * nj],
                        scalar1=sA[:, 2:3]))
                    dvren[j] = p.inc("dv")
                    p.wait("pe", "dv", dvren[j])
                    if j >= 1:
                        p.wait("pe", "ac", acev[j - 1])
                    for k in range(njj):
                        p.pe(lambda e, k=k, attn=attnj: e.transpose(
                            out=pbt[:, 128 * k:128 * (k + 1)],
                            in_=attn[:, 128 * k:128 * (k + 1)], identity=ident[:]))
                    petr[j] = p.inc("pe")
                    p.wait("ac", "pe", petr[j])
                    if j >= 2:
                        p.wait("ac", "pe", peav[j - 2])
                    p.ac(lambda e, nj=njj, attnT=attnTj: e.copy(
                        out=attnT[:, 0:128 * nj], in_=pbt[:, 0:128 * nj]))
                    acev[j] = p.inc("ac")
                    p.wait("pe", "ac", acev[j])
                    p.wait("pe", "ac", acat[j - 1])
                    for k, jt in enumerate(jtsj):
                        slot = (ring if jt >= 4 else prev_ring) * 4 + (jt % 4)
                        mm(pb2[offj:offj + 64, 0:128], vtring[:, hdj, slot, :],
                           attnTj[:, 128 * k:128 * (k + 1)], start=(k == 0),
                           stop=(k == njj - 1), tile_position=(0, offj))
                    peav[j] = p.inc("pe")
                    p.wait("ac", "pe", peav[j])
                    p.ac(lambda e, off=offj, pt_=ptj, ib=ibj: e.copy(
                        out=atno[off:off + 64, pt_, 128 * ib:128 * (ib + 1)],
                        in_=pb2[off:off + 64, 0:128]))
                    acat[j] = p.inc("ac")
                if i < nI:
                    p.wait("ac", "dv", dvmax[i])
                    if i >= 2:
                        p.wait("ac", "pe", petr[i - 2])
                    p.ac(lambda e, sim=sim, nj=nj, attn=attn, sA=sA: e.activation(
                        out=attn[:, 0:128 * nj], in_=sim[:, 0:128 * nj], func=AF.Exp,
                        bias=sA[:, 0:1], accum_out=sA[:, 1:2]))
                    acexp[i] = p.inc("ac")
                    if i + 2 < nI:
                        emit_sim(i + 2)
            p.bar()

            def wo_slab(w):
                return wbuf[:, w % 2, 0:4 * 256].rearrange("p (k c) -> p k c", k=4)

            wold = {}
            wope = {}
            wodv = {-2: 0, -1: 0}
            def wo_load(w):
                p.wait("sy", "pe", wope.get(w - 2, 0))
                p.wait("gq", "pe", wope.get(w - 2, 0))
                wold[w] = wload(wo_slab(w), f"wo{l}", INNER, 256 * w, 256, par=w % 2)
            wo_load(0)
            wo_load(1)
            for w in range(4):
                p.wait("pe", wold[w][0], wold[w][1])
                p.wait("pe", "dv", wodv[w - 2])
                slab = wo_slab(w)
                for kt in range(4):
                    for oi in range(2):
                        mm(BANKS[2 * (w % 2) + oi],
                           slab[:, kt, 128 * oi:128 * (oi + 1)],
                           atno[:, kt, :], start=(kt == 0), stop=(kt == 3))
                wope[w] = p.inc("pe")
                if w + 2 < 4:
                    wo_load(w + 2)
                p.wait("dv", "pe", wope[w])
                p.dv(lambda e, b=BANKS[2 * (w % 2)], ot=2 * w: e.tensor_add(
                    out=pxi[:, ot, 1:1 + AC], in0=pxi[:, ot, 1:1 + AC],
                    in1=b[:, 0:AC]))
                p.chain("dv")
                p.dv(lambda e, b=BANKS[2 * (w % 2) + 1], ot=2 * w + 1: e.tensor_add(
                    out=pxi[:, ot, 1:1 + AC], in0=pxi[:, ot, 1:1 + AC],
                    in1=b[:, 0:AC]))
                wodv[w] = p.inc("dv")
            p.bar()
            for ot in range(NPT):
                p.ac(lambda e, ot=ot: e.copy(out=pxib[:, ot, 1:1 + AC],
                                             in_=pxi[:, ot, 1:1 + AC]))
            p.bar()
            if STAGE == 2:
                break

            def w1_slab(w):
                return wbuf[:, w % 2, 0:NPT * 256].rearrange("p (k c) -> p k c", k=NPT)
            p.wait("sy", "pe", p.cnt["pe"])
            p.wait("gq", "pe", p.cnt["pe"])
            w1ld = {0: wload(w1_slab(0), f"w1p{l}", DM, 0, 256, par=0),
                    1: wload(w1_slab(1), f"w1p{l}", DM, 256, 256, par=1)}

            st_ = 2 * l + 1
            p.dv(lambda e, st_=st_: e.tensor_copy(pxib[:, :, 0], scarry[:, st_, :]))
            p.chain("dv")
            p.dv(lambda e, st_=st_: e.tensor_copy(scarry[:, st_, :], pxib[:, :, AC]))
            p.bar()

            def sh_ap(t):
                return pxib[:, t, 1:1 + AC] if t < NPT // 2 else pxib[:, t, 0:AC]

            ln_stats(sh_ap, AC, psA0)
            for t in range(NPT):
                p.dv(lambda e, t=t: e.tensor_sub(out=tf[:, 0, :], in0=sh_ap(t),
                                                 in1=psA0[:, 0:AC]))
                p.chain("dv")
                p.dv(lambda e, t=t: e.tensor_mul(out=tf[:, 1, :], in0=tf[:, 0, :],
                                                 in1=psA0[:, AC:2 * AC]))
                p.chain("dv")
                p.dv(lambda e, t=t, l=l: e.tensor_scalar(
                    out=ln2o[:, t, :], in0=tf[:, 1, :], scalar1=lng[:, l, t:t + 1],
                    scalar2=lnb[:, l, t:t + 1], op0=ALU.mult, op1=ALU.add))
                p.chain("dv")
            p.bar()

            w1pe = {}
            w1ac = {}
            w1dv = {-3: 0, -2: 0, -1: 0}
            def w1_load(w):
                p.wait("sy", "pe", w1pe.get(w - 2, 0))
                p.wait("gq", "pe", w1pe.get(w - 2, 0))
                w1ld[w] = wload(w1_slab(w), f"w1p{l}", DM, 256 * w, 256, par=w % 2)
            for w in range(NHT):
                p.wait("pe", w1ld[w][0], w1ld[w][1])
                p.wait("pe", "dv", w1dv.get(w - 3, 0))
                slab = w1_slab(w)
                for kt in range(NPT):
                    for oi in range(2):
                        mm(BANKS[2 * (w % 3) + oi],
                           slab[:, kt, 128 * oi:128 * (oi + 1)],
                           ln2o[:, kt, :], start=(kt == 0), stop=(kt == NPT - 1))
                w1pe[w] = p.inc("pe")
                if w + 2 < NHT:
                    w1_load(w + 2)
                p.wait("ac", "pe", w1pe[w])
                p.wait("ac", "dv", w1dv.get(w - 2, 0))
                gb_ = BANKS[2 * (w % 3) + 1]
                ab_ = BANKS[2 * (w % 3)]
                p.ac(lambda e, g_=gb_, w=w, l=l: e.activation(
                    out=gel[:, w % 2, :], in_=g_[:, 0:AC], func=AF.Gelu,
                    bias=b1g[:, l, w:w + 1]))
                w1ac[w] = p.inc("ac")
                p.wait("dv", "ac", w1ac[w])
                p.dv(lambda e, a_=ab_, w=w, l=l: e.scalar_tensor_tensor(
                    out=gated[:, w, :], in0=a_[:, 0:AC], scalar=b1a[:, l, w:w + 1],
                    in1=gel[:, w % 2, :], op0=ALU.add, op1=ALU.mult))
                w1dv[w] = p.inc("dv")
            p.bar()

            def w2_slab(i):
                return wbuf[:, i % 2, 0:11 * 128].rearrange("p (k c) -> p k c", k=11)

            seq2 = [(w, gpr) for w in range(8) for gpr in range(2)]
            w2ld = {}
            w2pe = {}
            w2dv = {-2: 0, -1: 0, 0: 0, 1: 0}
            def w2_load(i):
                w_, g_ = seq2[i]
                p.wait("sy", "pe", w2pe.get(i - 2, 0))
                p.wait("gq", "pe", w2pe.get(i - 2, 0))
                w2ld[i] = wload(w2_slab(i), f"w2p{l}", HIPD_ if False else HIDP,
                                128 * w_, 128, rt0=11 * g_, rtn=11, par=i % 2)
            w2_load(0)
            w2_load(1)
            for i in range(16):
                w, gpr = seq2[i]
                p.wait("pe", w2ld[i][0], w2ld[i][1])
                p.wait("pe", "dv", w2dv[i - 2])
                slab = w2_slab(i)
                for kk in range(11):
                    kt = 11 * gpr + kk
                    mm(BANKS[4 + w % 2], slab[:, kk, 0:128], gated[:, kt, :],
                       start=(kt == 0), stop=(kt == NHT - 1))
                w2pe[i] = p.inc("pe")
                if i + 2 < 16:
                    w2_load(i + 2)
                if gpr == 1:
                    p.wait("dv", "pe", w2pe[i])
                    p.dv(lambda e, w=w, l=l, b=BANKS[4 + w % 2]:
                         e.scalar_tensor_tensor(
                             out=pxi[:, w, 1:1 + AC], in0=b[:, 0:AC],
                             scalar=b2s[:, l, w:w + 1], in1=pxi[:, w, 1:1 + AC],
                             op0=ALU.add, op1=ALU.add))
                    w2dv[i] = p.inc("dv")
                else:
                    w2dv[i] = w2dv[i - 1]
            p.bar()
            if l == 0 and nlayers > 1:
                for w in range(NPT):
                    p.ac(lambda e, w=w: e.copy(out=pxib[:, w, 1:1 + AC],
                                               in_=pxi[:, w, 1:1 + AC]))
                p.bar()

        for t in range(NPT):
            p.dv(lambda e, t=t: e.tensor_add(out=px[:, t, :], in0=px[:, t, :],
                                             in1=pxi[:, t, 1:1 + AC]))
        p.bar()
        for t in range(NPT):
            p.sy(lambda e, t=t, c0=c0: e.dma_start(out=out_r[:, t, c0:c0 + AC],
                                                   in_=px[:, t, :]))
        p.bar()

    p.wait("sy", "sy", p.cnt["sy"])

    with nc.Block() as block:
        p.materialize(block)
    nc._keepalive_ctx = ctx
    return nc


def kernel(_ncores=NCORE, **inputs):
    g = _host_prep(inputs)
    x = inputs["L0_attn_x"]
    nc = build()
    in_maps = []
    for cix in range(_ncores):
        m = {"xT": np.ascontiguousarray(np.asarray(x[cix]).T)}
        m.update(g)
        in_maps.append(m)
    res = run_bass_kernel_spmd(nc, in_maps, core_ids=list(range(_ncores)))
    outs = [np.asarray(r["out"]).T for r in res.results]
    return np.ascontiguousarray(np.stack(outs, 0))


# version 09
# speedup vs baseline: 1.0377x; 1.0091x over previous
import numpy as np
import ml_dtypes
from contextlib import ExitStack

import concourse.bass as bass
import concourse.mybir as mybir
from concourse.bass_utils import run_bass_kernel_spmd

F32 = mybir.dt.float32
BF16 = mybir.dt.bfloat16
AF = mybir.ActivationFunctionType
ALU = mybir.AluOpType
AX = mybir.AxisListType

BF = ml_dtypes.bfloat16

NCORE = 8
LSEQ = 4096
DM = 1024
DI = 2048
NST = 16
DTR = 64
NPT = 8
AC = 512
HC = 256
NCH = LSEQ // AC
HEADS = 8
DH = 64
INNER = 512
HID = 2730
HIDP = 2816
NHT = 22
NEG = -1e30

NCHRUN = NCH
STAGE = 4


def _pp(v):
    return np.ascontiguousarray(np.asarray(v, np.float32).reshape(-1, 128).T)


def _pp3(a):
    a = np.asarray(a, np.float32)
    F, K = a.shape
    return np.ascontiguousarray(a.reshape(F // 128, 128, K).transpose(1, 0, 2))


def _host_prep(inp):
    g = {}
    g["w_inT"] = np.ascontiguousarray(inp["L0_attn_in_proj_w"].T)
    g["w_xpT"] = np.ascontiguousarray(inp["L0_attn_x_proj_w"].T)
    g["w_dtT"] = np.ascontiguousarray(inp["L0_attn_dt_proj_w"].T)
    g["w_outT"] = np.ascontiguousarray(inp["L0_attn_out_proj_w"].T)
    g["convw"] = _pp3(inp["L0_attn_conv_w"][:, 0, :])
    cw = _pp3(inp["L0_attn_conv_w"][:, 0, :])
    cd = np.zeros((16, 4, 128, 128), np.float32)
    ii128 = np.arange(128)
    for t in range(16):
        for k in range(4):
            cd[t, k, ii128, ii128] = cw[:, t, k]
    g["convdiag"] = np.ascontiguousarray(cd.reshape(16 * 4 * 128, 128))
    g["convb"] = _pp(inp["L0_attn_conv_b"])
    g["dtb"] = _pp(inp["L0_attn_dt_proj_b"])
    g["Alog"] = _pp3(inp["L0_attn_A_log"])
    g["Dp"] = _pp(inp["L0_attn_D_param"])
    for l in (0, 1):
        g[f"wqkv{l}"] = np.ascontiguousarray(inp["L0_ffn_wqkv"][l])
        g[f"wo{l}"] = np.ascontiguousarray(inp["L0_ffn_wo"][l])
        w1 = inp["L0_ffn_w1"][l]
        a = np.zeros((DM, HIDP), np.float32); a[:, :HID] = w1[:, :HID]
        gg = np.zeros((DM, HIDP), np.float32); gg[:, :HID] = w1[:, HID:]
        at = a.reshape(DM, NHT, 128); gt = gg.reshape(DM, NHT, 128)
        g[f"w1p{l}"] = np.ascontiguousarray(
            np.stack([at, gt], axis=2).reshape(DM, 2 * NHT * 128))
        w2 = np.zeros((HIDP, DM), np.float32); w2[:HID] = inp["L0_ffn_w2"][l]
        g[f"w2p{l}"] = w2
        b1 = inp["L0_ffn_b1"][l]
        b1a = np.zeros(HIDP, np.float32); b1a[:HID] = b1[:HID]
        b1g = np.zeros(HIDP, np.float32); b1g[:HID] = b1[HID:]
        g[f"b1a{l}"] = _pp(b1a); g[f"b1g{l}"] = _pp(b1g)
    g["qs"] = np.ascontiguousarray(np.stack(
        [np.tile(np.asarray(inp["L0_ffn_q_scale"][l], np.float32), 2) for l in (0, 1)], 1))
    g["ks"] = np.ascontiguousarray(np.stack(
        [np.tile(np.asarray(inp["L0_ffn_k_scale"][l], np.float32), 2) for l in (0, 1)], 1))
    g["lng"] = np.ascontiguousarray(np.stack([_pp(inp["L0_ffn_ln_g"][l]) for l in (0, 1)], 1))
    g["lnb"] = np.ascontiguousarray(np.stack([_pp(inp["L0_ffn_ln_b"][l]) for l in (0, 1)], 1))
    g["b2"] = np.ascontiguousarray(np.stack([_pp(inp["L0_ffn_b2"][l]) for l in (0, 1)], 1))

    ii = np.arange(128)
    g["m0"] = np.where(ii[None, :] < ii[:, None], NEG, 0.0).astype(BF)
    g["m4"] = np.where(ii[None, :] > ii[:, None], NEG, 0.0).astype(BF)
    g["ident"] = np.eye(128, dtype=BF)
    identD = np.zeros((128, 64), np.float32)
    for p_ in range(128):
        identD[p_, p_ % 64] = 1.0
    g["identD"] = identD.astype(BF)
    g["onescol"] = np.ones((1, 128), dtype=BF)
    g["onek"] = np.ones((128, 1), dtype=BF)
    g["cvals"] = np.tile(np.array([[1e-5, -0.5, 1.0 / DM, 8.0, -1.0, 1.0]], np.float32), (128, 1))
    bd = np.zeros((128, 4, 8), np.float32)
    for t in range(4):
        bd[:64, t, 2 * t] = 1.0; bd[64:, t, 2 * t + 1] = 1.0
    g["bdones"] = bd.astype(BF)
    sel8 = np.zeros((8, 4, 128), np.float32)
    for t in range(4):
        for p_ in range(128):
            sel8[2 * t + (p_ >= 64), t, p_] = 1.0
    g["sel8"] = np.ascontiguousarray(sel8.reshape(8, 512)).astype(BF)
    sel16 = np.zeros((16, 16, 128), np.float32)
    for n in range(16):
        sel16[n, n, :] = 1.0
    g["sel16"] = np.ascontiguousarray(sel16.reshape(16, 2048)).astype(BF)
    P = np.zeros((128, 128), np.float32)
    for di in range(128):
        h0 = (di // 64) * 64; d = di % 64
        if d < 32:
            P[h0 + d + 32, di] = -1.0
        else:
            P[h0 + d - 32, di] = 1.0
    g["rotP"] = P.astype(BF)

    dih = np.arange(64); j = dih % 32
    invfreq = 1.0 / 10000.0 ** ((2 * j) / DH)
    t = np.arange(LSEQ, dtype=np.float64)
    freqs = t[None, :] * invfreq[:, None]
    base = ((2 * j) + 0.4 * DH) / (1.4 * DH)
    power = (t - LSEQ // 2) / float(AC // 2)
    scale = base[:, None] ** power[None, :]
    cos = np.cos(freqs); sin = np.sin(freqs)
    rows2 = lambda m: np.ascontiguousarray(np.concatenate([m, m], 0).astype(BF))
    g["rcq"] = rows2(cos * scale); g["rsq"] = rows2(sin * scale)
    g["rck"] = rows2(cos / scale); g["rsk"] = rows2(sin / scale)
    return g


ENGINES = ("sy", "gq", "pe", "ac", "dv")
SEMS = ("sy", "gq", "pe", "ac", "dv", "gc", "wl0", "wl1", "wg0", "wg1", "wb0", "wb1")


class Prog:
    def __init__(self, nc):
        self.nc = nc
        self.prog = {e: [] for e in ENGINES}
        self.iswait = {e: [] for e in ENGINES}
        self.cnt = {e: 0 for e in SEMS}
        self.dirty = {e: False for e in SEMS}
        self.snap = dict(self.cnt)
        self.lastwait = {e: {q: 0 for q in SEMS} for e in ENGINES}
        self.pending = {e: False for e in ENGINES}
        self.sems = None

    def _emit(self, e, fn, compute=False):
        if self.pending[e]:
            self.pending[e] = False
            for q in SEMS:
                self.wait(e, q, self.snap[q])
        self.prog[e].append([fn, []])
        self.iswait[e].append(False)
        if e == "sy" or (e == "gq" and not compute):
            self.prog[e][-1][1] = [(e, 16)]
            self.cnt[e] += 16
        elif e == "gq":
            self.dirty["gc"] = True
        else:
            self.dirty[e] = True

    def sy(self, fn): self._emit("sy", fn)
    def gq(self, fn): self._emit("gq", fn)
    def gc(self, fn): self._emit("gq", fn, compute=True)

    def dma(self, queue, fn, sem):
        if self.pending[queue]:
            self.pending[queue] = False
            for q in SEMS:
                self.wait(queue, q, self.snap[q])
        self.prog[queue].append([fn, [(sem, 16)]])
        self.iswait[queue].append(False)
        self.cnt[sem] += 16
    def pe(self, fn): self._emit("pe", fn)
    def ac(self, fn): self._emit("ac", fn)
    def dv(self, fn): self._emit("dv", fn)

    def inc(self, e):
        stream = "gq" if e == "gc" else e
        assert self.prog[stream] and not self.iswait[stream][-1], e
        self.prog[stream][-1][1].append((e, 1))
        self.cnt[e] += 1
        self.dirty[e] = False
        return self.cnt[e]

    def wait(self, e, q, val):
        if val <= 0 or self.lastwait[e][q] >= val:
            return
        self.lastwait[e][q] = val
        self.prog[e].append(
            [lambda eng, q=q, val=val: eng.wait_ge(self.sems[q], val), []])
        self.iswait[e].append(True)

    def chain(self, e):
        self.inc(e)
        st = "gq" if e == "gc" else e
        self.wait(st, e, self.cnt[e])

    def bar(self):
        for e in ("pe", "ac", "dv", "gc"):
            if self.dirty[e]:
                self.inc(e)
        self.snap = dict(self.cnt)
        for e in ENGINES:
            self.pending[e] = True

    def materialize(self, block):
        sems = self.sems

        def runner(progl):
            def body(eng):
                for fn, incs in progl:
                    inst = fn(eng)
                    for s, n in incs:
                        inst.then_inc(sems[s], n)
            return body

        block.sync(runner(self.prog["sy"]))
        block.gpsimd(runner(self.prog["gq"]))
        block.tensor(runner(self.prog["pe"]))
        block.scalar(runner(self.prog["ac"]))
        block.vector(runner(self.prog["dv"]))


def build(debug=False):
    nc = bass.Bass("TRN2", target_bir_lowering=False)
    ctx = ExitStack()

    D = {}
    def din(name, shape, dt=F32):
        D[name] = nc.dram_tensor(name, list(shape), dt, kind="ExternalInput").ap()
    din("xT", (DM, LSEQ))
    din("w_inT", (DM, 2 * DI)); din("w_xpT", (DI, 96)); din("w_dtT", (DTR, DI))
    din("w_outT", (DI, DM))
    din("convw", (128, 16, 4)); din("convdiag", (16 * 4 * 128, 128)); din("convb", (128, 16)); din("dtb", (128, 16))
    din("Alog", (128, 16, 16)); din("Dp", (128, 16))
    for l in (0, 1):
        din(f"wqkv{l}", (DM, 3 * INNER)); din(f"wo{l}", (INNER, DM))
        din(f"w1p{l}", (DM, 2 * HIDP)); din(f"w2p{l}", (HIDP, DM))
        din(f"b1a{l}", (128, NHT)); din(f"b1g{l}", (128, NHT))
    din("qs", (128, 2)); din("ks", (128, 2))
    din("lng", (128, 2, 8)); din("lnb", (128, 2, 8)); din("b2", (128, 2, 8))
    din("m0", (128, 128), BF16); din("m4", (128, 128), BF16)
    din("ident", (128, 128), BF16); din("identD", (128, 64), BF16)
    din("onescol", (1, 128), BF16); din("onek", (128, 1), BF16)
    din("cvals", (128, 6))
    din("sel8", (8, 512), BF16); din("sel16", (16, 2048), BF16)
    din("bdones", (128, 4, 8), BF16)
    din("rotP", (128, 128), BF16)
    for nm in ("rcq", "rsq", "rck", "rsk"):
        din(nm, (128, LSEQ), BF16)
    out = nc.dram_tensor("out", [DM, LSEQ], F32, kind="ExternalOutput").ap()
    out_r = out.rearrange("(t p) l -> p t l", p=128)
    xT_r = D["xT"].rearrange("(t p) l -> p t l", p=128)

    _nm = [0]
    def sb(shape, dt=F32):
        _nm[0] += 1
        return ctx.enter_context(nc.sbuf_tensor(f"t{_nm[0]}", list(shape), dt))

    px = sb((128, NPT, AC))
    hcarry = sb((128, 16, 16))
    ccarry = sb((128, 16, 4))
    scarry = sb((128, 4, NPT), BF16)
    kringT = [sb((128, 4, 2 * AC), BF16) for _ in (0, 1)]
    vtringT = [sb((128, HEADS, 8, DH), BF16) for _ in (0, 1)]
    m0 = sb((128, 128), BF16); m4 = sb((128, 128), BF16)
    ident = sb((128, 128), BF16); identD = sb((128, 64), BF16)
    onescol = sb((1, 128), BF16); onek = sb((128, 1), BF16)
    cvals = sb((128, 6))
    sel8 = sb((8, 512), BF16); sel16 = sb((16, 2048), BF16)
    bdones = sb((128, 4, 8), BF16)
    rotP = sb((128, 128), BF16)
    Aneg = sb((128, 16, 16)); convw = sb((128, 16, 4))
    convb = sb((128, 16)); dtb = sb((128, 16)); Dp = sb((128, 16))
    qs8 = sb((128, 2)); ks1 = sb((128, 2))
    lng = sb((128, 2, NPT)); lnb = sb((128, 2, NPT)); b2s = sb((128, 2, NPT))
    b1a = sb((128, 2, NHT)); b1g = sb((128, 2, NHT))
    dtW = sb((DTR, DI), BF16)
    xpW = sb((128, 16, 96), BF16)
    dlr = sb((DTR, HC), BF16)
    bcrowB = sb((16, HC), BF16)
    bcrowC = sb((16, HC), BF16)
    rows1 = sb((1, 5 * AC))
    rowsb1 = sb((1, 2 * AC), BF16)
    nrm = sb((8, 2, AC))
    nrmb = sb((8, AC), BF16)
    sAB = sb((128, 8))
    wbuf = sb((128, 2, 2048), BF16)
    cbuf = sb((128, 2, 4, 128), BF16)

    A16N = 29184
    A32N = 13056
    ar16 = sb((128, A16N), BF16)
    ar32 = sb((128, A32N))

    def v16(off, shape):
        n = int(np.prod(shape[1:]))
        ap = ar16[:, off:off + n]
        if len(shape) == 3:
            ap = ap.rearrange("p (a b) -> p a b", a=shape[1])
        return ap, off + n

    def v32(off, shape):
        n = int(np.prod(shape[1:]))
        ap = ar32[:, off:off + n]
        if len(shape) == 3:
            ap = ap.rearrange("p (a b) -> p a b", a=shape[1])
        return ap, off + n

    o = 0
    lnout, o = v16(o, (128, NPT, HC))
    xinner, o = v16(o, (128, 16, HC + 4))
    zsil, o = v16(o, (128, 16, HC))
    xconv, o = v16(o, (128, 16, HC))
    Bb, o = v16(o, (128, 16, HC))
    Cb, o = v16(o, (128, 16, HC))
    wt, o = v16(o, (128, HC))
    ya, o = v16(o, (128, 4, HC))
    yb, o = v16(o, (128, 2, HC))
    yg, o = v16(o, (128, 16, HC))
    assert o <= A16N, o
    dBxA = ar16[:, 2048:2048 + 16 * HC].rearrange("p (a b) -> p a b", a=16)
    dBxP = (dBxA, dBxA)
    yah = ar16[:, 0:8 * HC].rearrange("p (a b) -> p a b", a=8)
    tt1 = ya
    tt2 = yb
    o = 0
    pxib, o = v16(o, (128, NPT, AC + 1))
    rott, o = v16(o, (128, 4, AC))
    q2b, o = v16(o, (128, 4, AC))
    q3b, o = v16(o, (128, 4, AC))
    vbf, o = v16(o, (128, 4, AC))
    sqb, o = v16(o, (128, 2, AC))
    atno, o = v16(o, (128, 4, AC))
    gated, o = v16(o, (128, NHT, AC))
    sqb2, o = v16(o, (128, 2, AC))
    assert o <= A16N, o
    _vb0 = vbf[:, 0:4, :].rearrange("p a b -> p (a b)")
    attnP = (_vb0[:, 0:640], _vb0[:, 640:1280])
    _qb0 = q2b[:, 0:4, :].rearrange("p a b -> p (a b)")
    attnTP = (_qb0[:, 0:640], _qb0[:, 640:1280])
    _q2off = NPT * (AC + 1) + 4 * AC
    ln2o = ar16[:, _q2off:_q2off + NPT * AC].rearrange("p (a b) -> p a b", a=NPT)
    gel = sqb

    o = 0
    dAb0, o = v32(o, (128, 16, HC))
    dAb1, o = v32(o, (128, 16, HC))
    dtt, o = v32(o, (128, 2, HC))
    yt, o = v32(o, (128, HC))
    assert o <= A32N, o
    dABUF = (dAb0, dAb1)
    o = 0
    pxi, o = v32(o, (128, NPT, AC + 1))
    qraw, o = v32(o, (128, 4, AC))
    tf, o = v32(o, (128, 2, AC))
    assert o <= A32N, o

    psA0 = ctx.enter_context(nc.psum_tensor([128, 1024], F32))
    psA1 = ctx.enter_context(nc.psum_tensor([128, 1024], F32))
    pb0 = ctx.enter_context(nc.psum_tensor([128, 512], F32))
    pb1 = ctx.enter_context(nc.psum_tensor([128, 512], F32))
    pb2 = ctx.enter_context(nc.psum_tensor([128, 512], F32))
    pbt = ctx.enter_context(nc.psum_tensor([128, 1024], BF16))
    BANKS = [psA0[:, 0:512], psA0[:, 512:1024],
             psA1[:, 0:512], psA1[:, 512:1024], pb0[:], pb1[:]]

    p = Prog(nc)
    p.sems = {e: ctx.enter_context(nc.semaphore(f"s_{e}")) for e in SEMS}

    def mm(bank_ap, lhsT_ap, rhs_ap, start, stop=None, tile_position=None):
        if stop is None:
            stop = start
        p.pe(lambda e, o_=bank_ap, l_=lhsT_ap, r_=rhs_ap, s_=start, t_=stop,
             tp=tile_position:
             e.matmul(out=o_, lhsT=l_, rhs=r_, start=s_, stop=t_, tile_position=tp))

    SCR16 = {}
    STORED = set()

    LASTWB = {}

    def wload(buf_ap, name, rows_, col0, cols, rt0=0, rtn=None, par=0):
        dram2d = D[name]
        if name not in SCR16:
            SCR16[name] = nc.dram_tensor(
                f"scr_{name}", [dram2d.shape[0], dram2d.shape[1]], BF16,
                kind="Internal").ap()
        scr = SCR16[name].rearrange("(t p) c -> p t c", p=128)
        src = dram2d.rearrange("(t p) c -> p t c", p=128)
        nt = rtn if rtn is not None else rows_ // 128
        step = 4
        key = (name, col0, rt0)
        first = key not in STORED
        STORED.add(key)
        wsem = f"wg{par}" if first else f"wl{par}"
        p.wait("gq" if first else "sy", f"wb{par}", LASTWB.get(par, 0))
        for tt in range(0, nt, step):
            n = min(step, nt - tt)
            if first:
                p.dma("gq", lambda e, d=buf_ap[:, tt:tt + n, :],
                      s=src[:, rt0 + tt:rt0 + tt + n, col0:col0 + cols]:
                      e.dma_start(out=d, in_=s), wsem)
            else:
                p.dma("sy", lambda e, d=buf_ap[:, tt:tt + n, :],
                      s=scr[:, rt0 + tt:rt0 + tt + n, col0:col0 + cols]:
                      e.dma_start(out=d, in_=s), wsem)
        if first:
            p.wait("sy", wsem, p.cnt[wsem])
            for tt in range(0, nt, step):
                n = min(step, nt - tt)
                p.dma("sy", lambda e, d=scr[:, rt0 + tt:rt0 + tt + n, col0:col0 + cols],
                      s=buf_ap[:, tt:tt + n, :]:
                      e.dma_start(out=d, in_=s), f"wb{par}")
            LASTWB[par] = p.cnt[f"wb{par}"]
        return (wsem, p.cnt[wsem])

    def loadc(dst_ap, src_ap, cast=False):
        (p.gq if cast else p.sy)(
            lambda e, d=dst_ap, s=src_ap: e.dma_start(out=d, in_=s))

    loadc(m0[:], D["m0"]); loadc(m4[:], D["m4"])
    loadc(ident[:], D["ident"]); loadc(identD[:], D["identD"])
    loadc(onescol[:], D["onescol"]); loadc(onek[:], D["onek"])
    loadc(cvals[:], D["cvals"])
    loadc(sel8[:], D["sel8"]); loadc(sel16[:], D["sel16"])
    loadc(bdones[:], D["bdones"]); loadc(rotP[:], D["rotP"])
    loadc(convw[:], D["convw"]); loadc(convb[:], D["convb"])
    loadc(dtb[:], D["dtb"]); loadc(Dp[:], D["Dp"])
    loadc(Aneg[:], D["Alog"])
    loadc(qs8[:], D["qs"]); loadc(ks1[:], D["ks"])
    loadc(lng[:], D["lng"]); loadc(lnb[:], D["lnb"]); loadc(b2s[:], D["b2"])
    for l in (0, 1):
        loadc(b1a[:, l, :], D[f"b1a{l}"]); loadc(b1g[:, l, :], D[f"b1g{l}"])
    loadc(dtW[:], D["w_dtT"], cast=True)
    loadc(xpW[:], D["w_xpT"].rearrange("(t p) c -> p t c", p=128), cast=True)
    p.bar()
    AnegF = Aneg[:].rearrange("p a b -> p (a b)")
    p.ac(lambda e: e.activation(out=AnegF, in_=AnegF, func=AF.Exp))
    p.ac(lambda e: e.activation(out=qs8[:], in_=qs8[:], func=AF.Copy, scale=cvals[:, 3:4]))
    p.dv(lambda e: e.memset(hcarry[:].rearrange("p a b -> p (a b)"), 0.0))
    p.dv(lambda e: e.memset(ccarry[:].rearrange("p a b -> p (a b)"), 0.0))
    p.dv(lambda e: e.memset(scarry[:].rearrange("p a b -> p (a b)"), 0.0))
    for l in (0, 1):
        p.dv(lambda e, l=l: e.memset(
            vtringT[l][:].rearrange("p a b c -> p (a b c)"), 0.0))
    p.bar()
    p.dv(lambda e: e.tensor_scalar_mul(out=AnegF, in0=AnegF, scalar1=-1.0))
    p.bar()

    def seg(k, n):
        return rows1[0:1, k * AC:k * AC + n]

    def ln_post(n, bank):
        p.ac(lambda e, n=n, bank=bank: e.activation(out=seg(0, n), in_=bank[0:1, 0:n],
                                         func=AF.Copy, scale=cvals[0:1, 2:3]))
        ac0 = p.inc("ac")
        p.wait("dv", "ac", ac0)
        p.dv(lambda e, n=n: e.tensor_mul(out=seg(1, n), in0=seg(0, n), in1=seg(0, n)))
        p.chain("dv")
        p.dv(lambda e, n=n, bank=bank: e.scalar_tensor_tensor(
            out=seg(2, n), in0=bank[0:1, n:2 * n], scalar=1.0 / DM,
            in1=seg(1, n), op0=ALU.mult, op1=ALU.subtract))
        dv0 = p.inc("dv")
        p.wait("ac", "dv", dv0)
        p.ac(lambda e, n=n: e.activation(out=seg(3, n), in_=seg(2, n),
                                         func=AF.Ln, bias=cvals[0:1, 0:1]))
        p.chain("ac")
        p.ac(lambda e, n=n: e.activation(out=seg(4, n), in_=seg(3, n),
                                         func=AF.Exp, scale=cvals[0:1, 1:2]))
        p.chain("ac")
        p.ac(lambda e, n=n: e.copy(out=rowsb1[0:1, 0:n], in_=seg(0, n)))
        p.chain("ac")
        p.ac(lambda e, n=n: e.copy(out=rowsb1[0:1, n:2 * n], in_=seg(4, n)))
        ac1 = p.inc("ac")
        p.wait("pe", "ac", ac1)
        if n >= 512:
            mm(bank[:, 0:n], onescol[:], rowsb1[0:1, 0:n], start=True)
            mm(bank[:, n:2 * n], onescol[:], rowsb1[0:1, n:2 * n], start=True)
        else:
            mm(bank[:, 0:2 * n], onescol[:], rowsb1[0:1, 0:2 * n], start=True)
        p.bar()

    def ln_stats(src_ap_fn, n, bank, s1=None, s2=None):
        if s1 is None:
            s1 = sqb
        if s2 is None:
            s2 = sqb2
        for t in range(NPT):
            half = t % 2
            p.ac(lambda e, t=t, n=n, half=half: e.copy(
                out=s1[:, half, 0:n], in_=src_ap_fn(t)))
            p.chain("ac")
            p.ac(lambda e, t=t, n=n, half=half: e.activation(
                out=s2[:, half, 0:n], in_=src_ap_fn(t), func=AF.Square))
            lnac = p.inc("ac")
            p.wait("pe", "ac", lnac)
            mm(bank[0:1, 0:n], onek[:], s1[:, half, 0:n], start=(t == 0),
               stop=(t == NPT - 1 and n >= 512))
            mm(bank[0:1, n:2 * n], onek[:], s2[:, half, 0:n],
               start=(t == 0 and n >= 512), stop=(t == NPT - 1))
            if t < NPT - 1:
                pef = p.inc("pe")
                p.wait("ac", "pe", pef)
        p.bar()
        ln_post(n, bank)

    for c in range(NCHRUN):
        c0 = c * AC
        for t in range(NPT):
            p.sy(lambda e, t=t, c0=c0: e.dma_start(out=px[:, t, :],
                                                   in_=xT_r[:, t, c0:c0 + AC]))
        p.bar()

        for hh in range(2):
            h0 = hh * HC
            lnscrA = ar16[:, 2048:2560].rearrange("p (a b) -> p a b", a=2)
            lnscrB = ar16[:, 2560:3072].rearrange("p (a b) -> p a b", a=2)
            ln_stats(lambda t, h0=h0: px[:, t, h0:h0 + HC], HC, pb2,
                     s1=lnscrA, s2=lnscrB)
            for t in range(NPT):
                p.dv(lambda e, t=t, h0=h0: e.tensor_sub(
                    out=tf[:, 0, 0:HC], in0=px[:, t, h0:h0 + HC], in1=pb2[:, 0:HC]))
                p.chain("dv")
                p.dv(lambda e, t=t: e.tensor_mul(out=lnout[:, t, :],
                                                 in0=tf[:, 0, 0:HC],
                                                 in1=pb2[:, HC:2 * HC]))
                p.chain("dv")
            p.bar()

            p.dv(lambda e: e.tensor_copy(xinner[:, :, 1:4], ccarry[:, :, 1:4]))
            p.bar()

            def ip_slab(w):
                return wbuf[:, w % 2, 0:NPT * 256].rearrange("p (k c) -> p k c", k=NPT)

            ipld = {}
            ippe = {}
            ipac = {-3: 0, -2: 0, -1: 0}
            def ip_load(w):
                p.wait("sy", "pe", ippe.get(w - 2, 0))
                p.wait("gq", "pe", ippe.get(w - 2, 0))
                ipld[w] = wload(ip_slab(w), "w_inT", DM, 256 * w, 256, par=w % 2)
            ip_load(0)
            ip_load(1)
            for w in range(16):
                p.wait("pe", ipld[w][0], ipld[w][1])
                p.wait("pe", "ac", ipac.get(w - 3, 0))
                slab = ip_slab(w)
                for kt in range(NPT):
                    for oi in range(2):
                        mm(BANKS[2 * (w % 3) + oi][:, 0:HC],
                           slab[:, kt, 128 * oi:128 * (oi + 1)],
                           lnout[:, kt, :], start=(kt == 0),
                           stop=(kt == NPT - 1))
                ippe[w] = p.inc("pe")
                if w + 2 < 16:
                    ip_load(w + 2)
                p.wait("ac", "pe", ippe[w])
                for oi in range(2):
                    ot = 2 * w + oi
                    bk = BANKS[2 * (w % 3) + oi]
                    if ot < 16:
                        p.ac(lambda e, b=bk, d=ot: e.copy(
                            out=xinner[:, d, 4:4 + HC], in_=b[:, 0:HC]))
                    else:
                        p.ac(lambda e, b=bk, d=ot - 16: e.activation(
                            out=zsil[:, d, :], in_=b[:, 0:HC], func=AF.Silu))
                ipac[w] = p.inc("ac")
            p.bar()
            p.dv(lambda e: e.tensor_copy(ccarry[:, :, 1:4],
                                         xinner[:, :, 1 + HC:4 + HC]))
            p.bar()

            def op_slab(w):
                return wbuf[:, w % 2, 0:16 * 128].rearrange("p (k c) -> p k c", k=16)
            if hh == 1:
                p.wait("sy", "pe", p.cnt["pe"])
                p.wait("gq", "pe", p.cnt["pe"])
                opld = {0: wload(op_slab(0), "w_outT", DI, 0, 128, par=0),
                        1: wload(op_slab(1), "w_outT", DI, 128, 128, par=1)}

            cvld = {}
            cvpe = {}
            cvac = {-2: 0, -1: 0}
            def cv_load(t):
                p.wait("sy", "pe", cvpe.get(t - 2, 0))
                p.wait("gq", "pe", cvpe.get(t - 2, 0))
                cvld[t] = wload(cbuf[:, t % 2, :, :], "convdiag", 16 * 4 * 128, 0,
                                128, rt0=4 * t, rtn=4, par=t % 2)
            cv_load(0)
            cv_load(1)
            for t in range(16):
                p.wait("pe", cvld[t][0], cvld[t][1])
                p.wait("pe", "ac", cvac[t - 2])
                for k in range(4):
                    mm(BANKS[t % 2][:, 0:HC], cbuf[:, t % 2, k, :],
                       xinner[:, t, 1 + k:1 + k + HC], start=(k == 0), stop=(k == 3))
                cvpe[t] = p.inc("pe")
                if t + 2 < 16:
                    cv_load(t + 2)
                p.wait("ac", "pe", cvpe[t])
                p.ac(lambda e, t=t, b=BANKS[t % 2]: e.activation(
                    out=xconv[:, t, :], in_=b[:, 0:HC], func=AF.Silu,
                    bias=convb[:, t:t + 1]))
                cvac[t] = p.inc("ac")
            p.bar()

            for kt in range(16):
                mm(pb0[0:64, 0:HC], xpW[:, kt, 0:64], xconv[:, kt, :], start=(kt == 0), stop=(kt == 15))
                mm(pb1[0:16, 0:HC], xpW[:, kt, 64:80], xconv[:, kt, :], start=(kt == 0), stop=(kt == 15))
                mm(pb2[0:16, 0:HC], xpW[:, kt, 80:96], xconv[:, kt, :], start=(kt == 0), stop=(kt == 15))
            p.bar()
            p.ac(lambda e: e.copy(out=dlr[:, :], in_=pb0[0:DTR, 0:HC]))
            p.ac(lambda e: e.copy(out=bcrowB[:, :], in_=pb1[0:16, 0:HC]))
            p.ac(lambda e: e.copy(out=bcrowC[:, :], in_=pb2[0:16, 0:HC]))
            p.bar()
            for src_, dst in ((bcrowB, Bb), (bcrowC, Cb)):
                for n4 in range(4):
                    for k4 in range(4):
                        n = 4 * n4 + k4
                        mm(psA0[:, 256 * k4:256 * k4 + HC],
                           sel16[:, 128 * n:128 * (n + 1)], src_[:, :], start=True)
                    p.bar()
                    p.ac(lambda e, n4=n4, dst=dst: e.copy(
                        out=dst[:, 4 * n4:4 * n4 + 4, :],
                        in_=psA0[:].rearrange("p (a b) -> p a b", a=4)))
                    p.bar()

            mm(pb0[:, 0:HC], dtW[:, 0:128], dlr[:, :], start=True)
            p.bar()
            pe_dt0 = p.cnt["pe"]
            def inj_slot(w):
                bank = (psA0, psA1)[w // 4]
                return bank[:, 256 * (w % 4):256 * (w % 4) + HC]
            if hh == 1:
                injpe = {}
                def inj_load(w):
                    p.wait("sy", "pe", injpe.get(w - 2, 0))
                    p.wait("gq", "pe", injpe.get(w - 2, 0))
                    opld[w] = wload(op_slab(w), "w_outT", DI, 128 * w, 128,
                                    par=w % 2)
                for w in range(8):
                    p.wait("pe", opld[w][0], opld[w][1])
                    slab = op_slab(w)
                    for kt in range(16):
                        mm(inj_slot(w), slab[:, kt, 0:128], yg[:, kt, :],
                           start=(kt == 0 and w % 2 == 0),
                           stop=(kt == 15 and w % 2 == 1))
                    injpe[w] = p.inc("pe")
                    if w + 2 < 8:
                        inj_load(w + 2)
                p.wait("sy", "pe", injpe[6])
                p.wait("gq", "pe", injpe[6])
                opld2 = {0: wload(op_slab(0), "w_outT", DI, 0, 128, par=0)}
                p.wait("sy", "pe", injpe[7])
                p.wait("gq", "pe", injpe[7])
                opld2[1] = wload(op_slab(1), "w_outT", DI, 128, 128, par=1)
            acdtt = {}
            acfin = {}
            dvtts = {}
            dvfin = {}
            pefin = {0: pe_dt0}
            for t in range(16):
                dAb = dABUF[t % 2]
                dBx = dBxP[t % 2]
                pbank = (pb0, pb1)[t % 2]
                if t < 15:
                    if t >= 1:
                        p.wait("pe", "ac", acdtt[t - 1])
                    mm((pb0, pb1)[(t + 1) % 2][:, 0:HC],
                       dtW[:, 128 * (t + 1):128 * (t + 2)], dlr[:, :], start=True)
                    pefin[t + 1] = p.inc("pe")
                p.wait("ac", "pe", pefin[t])
                if t >= 2:
                    p.wait("ac", "dv", dvtts[t - 2])
                if t >= 1:
                    p.wait("ac", "ac", acfin[t - 1])
                p.ac(lambda e, t=t, b=pbank, dAb=dAb: e.activation(
                    out=dAb[:, 0, :], in_=b[:, 0:HC], func=AF.Exp,
                    bias=dtb[:, t:t + 1]))
                p.chain("ac")
                p.ac(lambda e, t=t, dAb=dAb: e.activation(
                    out=dtt[:, t % 2, :], in_=dAb[:, 0, :], func=AF.Ln,
                    bias=cvals[:, 5:6]))
                acdtt[t] = p.inc("ac")
                p.wait("ac", "ac", acdtt[t])
                for n in range(16):
                    p.ac(lambda e, t=t, n=n, dAb=dAb: e.activation(
                        out=dAb[:, n, :], in_=dtt[:, t % 2, :], func=AF.Exp,
                        scale=Aneg[:, t, n:n + 1]))
                acfin[t] = p.inc("ac")
                p.wait("dv", "ac", acdtt[t])
                if t >= 1:
                    p.wait("dv", "dv", dvfin[t - 1])
                p.dv(lambda e, t=t: e.tensor_mul(out=wt[:, :], in0=dtt[:, t % 2, :],
                                                 in1=xconv[:, t, :]))
                p.chain("dv")
                p.dv(lambda e, dBx=dBx: e.tensor_tensor(
                    out=dBx[:],
                    in0=wt[:, :].unsqueeze(1).broadcast_to([128, 16, HC]),
                    in1=Bb[:], op=ALU.mult))
                p.chain("dv")
                p.wait("dv", "ac", acfin[t])
                p.dv(lambda e, t=t, dAb=dAb: e.tensor_mul(out=yt[:, 0:16], in0=dAb[:, :, 0],
                                                 in1=hcarry[:, t, :]))
                p.chain("dv")
                p.dv(lambda e, dBx=dBx: e.tensor_add(out=dBx[:, :, 0], in0=dBx[:, :, 0],
                                            in1=yt[:, 0:16]))
                p.chain("dv")
                p.dv(lambda e, dAb=dAb: e.memset(dAb[:, :, 0], 0.0))
                p.chain("dv")
                p.dv(lambda e, dAb=dAb, dBx=dBx: e.tensor_tensor_scan(
                    out=dBx[:].rearrange("p n l -> p (n l)"),
                    data0=dAb[:].rearrange("p n l -> p (n l)"),
                    data1=dBx[:].rearrange("p n l -> p (n l)"),
                    initial=0.0, op0=ALU.mult, op1=ALU.add))
                dvtts[t] = p.inc("dv")
                p.wait("dv", "dv", dvtts[t])
                p.dv(lambda e, t=t, dBx=dBx: e.tensor_copy(hcarry[:, t, :], dBx[:, :, HC - 1]))
                p.chain("dv")
                p.dv(lambda e, dBx=dBx: e.tensor_tensor(
                    out=dBx[:], in0=dBx[:], in1=Cb[:], op=ALU.mult))
                p.chain("dv")
                p.dv(lambda e, dBx=dBx: e.tensor_add(
                    out=yah[:], in0=dBx[:, 0:8, :], in1=dBx[:, 8:16, :]))
                p.chain("dv")
                p.dv(lambda e: e.tensor_add(out=tt1[:], in0=yah[:, 0:4, :],
                                            in1=yah[:, 4:8, :]))
                p.chain("dv")
                p.dv(lambda e: e.tensor_add(out=tt2[:], in0=tt1[:, 0:2, :],
                                            in1=tt1[:, 2:4, :]))
                p.chain("dv")
                p.dv(lambda e: e.tensor_add(out=yt[:, :], in0=tt2[:, 0, :],
                                            in1=tt2[:, 1, :]))
                p.chain("dv")
                p.dv(lambda e, t=t: e.scalar_tensor_tensor(
                    out=yt[:, :], in0=xconv[:, t, :], scalar=Dp[:, t:t + 1],
                    in1=yt[:, :], op0=ALU.mult, op1=ALU.add))
                p.chain("dv")
                if hh == 1 and t == 0:
                    p.wait("dv", "pe", injpe[7])
                p.dv(lambda e, t=t: e.tensor_mul(out=yg[:, t, :], in0=yt[:, :],
                                                 in1=zsil[:, t, :]))
                dvfin[t] = p.inc("dv")
            p.bar()

            if hh == 0:
                continue
            for w in range(8):
                p.dv(lambda e, w=w: e.tensor_add(
                    out=px[:, w, 0:HC], in0=px[:, w, 0:HC], in1=inj_slot(w)))
                p.chain("dv")
            p.bar()

            oppe = {}
            opdv = {-2: 0, -1: 0}
            def op_load(w):
                p.wait("sy", "pe", oppe.get(w - 2, 0))
                p.wait("gq", "pe", oppe.get(w - 2, 0))
                opld2[w] = wload(op_slab(w), "w_outT", DI, 128 * w, 128, par=w % 2)
            for w in range(8):
                p.wait("pe", opld2[w][0], opld2[w][1])
                p.wait("pe", "dv", opdv[w - 2])
                slab = op_slab(w)
                for kt in range(16):
                    mm(BANKS[w % 2][:, 0:HC], slab[:, kt, 0:128],
                       yg[:, kt, :], start=(kt == 0), stop=(kt == 15))
                oppe[w] = p.inc("pe")
                if w + 2 < 8:
                    op_load(w + 2)
                p.wait("dv", "pe", oppe[w])
                p.dv(lambda e, w=w, h0=h0, b=BANKS[w % 2]: e.tensor_add(
                    out=px[:, w, h0:h0 + HC], in0=px[:, w, h0:h0 + HC],
                    in1=b[:, 0:HC]))
                opdv[w] = p.inc("dv")
            p.bar()

        if STAGE == 1:
            for t in range(NPT):
                p.sy(lambda e, t=t, c0=c0: e.dma_start(out=out_r[:, t, c0:c0 + AC],
                                                       in_=px[:, t, :]))
            p.bar()
            continue

        def wq_slab(w):
            return wbuf[:, w % 2, 0:NPT * 256].rearrange("p (k c) -> p k c", k=NPT)
        p.wait("sy", "pe", p.cnt["pe"])
        p.wait("gq", "pe", p.cnt["pe"])
        qld_h = {0: wload(wq_slab(0), "wqkv0", DM, 0, 256, par=0),
                 1: wload(wq_slab(1), "wqkv0", DM, 256, 256, par=1)}

        ln_stats(lambda t: px[:, t, :], AC, psA0)
        dvn = 0
        for t in range(NPT):
            p.dv(lambda e, t=t: e.tensor_sub(out=tf[:, 0, :], in0=px[:, t, :],
                                             in1=psA0[:, 0:AC]))
            p.chain("dv")
            p.dv(lambda e, t=t: e.tensor_mul(out=pxi[:, t, 1:1 + AC],
                                             in0=tf[:, 0, :], in1=psA0[:, AC:2 * AC]))
            dvn = p.inc("dv")
            p.wait("dv", "dv", dvn)
            p.wait("ac", "dv", dvn)
            p.ac(lambda e, t=t: e.copy(out=pxib[:, t, 1:1 + AC],
                                       in_=pxi[:, t, 1:1 + AC]))
        p.bar()
        for i, nm in enumerate(("rcq", "rsq", "rck", "rsk")):
            p.sy(lambda e, i=i, nm=nm, c0=c0: e.dma_start(
                out=rott[:, i, :], in_=D[nm][:, c0:c0 + AC]))
        p.bar()

        nlayers = 1 if STAGE in (2, 3) else 2
        for l in range(nlayers):
            ring = c % 2
            prev_ring = (c + 1) % 2
            kring = kringT[l]
            vtring = vtringT[l]

            st_ = 2 * l
            p.dv(lambda e, st_=st_: e.tensor_copy(pxib[:, :, 0], scarry[:, st_, :]))
            p.chain("dv")
            p.dv(lambda e, st_=st_: e.tensor_copy(scarry[:, st_, :], pxib[:, :, AC]))
            p.bar()

            def qkv_rhs(kt):
                return pxib[:, kt, 1:1 + AC] if kt < NPT // 2 else pxib[:, kt, 0:AC]

            def l2norm_rotary(isq, l=l, kring=kring, ring=ring):
                qac = {}
                qpe = {}
                for t in range(4):
                    if t >= 2:
                        p.wait("ac", "pe", qpe[t - 2])
                    p.ac(lambda e, t=t: e.activation(out=sqb[:, t % 2, :],
                                                     in_=qraw[:, t, :], func=AF.Square))
                    qac[t] = p.inc("ac")
                    p.wait("pe", "ac", qac[t])
                    mm(pb0[0:8, 0:AC], bdones[:, t, :], sqb[:, t % 2, :],
                       start=(t == 0), stop=(t == 3))
                    qpe[t] = p.inc("pe")
                p.wait("ac", "pe", qpe[3])
                p.ac(lambda e: e.activation(out=nrm[:, 0, :], in_=pb0[0:8, 0:AC],
                                            func=AF.Ln))
                p.chain("ac")
                p.ac(lambda e: e.activation(out=nrm[:, 1, :], in_=nrm[:, 0, :],
                                            func=AF.Exp, scale=cvals[0:8, 1:2]))
                nac = p.inc("ac")
                p.wait("dv", "ac", nac)
                p.dv(lambda e: e.tensor_scalar_min(out=nrm[:, 1, :], in0=nrm[:, 1, :],
                                                   scalar1=1e12))
                ndv = p.inc("dv")
                p.wait("ac", "dv", ndv)
                p.ac(lambda e: e.copy(out=nrmb[:, :], in_=nrm[:, 1, :]))
                nac = p.inc("ac")
                sc = qs8 if isq else ks1
                ct, stt_ = (0, 1) if isq else (2, 3)
                bcd = {}
                rpd = {}
                sdv = {}
                rdv = {}
                p.wait("pe", "ac", nac)
                for t in range(4):
                    bk = (pb1, pb2)[t % 2]
                    if t >= 2:
                        p.wait("pe", "dv", rdv[t - 2])
                    mm(bk[:, 0:AC], sel8[:, 128 * t:128 * (t + 1)], nrmb[:, :],
                       start=True)
                    bcd[t] = p.inc("pe")
                    p.wait("dv", "pe", bcd[t])
                    p.dv(lambda e, t=t, sc=sc, b=bk, l=l: e.scalar_tensor_tensor(
                        out=q2b[:, t, :], in0=qraw[:, t, :], scalar=sc[:, l:l + 1],
                        in1=b[:, 0:AC], op0=ALU.mult, op1=ALU.mult))
                    sdv[t] = p.inc("dv")
                    p.wait("pe", "dv", sdv[t])
                    mm(bk[:, 0:AC], rotP[:], q2b[:, t, :], start=True)
                    rpd[t] = p.inc("pe")
                    p.dv(lambda e, t=t, ct=ct: e.tensor_mul(
                        out=tf[:, 0, :], in0=q2b[:, t, :], in1=rott[:, ct, :]))
                    p.chain("dv")
                    p.wait("dv", "pe", rpd[t])
                    p.dv(lambda e, t=t, stt_=stt_, b=bk: e.tensor_mul(
                        out=tf[:, 1, :], in0=b[:, 0:AC], in1=rott[:, stt_, :]))
                    rdv[t] = p.inc("dv")
                    p.wait("dv", "dv", rdv[t])
                    if isq:
                        p.dv(lambda e, t=t: e.tensor_add(
                            out=q3b[:, t, :], in0=tf[:, 0, :], in1=tf[:, 1, :]))
                    else:
                        p.dv(lambda e, t=t, ring=ring, kring=kring: e.tensor_add(
                            out=kring[:, t, ring * AC:ring * AC + AC],
                            in0=tf[:, 0, :], in1=tf[:, 1, :]))
                    p.chain("dv")
                p.bar()

            qld = {}
            qpe_ = {}
            qac_ = {-2: 0, -1: 0}
            def q_load(w):
                p.wait("sy", "pe", qpe_.get(w - 2, 0))
                p.wait("gq", "pe", qpe_.get(w - 2, 0))
                qld[w] = wload(wq_slab(w), f"wqkv{l}", DM, 256 * w, 256, par=w % 2)
            if l == 0:
                qld[0], qld[1] = qld_h[0], qld_h[1]
            else:
                q_load(0)
                q_load(1)
            for w in range(6):
                p.wait("pe", qld[w][0], qld[w][1])
                p.wait("pe", "ac", qac_.get(w - 3, 0))
                slab = wq_slab(w)
                for kt in range(NPT):
                    for oi in range(2):
                        mm(BANKS[2 * (w % 3) + oi],
                           slab[:, kt, 128 * oi:128 * (oi + 1)],
                           qkv_rhs(kt), start=(kt == 0), stop=(kt == NPT - 1))
                qpe_[w] = p.inc("pe")
                if w + 2 < 6:
                    q_load(w + 2)
                p.wait("ac", "pe", qpe_[w])
                for oi in range(2):
                    ot = 2 * w + oi
                    bk = BANKS[2 * (w % 3) + oi]
                    if ot < 8:
                        p.ac(lambda e, ot=ot, b=bk: e.copy(
                            out=qraw[:, ot % 4, :], in_=b[:, 0:AC]))
                    else:
                        p.ac(lambda e, ot=ot, b=bk: e.copy(
                            out=vbf[:, ot % 4, :], in_=b[:, 0:AC]))
                qac_[w] = p.inc("ac")
                if w == 1:
                    p.bar()
                    l2norm_rotary(True)
                elif w == 3:
                    p.bar()
                    l2norm_rotary(False)
            p.bar()

            for hd in range(HEADS):
                pt_, off = hd // 2, (hd % 2) * 64
                for jt in range(4):
                    p.pe(lambda e, pt_=pt_, off=off, jt=jt: e.transpose(
                        out=pbt[:, 64 * jt:64 * jt + 64],
                        in_=vbf[off:off + 64, pt_, 128 * jt:128 * (jt + 1)],
                        identity=identD[off:off + 64, :]))
                p.bar()
                p.ac(lambda e, hd=hd, ring=ring, vtring=vtring: e.copy(
                    out=vtring[:, hd, 4 * ring:4 * ring + 4, :].rearrange(
                        "p a b -> p (a b)"),
                    in_=pbt[:, 0:256]))
                p.bar()

            items = [(hd, ib) for hd in range(HEADS) for ib in range(4)]
            SIMR = (psA0, psA1)
            pesim = {}; dvmax = {}; acexp = {}; dvren = {}
            petr = {}; acev = {}; peav = {}; acat = {-1: 0}
            jts_of = lambda ib: (list(range(ib, ib + 5)) if c > 0
                                 else list(range(4, 5 + ib)))

            def emit_sim(i):
                hd, ib = items[i]
                pt_, off = hd // 2, (hd % 2) * 64
                sim = SIMR[i % 2]
                jts = jts_of(ib)
                if i >= 2:
                    p.wait("pe", "ac", acexp[i - 2])
                qblk = q3b[off:off + 64, pt_, 128 * ib:128 * (ib + 1)]
                for k, jt in enumerate(jts):
                    r = ring if jt >= 4 else prev_ring
                    colbase = r * AC + 128 * (jt % 4)
                    mm(sim[:, 128 * k:128 * (k + 1)], qblk,
                       kring[off:off + 64, pt_, colbase:colbase + 128], start=True)
                pesim[i] = p.inc("pe")

            emit_sim(0)
            emit_sim(1)
            nI = len(items)
            for i in range(nI + 1):
                if i < nI:
                    hd, ib = items[i]
                    pt_, off = hd // 2, (hd % 2) * 64
                    sim = SIMR[i % 2]
                    attn = attnP[i % 2]
                    sA = sAB[:, 4 * (i % 2):4 * (i % 2) + 4]
                    nj = len(jts_of(ib))
                    p.wait("dv", "pe", pesim[i])
                    if i >= 2:
                        p.wait("dv", "ac", acexp[i - 2])
                    if c > 0:
                        p.dv(lambda e, sim=sim: e.tensor_tensor(
                            out=sim[:, 0:128], in0=sim[:, 0:128], in1=m0[:],
                            op=ALU.add))
                        p.chain("dv")
                    p.dv(lambda e, sim=sim, nj=nj: e.tensor_tensor(
                        out=sim[:, 128 * (nj - 1):128 * nj],
                        in0=sim[:, 128 * (nj - 1):128 * nj], in1=m4[:], op=ALU.add))
                    p.chain("dv")
                    p.dv(lambda e, sim=sim, nj=nj, sA=sA: e.reduce_max(
                        out=sA[:, 0:1], in_=sim[:, 0:128 * nj], axis=AX.X,
                        negate=True))
                    dvmax[i] = p.inc("dv")
                if i >= 1:
                    j = i - 1
                    hdj, ibj = items[j]
                    ptj, offj = hdj // 2, (hdj % 2) * 64
                    attnj = attnP[j % 2]
                    attnTj = attnTP[j % 2]
                    sAj = sAB[:, 4 * (j % 2):4 * (j % 2) + 4]
                    njj = len(jts_of(ibj))
                    jtsj = jts_of(ibj)
                    p.wait("dv", "ac", acexp[j])
                    p.dv(lambda e, sA=sAj: e.reciprocal(out=sA[:, 2:3], in_=sA[:, 1:2]))
                    p.chain("dv")
                    p.dv(lambda e, nj=njj, attn=attnj, sA=sAj: e.tensor_scalar_mul(
                        out=attn[:, 0:128 * nj], in0=attn[:, 0:128 * nj],
                        scalar1=sA[:, 2:3]))
                    dvren[j] = p.inc("dv")
                    p.wait("pe", "dv", dvren[j])
                    if j >= 1:
                        p.wait("pe", "ac", acev[j - 1])
                    for k in range(njj):
                        p.pe(lambda e, k=k, attn=attnj: e.transpose(
                            out=pbt[:, 128 * k:128 * (k + 1)],
                            in_=attn[:, 128 * k:128 * (k + 1)], identity=ident[:]))
                    petr[j] = p.inc("pe")
                    p.wait("ac", "pe", petr[j])
                    if j >= 2:
                        p.wait("ac", "pe", peav[j - 2])
                    p.ac(lambda e, nj=njj, attnT=attnTj: e.copy(
                        out=attnT[:, 0:128 * nj], in_=pbt[:, 0:128 * nj]))
                    acev[j] = p.inc("ac")
                    p.wait("pe", "ac", acev[j])
                    p.wait("pe", "ac", acat[j - 1])
                    for k, jt in enumerate(jtsj):
                        slot = (ring if jt >= 4 else prev_ring) * 4 + (jt % 4)
                        mm(pb2[offj:offj + 64, 0:128], vtring[:, hdj, slot, :],
                           attnTj[:, 128 * k:128 * (k + 1)], start=(k == 0),
                           stop=(k == njj - 1), tile_position=(0, offj))
                    peav[j] = p.inc("pe")
                    p.wait("ac", "pe", peav[j])
                    p.ac(lambda e, off=offj, pt_=ptj, ib=ibj: e.copy(
                        out=atno[off:off + 64, pt_, 128 * ib:128 * (ib + 1)],
                        in_=pb2[off:off + 64, 0:128]))
                    acat[j] = p.inc("ac")
                if i < nI:
                    p.wait("ac", "dv", dvmax[i])
                    if i >= 2:
                        p.wait("ac", "pe", petr[i - 2])
                    p.ac(lambda e, sim=sim, nj=nj, attn=attn, sA=sA: e.activation(
                        out=attn[:, 0:128 * nj], in_=sim[:, 0:128 * nj], func=AF.Exp,
                        bias=sA[:, 0:1], accum_out=sA[:, 1:2]))
                    acexp[i] = p.inc("ac")
                    if i + 2 < nI:
                        emit_sim(i + 2)
            p.bar()

            def wo_slab(w):
                return wbuf[:, w % 2, 0:4 * 256].rearrange("p (k c) -> p k c", k=4)

            wold = {}
            wope = {}
            wodv = {-2: 0, -1: 0}
            def wo_load(w):
                p.wait("sy", "pe", wope.get(w - 2, 0))
                p.wait("gq", "pe", wope.get(w - 2, 0))
                wold[w] = wload(wo_slab(w), f"wo{l}", INNER, 256 * w, 256, par=w % 2)
            wo_load(0)
            wo_load(1)
            for w in range(4):
                p.wait("pe", wold[w][0], wold[w][1])
                p.wait("pe", "dv", wodv[w - 2])
                slab = wo_slab(w)
                for kt in range(4):
                    for oi in range(2):
                        mm(BANKS[2 * (w % 2) + oi],
                           slab[:, kt, 128 * oi:128 * (oi + 1)],
                           atno[:, kt, :], start=(kt == 0), stop=(kt == 3))
                wope[w] = p.inc("pe")
                if w + 2 < 4:
                    wo_load(w + 2)
                p.wait("dv", "pe", wope[w])
                p.dv(lambda e, b=BANKS[2 * (w % 2)], ot=2 * w: e.tensor_add(
                    out=pxi[:, ot, 1:1 + AC], in0=pxi[:, ot, 1:1 + AC],
                    in1=b[:, 0:AC]))
                p.chain("dv")
                p.dv(lambda e, b=BANKS[2 * (w % 2) + 1], ot=2 * w + 1: e.tensor_add(
                    out=pxi[:, ot, 1:1 + AC], in0=pxi[:, ot, 1:1 + AC],
                    in1=b[:, 0:AC]))
                wodv[w] = p.inc("dv")
            p.bar()
            for ot in range(NPT):
                p.ac(lambda e, ot=ot: e.copy(out=pxib[:, ot, 1:1 + AC],
                                             in_=pxi[:, ot, 1:1 + AC]))
            p.bar()
            if STAGE == 2:
                break

            def w1_slab(w):
                return wbuf[:, w % 2, 0:NPT * 256].rearrange("p (k c) -> p k c", k=NPT)
            p.wait("sy", "pe", p.cnt["pe"])
            p.wait("gq", "pe", p.cnt["pe"])
            w1ld = {0: wload(w1_slab(0), f"w1p{l}", DM, 0, 256, par=0),
                    1: wload(w1_slab(1), f"w1p{l}", DM, 256, 256, par=1)}

            st_ = 2 * l + 1
            p.dv(lambda e, st_=st_: e.tensor_copy(pxib[:, :, 0], scarry[:, st_, :]))
            p.chain("dv")
            p.dv(lambda e, st_=st_: e.tensor_copy(scarry[:, st_, :], pxib[:, :, AC]))
            p.bar()

            def sh_ap(t):
                return pxib[:, t, 1:1 + AC] if t < NPT // 2 else pxib[:, t, 0:AC]

            ln_stats(sh_ap, AC, psA0)
            for t in range(NPT):
                p.dv(lambda e, t=t: e.tensor_sub(out=tf[:, 0, :], in0=sh_ap(t),
                                                 in1=psA0[:, 0:AC]))
                p.chain("dv")
                p.dv(lambda e, t=t: e.tensor_mul(out=tf[:, 1, :], in0=tf[:, 0, :],
                                                 in1=psA0[:, AC:2 * AC]))
                p.chain("dv")
                p.dv(lambda e, t=t, l=l: e.tensor_scalar(
                    out=ln2o[:, t, :], in0=tf[:, 1, :], scalar1=lng[:, l, t:t + 1],
                    scalar2=lnb[:, l, t:t + 1], op0=ALU.mult, op1=ALU.add))
                p.chain("dv")
            p.bar()

            w1pe = {}
            w1ac = {}
            w1dv = {-3: 0, -2: 0, -1: 0}
            def w1_load(w):
                p.wait("sy", "pe", w1pe.get(w - 2, 0))
                p.wait("gq", "pe", w1pe.get(w - 2, 0))
                w1ld[w] = wload(w1_slab(w), f"w1p{l}", DM, 256 * w, 256, par=w % 2)
            for w in range(NHT):
                p.wait("pe", w1ld[w][0], w1ld[w][1])
                p.wait("pe", "dv", w1dv.get(w - 3, 0))
                slab = w1_slab(w)
                for kt in range(NPT):
                    for oi in range(2):
                        mm(BANKS[2 * (w % 3) + oi],
                           slab[:, kt, 128 * oi:128 * (oi + 1)],
                           ln2o[:, kt, :], start=(kt == 0), stop=(kt == NPT - 1))
                w1pe[w] = p.inc("pe")
                if w + 2 < NHT:
                    w1_load(w + 2)
                p.wait("ac", "pe", w1pe[w])
                p.wait("ac", "dv", w1dv.get(w - 2, 0))
                gb_ = BANKS[2 * (w % 3) + 1]
                ab_ = BANKS[2 * (w % 3)]
                p.ac(lambda e, g_=gb_, w=w, l=l: e.activation(
                    out=gel[:, w % 2, :], in_=g_[:, 0:AC], func=AF.Gelu,
                    bias=b1g[:, l, w:w + 1]))
                w1ac[w] = p.inc("ac")
                p.wait("dv", "ac", w1ac[w])
                p.dv(lambda e, a_=ab_, w=w, l=l: e.scalar_tensor_tensor(
                    out=gated[:, w, :], in0=a_[:, 0:AC], scalar=b1a[:, l, w:w + 1],
                    in1=gel[:, w % 2, :], op0=ALU.add, op1=ALU.mult))
                w1dv[w] = p.inc("dv")
            p.bar()

            def w2_slab(i):
                return wbuf[:, i % 2, 0:11 * 128].rearrange("p (k c) -> p k c", k=11)

            seq2 = [(w, gpr) for w in range(8) for gpr in range(2)]
            w2ld = {}
            w2pe = {}
            w2dv = {-2: 0, -1: 0, 0: 0, 1: 0}
            def w2_load(i):
                w_, g_ = seq2[i]
                p.wait("sy", "pe", w2pe.get(i - 2, 0))
                p.wait("gq", "pe", w2pe.get(i - 2, 0))
                w2ld[i] = wload(w2_slab(i), f"w2p{l}", HIPD_ if False else HIDP,
                                128 * w_, 128, rt0=11 * g_, rtn=11, par=i % 2)
            w2_load(0)
            w2_load(1)
            for i in range(16):
                w, gpr = seq2[i]
                p.wait("pe", w2ld[i][0], w2ld[i][1])
                p.wait("pe", "dv", w2dv[i - 2])
                slab = w2_slab(i)
                for kk in range(11):
                    kt = 11 * gpr + kk
                    mm(BANKS[4 + w % 2], slab[:, kk, 0:128], gated[:, kt, :],
                       start=(kt == 0), stop=(kt == NHT - 1))
                w2pe[i] = p.inc("pe")
                if i + 2 < 16:
                    w2_load(i + 2)
                if gpr == 1:
                    p.wait("dv", "pe", w2pe[i])
                    p.dv(lambda e, w=w, l=l, b=BANKS[4 + w % 2]:
                         e.scalar_tensor_tensor(
                             out=pxi[:, w, 1:1 + AC], in0=b[:, 0:AC],
                             scalar=b2s[:, l, w:w + 1], in1=pxi[:, w, 1:1 + AC],
                             op0=ALU.add, op1=ALU.add))
                    w2dv[i] = p.inc("dv")
                else:
                    w2dv[i] = w2dv[i - 1]
            p.bar()
            if l == 0 and nlayers > 1:
                for w in range(NPT):
                    p.ac(lambda e, w=w: e.copy(out=pxib[:, w, 1:1 + AC],
                                               in_=pxi[:, w, 1:1 + AC]))
                p.bar()

        for t in range(NPT):
            p.dv(lambda e, t=t: e.tensor_add(out=px[:, t, :], in0=px[:, t, :],
                                             in1=pxi[:, t, 1:1 + AC]))
        p.bar()
        for t in range(NPT):
            p.sy(lambda e, t=t, c0=c0: e.dma_start(out=out_r[:, t, c0:c0 + AC],
                                                   in_=px[:, t, :]))
        p.bar()

    p.wait("sy", "sy", p.cnt["sy"])

    with nc.Block() as block:
        p.materialize(block)
    nc._keepalive_ctx = ctx
    return nc


def kernel(_ncores=NCORE, **inputs):
    g = _host_prep(inputs)
    x = inputs["L0_attn_x"]
    nc = build()
    in_maps = []
    for cix in range(_ncores):
        m = {"xT": np.ascontiguousarray(np.asarray(x[cix]).T)}
        m.update(g)
        in_maps.append(m)
    res = run_bass_kernel_spmd(nc, in_maps, core_ids=list(range(_ncores)))
    outs = [np.asarray(r["out"]).T for r in res.results]
    return np.ascontiguousarray(np.stack(outs, 0))


# version 10
# speedup vs baseline: 1.0411x; 1.0033x over previous
import numpy as np
import ml_dtypes
from contextlib import ExitStack

import concourse.bass as bass
import concourse.mybir as mybir
from concourse.bass_utils import run_bass_kernel_spmd

F32 = mybir.dt.float32
BF16 = mybir.dt.bfloat16
AF = mybir.ActivationFunctionType
ALU = mybir.AluOpType
AX = mybir.AxisListType

BF = ml_dtypes.bfloat16

NCORE = 8
LSEQ = 4096
DM = 1024
DI = 2048
NST = 16
DTR = 64
NPT = 8
AC = 512
HC = 256
NCH = LSEQ // AC
HEADS = 8
DH = 64
INNER = 512
HID = 2730
HIDP = 2816
NHT = 22
NEG = -1e30

NCHRUN = NCH
STAGE = 4


def _pp(v):
    return np.ascontiguousarray(np.asarray(v, np.float32).reshape(-1, 128).T)


def _pp3(a):
    a = np.asarray(a, np.float32)
    F, K = a.shape
    return np.ascontiguousarray(a.reshape(F // 128, 128, K).transpose(1, 0, 2))


def _host_prep(inp):
    g = {}
    g["w_inT"] = np.ascontiguousarray(inp["L0_attn_in_proj_w"].T)
    g["w_xpT"] = np.ascontiguousarray(inp["L0_attn_x_proj_w"].T)
    g["w_dtT"] = np.ascontiguousarray(inp["L0_attn_dt_proj_w"].T)
    g["w_outT"] = np.ascontiguousarray(inp["L0_attn_out_proj_w"].T)
    g["convw"] = _pp3(inp["L0_attn_conv_w"][:, 0, :])
    cw = _pp3(inp["L0_attn_conv_w"][:, 0, :])
    cd = np.zeros((16, 4, 128, 128), np.float32)
    ii128 = np.arange(128)
    for t in range(16):
        for k in range(4):
            cd[t, k, ii128, ii128] = cw[:, t, k]
    g["convdiag"] = np.ascontiguousarray(cd.reshape(16 * 4 * 128, 128))
    g["convb"] = _pp(inp["L0_attn_conv_b"])
    g["dtb"] = _pp(inp["L0_attn_dt_proj_b"])
    g["Alog"] = _pp3(inp["L0_attn_A_log"])
    g["Dp"] = _pp(inp["L0_attn_D_param"])
    for l in (0, 1):
        g[f"wqkv{l}"] = np.ascontiguousarray(inp["L0_ffn_wqkv"][l])
        g[f"wo{l}"] = np.ascontiguousarray(inp["L0_ffn_wo"][l])
        w1 = inp["L0_ffn_w1"][l]
        a = np.zeros((DM, HIDP), np.float32); a[:, :HID] = w1[:, :HID]
        gg = np.zeros((DM, HIDP), np.float32); gg[:, :HID] = w1[:, HID:]
        at = a.reshape(DM, NHT, 128); gt = gg.reshape(DM, NHT, 128)
        g[f"w1p{l}"] = np.ascontiguousarray(
            np.stack([at, gt], axis=2).reshape(DM, 2 * NHT * 128))
        w2 = np.zeros((HIDP, DM), np.float32); w2[:HID] = inp["L0_ffn_w2"][l]
        g[f"w2p{l}"] = w2
        b1 = inp["L0_ffn_b1"][l]
        b1a = np.zeros(HIDP, np.float32); b1a[:HID] = b1[:HID]
        b1g = np.zeros(HIDP, np.float32); b1g[:HID] = b1[HID:]
        g[f"b1a{l}"] = _pp(b1a); g[f"b1g{l}"] = _pp(b1g)
    g["qs"] = np.ascontiguousarray(np.stack(
        [np.tile(np.asarray(inp["L0_ffn_q_scale"][l], np.float32), 2) for l in (0, 1)], 1))
    g["ks"] = np.ascontiguousarray(np.stack(
        [np.tile(np.asarray(inp["L0_ffn_k_scale"][l], np.float32), 2) for l in (0, 1)], 1))
    g["lng"] = np.ascontiguousarray(np.stack([_pp(inp["L0_ffn_ln_g"][l]) for l in (0, 1)], 1))
    g["lnb"] = np.ascontiguousarray(np.stack([_pp(inp["L0_ffn_ln_b"][l]) for l in (0, 1)], 1))
    g["b2"] = np.ascontiguousarray(np.stack([_pp(inp["L0_ffn_b2"][l]) for l in (0, 1)], 1))

    ii = np.arange(128)
    g["m0"] = np.where(ii[None, :] < ii[:, None], NEG, 0.0).astype(BF)
    g["m4"] = np.where(ii[None, :] > ii[:, None], NEG, 0.0).astype(BF)
    g["ident"] = np.eye(128, dtype=BF)
    identD = np.zeros((128, 64), np.float32)
    for p_ in range(128):
        identD[p_, p_ % 64] = 1.0
    g["identD"] = identD.astype(BF)
    g["onescol"] = np.ones((1, 128), dtype=BF)
    g["onek"] = np.ones((128, 1), dtype=BF)
    g["cvals"] = np.tile(np.array([[1e-5, -0.5, 1.0 / DM, 8.0, -1.0, 1.0]], np.float32), (128, 1))
    bd = np.zeros((128, 4, 8), np.float32)
    for t in range(4):
        bd[:64, t, 2 * t] = 1.0; bd[64:, t, 2 * t + 1] = 1.0
    g["bdones"] = bd.astype(BF)
    sel8 = np.zeros((8, 4, 128), np.float32)
    for t in range(4):
        for p_ in range(128):
            sel8[2 * t + (p_ >= 64), t, p_] = 1.0
    g["sel8"] = np.ascontiguousarray(sel8.reshape(8, 512)).astype(BF)
    sel16 = np.zeros((16, 16, 128), np.float32)
    for n in range(16):
        sel16[n, n, :] = 1.0
    g["sel16"] = np.ascontiguousarray(sel16.reshape(16, 2048)).astype(BF)
    P = np.zeros((128, 128), np.float32)
    for di in range(128):
        h0 = (di // 64) * 64; d = di % 64
        if d < 32:
            P[h0 + d + 32, di] = -1.0
        else:
            P[h0 + d - 32, di] = 1.0
    g["rotP"] = P.astype(BF)

    dih = np.arange(64); j = dih % 32
    invfreq = 1.0 / 10000.0 ** ((2 * j) / DH)
    t = np.arange(LSEQ, dtype=np.float64)
    freqs = t[None, :] * invfreq[:, None]
    base = ((2 * j) + 0.4 * DH) / (1.4 * DH)
    power = (t - LSEQ // 2) / float(AC // 2)
    scale = base[:, None] ** power[None, :]
    cos = np.cos(freqs); sin = np.sin(freqs)
    rows2 = lambda m: np.ascontiguousarray(np.concatenate([m, m], 0).astype(BF))
    g["rcq"] = rows2(cos * scale); g["rsq"] = rows2(sin * scale)
    g["rck"] = rows2(cos / scale); g["rsk"] = rows2(sin / scale)
    return g


ENGINES = ("sy", "gq", "pe", "ac", "dv")
SEMS = ("sy", "gq", "pe", "ac", "dv", "gc", "wl0", "wl1", "wg0", "wg1", "wb0", "wb1")


class Prog:
    def __init__(self, nc):
        self.nc = nc
        self.prog = {e: [] for e in ENGINES}
        self.iswait = {e: [] for e in ENGINES}
        self.cnt = {e: 0 for e in SEMS}
        self.dirty = {e: False for e in SEMS}
        self.snap = dict(self.cnt)
        self.lastwait = {e: {q: 0 for q in SEMS} for e in ENGINES}
        self.pending = {e: False for e in ENGINES}
        self.sems = None

    def _emit(self, e, fn, compute=False):
        if self.pending[e]:
            self.pending[e] = False
            for q in SEMS:
                self.wait(e, q, self.snap[q])
        self.prog[e].append([fn, []])
        self.iswait[e].append(False)
        if e == "sy" or (e == "gq" and not compute):
            self.prog[e][-1][1] = [(e, 16)]
            self.cnt[e] += 16
        elif e == "gq":
            self.dirty["gc"] = True
        else:
            self.dirty[e] = True

    def sy(self, fn): self._emit("sy", fn)
    def gq(self, fn): self._emit("gq", fn)
    def gc(self, fn): self._emit("gq", fn, compute=True)

    def dma(self, queue, fn, sem):
        if self.pending[queue]:
            self.pending[queue] = False
            for q in SEMS:
                self.wait(queue, q, self.snap[q])
        self.prog[queue].append([fn, [(sem, 16)]])
        self.iswait[queue].append(False)
        self.cnt[sem] += 16
    def pe(self, fn): self._emit("pe", fn)
    def ac(self, fn): self._emit("ac", fn)
    def dv(self, fn): self._emit("dv", fn)

    def inc(self, e):
        stream = "gq" if e == "gc" else e
        assert self.prog[stream] and not self.iswait[stream][-1], e
        self.prog[stream][-1][1].append((e, 1))
        self.cnt[e] += 1
        self.dirty[e] = False
        return self.cnt[e]

    def wait(self, e, q, val):
        if val <= 0 or self.lastwait[e][q] >= val:
            return
        self.lastwait[e][q] = val
        self.prog[e].append(
            [lambda eng, q=q, val=val: eng.wait_ge(self.sems[q], val), []])
        self.iswait[e].append(True)

    def chain(self, e):
        self.inc(e)
        st = "gq" if e == "gc" else e
        self.wait(st, e, self.cnt[e])

    def bar(self):
        for e in ("pe", "ac", "dv", "gc"):
            if self.dirty[e]:
                self.inc(e)
        self.snap = dict(self.cnt)
        for e in ENGINES:
            self.pending[e] = True

    def materialize(self, block):
        sems = self.sems

        def runner(progl):
            def body(eng):
                for fn, incs in progl:
                    inst = fn(eng)
                    for s, n in incs:
                        inst.then_inc(sems[s], n)
            return body

        block.sync(runner(self.prog["sy"]))
        block.gpsimd(runner(self.prog["gq"]))
        block.tensor(runner(self.prog["pe"]))
        block.scalar(runner(self.prog["ac"]))
        block.vector(runner(self.prog["dv"]))


def build(debug=False):
    nc = bass.Bass("TRN2", target_bir_lowering=False)
    ctx = ExitStack()

    D = {}
    def din(name, shape, dt=F32):
        D[name] = nc.dram_tensor(name, list(shape), dt, kind="ExternalInput").ap()
    din("xT", (DM, LSEQ))
    din("w_inT", (DM, 2 * DI)); din("w_xpT", (DI, 96)); din("w_dtT", (DTR, DI))
    din("w_outT", (DI, DM))
    din("convw", (128, 16, 4)); din("convdiag", (16 * 4 * 128, 128)); din("convb", (128, 16)); din("dtb", (128, 16))
    din("Alog", (128, 16, 16)); din("Dp", (128, 16))
    for l in (0, 1):
        din(f"wqkv{l}", (DM, 3 * INNER)); din(f"wo{l}", (INNER, DM))
        din(f"w1p{l}", (DM, 2 * HIDP)); din(f"w2p{l}", (HIDP, DM))
        din(f"b1a{l}", (128, NHT)); din(f"b1g{l}", (128, NHT))
    din("qs", (128, 2)); din("ks", (128, 2))
    din("lng", (128, 2, 8)); din("lnb", (128, 2, 8)); din("b2", (128, 2, 8))
    din("m0", (128, 128), BF16); din("m4", (128, 128), BF16)
    din("ident", (128, 128), BF16); din("identD", (128, 64), BF16)
    din("onescol", (1, 128), BF16); din("onek", (128, 1), BF16)
    din("cvals", (128, 6))
    din("sel8", (8, 512), BF16); din("sel16", (16, 2048), BF16)
    din("bdones", (128, 4, 8), BF16)
    din("rotP", (128, 128), BF16)
    for nm in ("rcq", "rsq", "rck", "rsk"):
        din(nm, (128, LSEQ), BF16)
    out = nc.dram_tensor("out", [DM, LSEQ], F32, kind="ExternalOutput").ap()
    out_r = out.rearrange("(t p) l -> p t l", p=128)
    xT_r = D["xT"].rearrange("(t p) l -> p t l", p=128)

    _nm = [0]
    def sb(shape, dt=F32):
        _nm[0] += 1
        return ctx.enter_context(nc.sbuf_tensor(f"t{_nm[0]}", list(shape), dt))

    px = sb((128, NPT, AC))
    hcarry = sb((128, 16, 16))
    ccarry = sb((128, 16, 4))
    scarry = sb((128, 4, NPT), BF16)
    kringT = [sb((128, 4, 2 * AC), BF16) for _ in (0, 1)]
    vtringT = [sb((128, HEADS, 8, DH), BF16) for _ in (0, 1)]
    m0 = sb((128, 128), BF16); m4 = sb((128, 128), BF16)
    ident = sb((128, 128), BF16); identD = sb((128, 64), BF16)
    onescol = sb((1, 128), BF16); onek = sb((128, 1), BF16)
    cvals = sb((128, 6))
    sel8 = sb((8, 512), BF16); sel16 = sb((16, 2048), BF16)
    bdones = sb((128, 4, 8), BF16)
    rotP = sb((128, 128), BF16)
    Aneg = sb((128, 16, 16)); convw = sb((128, 16, 4))
    convb = sb((128, 16)); dtb = sb((128, 16)); Dp = sb((128, 16))
    qs8 = sb((128, 2)); ks1 = sb((128, 2))
    lng = sb((128, 2, NPT)); lnb = sb((128, 2, NPT)); b2s = sb((128, 2, NPT))
    b1a = sb((128, 2, NHT)); b1g = sb((128, 2, NHT))
    dtW = sb((DTR, DI), BF16)
    xpW = sb((128, 16, 96), BF16)
    dlr = sb((DTR, HC), BF16)
    bcrowB = sb((16, HC), BF16)
    bcrowC = sb((16, HC), BF16)
    rows1 = sb((1, 5 * AC))
    rowsb1 = sb((1, 2 * AC), BF16)
    nrm = sb((8, 2, AC))
    nrmb = sb((8, AC), BF16)
    sAB = sb((128, 8))
    wbuf = sb((128, 2, 2048), BF16)
    cbuf = sb((128, 2, 4, 128), BF16)

    A16N = 29184
    A32N = 13056
    ar16 = sb((128, A16N), BF16)
    ar32 = sb((128, A32N))

    def v16(off, shape):
        n = int(np.prod(shape[1:]))
        ap = ar16[:, off:off + n]
        if len(shape) == 3:
            ap = ap.rearrange("p (a b) -> p a b", a=shape[1])
        return ap, off + n

    def v32(off, shape):
        n = int(np.prod(shape[1:]))
        ap = ar32[:, off:off + n]
        if len(shape) == 3:
            ap = ap.rearrange("p (a b) -> p a b", a=shape[1])
        return ap, off + n

    o = 0
    lnout, o = v16(o, (128, NPT, HC))
    xinner, o = v16(o, (128, 16, HC + 4))
    zsil, o = v16(o, (128, 16, HC))
    xconv, o = v16(o, (128, 16, HC))
    Bb, o = v16(o, (128, 16, HC))
    Cb, o = v16(o, (128, 16, HC))
    wt, o = v16(o, (128, HC))
    ya, o = v16(o, (128, 4, HC))
    yb, o = v16(o, (128, 2, HC))
    yg, o = v16(o, (128, 16, HC))
    assert o <= A16N, o
    dBxA = ar16[:, 2048:2048 + 16 * HC].rearrange("p (a b) -> p a b", a=16)
    dBxP = (dBxA, dBxA)
    yah = ar16[:, 0:8 * HC].rearrange("p (a b) -> p a b", a=8)
    tt1 = ya
    tt2 = yb
    o = 0
    pxib, o = v16(o, (128, NPT, AC + 1))
    rott, o = v16(o, (128, 4, AC))
    q2b, o = v16(o, (128, 4, AC))
    q3b, o = v16(o, (128, 4, AC))
    vbf, o = v16(o, (128, 4, AC))
    sqb, o = v16(o, (128, 2, AC))
    atno, o = v16(o, (128, 4, AC))
    gated, o = v16(o, (128, NHT, AC))
    sqb2, o = v16(o, (128, 2, AC))
    assert o <= A16N, o
    _vb0 = vbf[:, 0:4, :].rearrange("p a b -> p (a b)")
    attnP = (_vb0[:, 0:640], _vb0[:, 640:1280])
    _qb0 = q2b[:, 0:4, :].rearrange("p a b -> p (a b)")
    attnTP = (_qb0[:, 0:640], _qb0[:, 640:1280])
    _q2off = NPT * (AC + 1) + 4 * AC
    ln2o = ar16[:, _q2off:_q2off + NPT * AC].rearrange("p (a b) -> p a b", a=NPT)
    gel = sqb

    o = 0
    dAb0, o = v32(o, (128, 16, HC))
    dAb1, o = v32(o, (128, 16, HC))
    dtt, o = v32(o, (128, 2, HC))
    yt, o = v32(o, (128, HC))
    assert o <= A32N, o
    dABUF = (dAb0, dAb1)
    o = 0
    pxi, o = v32(o, (128, NPT, AC + 1))
    qraw, o = v32(o, (128, 4, AC))
    tf, o = v32(o, (128, 2, AC))
    assert o <= A32N, o

    psA0 = ctx.enter_context(nc.psum_tensor([128, 1024], F32))
    psA1 = ctx.enter_context(nc.psum_tensor([128, 1024], F32))
    pb0 = ctx.enter_context(nc.psum_tensor([128, 512], F32))
    pb1 = ctx.enter_context(nc.psum_tensor([128, 512], F32))
    pb2 = ctx.enter_context(nc.psum_tensor([128, 512], F32))
    pbt = ctx.enter_context(nc.psum_tensor([128, 1024], BF16))
    BANKS = [psA0[:, 0:512], psA0[:, 512:1024],
             psA1[:, 0:512], psA1[:, 512:1024], pb0[:], pb1[:]]

    p = Prog(nc)
    p.sems = {e: ctx.enter_context(nc.semaphore(f"s_{e}")) for e in SEMS}

    def mm(bank_ap, lhsT_ap, rhs_ap, start, stop=None, tile_position=None):
        if stop is None:
            stop = start
        p.pe(lambda e, o_=bank_ap, l_=lhsT_ap, r_=rhs_ap, s_=start, t_=stop,
             tp=tile_position:
             e.matmul(out=o_, lhsT=l_, rhs=r_, start=s_, stop=t_, tile_position=tp))

    SCR16 = {}
    STORED = set()

    LASTWB = {}

    def wload(buf_ap, name, rows_, col0, cols, rt0=0, rtn=None, par=0):
        dram2d = D[name]
        if name not in SCR16:
            SCR16[name] = nc.dram_tensor(
                f"scr_{name}", [dram2d.shape[0], dram2d.shape[1]], BF16,
                kind="Internal").ap()
        scr = SCR16[name].rearrange("(t p) c -> p t c", p=128)
        src = dram2d.rearrange("(t p) c -> p t c", p=128)
        nt = rtn if rtn is not None else rows_ // 128
        step = 4
        key = (name, col0, rt0)
        first = key not in STORED
        STORED.add(key)
        wsem = f"wg{par}" if first else f"wl{par}"
        p.wait("gq" if first else "sy", f"wb{par}", LASTWB.get(par, 0))
        for tt in range(0, nt, step):
            n = min(step, nt - tt)
            if first:
                p.dma("gq", lambda e, d=buf_ap[:, tt:tt + n, :],
                      s=src[:, rt0 + tt:rt0 + tt + n, col0:col0 + cols]:
                      e.dma_start(out=d, in_=s), wsem)
            else:
                p.dma("sy", lambda e, d=buf_ap[:, tt:tt + n, :],
                      s=scr[:, rt0 + tt:rt0 + tt + n, col0:col0 + cols]:
                      e.dma_start(out=d, in_=s), wsem)
        if first:
            p.wait("sy", wsem, p.cnt[wsem])
            for tt in range(0, nt, step):
                n = min(step, nt - tt)
                p.dma("sy", lambda e, d=scr[:, rt0 + tt:rt0 + tt + n, col0:col0 + cols],
                      s=buf_ap[:, tt:tt + n, :]:
                      e.dma_start(out=d, in_=s), f"wb{par}")
            LASTWB[par] = p.cnt[f"wb{par}"]
        return (wsem, p.cnt[wsem])

    def loadc(dst_ap, src_ap, cast=False):
        (p.gq if cast else p.sy)(
            lambda e, d=dst_ap, s=src_ap: e.dma_start(out=d, in_=s))

    loadc(m0[:], D["m0"]); loadc(m4[:], D["m4"])
    loadc(ident[:], D["ident"]); loadc(identD[:], D["identD"])
    loadc(onescol[:], D["onescol"]); loadc(onek[:], D["onek"])
    loadc(cvals[:], D["cvals"])
    loadc(sel8[:], D["sel8"]); loadc(sel16[:], D["sel16"])
    loadc(bdones[:], D["bdones"]); loadc(rotP[:], D["rotP"])
    loadc(convw[:], D["convw"]); loadc(convb[:], D["convb"])
    loadc(dtb[:], D["dtb"]); loadc(Dp[:], D["Dp"])
    loadc(Aneg[:], D["Alog"])
    loadc(qs8[:], D["qs"]); loadc(ks1[:], D["ks"])
    loadc(lng[:], D["lng"]); loadc(lnb[:], D["lnb"]); loadc(b2s[:], D["b2"])
    for l in (0, 1):
        loadc(b1a[:, l, :], D[f"b1a{l}"]); loadc(b1g[:, l, :], D[f"b1g{l}"])
    loadc(dtW[:], D["w_dtT"], cast=True)
    loadc(xpW[:], D["w_xpT"].rearrange("(t p) c -> p t c", p=128), cast=True)
    p.bar()
    AnegF = Aneg[:].rearrange("p a b -> p (a b)")
    p.ac(lambda e: e.activation(out=AnegF, in_=AnegF, func=AF.Exp))
    p.ac(lambda e: e.activation(out=qs8[:], in_=qs8[:], func=AF.Copy, scale=cvals[:, 3:4]))
    p.dv(lambda e: e.memset(hcarry[:].rearrange("p a b -> p (a b)"), 0.0))
    p.dv(lambda e: e.memset(ccarry[:].rearrange("p a b -> p (a b)"), 0.0))
    p.dv(lambda e: e.memset(scarry[:].rearrange("p a b -> p (a b)"), 0.0))
    for l in (0, 1):
        p.dv(lambda e, l=l: e.memset(
            vtringT[l][:].rearrange("p a b c -> p (a b c)"), 0.0))
    p.bar()
    p.dv(lambda e: e.tensor_scalar_mul(out=AnegF, in0=AnegF, scalar1=-1.0))
    p.bar()

    def seg(k, n):
        return rows1[0:1, k * AC:k * AC + n]

    def ln_post(n, bank):
        p.ac(lambda e, n=n, bank=bank: e.activation(out=seg(0, n), in_=bank[0:1, 0:n],
                                         func=AF.Copy, scale=cvals[0:1, 2:3]))
        ac0 = p.inc("ac")
        p.wait("dv", "ac", ac0)
        p.dv(lambda e, n=n: e.tensor_mul(out=seg(1, n), in0=seg(0, n), in1=seg(0, n)))
        p.chain("dv")
        p.dv(lambda e, n=n, bank=bank: e.scalar_tensor_tensor(
            out=seg(2, n), in0=bank[0:1, n:2 * n], scalar=1.0 / DM,
            in1=seg(1, n), op0=ALU.mult, op1=ALU.subtract))
        dv0 = p.inc("dv")
        p.wait("ac", "dv", dv0)
        p.ac(lambda e, n=n: e.activation(out=seg(3, n), in_=seg(2, n),
                                         func=AF.Ln, bias=cvals[0:1, 0:1]))
        p.chain("ac")
        p.ac(lambda e, n=n: e.activation(out=seg(4, n), in_=seg(3, n),
                                         func=AF.Exp, scale=cvals[0:1, 1:2]))
        p.chain("ac")
        p.ac(lambda e, n=n: e.copy(out=rowsb1[0:1, 0:n], in_=seg(0, n)))
        p.chain("ac")
        p.ac(lambda e, n=n: e.copy(out=rowsb1[0:1, n:2 * n], in_=seg(4, n)))
        ac1 = p.inc("ac")
        p.wait("pe", "ac", ac1)
        if n >= 512:
            mm(bank[:, 0:n], onescol[:], rowsb1[0:1, 0:n], start=True)
            mm(bank[:, n:2 * n], onescol[:], rowsb1[0:1, n:2 * n], start=True)
        else:
            mm(bank[:, 0:2 * n], onescol[:], rowsb1[0:1, 0:2 * n], start=True)
        p.bar()

    def ln_stats(src_ap_fn, n, bank, s1=None, s2=None):
        if s1 is None:
            s1 = sqb
        if s2 is None:
            s2 = sqb2
        for t in range(NPT):
            half = t % 2
            p.ac(lambda e, t=t, n=n, half=half: e.copy(
                out=s1[:, half, 0:n], in_=src_ap_fn(t)))
            p.chain("ac")
            p.ac(lambda e, t=t, n=n, half=half: e.activation(
                out=s2[:, half, 0:n], in_=src_ap_fn(t), func=AF.Square))
            lnac = p.inc("ac")
            p.wait("pe", "ac", lnac)
            mm(bank[0:1, 0:n], onek[:], s1[:, half, 0:n], start=(t == 0),
               stop=(t == NPT - 1 and n >= 512))
            mm(bank[0:1, n:2 * n], onek[:], s2[:, half, 0:n],
               start=(t == 0 and n >= 512), stop=(t == NPT - 1))
            if t < NPT - 1:
                pef = p.inc("pe")
                p.wait("ac", "pe", pef)
        p.bar()
        ln_post(n, bank)

    for c in range(NCHRUN):
        c0 = c * AC
        for t in range(NPT):
            p.sy(lambda e, t=t, c0=c0: e.dma_start(out=px[:, t, :],
                                                   in_=xT_r[:, t, c0:c0 + AC]))
        p.bar()

        for hh in range(2):
            h0 = hh * HC
            lnscrA = ar16[:, 2048:2560].rearrange("p (a b) -> p a b", a=2)
            lnscrB = ar16[:, 2560:3072].rearrange("p (a b) -> p a b", a=2)
            ln_stats(lambda t, h0=h0: px[:, t, h0:h0 + HC], HC, pb2,
                     s1=lnscrA, s2=lnscrB)
            for t in range(NPT):
                p.dv(lambda e, t=t, h0=h0: e.tensor_sub(
                    out=tf[:, 0, 0:HC], in0=px[:, t, h0:h0 + HC], in1=pb2[:, 0:HC]))
                p.chain("dv")
                p.dv(lambda e, t=t: e.tensor_mul(out=lnout[:, t, :],
                                                 in0=tf[:, 0, 0:HC],
                                                 in1=pb2[:, HC:2 * HC]))
                p.chain("dv")
            p.bar()

            p.dv(lambda e: e.tensor_copy(xinner[:, :, 1:4], ccarry[:, :, 1:4]))
            p.bar()

            def ip_slab(w):
                return wbuf[:, w % 2, 0:NPT * 256].rearrange("p (k c) -> p k c", k=NPT)

            ipld = {}
            ippe = {}
            ipac = {-3: 0, -2: 0, -1: 0}
            def ip_load(w):
                p.wait("sy", "pe", ippe.get(w - 2, 0))
                p.wait("gq", "pe", ippe.get(w - 2, 0))
                ipld[w] = wload(ip_slab(w), "w_inT", DM, 256 * w, 256, par=w % 2)
            ip_load(0)
            ip_load(1)
            for w in range(16):
                p.wait("pe", ipld[w][0], ipld[w][1])
                p.wait("pe", "ac", ipac.get(w - 3, 0))
                slab = ip_slab(w)
                for kt in range(NPT):
                    for oi in range(2):
                        mm(BANKS[2 * (w % 3) + oi][:, 0:HC],
                           slab[:, kt, 128 * oi:128 * (oi + 1)],
                           lnout[:, kt, :], start=(kt == 0),
                           stop=(kt == NPT - 1))
                ippe[w] = p.inc("pe")
                if w + 2 < 16:
                    ip_load(w + 2)
                p.wait("ac", "pe", ippe[w])
                for oi in range(2):
                    ot = 2 * w + oi
                    bk = BANKS[2 * (w % 3) + oi]
                    if ot < 16:
                        p.ac(lambda e, b=bk, d=ot: e.copy(
                            out=xinner[:, d, 4:4 + HC], in_=b[:, 0:HC]))
                    else:
                        p.ac(lambda e, b=bk, d=ot - 16: e.activation(
                            out=zsil[:, d, :], in_=b[:, 0:HC], func=AF.Silu))
                ipac[w] = p.inc("ac")
            p.bar()
            p.dv(lambda e: e.tensor_copy(ccarry[:, :, 1:4],
                                         xinner[:, :, 1 + HC:4 + HC]))
            p.bar()

            def op_slab(w):
                return wbuf[:, w % 2, 0:16 * 128].rearrange("p (k c) -> p k c", k=16)
            if hh == 1:
                p.wait("sy", "pe", p.cnt["pe"])
                p.wait("gq", "pe", p.cnt["pe"])
                opld = {0: wload(op_slab(0), "w_outT", DI, 0, 128, par=0),
                        1: wload(op_slab(1), "w_outT", DI, 128, 128, par=1)}

            cvld = {}
            cvpe = {}
            cvac = {-2: 0, -1: 0}
            def cv_load(t):
                p.wait("sy", "pe", cvpe.get(t - 2, 0))
                p.wait("gq", "pe", cvpe.get(t - 2, 0))
                cvld[t] = wload(cbuf[:, t % 2, :, :], "convdiag", 16 * 4 * 128, 0,
                                128, rt0=4 * t, rtn=4, par=t % 2)
            cv_load(0)
            cv_load(1)
            for t in range(16):
                p.wait("pe", cvld[t][0], cvld[t][1])
                p.wait("pe", "ac", cvac[t - 2])
                for k in range(4):
                    mm(BANKS[t % 2][:, 0:HC], cbuf[:, t % 2, k, :],
                       xinner[:, t, 1 + k:1 + k + HC], start=(k == 0), stop=(k == 3))
                cvpe[t] = p.inc("pe")
                if t + 2 < 16:
                    cv_load(t + 2)
                p.wait("ac", "pe", cvpe[t])
                p.ac(lambda e, t=t, b=BANKS[t % 2]: e.activation(
                    out=xconv[:, t, :], in_=b[:, 0:HC], func=AF.Silu,
                    bias=convb[:, t:t + 1]))
                cvac[t] = p.inc("ac")
            p.bar()

            for kt in range(16):
                mm(pb0[0:64, 0:HC], xpW[:, kt, 0:64], xconv[:, kt, :], start=(kt == 0), stop=(kt == 15))
                mm(pb1[0:16, 0:HC], xpW[:, kt, 64:80], xconv[:, kt, :], start=(kt == 0), stop=(kt == 15))
                mm(pb2[0:16, 0:HC], xpW[:, kt, 80:96], xconv[:, kt, :], start=(kt == 0), stop=(kt == 15))
            p.bar()
            p.ac(lambda e: e.copy(out=dlr[:, :], in_=pb0[0:DTR, 0:HC]))
            p.ac(lambda e: e.copy(out=bcrowB[:, :], in_=pb1[0:16, 0:HC]))
            p.ac(lambda e: e.copy(out=bcrowC[:, :], in_=pb2[0:16, 0:HC]))
            p.bar()
            for src_, dst in ((bcrowB, Bb), (bcrowC, Cb)):
                for n4 in range(4):
                    for k4 in range(4):
                        n = 4 * n4 + k4
                        mm(psA0[:, 256 * k4:256 * k4 + HC],
                           sel16[:, 128 * n:128 * (n + 1)], src_[:, :], start=True)
                    p.bar()
                    p.ac(lambda e, n4=n4, dst=dst: e.copy(
                        out=dst[:, 4 * n4:4 * n4 + 4, :],
                        in_=psA0[:].rearrange("p (a b) -> p a b", a=4)))
                    p.bar()

            mm(pb0[:, 0:HC], dtW[:, 0:128], dlr[:, :], start=True)
            p.bar()
            pe_dt0 = p.cnt["pe"]
            def inj_slot(w):
                bank = (psA0, psA1)[w // 4]
                return bank[:, 256 * (w % 4):256 * (w % 4) + HC]
            if hh == 1:
                injpe = {}
                def inj_load(w):
                    p.wait("sy", "pe", injpe.get(w - 2, 0))
                    p.wait("gq", "pe", injpe.get(w - 2, 0))
                    opld[w] = wload(op_slab(w), "w_outT", DI, 128 * w, 128,
                                    par=w % 2)
                for w in range(8):
                    p.wait("pe", opld[w][0], opld[w][1])
                    slab = op_slab(w)
                    for kt in range(16):
                        mm(inj_slot(w), slab[:, kt, 0:128], yg[:, kt, :],
                           start=(kt == 0 and w % 2 == 0),
                           stop=(kt == 15 and w % 2 == 1))
                    injpe[w] = p.inc("pe")
                    if w + 2 < 8:
                        inj_load(w + 2)
                p.wait("sy", "pe", injpe[6])
                p.wait("gq", "pe", injpe[6])
                opld2 = {0: wload(op_slab(0), "w_outT", DI, 0, 128, par=0)}
                p.wait("sy", "pe", injpe[7])
                p.wait("gq", "pe", injpe[7])
                opld2[1] = wload(op_slab(1), "w_outT", DI, 128, 128, par=1)
            acdtt = {}
            acfin = {}
            dvtts = {}
            dvfin = {}
            pefin = {0: pe_dt0}
            for t in range(16):
                dAb = dABUF[t % 2]
                dBx = dBxP[t % 2]
                pbank = (pb0, pb1)[t % 2]
                if t < 15:
                    if t >= 1:
                        p.wait("pe", "ac", acdtt[t - 1])
                    mm((pb0, pb1)[(t + 1) % 2][:, 0:HC],
                       dtW[:, 128 * (t + 1):128 * (t + 2)], dlr[:, :], start=True)
                    pefin[t + 1] = p.inc("pe")
                p.wait("ac", "pe", pefin[t])
                if t >= 2:
                    p.wait("ac", "dv", dvtts[t - 2])
                if t >= 1:
                    p.wait("ac", "ac", acfin[t - 1])
                p.ac(lambda e, t=t, b=pbank, dAb=dAb: e.activation(
                    out=dAb[:, 0, :], in_=b[:, 0:HC], func=AF.Exp,
                    bias=dtb[:, t:t + 1]))
                p.chain("ac")
                p.ac(lambda e, t=t, dAb=dAb: e.activation(
                    out=dtt[:, t % 2, :], in_=dAb[:, 0, :], func=AF.Ln,
                    bias=cvals[:, 5:6]))
                acdtt[t] = p.inc("ac")
                p.wait("ac", "ac", acdtt[t])
                for n in range(16):
                    p.ac(lambda e, t=t, n=n, dAb=dAb: e.activation(
                        out=dAb[:, n, :], in_=dtt[:, t % 2, :], func=AF.Exp,
                        scale=Aneg[:, t, n:n + 1]))
                acfin[t] = p.inc("ac")
                p.wait("dv", "ac", acdtt[t])
                if t >= 1:
                    p.wait("dv", "dv", dvfin[t - 1])
                p.dv(lambda e, t=t: e.tensor_mul(out=wt[:, :], in0=dtt[:, t % 2, :],
                                                 in1=xconv[:, t, :]))
                p.chain("dv")
                p.dv(lambda e, dBx=dBx: e.tensor_tensor(
                    out=dBx[:],
                    in0=wt[:, :].unsqueeze(1).broadcast_to([128, 16, HC]),
                    in1=Bb[:], op=ALU.mult))
                p.chain("dv")
                p.wait("dv", "ac", acfin[t])
                p.dv(lambda e, t=t, dAb=dAb: e.tensor_mul(out=yt[:, 0:16], in0=dAb[:, :, 0],
                                                 in1=hcarry[:, t, :]))
                p.chain("dv")
                p.dv(lambda e, dBx=dBx: e.tensor_add(out=dBx[:, :, 0], in0=dBx[:, :, 0],
                                            in1=yt[:, 0:16]))
                p.chain("dv")
                p.dv(lambda e, dAb=dAb: e.memset(dAb[:, :, 0], 0.0))
                p.chain("dv")
                p.dv(lambda e, dAb=dAb, dBx=dBx: e.tensor_tensor_scan(
                    out=dBx[:].rearrange("p n l -> p (n l)"),
                    data0=dAb[:].rearrange("p n l -> p (n l)"),
                    data1=dBx[:].rearrange("p n l -> p (n l)"),
                    initial=0.0, op0=ALU.mult, op1=ALU.add))
                dvtts[t] = p.inc("dv")
                p.wait("dv", "dv", dvtts[t])
                p.dv(lambda e, t=t, dBx=dBx: e.tensor_copy(hcarry[:, t, :], dBx[:, :, HC - 1]))
                p.chain("dv")
                p.dv(lambda e, dBx=dBx: e.tensor_tensor(
                    out=dBx[:], in0=dBx[:], in1=Cb[:], op=ALU.mult))
                p.chain("dv")
                p.dv(lambda e, dBx=dBx: e.tensor_add(
                    out=yah[:], in0=dBx[:, 0:8, :], in1=dBx[:, 8:16, :]))
                p.chain("dv")
                p.dv(lambda e: e.tensor_add(out=tt1[:], in0=yah[:, 0:4, :],
                                            in1=yah[:, 4:8, :]))
                p.chain("dv")
                p.dv(lambda e: e.tensor_add(out=tt2[:], in0=tt1[:, 0:2, :],
                                            in1=tt1[:, 2:4, :]))
                p.chain("dv")
                p.dv(lambda e: e.tensor_add(out=yt[:, :], in0=tt2[:, 0, :],
                                            in1=tt2[:, 1, :]))
                p.chain("dv")
                p.dv(lambda e, t=t: e.scalar_tensor_tensor(
                    out=yt[:, :], in0=xconv[:, t, :], scalar=Dp[:, t:t + 1],
                    in1=yt[:, :], op0=ALU.mult, op1=ALU.add))
                p.chain("dv")
                if hh == 1 and t == 0:
                    p.wait("dv", "pe", injpe[7])
                p.dv(lambda e, t=t: e.tensor_mul(out=yg[:, t, :], in0=yt[:, :],
                                                 in1=zsil[:, t, :]))
                dvfin[t] = p.inc("dv")
            p.bar()

            if hh == 0:
                continue
            for w in range(8):
                p.dv(lambda e, w=w: e.tensor_add(
                    out=px[:, w, 0:HC], in0=px[:, w, 0:HC], in1=inj_slot(w)))
                p.chain("dv")
            p.bar()

            oppe = {}
            opdv = {-2: 0, -1: 0}
            lnpe = {}
            def op_load(w):
                p.wait("sy", "pe", oppe.get(w - 2, 0))
                p.wait("gq", "pe", oppe.get(w - 2, 0))
                opld2[w] = wload(op_slab(w), "w_outT", DI, 128 * w, 128, par=w % 2)
            ls1 = atno[:, 0:2, :]
            ls2 = atno[:, 2:4, :]
            for w in range(8):
                p.wait("pe", opld2[w][0], opld2[w][1])
                p.wait("pe", "dv", opdv[w - 2])
                slab = op_slab(w)
                for kt in range(16):
                    mm(BANKS[w % 2][:, 0:HC], slab[:, kt, 0:128],
                       yg[:, kt, :], start=(kt == 0), stop=(kt == 15))
                oppe[w] = p.inc("pe")
                if w + 2 < 8:
                    op_load(w + 2)
                p.wait("dv", "pe", oppe[w])
                p.dv(lambda e, w=w, h0=h0, b=BANKS[w % 2]: e.tensor_add(
                    out=px[:, w, h0:h0 + HC], in0=px[:, w, h0:h0 + HC],
                    in1=b[:, 0:HC]))
                opdv[w] = p.inc("dv")
                if STAGE != 1:
                    p.wait("ac", "dv", opdv[w])
                    p.wait("ac", "pe", lnpe.get(w - 2, 0))
                    p.ac(lambda e, w=w: e.copy(out=ls1[:, w % 2, :],
                                               in_=px[:, w, :]))
                    p.chain("ac")
                    p.ac(lambda e, w=w: e.activation(out=ls2[:, w % 2, :],
                                                     in_=px[:, w, :],
                                                     func=AF.Square))
                    lnac = p.inc("ac")
                    p.wait("pe", "ac", lnac)
                    mm(psA1[0:1, 0:AC], onek[:], ls1[:, w % 2, :],
                       start=(w == 0), stop=(w == 7))
                    mm(psA1[0:1, AC:2 * AC], onek[:], ls2[:, w % 2, :],
                       start=(w == 0), stop=(w == 7))
                    lnpe[w] = p.inc("pe")
            p.bar()

        if STAGE == 1:
            for t in range(NPT):
                p.sy(lambda e, t=t, c0=c0: e.dma_start(out=out_r[:, t, c0:c0 + AC],
                                                       in_=px[:, t, :]))
            p.bar()
            continue

        def wq_slab(w):
            return wbuf[:, w % 2, 0:NPT * 256].rearrange("p (k c) -> p k c", k=NPT)
        p.wait("sy", "pe", p.cnt["pe"])
        p.wait("gq", "pe", p.cnt["pe"])
        qld_h = {0: wload(wq_slab(0), "wqkv0", DM, 0, 256, par=0),
                 1: wload(wq_slab(1), "wqkv0", DM, 256, 256, par=1)}

        ln_post(AC, psA1)
        dvn = 0
        for t in range(NPT):
            p.dv(lambda e, t=t: e.tensor_sub(out=tf[:, 0, :], in0=px[:, t, :],
                                             in1=psA1[:, 0:AC]))
            p.chain("dv")
            p.dv(lambda e, t=t: e.tensor_mul(out=pxi[:, t, 1:1 + AC],
                                             in0=tf[:, 0, :], in1=psA1[:, AC:2 * AC]))
            dvn = p.inc("dv")
            p.wait("dv", "dv", dvn)
            p.wait("ac", "dv", dvn)
            p.ac(lambda e, t=t: e.copy(out=pxib[:, t, 1:1 + AC],
                                       in_=pxi[:, t, 1:1 + AC]))
        p.bar()
        for i, nm in enumerate(("rcq", "rsq", "rck", "rsk")):
            p.sy(lambda e, i=i, nm=nm, c0=c0: e.dma_start(
                out=rott[:, i, :], in_=D[nm][:, c0:c0 + AC]))
        p.bar()

        nlayers = 1 if STAGE in (2, 3) else 2
        for l in range(nlayers):
            ring = c % 2
            prev_ring = (c + 1) % 2
            kring = kringT[l]
            vtring = vtringT[l]

            st_ = 2 * l
            p.dv(lambda e, st_=st_: e.tensor_copy(pxib[:, :, 0], scarry[:, st_, :]))
            p.chain("dv")
            p.dv(lambda e, st_=st_: e.tensor_copy(scarry[:, st_, :], pxib[:, :, AC]))
            p.bar()

            def qkv_rhs(kt):
                return pxib[:, kt, 1:1 + AC] if kt < NPT // 2 else pxib[:, kt, 0:AC]

            def l2norm_rotary(isq, l=l, kring=kring, ring=ring):
                qac = {}
                qpe = {}
                for t in range(4):
                    if t >= 2:
                        p.wait("ac", "pe", qpe[t - 2])
                    p.ac(lambda e, t=t: e.activation(out=sqb[:, t % 2, :],
                                                     in_=qraw[:, t, :], func=AF.Square))
                    qac[t] = p.inc("ac")
                    p.wait("pe", "ac", qac[t])
                    mm(pb0[0:8, 0:AC], bdones[:, t, :], sqb[:, t % 2, :],
                       start=(t == 0), stop=(t == 3))
                    qpe[t] = p.inc("pe")
                p.wait("ac", "pe", qpe[3])
                p.ac(lambda e: e.activation(out=nrm[:, 0, :], in_=pb0[0:8, 0:AC],
                                            func=AF.Ln))
                p.chain("ac")
                p.ac(lambda e: e.activation(out=nrm[:, 1, :], in_=nrm[:, 0, :],
                                            func=AF.Exp, scale=cvals[0:8, 1:2]))
                nac = p.inc("ac")
                p.wait("dv", "ac", nac)
                p.dv(lambda e: e.tensor_scalar_min(out=nrm[:, 1, :], in0=nrm[:, 1, :],
                                                   scalar1=1e12))
                ndv = p.inc("dv")
                p.wait("ac", "dv", ndv)
                p.ac(lambda e: e.copy(out=nrmb[:, :], in_=nrm[:, 1, :]))
                nac = p.inc("ac")
                sc = qs8 if isq else ks1
                ct, stt_ = (0, 1) if isq else (2, 3)
                bcd = {}
                rpd = {}
                sdv = {}
                rdv = {}
                p.wait("pe", "ac", nac)
                for t in range(4):
                    bk = (pb1, pb2)[t % 2]
                    if t >= 2:
                        p.wait("pe", "dv", rdv[t - 2])
                    mm(bk[:, 0:AC], sel8[:, 128 * t:128 * (t + 1)], nrmb[:, :],
                       start=True)
                    bcd[t] = p.inc("pe")
                    p.wait("dv", "pe", bcd[t])
                    p.dv(lambda e, t=t, sc=sc, b=bk, l=l: e.scalar_tensor_tensor(
                        out=q2b[:, t, :], in0=qraw[:, t, :], scalar=sc[:, l:l + 1],
                        in1=b[:, 0:AC], op0=ALU.mult, op1=ALU.mult))
                    sdv[t] = p.inc("dv")
                    p.wait("pe", "dv", sdv[t])
                    mm(bk[:, 0:AC], rotP[:], q2b[:, t, :], start=True)
                    rpd[t] = p.inc("pe")
                    p.dv(lambda e, t=t, ct=ct: e.tensor_mul(
                        out=tf[:, 0, :], in0=q2b[:, t, :], in1=rott[:, ct, :]))
                    p.chain("dv")
                    p.wait("dv", "pe", rpd[t])
                    p.dv(lambda e, t=t, stt_=stt_, b=bk: e.tensor_mul(
                        out=tf[:, 1, :], in0=b[:, 0:AC], in1=rott[:, stt_, :]))
                    rdv[t] = p.inc("dv")
                    p.wait("dv", "dv", rdv[t])
                    if isq:
                        p.dv(lambda e, t=t: e.tensor_add(
                            out=q3b[:, t, :], in0=tf[:, 0, :], in1=tf[:, 1, :]))
                    else:
                        p.dv(lambda e, t=t, ring=ring, kring=kring: e.tensor_add(
                            out=kring[:, t, ring * AC:ring * AC + AC],
                            in0=tf[:, 0, :], in1=tf[:, 1, :]))
                    p.chain("dv")
                p.bar()

            qld = {}
            qpe_ = {}
            qac_ = {-2: 0, -1: 0}
            def q_load(w):
                p.wait("sy", "pe", qpe_.get(w - 2, 0))
                p.wait("gq", "pe", qpe_.get(w - 2, 0))
                qld[w] = wload(wq_slab(w), f"wqkv{l}", DM, 256 * w, 256, par=w % 2)
            if l == 0:
                qld[0], qld[1] = qld_h[0], qld_h[1]
            else:
                q_load(0)
                q_load(1)
            for w in range(6):
                p.wait("pe", qld[w][0], qld[w][1])
                p.wait("pe", "ac", qac_.get(w - 3, 0))
                slab = wq_slab(w)
                for kt in range(NPT):
                    for oi in range(2):
                        mm(BANKS[2 * (w % 3) + oi],
                           slab[:, kt, 128 * oi:128 * (oi + 1)],
                           qkv_rhs(kt), start=(kt == 0), stop=(kt == NPT - 1))
                qpe_[w] = p.inc("pe")
                if w + 2 < 6:
                    q_load(w + 2)
                p.wait("ac", "pe", qpe_[w])
                for oi in range(2):
                    ot = 2 * w + oi
                    bk = BANKS[2 * (w % 3) + oi]
                    if ot < 8:
                        p.ac(lambda e, ot=ot, b=bk: e.copy(
                            out=qraw[:, ot % 4, :], in_=b[:, 0:AC]))
                    else:
                        p.ac(lambda e, ot=ot, b=bk: e.copy(
                            out=vbf[:, ot % 4, :], in_=b[:, 0:AC]))
                qac_[w] = p.inc("ac")
                if w == 1:
                    p.bar()
                    l2norm_rotary(True)
                elif w == 3:
                    p.bar()
                    l2norm_rotary(False)
            p.bar()

            for hd in range(HEADS):
                pt_, off = hd // 2, (hd % 2) * 64
                for jt in range(4):
                    p.pe(lambda e, pt_=pt_, off=off, jt=jt: e.transpose(
                        out=pbt[:, 64 * jt:64 * jt + 64],
                        in_=vbf[off:off + 64, pt_, 128 * jt:128 * (jt + 1)],
                        identity=identD[off:off + 64, :]))
                p.bar()
                p.ac(lambda e, hd=hd, ring=ring, vtring=vtring: e.copy(
                    out=vtring[:, hd, 4 * ring:4 * ring + 4, :].rearrange(
                        "p a b -> p (a b)"),
                    in_=pbt[:, 0:256]))
                p.bar()

            items = [(hd, ib) for hd in range(HEADS) for ib in range(4)]
            SIMR = (psA0, psA1)
            pesim = {}; dvmax = {}; acexp = {}; dvren = {}
            petr = {}; acev = {}; peav = {}; acat = {-1: 0}
            jts_of = lambda ib: (list(range(ib, ib + 5)) if c > 0
                                 else list(range(4, 5 + ib)))

            def emit_sim(i):
                hd, ib = items[i]
                pt_, off = hd // 2, (hd % 2) * 64
                sim = SIMR[i % 2]
                jts = jts_of(ib)
                if i >= 2:
                    p.wait("pe", "ac", acexp[i - 2])
                qblk = q3b[off:off + 64, pt_, 128 * ib:128 * (ib + 1)]
                for k, jt in enumerate(jts):
                    r = ring if jt >= 4 else prev_ring
                    colbase = r * AC + 128 * (jt % 4)
                    mm(sim[:, 128 * k:128 * (k + 1)], qblk,
                       kring[off:off + 64, pt_, colbase:colbase + 128], start=True)
                pesim[i] = p.inc("pe")

            emit_sim(0)
            emit_sim(1)
            nI = len(items)
            for i in range(nI + 1):
                if i < nI:
                    hd, ib = items[i]
                    pt_, off = hd // 2, (hd % 2) * 64
                    sim = SIMR[i % 2]
                    attn = attnP[i % 2]
                    sA = sAB[:, 4 * (i % 2):4 * (i % 2) + 4]
                    nj = len(jts_of(ib))
                    p.wait("dv", "pe", pesim[i])
                    if i >= 2:
                        p.wait("dv", "ac", acexp[i - 2])
                    if c > 0:
                        p.dv(lambda e, sim=sim: e.tensor_tensor(
                            out=sim[:, 0:128], in0=sim[:, 0:128], in1=m0[:],
                            op=ALU.add))
                        p.chain("dv")
                    p.dv(lambda e, sim=sim, nj=nj: e.tensor_tensor(
                        out=sim[:, 128 * (nj - 1):128 * nj],
                        in0=sim[:, 128 * (nj - 1):128 * nj], in1=m4[:], op=ALU.add))
                    p.chain("dv")
                    p.dv(lambda e, sim=sim, nj=nj, sA=sA: e.reduce_max(
                        out=sA[:, 0:1], in_=sim[:, 0:128 * nj], axis=AX.X,
                        negate=True))
                    dvmax[i] = p.inc("dv")
                if i >= 1:
                    j = i - 1
                    hdj, ibj = items[j]
                    ptj, offj = hdj // 2, (hdj % 2) * 64
                    attnj = attnP[j % 2]
                    attnTj = attnTP[j % 2]
                    sAj = sAB[:, 4 * (j % 2):4 * (j % 2) + 4]
                    njj = len(jts_of(ibj))
                    jtsj = jts_of(ibj)
                    p.wait("dv", "ac", acexp[j])
                    p.dv(lambda e, sA=sAj: e.reciprocal(out=sA[:, 2:3], in_=sA[:, 1:2]))
                    p.chain("dv")
                    p.dv(lambda e, nj=njj, attn=attnj, sA=sAj: e.tensor_scalar_mul(
                        out=attn[:, 0:128 * nj], in0=attn[:, 0:128 * nj],
                        scalar1=sA[:, 2:3]))
                    dvren[j] = p.inc("dv")
                    p.wait("pe", "dv", dvren[j])
                    if j >= 1:
                        p.wait("pe", "ac", acev[j - 1])
                    for k in range(njj):
                        p.pe(lambda e, k=k, attn=attnj: e.transpose(
                            out=pbt[:, 128 * k:128 * (k + 1)],
                            in_=attn[:, 128 * k:128 * (k + 1)], identity=ident[:]))
                    petr[j] = p.inc("pe")
                    p.wait("ac", "pe", petr[j])
                    if j >= 2:
                        p.wait("ac", "pe", peav[j - 2])
                    p.ac(lambda e, nj=njj, attnT=attnTj: e.copy(
                        out=attnT[:, 0:128 * nj], in_=pbt[:, 0:128 * nj]))
                    acev[j] = p.inc("ac")
                    p.wait("pe", "ac", acev[j])
                    p.wait("pe", "ac", acat[j - 1])
                    for k, jt in enumerate(jtsj):
                        slot = (ring if jt >= 4 else prev_ring) * 4 + (jt % 4)
                        mm(pb2[offj:offj + 64, 0:128], vtring[:, hdj, slot, :],
                           attnTj[:, 128 * k:128 * (k + 1)], start=(k == 0),
                           stop=(k == njj - 1), tile_position=(0, offj))
                    peav[j] = p.inc("pe")
                    p.wait("ac", "pe", peav[j])
                    p.ac(lambda e, off=offj, pt_=ptj, ib=ibj: e.copy(
                        out=atno[off:off + 64, pt_, 128 * ib:128 * (ib + 1)],
                        in_=pb2[off:off + 64, 0:128]))
                    acat[j] = p.inc("ac")
                if i < nI:
                    p.wait("ac", "dv", dvmax[i])
                    if i >= 2:
                        p.wait("ac", "pe", petr[i - 2])
                    p.ac(lambda e, sim=sim, nj=nj, attn=attn, sA=sA: e.activation(
                        out=attn[:, 0:128 * nj], in_=sim[:, 0:128 * nj], func=AF.Exp,
                        bias=sA[:, 0:1], accum_out=sA[:, 1:2]))
                    acexp[i] = p.inc("ac")
                    if i + 2 < nI:
                        emit_sim(i + 2)
            p.bar()

            def wo_slab(w):
                return wbuf[:, w % 2, 0:4 * 256].rearrange("p (k c) -> p k c", k=4)

            wold = {}
            wope = {}
            wodv = {-2: 0, -1: 0}
            def wo_load(w):
                p.wait("sy", "pe", wope.get(w - 2, 0))
                p.wait("gq", "pe", wope.get(w - 2, 0))
                wold[w] = wload(wo_slab(w), f"wo{l}", INNER, 256 * w, 256, par=w % 2)
            wo_load(0)
            wo_load(1)
            for w in range(4):
                p.wait("pe", wold[w][0], wold[w][1])
                p.wait("pe", "dv", wodv[w - 2])
                slab = wo_slab(w)
                for kt in range(4):
                    for oi in range(2):
                        mm(BANKS[2 * (w % 2) + oi],
                           slab[:, kt, 128 * oi:128 * (oi + 1)],
                           atno[:, kt, :], start=(kt == 0), stop=(kt == 3))
                wope[w] = p.inc("pe")
                if w + 2 < 4:
                    wo_load(w + 2)
                p.wait("dv", "pe", wope[w])
                p.dv(lambda e, b=BANKS[2 * (w % 2)], ot=2 * w: e.tensor_add(
                    out=pxi[:, ot, 1:1 + AC], in0=pxi[:, ot, 1:1 + AC],
                    in1=b[:, 0:AC]))
                p.chain("dv")
                p.dv(lambda e, b=BANKS[2 * (w % 2) + 1], ot=2 * w + 1: e.tensor_add(
                    out=pxi[:, ot, 1:1 + AC], in0=pxi[:, ot, 1:1 + AC],
                    in1=b[:, 0:AC]))
                wodv[w] = p.inc("dv")
            p.bar()
            for ot in range(NPT):
                p.ac(lambda e, ot=ot: e.copy(out=pxib[:, ot, 1:1 + AC],
                                             in_=pxi[:, ot, 1:1 + AC]))
            p.bar()
            if STAGE == 2:
                break

            def w1_slab(w):
                return wbuf[:, w % 2, 0:NPT * 256].rearrange("p (k c) -> p k c", k=NPT)
            p.wait("sy", "pe", p.cnt["pe"])
            p.wait("gq", "pe", p.cnt["pe"])
            w1ld = {0: wload(w1_slab(0), f"w1p{l}", DM, 0, 256, par=0),
                    1: wload(w1_slab(1), f"w1p{l}", DM, 256, 256, par=1)}

            st_ = 2 * l + 1
            p.dv(lambda e, st_=st_: e.tensor_copy(pxib[:, :, 0], scarry[:, st_, :]))
            p.chain("dv")
            p.dv(lambda e, st_=st_: e.tensor_copy(scarry[:, st_, :], pxib[:, :, AC]))
            p.bar()

            def sh_ap(t):
                return pxib[:, t, 1:1 + AC] if t < NPT // 2 else pxib[:, t, 0:AC]

            ln_stats(sh_ap, AC, psA0)
            for t in range(NPT):
                p.dv(lambda e, t=t: e.tensor_sub(out=tf[:, 0, :], in0=sh_ap(t),
                                                 in1=psA0[:, 0:AC]))
                p.chain("dv")
                p.dv(lambda e, t=t: e.tensor_mul(out=tf[:, 1, :], in0=tf[:, 0, :],
                                                 in1=psA0[:, AC:2 * AC]))
                p.chain("dv")
                p.dv(lambda e, t=t, l=l: e.tensor_scalar(
                    out=ln2o[:, t, :], in0=tf[:, 1, :], scalar1=lng[:, l, t:t + 1],
                    scalar2=lnb[:, l, t:t + 1], op0=ALU.mult, op1=ALU.add))
                p.chain("dv")
            p.bar()

            w1pe = {}
            w1ac = {}
            w1dv = {-3: 0, -2: 0, -1: 0}
            def w1_load(w):
                p.wait("sy", "pe", w1pe.get(w - 2, 0))
                p.wait("gq", "pe", w1pe.get(w - 2, 0))
                w1ld[w] = wload(w1_slab(w), f"w1p{l}", DM, 256 * w, 256, par=w % 2)
            for w in range(NHT):
                p.wait("pe", w1ld[w][0], w1ld[w][1])
                p.wait("pe", "dv", w1dv.get(w - 3, 0))
                slab = w1_slab(w)
                for kt in range(NPT):
                    for oi in range(2):
                        mm(BANKS[2 * (w % 3) + oi],
                           slab[:, kt, 128 * oi:128 * (oi + 1)],
                           ln2o[:, kt, :], start=(kt == 0), stop=(kt == NPT - 1))
                w1pe[w] = p.inc("pe")
                if w + 2 < NHT:
                    w1_load(w + 2)
                p.wait("ac", "pe", w1pe[w])
                p.wait("ac", "dv", w1dv.get(w - 2, 0))
                gb_ = BANKS[2 * (w % 3) + 1]
                ab_ = BANKS[2 * (w % 3)]
                p.ac(lambda e, g_=gb_, w=w, l=l: e.activation(
                    out=gel[:, w % 2, :], in_=g_[:, 0:AC], func=AF.Gelu,
                    bias=b1g[:, l, w:w + 1]))
                w1ac[w] = p.inc("ac")
                p.wait("dv", "ac", w1ac[w])
                p.dv(lambda e, a_=ab_, w=w, l=l: e.scalar_tensor_tensor(
                    out=gated[:, w, :], in0=a_[:, 0:AC], scalar=b1a[:, l, w:w + 1],
                    in1=gel[:, w % 2, :], op0=ALU.add, op1=ALU.mult))
                w1dv[w] = p.inc("dv")
            p.bar()

            def w2_slab(i):
                return wbuf[:, i % 2, 0:11 * 128].rearrange("p (k c) -> p k c", k=11)

            seq2 = [(w, gpr) for w in range(8) for gpr in range(2)]
            w2ld = {}
            w2pe = {}
            w2dv = {-2: 0, -1: 0, 0: 0, 1: 0}
            def w2_load(i):
                w_, g_ = seq2[i]
                p.wait("sy", "pe", w2pe.get(i - 2, 0))
                p.wait("gq", "pe", w2pe.get(i - 2, 0))
                w2ld[i] = wload(w2_slab(i), f"w2p{l}", HIPD_ if False else HIDP,
                                128 * w_, 128, rt0=11 * g_, rtn=11, par=i % 2)
            w2_load(0)
            w2_load(1)
            for i in range(16):
                w, gpr = seq2[i]
                p.wait("pe", w2ld[i][0], w2ld[i][1])
                p.wait("pe", "dv", w2dv[i - 2])
                slab = w2_slab(i)
                for kk in range(11):
                    kt = 11 * gpr + kk
                    mm(BANKS[4 + w % 2], slab[:, kk, 0:128], gated[:, kt, :],
                       start=(kt == 0), stop=(kt == NHT - 1))
                w2pe[i] = p.inc("pe")
                if i + 2 < 16:
                    w2_load(i + 2)
                if gpr == 1:
                    p.wait("dv", "pe", w2pe[i])
                    p.dv(lambda e, w=w, l=l, b=BANKS[4 + w % 2]:
                         e.scalar_tensor_tensor(
                             out=pxi[:, w, 1:1 + AC], in0=b[:, 0:AC],
                             scalar=b2s[:, l, w:w + 1], in1=pxi[:, w, 1:1 + AC],
                             op0=ALU.add, op1=ALU.add))
                    w2dv[i] = p.inc("dv")
                else:
                    w2dv[i] = w2dv[i - 1]
            p.bar()
            if l == 0 and nlayers > 1:
                for w in range(NPT):
                    p.ac(lambda e, w=w: e.copy(out=pxib[:, w, 1:1 + AC],
                                               in_=pxi[:, w, 1:1 + AC]))
                p.bar()

        for t in range(NPT):
            p.dv(lambda e, t=t: e.tensor_add(out=px[:, t, :], in0=px[:, t, :],
                                             in1=pxi[:, t, 1:1 + AC]))
        p.bar()
        for t in range(NPT):
            p.sy(lambda e, t=t, c0=c0: e.dma_start(out=out_r[:, t, c0:c0 + AC],
                                                   in_=px[:, t, :]))
        p.bar()

    p.wait("sy", "sy", p.cnt["sy"])

    with nc.Block() as block:
        p.materialize(block)
    nc._keepalive_ctx = ctx
    return nc


def kernel(_ncores=NCORE, **inputs):
    g = _host_prep(inputs)
    x = inputs["L0_attn_x"]
    nc = build()
    in_maps = []
    for cix in range(_ncores):
        m = {"xT": np.ascontiguousarray(np.asarray(x[cix]).T)}
        m.update(g)
        in_maps.append(m)
    res = run_bass_kernel_spmd(nc, in_maps, core_ids=list(range(_ncores)))
    outs = [np.asarray(r["out"]).T for r in res.results]
    return np.ascontiguousarray(np.stack(outs, 0))


# version 11
# speedup vs baseline: 1.0502x; 1.0087x over previous
import numpy as np
import ml_dtypes
from contextlib import ExitStack

import concourse.bass as bass
import concourse.mybir as mybir
from concourse.bass_utils import run_bass_kernel_spmd

F32 = mybir.dt.float32
BF16 = mybir.dt.bfloat16
AF = mybir.ActivationFunctionType
ALU = mybir.AluOpType
AX = mybir.AxisListType

BF = ml_dtypes.bfloat16

NCORE = 8
LSEQ = 4096
DM = 1024
DI = 2048
NST = 16
DTR = 64
NPT = 8
AC = 512
HC = 256
NCH = LSEQ // AC
HEADS = 8
DH = 64
INNER = 512
HID = 2730
HIDP = 2816
NHT = 22
NEG = -1e30

NCHRUN = NCH
STAGE = 4


def _pp(v):
    return np.ascontiguousarray(np.asarray(v, np.float32).reshape(-1, 128).T)


def _pp3(a):
    a = np.asarray(a, np.float32)
    F, K = a.shape
    return np.ascontiguousarray(a.reshape(F // 128, 128, K).transpose(1, 0, 2))


def _host_prep(inp):
    g = {}
    g["w_inT"] = np.ascontiguousarray(inp["L0_attn_in_proj_w"].T)
    g["w_xpT"] = np.ascontiguousarray(inp["L0_attn_x_proj_w"].T)
    g["w_dtT"] = np.ascontiguousarray(inp["L0_attn_dt_proj_w"].T)
    g["w_outT"] = np.ascontiguousarray(inp["L0_attn_out_proj_w"].T)
    g["convw"] = _pp3(inp["L0_attn_conv_w"][:, 0, :])
    cw = _pp3(inp["L0_attn_conv_w"][:, 0, :])
    cd = np.zeros((16, 4, 128, 128), np.float32)
    ii128 = np.arange(128)
    for t in range(16):
        for k in range(4):
            cd[t, k, ii128, ii128] = cw[:, t, k]
    g["convdiag"] = np.ascontiguousarray(cd.reshape(16 * 4 * 128, 128))
    g["convb"] = _pp(inp["L0_attn_conv_b"])
    g["dtb"] = _pp(inp["L0_attn_dt_proj_b"])
    g["Alog"] = _pp3(inp["L0_attn_A_log"])
    g["Dp"] = _pp(inp["L0_attn_D_param"])
    for l in (0, 1):
        g[f"wqkv{l}"] = np.ascontiguousarray(inp["L0_ffn_wqkv"][l])
        g[f"wo{l}"] = np.ascontiguousarray(inp["L0_ffn_wo"][l])
        w1 = inp["L0_ffn_w1"][l]
        a = np.zeros((DM, HIDP), np.float32); a[:, :HID] = w1[:, :HID]
        gg = np.zeros((DM, HIDP), np.float32); gg[:, :HID] = w1[:, HID:]
        at = a.reshape(DM, NHT, 128); gt = gg.reshape(DM, NHT, 128)
        g[f"w1p{l}"] = np.ascontiguousarray(
            np.stack([at, gt], axis=2).reshape(DM, 2 * NHT * 128))
        w2 = np.zeros((HIDP, DM), np.float32); w2[:HID] = inp["L0_ffn_w2"][l]
        g[f"w2p{l}"] = w2
        b1 = inp["L0_ffn_b1"][l]
        b1a = np.zeros(HIDP, np.float32); b1a[:HID] = b1[:HID]
        b1g = np.zeros(HIDP, np.float32); b1g[:HID] = b1[HID:]
        g[f"b1a{l}"] = _pp(b1a); g[f"b1g{l}"] = _pp(b1g)
    g["qs"] = np.ascontiguousarray(np.stack(
        [np.tile(np.asarray(inp["L0_ffn_q_scale"][l], np.float32), 2) for l in (0, 1)], 1))
    g["ks"] = np.ascontiguousarray(np.stack(
        [np.tile(np.asarray(inp["L0_ffn_k_scale"][l], np.float32), 2) for l in (0, 1)], 1))
    g["lng"] = np.ascontiguousarray(np.stack([_pp(inp["L0_ffn_ln_g"][l]) for l in (0, 1)], 1))
    g["lnb"] = np.ascontiguousarray(np.stack([_pp(inp["L0_ffn_ln_b"][l]) for l in (0, 1)], 1))
    g["b2"] = np.ascontiguousarray(np.stack([_pp(inp["L0_ffn_b2"][l]) for l in (0, 1)], 1))

    ii = np.arange(128)
    g["m0"] = np.where(ii[None, :] < ii[:, None], NEG, 0.0).astype(BF)
    g["m4"] = np.where(ii[None, :] > ii[:, None], NEG, 0.0).astype(BF)
    g["ident"] = np.eye(128, dtype=BF)
    identD = np.zeros((128, 64), np.float32)
    for p_ in range(128):
        identD[p_, p_ % 64] = 1.0
    g["identD"] = identD.astype(BF)
    g["onescol"] = np.ones((1, 128), dtype=BF)
    g["onek"] = np.ones((128, 1), dtype=BF)
    g["cvals"] = np.tile(np.array([[1e-5, -0.5, 1.0 / DM, 8.0, -1.0, 1.0]], np.float32), (128, 1))
    bd = np.zeros((128, 4, 8), np.float32)
    for t in range(4):
        bd[:64, t, 2 * t] = 1.0; bd[64:, t, 2 * t + 1] = 1.0
    g["bdones"] = bd.astype(BF)
    sel8 = np.zeros((8, 4, 128), np.float32)
    for t in range(4):
        for p_ in range(128):
            sel8[2 * t + (p_ >= 64), t, p_] = 1.0
    g["sel8"] = np.ascontiguousarray(sel8.reshape(8, 512)).astype(BF)
    sel16 = np.zeros((16, 16, 128), np.float32)
    for n in range(16):
        sel16[n, n, :] = 1.0
    g["sel16"] = np.ascontiguousarray(sel16.reshape(16, 2048)).astype(BF)
    P = np.zeros((128, 128), np.float32)
    for di in range(128):
        h0 = (di // 64) * 64; d = di % 64
        if d < 32:
            P[h0 + d + 32, di] = -1.0
        else:
            P[h0 + d - 32, di] = 1.0
    g["rotP"] = P.astype(BF)

    dih = np.arange(64); j = dih % 32
    invfreq = 1.0 / 10000.0 ** ((2 * j) / DH)
    t = np.arange(LSEQ, dtype=np.float64)
    freqs = t[None, :] * invfreq[:, None]
    base = ((2 * j) + 0.4 * DH) / (1.4 * DH)
    power = (t - LSEQ // 2) / float(AC // 2)
    scale = base[:, None] ** power[None, :]
    cos = np.cos(freqs); sin = np.sin(freqs)
    rows2 = lambda m: np.ascontiguousarray(np.concatenate([m, m], 0).astype(BF))
    g["rcq"] = rows2(cos * scale); g["rsq"] = rows2(sin * scale)
    g["rck"] = rows2(cos / scale); g["rsk"] = rows2(sin / scale)
    return g


ENGINES = ("sy", "gq", "pe", "ac", "dv")
SEMS = ("sy", "gq", "pe", "ac", "dv", "gc", "wl0", "wl1", "wg0", "wg1", "wb0", "wb1")


class Prog:
    def __init__(self, nc):
        self.nc = nc
        self.prog = {e: [] for e in ENGINES}
        self.iswait = {e: [] for e in ENGINES}
        self.cnt = {e: 0 for e in SEMS}
        self.dirty = {e: False for e in SEMS}
        self.snap = dict(self.cnt)
        self.lastwait = {e: {q: 0 for q in SEMS} for e in ENGINES}
        self.pending = {e: False for e in ENGINES}
        self.sems = None

    def _emit(self, e, fn, compute=False):
        if self.pending[e]:
            self.pending[e] = False
            for q in SEMS:
                self.wait(e, q, self.snap[q])
        self.prog[e].append([fn, []])
        self.iswait[e].append(False)
        if e == "sy" or (e == "gq" and not compute):
            self.prog[e][-1][1] = [(e, 16)]
            self.cnt[e] += 16
        elif e == "gq":
            self.dirty["gc"] = True
        else:
            self.dirty[e] = True

    def sy(self, fn): self._emit("sy", fn)
    def gq(self, fn): self._emit("gq", fn)
    def gc(self, fn): self._emit("gq", fn, compute=True)

    def dma(self, queue, fn, sem):
        if self.pending[queue]:
            self.pending[queue] = False
            for q in SEMS:
                self.wait(queue, q, self.snap[q])
        self.prog[queue].append([fn, [(sem, 16)]])
        self.iswait[queue].append(False)
        self.cnt[sem] += 16
    def pe(self, fn): self._emit("pe", fn)
    def ac(self, fn): self._emit("ac", fn)
    def dv(self, fn): self._emit("dv", fn)

    def inc(self, e):
        stream = "gq" if e == "gc" else e
        assert self.prog[stream] and not self.iswait[stream][-1], e
        self.prog[stream][-1][1].append((e, 1))
        self.cnt[e] += 1
        self.dirty[e] = False
        return self.cnt[e]

    def wait(self, e, q, val):
        if val <= 0 or self.lastwait[e][q] >= val:
            return
        self.lastwait[e][q] = val
        self.prog[e].append(
            [lambda eng, q=q, val=val: eng.wait_ge(self.sems[q], val), []])
        self.iswait[e].append(True)

    def chain(self, e):
        self.inc(e)
        st = "gq" if e == "gc" else e
        self.wait(st, e, self.cnt[e])

    def bar(self):
        for e in ("pe", "ac", "dv", "gc"):
            if self.dirty[e]:
                self.inc(e)
        self.snap = dict(self.cnt)
        for e in ENGINES:
            self.pending[e] = True

    def materialize(self, block):
        sems = self.sems

        def runner(progl):
            def body(eng):
                for fn, incs in progl:
                    inst = fn(eng)
                    for s, n in incs:
                        inst.then_inc(sems[s], n)
            return body

        block.sync(runner(self.prog["sy"]))
        block.gpsimd(runner(self.prog["gq"]))
        block.tensor(runner(self.prog["pe"]))
        block.scalar(runner(self.prog["ac"]))
        block.vector(runner(self.prog["dv"]))


def build(debug=False):
    nc = bass.Bass("TRN2", target_bir_lowering=False)
    ctx = ExitStack()

    D = {}
    def din(name, shape, dt=F32):
        D[name] = nc.dram_tensor(name, list(shape), dt, kind="ExternalInput").ap()
    din("xT", (DM, LSEQ))
    din("w_inT", (DM, 2 * DI)); din("w_xpT", (DI, 96)); din("w_dtT", (DTR, DI))
    din("w_outT", (DI, DM))
    din("convw", (128, 16, 4)); din("convdiag", (16 * 4 * 128, 128)); din("convb", (128, 16)); din("dtb", (128, 16))
    din("Alog", (128, 16, 16)); din("Dp", (128, 16))
    for l in (0, 1):
        din(f"wqkv{l}", (DM, 3 * INNER)); din(f"wo{l}", (INNER, DM))
        din(f"w1p{l}", (DM, 2 * HIDP)); din(f"w2p{l}", (HIDP, DM))
        din(f"b1a{l}", (128, NHT)); din(f"b1g{l}", (128, NHT))
    din("qs", (128, 2)); din("ks", (128, 2))
    din("lng", (128, 2, 8)); din("lnb", (128, 2, 8)); din("b2", (128, 2, 8))
    din("m0", (128, 128), BF16); din("m4", (128, 128), BF16)
    din("ident", (128, 128), BF16); din("identD", (128, 64), BF16)
    din("onescol", (1, 128), BF16); din("onek", (128, 1), BF16)
    din("cvals", (128, 6))
    din("sel8", (8, 512), BF16); din("sel16", (16, 2048), BF16)
    din("bdones", (128, 4, 8), BF16)
    din("rotP", (128, 128), BF16)
    for nm in ("rcq", "rsq", "rck", "rsk"):
        din(nm, (128, LSEQ), BF16)
    out = nc.dram_tensor("out", [DM, LSEQ], F32, kind="ExternalOutput").ap()
    out_r = out.rearrange("(t p) l -> p t l", p=128)
    xT_r = D["xT"].rearrange("(t p) l -> p t l", p=128)

    _nm = [0]
    def sb(shape, dt=F32):
        _nm[0] += 1
        return ctx.enter_context(nc.sbuf_tensor(f"t{_nm[0]}", list(shape), dt))

    px = sb((128, NPT, AC))
    hcarry = sb((128, 16, 16))
    ccarry = sb((128, 16, 4))
    scarry = sb((128, 4, NPT), BF16)
    kringT = [sb((128, 4, 2 * AC), BF16) for _ in (0, 1)]
    vtringT = [sb((128, HEADS, 8, DH), BF16) for _ in (0, 1)]
    m0 = sb((128, 128), BF16); m4 = sb((128, 128), BF16)
    ident = sb((128, 128), BF16); identD = sb((128, 64), BF16)
    onescol = sb((1, 128), BF16); onek = sb((128, 1), BF16)
    cvals = sb((128, 6))
    sel8 = sb((8, 512), BF16); sel16 = sb((16, 2048), BF16)
    bdones = sb((128, 4, 8), BF16)
    rotP = sb((128, 128), BF16)
    Aneg = sb((128, 16, 16)); convw = sb((128, 16, 4))
    convb = sb((128, 16)); dtb = sb((128, 16)); Dp = sb((128, 16))
    qs8 = sb((128, 2)); ks1 = sb((128, 2))
    lng = sb((128, 2, NPT)); lnb = sb((128, 2, NPT)); b2s = sb((128, 2, NPT))
    b1a = sb((128, 2, NHT)); b1g = sb((128, 2, NHT))
    dtW = sb((DTR, DI), BF16)
    xpW = sb((128, 16, 96), BF16)
    dlr = sb((DTR, HC), BF16)
    bcrowB = sb((16, HC), BF16)
    bcrowC = sb((16, HC), BF16)
    rows1 = sb((1, 5 * AC))
    rowsb1 = sb((1, 2 * AC), BF16)
    nrm = sb((8, 2, AC))
    nrmb = sb((8, AC), BF16)
    sAB = sb((128, 8))
    wbuf = sb((128, 2, 2048), BF16)
    cbuf = sb((128, 2, 4, 128), BF16)

    A16N = 29184
    A32N = 13056
    ar16 = sb((128, A16N), BF16)
    ar32 = sb((128, A32N))

    def v16(off, shape):
        n = int(np.prod(shape[1:]))
        ap = ar16[:, off:off + n]
        if len(shape) == 3:
            ap = ap.rearrange("p (a b) -> p a b", a=shape[1])
        return ap, off + n

    def v32(off, shape):
        n = int(np.prod(shape[1:]))
        ap = ar32[:, off:off + n]
        if len(shape) == 3:
            ap = ap.rearrange("p (a b) -> p a b", a=shape[1])
        return ap, off + n

    o = 0
    lnout, o = v16(o, (128, NPT, HC))
    xinner, o = v16(o, (128, 16, HC + 4))
    zsil, o = v16(o, (128, 16, HC))
    xconv, o = v16(o, (128, 16, HC))
    Bb, o = v16(o, (128, 16, HC))
    Cb, o = v16(o, (128, 16, HC))
    wt, o = v16(o, (128, HC))
    ya, o = v16(o, (128, 4, HC))
    yb, o = v16(o, (128, 2, HC))
    yg, o = v16(o, (128, 16, HC))
    assert o <= A16N, o
    dBxA = ar16[:, 2048:2048 + 16 * HC].rearrange("p (a b) -> p a b", a=16)
    dBxP = (dBxA, dBxA)
    yah = ar16[:, 0:8 * HC].rearrange("p (a b) -> p a b", a=8)
    tt1 = ya
    tt2 = yb
    o = 0
    pxib, o = v16(o, (128, NPT, AC + 1))
    rott, o = v16(o, (128, 4, AC))
    q2b, o = v16(o, (128, 4, AC))
    q3b, o = v16(o, (128, 4, AC))
    vbf, o = v16(o, (128, 4, AC))
    sqb, o = v16(o, (128, 2, AC))
    atno, o = v16(o, (128, 4, AC))
    gated, o = v16(o, (128, NHT, AC))
    sqb2, o = v16(o, (128, 2, AC))
    assert o <= A16N, o
    _vb0 = vbf[:, 0:4, :].rearrange("p a b -> p (a b)")
    attnP = (_vb0[:, 0:640], _vb0[:, 640:1280])
    _qb0 = q2b[:, 0:4, :].rearrange("p a b -> p (a b)")
    attnTP = (_qb0[:, 0:640], _qb0[:, 640:1280])
    _q2off = NPT * (AC + 1) + 4 * AC
    ln2o = ar16[:, _q2off:_q2off + NPT * AC].rearrange("p (a b) -> p a b", a=NPT)
    gel = sqb

    o = 0
    dAb0, o = v32(o, (128, 16, HC))
    dAb1, o = v32(o, (128, 16, HC))
    dtt, o = v32(o, (128, 2, HC))
    yt, o = v32(o, (128, HC))
    assert o <= A32N, o
    dABUF = (dAb0, dAb1)
    o = 0
    pxi, o = v32(o, (128, NPT, AC + 1))
    qraw, o = v32(o, (128, 4, AC))
    tf, o = v32(o, (128, 2, AC))
    assert o <= A32N, o

    psA0 = ctx.enter_context(nc.psum_tensor([128, 1024], F32))
    psA1 = ctx.enter_context(nc.psum_tensor([128, 1024], F32))
    pb0 = ctx.enter_context(nc.psum_tensor([128, 512], F32))
    pb1 = ctx.enter_context(nc.psum_tensor([128, 512], F32))
    pb2 = ctx.enter_context(nc.psum_tensor([128, 512], F32))
    pbt = ctx.enter_context(nc.psum_tensor([128, 1024], BF16))
    BANKS = [psA0[:, 0:512], psA0[:, 512:1024],
             psA1[:, 0:512], psA1[:, 512:1024], pb0[:], pb1[:]]

    p = Prog(nc)
    p.sems = {e: ctx.enter_context(nc.semaphore(f"s_{e}")) for e in SEMS}

    def mm(bank_ap, lhsT_ap, rhs_ap, start, stop=None, tile_position=None):
        if stop is None:
            stop = start
        p.pe(lambda e, o_=bank_ap, l_=lhsT_ap, r_=rhs_ap, s_=start, t_=stop,
             tp=tile_position:
             e.matmul(out=o_, lhsT=l_, rhs=r_, start=s_, stop=t_, tile_position=tp))

    SCR16 = {}
    STORED = set()

    LASTWB = {}

    def wload(buf_ap, name, rows_, col0, cols, rt0=0, rtn=None, par=0):
        dram2d = D[name]
        if name not in SCR16:
            SCR16[name] = nc.dram_tensor(
                f"scr_{name}", [dram2d.shape[0], dram2d.shape[1]], BF16,
                kind="Internal").ap()
        scr = SCR16[name].rearrange("(t p) c -> p t c", p=128)
        src = dram2d.rearrange("(t p) c -> p t c", p=128)
        nt = rtn if rtn is not None else rows_ // 128
        step = 4
        key = (name, col0, rt0)
        first = key not in STORED
        STORED.add(key)
        wsem = f"wg{par}" if first else f"wl{par}"
        p.wait("gq" if first else "sy", f"wb{par}", LASTWB.get(par, 0))
        for tt in range(0, nt, step):
            n = min(step, nt - tt)
            if first:
                p.dma("gq", lambda e, d=buf_ap[:, tt:tt + n, :],
                      s=src[:, rt0 + tt:rt0 + tt + n, col0:col0 + cols]:
                      e.dma_start(out=d, in_=s), wsem)
            else:
                p.dma("sy", lambda e, d=buf_ap[:, tt:tt + n, :],
                      s=scr[:, rt0 + tt:rt0 + tt + n, col0:col0 + cols]:
                      e.dma_start(out=d, in_=s), wsem)
        if first:
            p.wait("sy", wsem, p.cnt[wsem])
            for tt in range(0, nt, step):
                n = min(step, nt - tt)
                p.dma("sy", lambda e, d=scr[:, rt0 + tt:rt0 + tt + n, col0:col0 + cols],
                      s=buf_ap[:, tt:tt + n, :]:
                      e.dma_start(out=d, in_=s), f"wb{par}")
            LASTWB[par] = p.cnt[f"wb{par}"]
        return (wsem, p.cnt[wsem])

    def loadc(dst_ap, src_ap, cast=False):
        (p.gq if cast else p.sy)(
            lambda e, d=dst_ap, s=src_ap: e.dma_start(out=d, in_=s))

    loadc(m0[:], D["m0"]); loadc(m4[:], D["m4"])
    loadc(ident[:], D["ident"]); loadc(identD[:], D["identD"])
    loadc(onescol[:], D["onescol"]); loadc(onek[:], D["onek"])
    loadc(cvals[:], D["cvals"])
    loadc(sel8[:], D["sel8"]); loadc(sel16[:], D["sel16"])
    loadc(bdones[:], D["bdones"]); loadc(rotP[:], D["rotP"])
    loadc(convw[:], D["convw"]); loadc(convb[:], D["convb"])
    loadc(dtb[:], D["dtb"]); loadc(Dp[:], D["Dp"])
    loadc(Aneg[:], D["Alog"])
    loadc(qs8[:], D["qs"]); loadc(ks1[:], D["ks"])
    loadc(lng[:], D["lng"]); loadc(lnb[:], D["lnb"]); loadc(b2s[:], D["b2"])
    for l in (0, 1):
        loadc(b1a[:, l, :], D[f"b1a{l}"]); loadc(b1g[:, l, :], D[f"b1g{l}"])
    loadc(dtW[:], D["w_dtT"], cast=True)
    loadc(xpW[:], D["w_xpT"].rearrange("(t p) c -> p t c", p=128), cast=True)
    p.bar()
    AnegF = Aneg[:].rearrange("p a b -> p (a b)")
    p.ac(lambda e: e.activation(out=AnegF, in_=AnegF, func=AF.Exp))
    p.ac(lambda e: e.activation(out=qs8[:], in_=qs8[:], func=AF.Copy, scale=cvals[:, 3:4]))
    p.dv(lambda e: e.memset(hcarry[:].rearrange("p a b -> p (a b)"), 0.0))
    p.dv(lambda e: e.memset(ccarry[:].rearrange("p a b -> p (a b)"), 0.0))
    p.dv(lambda e: e.memset(scarry[:].rearrange("p a b -> p (a b)"), 0.0))
    for l in (0, 1):
        p.dv(lambda e, l=l: e.memset(
            vtringT[l][:].rearrange("p a b c -> p (a b c)"), 0.0))
    p.bar()
    p.dv(lambda e: e.tensor_scalar_mul(out=AnegF, in0=AnegF, scalar1=-1.0))
    p.bar()

    def seg(k, n):
        return rows1[0:1, k * AC:k * AC + n]

    def ln_post(n, bank):
        p.ac(lambda e, n=n, bank=bank: e.activation(out=seg(0, n), in_=bank[0:1, 0:n],
                                         func=AF.Copy, scale=cvals[0:1, 2:3]))
        ac0 = p.inc("ac")
        p.wait("dv", "ac", ac0)
        p.dv(lambda e, n=n: e.tensor_mul(out=seg(1, n), in0=seg(0, n), in1=seg(0, n)))
        p.chain("dv")
        p.dv(lambda e, n=n, bank=bank: e.scalar_tensor_tensor(
            out=seg(2, n), in0=bank[0:1, n:2 * n], scalar=1.0 / DM,
            in1=seg(1, n), op0=ALU.mult, op1=ALU.subtract))
        dv0 = p.inc("dv")
        p.wait("ac", "dv", dv0)
        p.ac(lambda e, n=n: e.activation(out=seg(3, n), in_=seg(2, n),
                                         func=AF.Ln, bias=cvals[0:1, 0:1]))
        p.chain("ac")
        p.ac(lambda e, n=n: e.activation(out=seg(4, n), in_=seg(3, n),
                                         func=AF.Exp, scale=cvals[0:1, 1:2]))
        p.chain("ac")
        p.ac(lambda e, n=n: e.copy(out=rowsb1[0:1, 0:n], in_=seg(0, n)))
        p.chain("ac")
        p.ac(lambda e, n=n: e.copy(out=rowsb1[0:1, n:2 * n], in_=seg(4, n)))
        ac1 = p.inc("ac")
        p.wait("pe", "ac", ac1)
        if n >= 512:
            mm(bank[:, 0:n], onescol[:], rowsb1[0:1, 0:n], start=True)
            mm(bank[:, n:2 * n], onescol[:], rowsb1[0:1, n:2 * n], start=True)
        else:
            mm(bank[:, 0:2 * n], onescol[:], rowsb1[0:1, 0:2 * n], start=True)
        p.bar()

    def ln_stats(src_ap_fn, n, bank, s1=None, s2=None):
        if s1 is None:
            s1 = sqb
        if s2 is None:
            s2 = sqb2
        for t in range(NPT):
            half = t % 2
            p.ac(lambda e, t=t, n=n, half=half: e.copy(
                out=s1[:, half, 0:n], in_=src_ap_fn(t)))
            p.chain("ac")
            p.ac(lambda e, t=t, n=n, half=half: e.activation(
                out=s2[:, half, 0:n], in_=src_ap_fn(t), func=AF.Square))
            lnac = p.inc("ac")
            p.wait("pe", "ac", lnac)
            mm(bank[0:1, 0:n], onek[:], s1[:, half, 0:n], start=(t == 0),
               stop=(t == NPT - 1 and n >= 512))
            mm(bank[0:1, n:2 * n], onek[:], s2[:, half, 0:n],
               start=(t == 0 and n >= 512), stop=(t == NPT - 1))
            if t < NPT - 1:
                pef = p.inc("pe")
                p.wait("ac", "pe", pef)
        p.bar()
        ln_post(n, bank)

    for c in range(NCHRUN):
        c0 = c * AC
        for t in range(NPT):
            p.sy(lambda e, t=t, c0=c0: e.dma_start(out=px[:, t, :],
                                                   in_=xT_r[:, t, c0:c0 + AC]))
        p.bar()

        for hh in range(2):
            h0 = hh * HC
            lnscrA = ar16[:, 2048:2560].rearrange("p (a b) -> p a b", a=2)
            lnscrB = ar16[:, 2560:3072].rearrange("p (a b) -> p a b", a=2)
            ln_stats(lambda t, h0=h0: px[:, t, h0:h0 + HC], HC, pb2,
                     s1=lnscrA, s2=lnscrB)
            for t in range(NPT):
                p.dv(lambda e, t=t, h0=h0: e.tensor_sub(
                    out=tf[:, 0, 0:HC], in0=px[:, t, h0:h0 + HC], in1=pb2[:, 0:HC]))
                p.chain("dv")
                p.dv(lambda e, t=t: e.tensor_mul(out=lnout[:, t, :],
                                                 in0=tf[:, 0, 0:HC],
                                                 in1=pb2[:, HC:2 * HC]))
                p.chain("dv")
            p.bar()

            p.dv(lambda e: e.tensor_copy(xinner[:, :, 1:4], ccarry[:, :, 1:4]))
            p.bar()

            def ip_slab(w):
                return wbuf[:, w % 2, 0:NPT * 256].rearrange("p (k c) -> p k c", k=NPT)

            ipld = {}
            ippe = {}
            ipac = {-3: 0, -2: 0, -1: 0}
            def ip_load(w):
                p.wait("sy", "pe", ippe.get(w - 2, 0))
                p.wait("gq", "pe", ippe.get(w - 2, 0))
                ipld[w] = wload(ip_slab(w), "w_inT", DM, 256 * w, 256, par=w % 2)
            ip_load(0)
            ip_load(1)
            for w in range(16):
                p.wait("pe", ipld[w][0], ipld[w][1])
                p.wait("pe", "ac", ipac.get(w - 3, 0))
                slab = ip_slab(w)
                for kt in range(NPT):
                    for oi in range(2):
                        mm(BANKS[2 * (w % 3) + oi][:, 0:HC],
                           slab[:, kt, 128 * oi:128 * (oi + 1)],
                           lnout[:, kt, :], start=(kt == 0),
                           stop=(kt == NPT - 1))
                ippe[w] = p.inc("pe")
                if w + 2 < 16:
                    ip_load(w + 2)
                p.wait("ac", "pe", ippe[w])
                for oi in range(2):
                    ot = 2 * w + oi
                    bk = BANKS[2 * (w % 3) + oi]
                    if ot < 16:
                        p.ac(lambda e, b=bk, d=ot: e.copy(
                            out=xinner[:, d, 4:4 + HC], in_=b[:, 0:HC]))
                    else:
                        p.ac(lambda e, b=bk, d=ot - 16: e.activation(
                            out=zsil[:, d, :], in_=b[:, 0:HC], func=AF.Silu))
                ipac[w] = p.inc("ac")
            p.bar()
            p.dv(lambda e: e.tensor_copy(ccarry[:, :, 1:4],
                                         xinner[:, :, 1 + HC:4 + HC]))
            p.bar()

            def op_slab(w):
                return wbuf[:, w % 2, 0:16 * 128].rearrange("p (k c) -> p k c", k=16)
            if hh == 1:
                p.wait("sy", "pe", p.cnt["pe"])
                p.wait("gq", "pe", p.cnt["pe"])
                opld = {0: wload(op_slab(0), "w_outT", DI, 0, 128, par=0),
                        1: wload(op_slab(1), "w_outT", DI, 128, 128, par=1)}

            cvld = {}
            cvpe = {}
            cvac = {-2: 0, -1: 0}
            def cv_load(t):
                p.wait("sy", "pe", cvpe.get(t - 2, 0))
                p.wait("gq", "pe", cvpe.get(t - 2, 0))
                cvld[t] = wload(cbuf[:, t % 2, :, :], "convdiag", 16 * 4 * 128, 0,
                                128, rt0=4 * t, rtn=4, par=t % 2)
            cv_load(0)
            cv_load(1)
            for t in range(16):
                p.wait("pe", cvld[t][0], cvld[t][1])
                p.wait("pe", "ac", cvac[t - 2])
                for k in range(4):
                    mm(BANKS[t % 2][:, 0:HC], cbuf[:, t % 2, k, :],
                       xinner[:, t, 1 + k:1 + k + HC], start=(k == 0), stop=(k == 3))
                cvpe[t] = p.inc("pe")
                if t + 2 < 16:
                    cv_load(t + 2)
                p.wait("ac", "pe", cvpe[t])
                p.ac(lambda e, t=t, b=BANKS[t % 2]: e.activation(
                    out=xconv[:, t, :], in_=b[:, 0:HC], func=AF.Silu,
                    bias=convb[:, t:t + 1]))
                cvac[t] = p.inc("ac")
            p.bar()

            for kt in range(16):
                mm(pb0[0:64, 0:HC], xpW[:, kt, 0:64], xconv[:, kt, :], start=(kt == 0), stop=(kt == 15))
                mm(pb1[0:16, 0:HC], xpW[:, kt, 64:80], xconv[:, kt, :], start=(kt == 0), stop=(kt == 15))
                mm(pb2[0:16, 0:HC], xpW[:, kt, 80:96], xconv[:, kt, :], start=(kt == 0), stop=(kt == 15))
            p.bar()
            p.ac(lambda e: e.copy(out=dlr[:, :], in_=pb0[0:DTR, 0:HC]))
            p.ac(lambda e: e.copy(out=bcrowB[:, :], in_=pb1[0:16, 0:HC]))
            p.ac(lambda e: e.copy(out=bcrowC[:, :], in_=pb2[0:16, 0:HC]))
            p.bar()
            for src_, dst in ((bcrowB, Bb), (bcrowC, Cb)):
                for n4 in range(4):
                    for k4 in range(4):
                        n = 4 * n4 + k4
                        mm(psA0[:, 256 * k4:256 * k4 + HC],
                           sel16[:, 128 * n:128 * (n + 1)], src_[:, :], start=True)
                    p.bar()
                    p.ac(lambda e, n4=n4, dst=dst: e.copy(
                        out=dst[:, 4 * n4:4 * n4 + 4, :],
                        in_=psA0[:].rearrange("p (a b) -> p a b", a=4)))
                    p.bar()

            mm(pb0[:, 0:HC], dtW[:, 0:128], dlr[:, :], start=True)
            p.bar()
            pe_dt0 = p.cnt["pe"]
            def inj_slot(w):
                bank = (psA0, psA1)[w // 4]
                return bank[:, 256 * (w % 4):256 * (w % 4) + HC]
            if hh == 1:
                injpe = {}
                def inj_load(w):
                    p.wait("sy", "pe", injpe.get(w - 2, 0))
                    p.wait("gq", "pe", injpe.get(w - 2, 0))
                    opld[w] = wload(op_slab(w), "w_outT", DI, 128 * w, 128,
                                    par=w % 2)
                for w in range(8):
                    p.wait("pe", opld[w][0], opld[w][1])
                    slab = op_slab(w)
                    for kt in range(16):
                        mm(inj_slot(w), slab[:, kt, 0:128], yg[:, kt, :],
                           start=(kt == 0 and w % 2 == 0),
                           stop=(kt == 15 and w % 2 == 1))
                    injpe[w] = p.inc("pe")
                    if w + 2 < 8:
                        inj_load(w + 2)
                p.wait("sy", "pe", injpe[6])
                p.wait("gq", "pe", injpe[6])
                opld2 = {0: wload(op_slab(0), "w_outT", DI, 0, 128, par=0)}
                p.wait("sy", "pe", injpe[7])
                p.wait("gq", "pe", injpe[7])
                opld2[1] = wload(op_slab(1), "w_outT", DI, 128, 128, par=1)
            acdtt = {}
            acfin = {}
            dvtts = {}
            dvfin = {}
            pefin = {0: pe_dt0}
            for t in range(16):
                dAb = dABUF[t % 2]
                dBx = dBxP[t % 2]
                pbank = (pb0, pb1)[t % 2]
                if t < 15:
                    if t >= 1:
                        p.wait("pe", "ac", acdtt[t - 1])
                    mm((pb0, pb1)[(t + 1) % 2][:, 0:HC],
                       dtW[:, 128 * (t + 1):128 * (t + 2)], dlr[:, :], start=True)
                    pefin[t + 1] = p.inc("pe")
                p.wait("ac", "pe", pefin[t])
                if t >= 2:
                    p.wait("ac", "dv", dvtts[t - 2])
                if t >= 1:
                    p.wait("ac", "ac", acfin[t - 1])
                p.ac(lambda e, t=t, b=pbank, dAb=dAb: e.activation(
                    out=dAb[:, 0, :], in_=b[:, 0:HC], func=AF.Exp,
                    bias=dtb[:, t:t + 1]))
                p.chain("ac")
                p.ac(lambda e, t=t, dAb=dAb: e.activation(
                    out=dtt[:, t % 2, :], in_=dAb[:, 0, :], func=AF.Ln,
                    bias=cvals[:, 5:6]))
                acdtt[t] = p.inc("ac")
                p.wait("ac", "ac", acdtt[t])
                for n in range(16):
                    p.ac(lambda e, t=t, n=n, dAb=dAb: e.activation(
                        out=dAb[:, n, :], in_=dtt[:, t % 2, :], func=AF.Exp,
                        scale=Aneg[:, t, n:n + 1]))
                acfin[t] = p.inc("ac")
                p.wait("dv", "ac", acdtt[t])
                if t >= 1:
                    p.wait("dv", "dv", dvfin[t - 1])
                p.dv(lambda e, t=t: e.tensor_mul(out=wt[:, :], in0=dtt[:, t % 2, :],
                                                 in1=xconv[:, t, :]))
                p.chain("dv")
                p.dv(lambda e, dBx=dBx: e.tensor_tensor(
                    out=dBx[:],
                    in0=wt[:, :].unsqueeze(1).broadcast_to([128, 16, HC]),
                    in1=Bb[:], op=ALU.mult))
                p.chain("dv")
                p.wait("dv", "ac", acfin[t])
                p.dv(lambda e, t=t, dAb=dAb: e.tensor_mul(out=yt[:, 0:16], in0=dAb[:, :, 0],
                                                 in1=hcarry[:, t, :]))
                p.chain("dv")
                p.dv(lambda e, dBx=dBx: e.tensor_add(out=dBx[:, :, 0], in0=dBx[:, :, 0],
                                            in1=yt[:, 0:16]))
                p.chain("dv")
                p.dv(lambda e, dAb=dAb: e.memset(dAb[:, :, 0], 0.0))
                p.chain("dv")
                p.dv(lambda e, dAb=dAb, dBx=dBx: e.tensor_tensor_scan(
                    out=dBx[:].rearrange("p n l -> p (n l)"),
                    data0=dAb[:].rearrange("p n l -> p (n l)"),
                    data1=dBx[:].rearrange("p n l -> p (n l)"),
                    initial=0.0, op0=ALU.mult, op1=ALU.add))
                dvtts[t] = p.inc("dv")
                p.wait("dv", "dv", dvtts[t])
                p.dv(lambda e, t=t, dBx=dBx: e.tensor_copy(hcarry[:, t, :], dBx[:, :, HC - 1]))
                p.chain("dv")
                p.dv(lambda e, dBx=dBx: e.tensor_tensor(
                    out=dBx[:], in0=dBx[:], in1=Cb[:], op=ALU.mult))
                p.chain("dv")
                p.dv(lambda e, dBx=dBx: e.tensor_add(
                    out=yah[:], in0=dBx[:, 0:8, :], in1=dBx[:, 8:16, :]))
                p.chain("dv")
                p.dv(lambda e: e.tensor_add(out=tt1[:], in0=yah[:, 0:4, :],
                                            in1=yah[:, 4:8, :]))
                p.chain("dv")
                p.dv(lambda e: e.tensor_add(out=tt2[:], in0=tt1[:, 0:2, :],
                                            in1=tt1[:, 2:4, :]))
                p.chain("dv")
                p.dv(lambda e: e.tensor_add(out=yt[:, :], in0=tt2[:, 0, :],
                                            in1=tt2[:, 1, :]))
                p.chain("dv")
                p.dv(lambda e, t=t: e.scalar_tensor_tensor(
                    out=yt[:, :], in0=xconv[:, t, :], scalar=Dp[:, t:t + 1],
                    in1=yt[:, :], op0=ALU.mult, op1=ALU.add))
                p.chain("dv")
                if hh == 1 and t == 0:
                    p.wait("dv", "pe", injpe[7])
                p.dv(lambda e, t=t: e.tensor_mul(out=yg[:, t, :], in0=yt[:, :],
                                                 in1=zsil[:, t, :]))
                dvfin[t] = p.inc("dv")
            p.bar()

            if hh == 0:
                continue
            for w in range(8):
                p.dv(lambda e, w=w: e.tensor_add(
                    out=px[:, w, 0:HC], in0=px[:, w, 0:HC], in1=inj_slot(w)))
                p.chain("dv")
            p.bar()

            oppe = {}
            opdv = {-2: 0, -1: 0}
            lnpe = {}
            def op_load(w):
                p.wait("sy", "pe", oppe.get(w - 2, 0))
                p.wait("gq", "pe", oppe.get(w - 2, 0))
                opld2[w] = wload(op_slab(w), "w_outT", DI, 128 * w, 128, par=w % 2)
            ls1 = atno[:, 0:2, :]
            ls2 = atno[:, 2:4, :]
            def ln2_stats(j):
                p.wait("ac", "dv", opdv[j])
                p.wait("ac", "pe", lnpe.get(j - 2, 0))
                p.ac(lambda e, j=j: e.copy(out=ls1[:, j % 2, :],
                                           in_=px[:, j, :]))
                p.chain("ac")
                p.ac(lambda e, j=j: e.activation(out=ls2[:, j % 2, :],
                                                 in_=px[:, j, :],
                                                 func=AF.Square))
                lnac = p.inc("ac")
                p.wait("pe", "ac", lnac)
                mm(psA1[0:1, 0:AC], onek[:], ls1[:, j % 2, :],
                   start=(j == 0), stop=(j == 7))
                mm(psA1[0:1, AC:2 * AC], onek[:], ls2[:, j % 2, :],
                   start=(j == 0), stop=(j == 7))
                lnpe[j] = p.inc("pe")
            for w in range(8):
                p.wait("pe", opld2[w][0], opld2[w][1])
                p.wait("pe", "dv", opdv[w - 2])
                slab = op_slab(w)
                for kt in range(16):
                    mm(BANKS[w % 2][:, 0:HC], slab[:, kt, 0:128],
                       yg[:, kt, :], start=(kt == 0), stop=(kt == 15))
                oppe[w] = p.inc("pe")
                if w + 2 < 8:
                    op_load(w + 2)
                p.wait("dv", "pe", oppe[w])
                p.dv(lambda e, w=w, h0=h0, b=BANKS[w % 2]: e.tensor_add(
                    out=px[:, w, h0:h0 + HC], in0=px[:, w, h0:h0 + HC],
                    in1=b[:, 0:HC]))
                opdv[w] = p.inc("dv")
                if STAGE != 1 and w >= 1:
                    ln2_stats(w - 1)
            if STAGE != 1:
                ln2_stats(7)
            p.bar()

        if STAGE == 1:
            for t in range(NPT):
                p.sy(lambda e, t=t, c0=c0: e.dma_start(out=out_r[:, t, c0:c0 + AC],
                                                       in_=px[:, t, :]))
            p.bar()
            continue

        def wq_slab(w):
            return wbuf[:, w % 2, 0:NPT * 256].rearrange("p (k c) -> p k c", k=NPT)
        p.wait("sy", "pe", p.cnt["pe"])
        p.wait("gq", "pe", p.cnt["pe"])
        qld_h = {0: wload(wq_slab(0), "wqkv0", DM, 0, 256, par=0),
                 1: wload(wq_slab(1), "wqkv0", DM, 256, 256, par=1)}

        ln_post(AC, psA1)
        dvn = 0
        for t in range(NPT):
            p.dv(lambda e, t=t: e.tensor_sub(out=tf[:, 0, :], in0=px[:, t, :],
                                             in1=psA1[:, 0:AC]))
            p.chain("dv")
            p.dv(lambda e, t=t: e.tensor_mul(out=pxi[:, t, 1:1 + AC],
                                             in0=tf[:, 0, :], in1=psA1[:, AC:2 * AC]))
            dvn = p.inc("dv")
            p.wait("dv", "dv", dvn)
            p.wait("ac", "dv", dvn)
            p.ac(lambda e, t=t: e.copy(out=pxib[:, t, 1:1 + AC],
                                       in_=pxi[:, t, 1:1 + AC]))
        p.bar()
        for i, nm in enumerate(("rcq", "rsq", "rck", "rsk")):
            p.sy(lambda e, i=i, nm=nm, c0=c0: e.dma_start(
                out=rott[:, i, :], in_=D[nm][:, c0:c0 + AC]))
        p.bar()

        nlayers = 1 if STAGE in (2, 3) else 2
        for l in range(nlayers):
            ring = c % 2
            prev_ring = (c + 1) % 2
            kring = kringT[l]
            vtring = vtringT[l]

            st_ = 2 * l
            p.dv(lambda e, st_=st_: e.tensor_copy(pxib[:, :, 0], scarry[:, st_, :]))
            p.chain("dv")
            p.dv(lambda e, st_=st_: e.tensor_copy(scarry[:, st_, :], pxib[:, :, AC]))
            p.bar()

            def qkv_rhs(kt):
                return pxib[:, kt, 1:1 + AC] if kt < NPT // 2 else pxib[:, kt, 0:AC]

            def l2norm_rotary(isq, l=l, kring=kring, ring=ring):
                qac = {}
                qpe = {}
                for t in range(4):
                    if t >= 2:
                        p.wait("ac", "pe", qpe[t - 2])
                    p.ac(lambda e, t=t: e.activation(out=sqb[:, t % 2, :],
                                                     in_=qraw[:, t, :], func=AF.Square))
                    qac[t] = p.inc("ac")
                    p.wait("pe", "ac", qac[t])
                    mm(pb0[0:8, 0:AC], bdones[:, t, :], sqb[:, t % 2, :],
                       start=(t == 0), stop=(t == 3))
                    qpe[t] = p.inc("pe")
                p.wait("ac", "pe", qpe[3])
                p.ac(lambda e: e.activation(out=nrm[:, 0, :], in_=pb0[0:8, 0:AC],
                                            func=AF.Ln))
                p.chain("ac")
                p.ac(lambda e: e.activation(out=nrm[:, 1, :], in_=nrm[:, 0, :],
                                            func=AF.Exp, scale=cvals[0:8, 1:2]))
                nac = p.inc("ac")
                p.wait("dv", "ac", nac)
                p.dv(lambda e: e.tensor_scalar_min(out=nrm[:, 1, :], in0=nrm[:, 1, :],
                                                   scalar1=1e12))
                ndv = p.inc("dv")
                p.wait("ac", "dv", ndv)
                p.ac(lambda e: e.copy(out=nrmb[:, :], in_=nrm[:, 1, :]))
                nac = p.inc("ac")
                sc = qs8 if isq else ks1
                ct, stt_ = (0, 1) if isq else (2, 3)
                bcd = {}
                rpd = {}
                sdv = {}
                rdv = {}
                p.wait("pe", "ac", nac)
                for t in range(4):
                    bk = (pb1, pb2)[t % 2]
                    if t >= 2:
                        p.wait("pe", "dv", rdv[t - 2])
                    mm(bk[:, 0:AC], sel8[:, 128 * t:128 * (t + 1)], nrmb[:, :],
                       start=True)
                    bcd[t] = p.inc("pe")
                    p.wait("dv", "pe", bcd[t])
                    p.dv(lambda e, t=t, sc=sc, b=bk, l=l: e.scalar_tensor_tensor(
                        out=q2b[:, t, :], in0=qraw[:, t, :], scalar=sc[:, l:l + 1],
                        in1=b[:, 0:AC], op0=ALU.mult, op1=ALU.mult))
                    sdv[t] = p.inc("dv")
                    p.wait("pe", "dv", sdv[t])
                    mm(bk[:, 0:AC], rotP[:], q2b[:, t, :], start=True)
                    rpd[t] = p.inc("pe")
                    p.dv(lambda e, t=t, ct=ct: e.tensor_mul(
                        out=tf[:, 0, :], in0=q2b[:, t, :], in1=rott[:, ct, :]))
                    p.chain("dv")
                    p.wait("dv", "pe", rpd[t])
                    p.dv(lambda e, t=t, stt_=stt_, b=bk: e.tensor_mul(
                        out=tf[:, 1, :], in0=b[:, 0:AC], in1=rott[:, stt_, :]))
                    rdv[t] = p.inc("dv")
                    p.wait("dv", "dv", rdv[t])
                    if isq:
                        p.dv(lambda e, t=t: e.tensor_add(
                            out=q3b[:, t, :], in0=tf[:, 0, :], in1=tf[:, 1, :]))
                    else:
                        p.dv(lambda e, t=t, ring=ring, kring=kring: e.tensor_add(
                            out=kring[:, t, ring * AC:ring * AC + AC],
                            in0=tf[:, 0, :], in1=tf[:, 1, :]))
                    p.chain("dv")
                p.bar()

            qld = {}
            qpe_ = {}
            qac_ = {-2: 0, -1: 0}
            def q_load(w):
                p.wait("sy", "pe", qpe_.get(w - 2, 0))
                p.wait("gq", "pe", qpe_.get(w - 2, 0))
                qld[w] = wload(wq_slab(w), f"wqkv{l}", DM, 256 * w, 256, par=w % 2)
            if l == 0:
                qld[0], qld[1] = qld_h[0], qld_h[1]
            else:
                q_load(0)
                q_load(1)
            for w in range(6):
                p.wait("pe", qld[w][0], qld[w][1])
                p.wait("pe", "ac", qac_.get(w - 3, 0))
                slab = wq_slab(w)
                for kt in range(NPT):
                    for oi in range(2):
                        mm(BANKS[2 * (w % 3) + oi],
                           slab[:, kt, 128 * oi:128 * (oi + 1)],
                           qkv_rhs(kt), start=(kt == 0), stop=(kt == NPT - 1))
                qpe_[w] = p.inc("pe")
                if w + 2 < 6:
                    q_load(w + 2)
                p.wait("ac", "pe", qpe_[w])
                for oi in range(2):
                    ot = 2 * w + oi
                    bk = BANKS[2 * (w % 3) + oi]
                    if ot < 8:
                        p.ac(lambda e, ot=ot, b=bk: e.copy(
                            out=qraw[:, ot % 4, :], in_=b[:, 0:AC]))
                    else:
                        p.ac(lambda e, ot=ot, b=bk: e.copy(
                            out=vbf[:, ot % 4, :], in_=b[:, 0:AC]))
                qac_[w] = p.inc("ac")
                if w == 1:
                    p.bar()
                    l2norm_rotary(True)
                elif w == 3:
                    p.bar()
                    l2norm_rotary(False)
            p.bar()

            for hd in range(HEADS):
                pt_, off = hd // 2, (hd % 2) * 64
                for jt in range(4):
                    p.pe(lambda e, pt_=pt_, off=off, jt=jt: e.transpose(
                        out=pbt[:, 64 * jt:64 * jt + 64],
                        in_=vbf[off:off + 64, pt_, 128 * jt:128 * (jt + 1)],
                        identity=identD[off:off + 64, :]))
                p.bar()
                p.ac(lambda e, hd=hd, ring=ring, vtring=vtring: e.copy(
                    out=vtring[:, hd, 4 * ring:4 * ring + 4, :].rearrange(
                        "p a b -> p (a b)"),
                    in_=pbt[:, 0:256]))
                p.bar()

            items = [(hd, ib) for hd in range(HEADS) for ib in range(4)]
            SIMR = (psA0, psA1)
            pesim = {}; dvmax = {}; acexp = {}; dvren = {}
            petr = {}; acev = {}; peav = {}; acat = {-1: 0}
            jts_of = lambda ib: (list(range(ib, ib + 5)) if c > 0
                                 else list(range(4, 5 + ib)))

            def emit_sim(i):
                hd, ib = items[i]
                pt_, off = hd // 2, (hd % 2) * 64
                sim = SIMR[i % 2]
                jts = jts_of(ib)
                if i >= 2:
                    p.wait("pe", "ac", acexp[i - 2])
                qblk = q3b[off:off + 64, pt_, 128 * ib:128 * (ib + 1)]
                for k, jt in enumerate(jts):
                    r = ring if jt >= 4 else prev_ring
                    colbase = r * AC + 128 * (jt % 4)
                    mm(sim[:, 128 * k:128 * (k + 1)], qblk,
                       kring[off:off + 64, pt_, colbase:colbase + 128], start=True)
                pesim[i] = p.inc("pe")

            emit_sim(0)
            emit_sim(1)
            nI = len(items)
            for i in range(nI + 1):
                if i < nI:
                    hd, ib = items[i]
                    pt_, off = hd // 2, (hd % 2) * 64
                    sim = SIMR[i % 2]
                    attn = attnP[i % 2]
                    sA = sAB[:, 4 * (i % 2):4 * (i % 2) + 4]
                    nj = len(jts_of(ib))
                    p.wait("dv", "pe", pesim[i])
                    if i >= 2:
                        p.wait("dv", "ac", acexp[i - 2])
                    if c > 0:
                        p.dv(lambda e, sim=sim: e.tensor_tensor(
                            out=sim[:, 0:128], in0=sim[:, 0:128], in1=m0[:],
                            op=ALU.add))
                        p.chain("dv")
                    p.dv(lambda e, sim=sim, nj=nj: e.tensor_tensor(
                        out=sim[:, 128 * (nj - 1):128 * nj],
                        in0=sim[:, 128 * (nj - 1):128 * nj], in1=m4[:], op=ALU.add))
                    p.chain("dv")
                    p.dv(lambda e, sim=sim, nj=nj, sA=sA: e.reduce_max(
                        out=sA[:, 0:1], in_=sim[:, 0:128 * nj], axis=AX.X,
                        negate=True))
                    dvmax[i] = p.inc("dv")
                if i >= 1:
                    j = i - 1
                    hdj, ibj = items[j]
                    ptj, offj = hdj // 2, (hdj % 2) * 64
                    attnj = attnP[j % 2]
                    attnTj = attnTP[j % 2]
                    sAj = sAB[:, 4 * (j % 2):4 * (j % 2) + 4]
                    njj = len(jts_of(ibj))
                    jtsj = jts_of(ibj)
                    p.wait("dv", "ac", acexp[j])
                    p.dv(lambda e, sA=sAj: e.reciprocal(out=sA[:, 2:3], in_=sA[:, 1:2]))
                    p.chain("dv")
                    p.dv(lambda e, nj=njj, attn=attnj, sA=sAj: e.tensor_scalar_mul(
                        out=attn[:, 0:128 * nj], in0=attn[:, 0:128 * nj],
                        scalar1=sA[:, 2:3]))
                    dvren[j] = p.inc("dv")
                    p.wait("pe", "dv", dvren[j])
                    if j >= 1:
                        p.wait("pe", "ac", acev[j - 1])
                    for k in range(njj):
                        p.pe(lambda e, k=k, attn=attnj: e.transpose(
                            out=pbt[:, 128 * k:128 * (k + 1)],
                            in_=attn[:, 128 * k:128 * (k + 1)], identity=ident[:]))
                    petr[j] = p.inc("pe")
                    p.wait("ac", "pe", petr[j])
                    if j >= 2:
                        p.wait("ac", "pe", peav[j - 2])
                    p.ac(lambda e, nj=njj, attnT=attnTj: e.copy(
                        out=attnT[:, 0:128 * nj], in_=pbt[:, 0:128 * nj]))
                    acev[j] = p.inc("ac")
                    p.wait("pe", "ac", acev[j])
                    p.wait("pe", "ac", acat[j - 1])
                    for k, jt in enumerate(jtsj):
                        slot = (ring if jt >= 4 else prev_ring) * 4 + (jt % 4)
                        mm(pb2[offj:offj + 64, 0:128], vtring[:, hdj, slot, :],
                           attnTj[:, 128 * k:128 * (k + 1)], start=(k == 0),
                           stop=(k == njj - 1), tile_position=(0, offj))
                    peav[j] = p.inc("pe")
                    p.wait("ac", "pe", peav[j])
                    p.ac(lambda e, off=offj, pt_=ptj, ib=ibj: e.copy(
                        out=atno[off:off + 64, pt_, 128 * ib:128 * (ib + 1)],
                        in_=pb2[off:off + 64, 0:128]))
                    acat[j] = p.inc("ac")
                if i < nI:
                    p.wait("ac", "dv", dvmax[i])
                    if i >= 2:
                        p.wait("ac", "pe", petr[i - 2])
                    p.ac(lambda e, sim=sim, nj=nj, attn=attn, sA=sA: e.activation(
                        out=attn[:, 0:128 * nj], in_=sim[:, 0:128 * nj], func=AF.Exp,
                        bias=sA[:, 0:1], accum_out=sA[:, 1:2]))
                    acexp[i] = p.inc("ac")
                    if i + 2 < nI:
                        emit_sim(i + 2)
            p.bar()

            def wo_slab(w):
                return wbuf[:, w % 2, 0:4 * 256].rearrange("p (k c) -> p k c", k=4)

            wold = {}
            wope = {}
            wodv = {-2: 0, -1: 0}
            def wo_load(w):
                p.wait("sy", "pe", wope.get(w - 2, 0))
                p.wait("gq", "pe", wope.get(w - 2, 0))
                wold[w] = wload(wo_slab(w), f"wo{l}", INNER, 256 * w, 256, par=w % 2)
            wo_load(0)
            wo_load(1)
            for w in range(4):
                p.wait("pe", wold[w][0], wold[w][1])
                p.wait("pe", "dv", wodv[w - 2])
                slab = wo_slab(w)
                for kt in range(4):
                    for oi in range(2):
                        mm(BANKS[2 * (w % 2) + oi],
                           slab[:, kt, 128 * oi:128 * (oi + 1)],
                           atno[:, kt, :], start=(kt == 0), stop=(kt == 3))
                wope[w] = p.inc("pe")
                if w + 2 < 4:
                    wo_load(w + 2)
                p.wait("dv", "pe", wope[w])
                p.dv(lambda e, b=BANKS[2 * (w % 2)], ot=2 * w: e.tensor_add(
                    out=pxi[:, ot, 1:1 + AC], in0=pxi[:, ot, 1:1 + AC],
                    in1=b[:, 0:AC]))
                p.chain("dv")
                p.dv(lambda e, b=BANKS[2 * (w % 2) + 1], ot=2 * w + 1: e.tensor_add(
                    out=pxi[:, ot, 1:1 + AC], in0=pxi[:, ot, 1:1 + AC],
                    in1=b[:, 0:AC]))
                wodv[w] = p.inc("dv")
            p.bar()
            for ot in range(NPT):
                p.ac(lambda e, ot=ot: e.copy(out=pxib[:, ot, 1:1 + AC],
                                             in_=pxi[:, ot, 1:1 + AC]))
            p.bar()
            if STAGE == 2:
                break

            def w1_slab(w):
                return wbuf[:, w % 2, 0:NPT * 256].rearrange("p (k c) -> p k c", k=NPT)
            p.wait("sy", "pe", p.cnt["pe"])
            p.wait("gq", "pe", p.cnt["pe"])
            w1ld = {0: wload(w1_slab(0), f"w1p{l}", DM, 0, 256, par=0),
                    1: wload(w1_slab(1), f"w1p{l}", DM, 256, 256, par=1)}

            st_ = 2 * l + 1
            p.dv(lambda e, st_=st_: e.tensor_copy(pxib[:, :, 0], scarry[:, st_, :]))
            p.chain("dv")
            p.dv(lambda e, st_=st_: e.tensor_copy(scarry[:, st_, :], pxib[:, :, AC]))
            p.bar()

            def sh_ap(t):
                return pxib[:, t, 1:1 + AC] if t < NPT // 2 else pxib[:, t, 0:AC]

            ln_stats(sh_ap, AC, psA0)
            for t in range(NPT):
                p.dv(lambda e, t=t: e.tensor_sub(out=tf[:, 0, :], in0=sh_ap(t),
                                                 in1=psA0[:, 0:AC]))
                p.chain("dv")
                p.dv(lambda e, t=t: e.tensor_mul(out=tf[:, 1, :], in0=tf[:, 0, :],
                                                 in1=psA0[:, AC:2 * AC]))
                p.chain("dv")
                p.dv(lambda e, t=t, l=l: e.tensor_scalar(
                    out=ln2o[:, t, :], in0=tf[:, 1, :], scalar1=lng[:, l, t:t + 1],
                    scalar2=lnb[:, l, t:t + 1], op0=ALU.mult, op1=ALU.add))
                p.chain("dv")
            p.bar()

            w1pe = {}
            w1ac = {}
            w1dv = {-3: 0, -2: 0, -1: 0}
            def w1_load(w):
                p.wait("sy", "pe", w1pe.get(w - 2, 0))
                p.wait("gq", "pe", w1pe.get(w - 2, 0))
                w1ld[w] = wload(w1_slab(w), f"w1p{l}", DM, 256 * w, 256, par=w % 2)
            for w in range(NHT):
                p.wait("pe", w1ld[w][0], w1ld[w][1])
                p.wait("pe", "dv", w1dv.get(w - 3, 0))
                slab = w1_slab(w)
                for kt in range(NPT):
                    for oi in range(2):
                        mm(BANKS[2 * (w % 3) + oi],
                           slab[:, kt, 128 * oi:128 * (oi + 1)],
                           ln2o[:, kt, :], start=(kt == 0), stop=(kt == NPT - 1))
                w1pe[w] = p.inc("pe")
                if w + 2 < NHT:
                    w1_load(w + 2)
                p.wait("ac", "pe", w1pe[w])
                p.wait("ac", "dv", w1dv.get(w - 2, 0))
                gb_ = BANKS[2 * (w % 3) + 1]
                ab_ = BANKS[2 * (w % 3)]
                p.ac(lambda e, g_=gb_, w=w, l=l: e.activation(
                    out=gel[:, w % 2, :], in_=g_[:, 0:AC], func=AF.Gelu,
                    bias=b1g[:, l, w:w + 1]))
                w1ac[w] = p.inc("ac")
                p.wait("dv", "ac", w1ac[w])
                p.dv(lambda e, a_=ab_, w=w, l=l: e.scalar_tensor_tensor(
                    out=gated[:, w, :], in0=a_[:, 0:AC], scalar=b1a[:, l, w:w + 1],
                    in1=gel[:, w % 2, :], op0=ALU.add, op1=ALU.mult))
                w1dv[w] = p.inc("dv")
            p.bar()

            def w2_slab(i):
                return wbuf[:, i % 2, 0:11 * 128].rearrange("p (k c) -> p k c", k=11)

            seq2 = [(w, gpr) for w in range(8) for gpr in range(2)]
            w2ld = {}
            w2pe = {}
            w2dv = {-2: 0, -1: 0, 0: 0, 1: 0}
            def w2_load(i):
                w_, g_ = seq2[i]
                p.wait("sy", "pe", w2pe.get(i - 2, 0))
                p.wait("gq", "pe", w2pe.get(i - 2, 0))
                w2ld[i] = wload(w2_slab(i), f"w2p{l}", HIPD_ if False else HIDP,
                                128 * w_, 128, rt0=11 * g_, rtn=11, par=i % 2)
            w2_load(0)
            w2_load(1)
            W2B = (pb0, pb1, pb2)
            for i in range(16):
                w, gpr = seq2[i]
                p.wait("pe", w2ld[i][0], w2ld[i][1])
                p.wait("pe", "dv", w2dv.get(2 * (w - 3) + 1, 0))
                slab = w2_slab(i)
                for kk in range(11):
                    kt = 11 * gpr + kk
                    mm(W2B[w % 3][:], slab[:, kk, 0:128], gated[:, kt, :],
                       start=(kt == 0), stop=(kt == NHT - 1))
                w2pe[i] = p.inc("pe")
                if i + 2 < 16:
                    w2_load(i + 2)
                if gpr == 1:
                    p.wait("dv", "pe", w2pe[i])
                    p.dv(lambda e, w=w, l=l, b=W2B[w % 3]:
                         e.scalar_tensor_tensor(
                             out=pxi[:, w, 1:1 + AC], in0=b[:, 0:AC],
                             scalar=b2s[:, l, w:w + 1], in1=pxi[:, w, 1:1 + AC],
                             op0=ALU.add, op1=ALU.add))
                    w2dv[i] = p.inc("dv")
                else:
                    w2dv[i] = w2dv[i - 1]
            p.bar()
            if l == 0 and nlayers > 1:
                for w in range(NPT):
                    p.ac(lambda e, w=w: e.copy(out=pxib[:, w, 1:1 + AC],
                                               in_=pxi[:, w, 1:1 + AC]))
                p.bar()

        for t in range(NPT):
            p.dv(lambda e, t=t: e.tensor_add(out=px[:, t, :], in0=px[:, t, :],
                                             in1=pxi[:, t, 1:1 + AC]))
        p.bar()
        for t in range(NPT):
            p.sy(lambda e, t=t, c0=c0: e.dma_start(out=out_r[:, t, c0:c0 + AC],
                                                   in_=px[:, t, :]))
        p.bar()

    p.wait("sy", "sy", p.cnt["sy"])

    with nc.Block() as block:
        p.materialize(block)
    nc._keepalive_ctx = ctx
    return nc


def kernel(_ncores=NCORE, **inputs):
    g = _host_prep(inputs)
    x = inputs["L0_attn_x"]
    nc = build()
    in_maps = []
    for cix in range(_ncores):
        m = {"xT": np.ascontiguousarray(np.asarray(x[cix]).T)}
        m.update(g)
        in_maps.append(m)
    res = run_bass_kernel_spmd(nc, in_maps, core_ids=list(range(_ncores)))
    outs = [np.asarray(r["out"]).T for r in res.results]
    return np.ascontiguousarray(np.stack(outs, 0))


# version 12
# speedup vs baseline: 1.0503x; 1.0001x over previous
import numpy as np
import ml_dtypes
from contextlib import ExitStack

import concourse.bass as bass
import concourse.mybir as mybir
from concourse.bass_utils import run_bass_kernel_spmd

F32 = mybir.dt.float32
BF16 = mybir.dt.bfloat16
AF = mybir.ActivationFunctionType
ALU = mybir.AluOpType
AX = mybir.AxisListType

BF = ml_dtypes.bfloat16

NCORE = 8
LSEQ = 4096
DM = 1024
DI = 2048
NST = 16
DTR = 64
NPT = 8
AC = 512
HC = 256
NCH = LSEQ // AC
HEADS = 8
DH = 64
INNER = 512
HID = 2730
HIDP = 2816
NHT = 22
NEG = -1e30

NCHRUN = NCH
STAGE = 4


def _pp(v):
    return np.ascontiguousarray(np.asarray(v, np.float32).reshape(-1, 128).T)


def _pp3(a):
    a = np.asarray(a, np.float32)
    F, K = a.shape
    return np.ascontiguousarray(a.reshape(F // 128, 128, K).transpose(1, 0, 2))


def _host_prep(inp):
    g = {}
    g["w_inT"] = np.ascontiguousarray(inp["L0_attn_in_proj_w"].T)
    g["w_xpT"] = np.ascontiguousarray(inp["L0_attn_x_proj_w"].T)
    g["w_dtT"] = np.ascontiguousarray(inp["L0_attn_dt_proj_w"].T)
    g["w_outT"] = np.ascontiguousarray(inp["L0_attn_out_proj_w"].T)
    g["convw"] = _pp3(inp["L0_attn_conv_w"][:, 0, :])
    cw = _pp3(inp["L0_attn_conv_w"][:, 0, :])
    cd = np.zeros((16, 4, 128, 128), np.float32)
    ii128 = np.arange(128)
    for t in range(16):
        for k in range(4):
            cd[t, k, ii128, ii128] = cw[:, t, k]
    g["convdiag"] = np.ascontiguousarray(cd.reshape(16 * 4 * 128, 128))
    g["convb"] = _pp(inp["L0_attn_conv_b"])
    g["dtb"] = _pp(inp["L0_attn_dt_proj_b"])
    g["Alog"] = _pp3(inp["L0_attn_A_log"])
    g["Dp"] = _pp(inp["L0_attn_D_param"])
    for l in (0, 1):
        g[f"wqkv{l}"] = np.ascontiguousarray(inp["L0_ffn_wqkv"][l])
        g[f"wo{l}"] = np.ascontiguousarray(inp["L0_ffn_wo"][l])
        w1 = inp["L0_ffn_w1"][l]
        a = np.zeros((DM, HIDP), np.float32); a[:, :HID] = w1[:, :HID]
        gg = np.zeros((DM, HIDP), np.float32); gg[:, :HID] = w1[:, HID:]
        at = a.reshape(DM, NHT, 128); gt = gg.reshape(DM, NHT, 128)
        g[f"w1p{l}"] = np.ascontiguousarray(
            np.stack([at, gt], axis=2).reshape(DM, 2 * NHT * 128))
        w2 = np.zeros((HIDP, DM), np.float32); w2[:HID] = inp["L0_ffn_w2"][l]
        g[f"w2p{l}"] = w2
        b1 = inp["L0_ffn_b1"][l]
        b1a = np.zeros(HIDP, np.float32); b1a[:HID] = b1[:HID]
        b1g = np.zeros(HIDP, np.float32); b1g[:HID] = b1[HID:]
        g[f"b1a{l}"] = _pp(b1a); g[f"b1g{l}"] = _pp(b1g)
    g["qs"] = np.ascontiguousarray(np.stack(
        [np.tile(np.asarray(inp["L0_ffn_q_scale"][l], np.float32), 2) for l in (0, 1)], 1))
    g["ks"] = np.ascontiguousarray(np.stack(
        [np.tile(np.asarray(inp["L0_ffn_k_scale"][l], np.float32), 2) for l in (0, 1)], 1))
    g["lng"] = np.ascontiguousarray(np.stack([_pp(inp["L0_ffn_ln_g"][l]) for l in (0, 1)], 1))
    g["lnb"] = np.ascontiguousarray(np.stack([_pp(inp["L0_ffn_ln_b"][l]) for l in (0, 1)], 1))
    g["b2"] = np.ascontiguousarray(np.stack([_pp(inp["L0_ffn_b2"][l]) for l in (0, 1)], 1))

    ii = np.arange(128)
    g["m0"] = np.where(ii[None, :] < ii[:, None], NEG, 0.0).astype(BF)
    g["m4"] = np.where(ii[None, :] > ii[:, None], NEG, 0.0).astype(BF)
    g["ident"] = np.eye(128, dtype=BF)
    identD = np.zeros((128, 64), np.float32)
    for p_ in range(128):
        identD[p_, p_ % 64] = 1.0
    g["identD"] = identD.astype(BF)
    g["onescol"] = np.ones((1, 128), dtype=BF)
    g["onek"] = np.ones((128, 1), dtype=BF)
    g["cvals"] = np.tile(np.array([[1e-5, -0.5, 1.0 / DM, 8.0, -1.0, 1.0]], np.float32), (128, 1))
    bd = np.zeros((128, 4, 8), np.float32)
    for t in range(4):
        bd[:64, t, 2 * t] = 1.0; bd[64:, t, 2 * t + 1] = 1.0
    g["bdones"] = bd.astype(BF)
    sel8 = np.zeros((8, 4, 128), np.float32)
    for t in range(4):
        for p_ in range(128):
            sel8[2 * t + (p_ >= 64), t, p_] = 1.0
    g["sel8"] = np.ascontiguousarray(sel8.reshape(8, 512)).astype(BF)
    sel16 = np.zeros((16, 16, 128), np.float32)
    for n in range(16):
        sel16[n, n, :] = 1.0
    g["sel16"] = np.ascontiguousarray(sel16.reshape(16, 2048)).astype(BF)
    P = np.zeros((128, 128), np.float32)
    for di in range(128):
        h0 = (di // 64) * 64; d = di % 64
        if d < 32:
            P[h0 + d + 32, di] = -1.0
        else:
            P[h0 + d - 32, di] = 1.0
    g["rotP"] = P.astype(BF)

    dih = np.arange(64); j = dih % 32
    invfreq = 1.0 / 10000.0 ** ((2 * j) / DH)
    t = np.arange(LSEQ, dtype=np.float64)
    freqs = t[None, :] * invfreq[:, None]
    base = ((2 * j) + 0.4 * DH) / (1.4 * DH)
    power = (t - LSEQ // 2) / float(AC // 2)
    scale = base[:, None] ** power[None, :]
    cos = np.cos(freqs); sin = np.sin(freqs)
    rows2 = lambda m: np.ascontiguousarray(np.concatenate([m, m], 0).astype(BF))
    g["rcq"] = rows2(cos * scale); g["rsq"] = rows2(sin * scale)
    g["rck"] = rows2(cos / scale); g["rsk"] = rows2(sin / scale)
    return g


ENGINES = ("sy", "gq", "pe", "ac", "dv")
SEMS = ("sy", "gq", "pe", "ac", "dv", "gc", "wl0", "wl1", "wg0", "wg1", "wb0", "wb1")


class Prog:
    def __init__(self, nc):
        self.nc = nc
        self.prog = {e: [] for e in ENGINES}
        self.iswait = {e: [] for e in ENGINES}
        self.cnt = {e: 0 for e in SEMS}
        self.dirty = {e: False for e in SEMS}
        self.snap = dict(self.cnt)
        self.lastwait = {e: {q: 0 for q in SEMS} for e in ENGINES}
        self.pending = {e: False for e in ENGINES}
        self.sems = None

    def _emit(self, e, fn, compute=False):
        if self.pending[e]:
            self.pending[e] = False
            for q in SEMS:
                self.wait(e, q, self.snap[q])
        self.prog[e].append([fn, []])
        self.iswait[e].append(False)
        if e == "sy" or (e == "gq" and not compute):
            self.prog[e][-1][1] = [(e, 16)]
            self.cnt[e] += 16
        elif e == "gq":
            self.dirty["gc"] = True
        else:
            self.dirty[e] = True

    def sy(self, fn): self._emit("sy", fn)
    def gq(self, fn): self._emit("gq", fn)
    def gc(self, fn): self._emit("gq", fn, compute=True)

    def dma(self, queue, fn, sem):
        if self.pending[queue]:
            self.pending[queue] = False
            for q in SEMS:
                self.wait(queue, q, self.snap[q])
        self.prog[queue].append([fn, [(sem, 16)]])
        self.iswait[queue].append(False)
        self.cnt[sem] += 16
    def pe(self, fn): self._emit("pe", fn)
    def ac(self, fn): self._emit("ac", fn)
    def dv(self, fn): self._emit("dv", fn)

    def inc(self, e):
        stream = "gq" if e == "gc" else e
        assert self.prog[stream] and not self.iswait[stream][-1], e
        self.prog[stream][-1][1].append((e, 1))
        self.cnt[e] += 1
        self.dirty[e] = False
        return self.cnt[e]

    def wait(self, e, q, val):
        if val <= 0 or self.lastwait[e][q] >= val:
            return
        self.lastwait[e][q] = val
        self.prog[e].append(
            [lambda eng, q=q, val=val: eng.wait_ge(self.sems[q], val), []])
        self.iswait[e].append(True)

    def chain(self, e):
        self.inc(e)
        st = "gq" if e == "gc" else e
        self.wait(st, e, self.cnt[e])

    def bar(self):
        for e in ("pe", "ac", "dv", "gc"):
            if self.dirty[e]:
                self.inc(e)
        self.snap = dict(self.cnt)
        for e in ENGINES:
            self.pending[e] = True

    def materialize(self, block):
        sems = self.sems

        def runner(progl):
            def body(eng):
                for fn, incs in progl:
                    inst = fn(eng)
                    for s, n in incs:
                        inst.then_inc(sems[s], n)
            return body

        block.sync(runner(self.prog["sy"]))
        block.gpsimd(runner(self.prog["gq"]))
        block.tensor(runner(self.prog["pe"]))
        block.scalar(runner(self.prog["ac"]))
        block.vector(runner(self.prog["dv"]))


def build(debug=False):
    nc = bass.Bass("TRN2", target_bir_lowering=False)
    ctx = ExitStack()

    D = {}
    def din(name, shape, dt=F32):
        D[name] = nc.dram_tensor(name, list(shape), dt, kind="ExternalInput").ap()
    din("xT", (DM, LSEQ))
    din("w_inT", (DM, 2 * DI)); din("w_xpT", (DI, 96)); din("w_dtT", (DTR, DI))
    din("w_outT", (DI, DM))
    din("convw", (128, 16, 4)); din("convdiag", (16 * 4 * 128, 128)); din("convb", (128, 16)); din("dtb", (128, 16))
    din("Alog", (128, 16, 16)); din("Dp", (128, 16))
    for l in (0, 1):
        din(f"wqkv{l}", (DM, 3 * INNER)); din(f"wo{l}", (INNER, DM))
        din(f"w1p{l}", (DM, 2 * HIDP)); din(f"w2p{l}", (HIDP, DM))
        din(f"b1a{l}", (128, NHT)); din(f"b1g{l}", (128, NHT))
    din("qs", (128, 2)); din("ks", (128, 2))
    din("lng", (128, 2, 8)); din("lnb", (128, 2, 8)); din("b2", (128, 2, 8))
    din("m0", (128, 128), BF16); din("m4", (128, 128), BF16)
    din("ident", (128, 128), BF16); din("identD", (128, 64), BF16)
    din("onescol", (1, 128), BF16); din("onek", (128, 1), BF16)
    din("cvals", (128, 6))
    din("sel8", (8, 512), BF16); din("sel16", (16, 2048), BF16)
    din("bdones", (128, 4, 8), BF16)
    din("rotP", (128, 128), BF16)
    for nm in ("rcq", "rsq", "rck", "rsk"):
        din(nm, (128, LSEQ), BF16)
    out = nc.dram_tensor("out", [DM, LSEQ], F32, kind="ExternalOutput").ap()
    out_r = out.rearrange("(t p) l -> p t l", p=128)
    xT_r = D["xT"].rearrange("(t p) l -> p t l", p=128)

    _nm = [0]
    def sb(shape, dt=F32):
        _nm[0] += 1
        return ctx.enter_context(nc.sbuf_tensor(f"t{_nm[0]}", list(shape), dt))

    px = sb((128, NPT, AC))
    hcarry = sb((128, 16, 16))
    ccarry = sb((128, 16, 4))
    scarry = sb((128, 4, NPT), BF16)
    kringT = [sb((128, 4, 2 * AC), BF16) for _ in (0, 1)]
    vtringT = [sb((128, HEADS, 8, DH), BF16) for _ in (0, 1)]
    m0 = sb((128, 128), BF16); m4 = sb((128, 128), BF16)
    ident = sb((128, 128), BF16); identD = sb((128, 64), BF16)
    onescol = sb((1, 128), BF16); onek = sb((128, 1), BF16)
    cvals = sb((128, 6))
    sel8 = sb((8, 512), BF16); sel16 = sb((16, 2048), BF16)
    bdones = sb((128, 4, 8), BF16)
    rotP = sb((128, 128), BF16)
    Aneg = sb((128, 16, 16)); convw = sb((128, 16, 4))
    convb = sb((128, 16)); dtb = sb((128, 16)); Dp = sb((128, 16))
    qs8 = sb((128, 2)); ks1 = sb((128, 2))
    lng = sb((128, 2, NPT)); lnb = sb((128, 2, NPT)); b2s = sb((128, 2, NPT))
    b1a = sb((128, 2, NHT)); b1g = sb((128, 2, NHT))
    dtW = sb((DTR, DI), BF16)
    xpW = sb((128, 16, 96), BF16)
    dlr = sb((DTR, HC), BF16)
    bcrowB = sb((16, HC), BF16)
    bcrowC = sb((16, HC), BF16)
    rows1 = sb((1, 5 * AC))
    rowsb1 = sb((1, 2 * AC), BF16)
    nrm = sb((8, 2, AC))
    nrmb = sb((8, AC), BF16)
    sAB = sb((128, 8))
    wbuf = sb((128, 2, 2048), BF16)
    cbuf = sb((128, 2, 4, 128), BF16)

    A16N = 29184
    A32N = 13056
    ar16 = sb((128, A16N), BF16)
    ar32 = sb((128, A32N))

    def v16(off, shape):
        n = int(np.prod(shape[1:]))
        ap = ar16[:, off:off + n]
        if len(shape) == 3:
            ap = ap.rearrange("p (a b) -> p a b", a=shape[1])
        return ap, off + n

    def v32(off, shape):
        n = int(np.prod(shape[1:]))
        ap = ar32[:, off:off + n]
        if len(shape) == 3:
            ap = ap.rearrange("p (a b) -> p a b", a=shape[1])
        return ap, off + n

    o = 0
    lnout, o = v16(o, (128, NPT, HC))
    xinner, o = v16(o, (128, 16, HC + 4))
    zsil, o = v16(o, (128, 16, HC))
    xconv, o = v16(o, (128, 16, HC))
    Bb, o = v16(o, (128, 16, HC))
    Cb, o = v16(o, (128, 16, HC))
    wt, o = v16(o, (128, HC))
    ya, o = v16(o, (128, 4, HC))
    yb, o = v16(o, (128, 2, HC))
    yg, o = v16(o, (128, 16, HC))
    assert o <= A16N, o
    dBxA = ar16[:, 2048:2048 + 16 * HC].rearrange("p (a b) -> p a b", a=16)
    dBxP = (dBxA, dBxA)
    yah = ar16[:, 0:8 * HC].rearrange("p (a b) -> p a b", a=8)
    tt1 = ya
    tt2 = yb
    o = 0
    pxib, o = v16(o, (128, NPT, AC + 1))
    rott, o = v16(o, (128, 4, AC))
    q2b, o = v16(o, (128, 4, AC))
    q3b, o = v16(o, (128, 4, AC))
    vbf, o = v16(o, (128, 4, AC))
    sqb, o = v16(o, (128, 2, AC))
    atno, o = v16(o, (128, 4, AC))
    gated, o = v16(o, (128, NHT, AC))
    sqb2, o = v16(o, (128, 2, AC))
    assert o <= A16N, o
    _vb0 = vbf[:, 0:4, :].rearrange("p a b -> p (a b)")
    attnP = (_vb0[:, 0:640], _vb0[:, 640:1280])
    _qb0 = q2b[:, 0:4, :].rearrange("p a b -> p (a b)")
    attnTP = (_qb0[:, 0:640], _qb0[:, 640:1280])
    _q2off = NPT * (AC + 1) + 4 * AC
    ln2o = ar16[:, _q2off:_q2off + NPT * AC].rearrange("p (a b) -> p a b", a=NPT)
    gel = sqb

    o = 0
    dAb0, o = v32(o, (128, 16, HC))
    dAb1, o = v32(o, (128, 16, HC))
    dtt, o = v32(o, (128, 2, HC))
    yt, o = v32(o, (128, HC))
    assert o <= A32N, o
    dABUF = (dAb0, dAb1)
    o = 0
    pxi, o = v32(o, (128, NPT, AC + 1))
    qraw, o = v32(o, (128, 4, AC))
    tf, o = v32(o, (128, 2, AC))
    assert o <= A32N, o

    psA0 = ctx.enter_context(nc.psum_tensor([128, 1024], F32))
    psA1 = ctx.enter_context(nc.psum_tensor([128, 1024], F32))
    pb0 = ctx.enter_context(nc.psum_tensor([128, 512], F32))
    pb1 = ctx.enter_context(nc.psum_tensor([128, 512], F32))
    pb2 = ctx.enter_context(nc.psum_tensor([128, 512], F32))
    pbt = ctx.enter_context(nc.psum_tensor([128, 1024], BF16))
    BANKS = [psA0[:, 0:512], psA0[:, 512:1024],
             psA1[:, 0:512], psA1[:, 512:1024], pb0[:], pb1[:]]

    p = Prog(nc)
    p.sems = {e: ctx.enter_context(nc.semaphore(f"s_{e}")) for e in SEMS}

    def mm(bank_ap, lhsT_ap, rhs_ap, start, stop=None, tile_position=None):
        if stop is None:
            stop = start
        p.pe(lambda e, o_=bank_ap, l_=lhsT_ap, r_=rhs_ap, s_=start, t_=stop,
             tp=tile_position:
             e.matmul(out=o_, lhsT=l_, rhs=r_, start=s_, stop=t_, tile_position=tp))

    SCR16 = {}
    STORED = set()

    LASTWB = {}

    def wload(buf_ap, name, rows_, col0, cols, rt0=0, rtn=None, par=0):
        dram2d = D[name]
        if name not in SCR16:
            SCR16[name] = nc.dram_tensor(
                f"scr_{name}", [dram2d.shape[0], dram2d.shape[1]], BF16,
                kind="Internal").ap()
        scr = SCR16[name].rearrange("(t p) c -> p t c", p=128)
        src = dram2d.rearrange("(t p) c -> p t c", p=128)
        nt = rtn if rtn is not None else rows_ // 128
        step = 4
        key = (name, col0, rt0)
        first = key not in STORED
        STORED.add(key)
        wsem = f"wg{par}" if first else f"wl{par}"
        p.wait("gq" if first else "sy", f"wb{par}", LASTWB.get(par, 0))
        for tt in range(0, nt, step):
            n = min(step, nt - tt)
            if first:
                p.dma("gq", lambda e, d=buf_ap[:, tt:tt + n, :],
                      s=src[:, rt0 + tt:rt0 + tt + n, col0:col0 + cols]:
                      e.dma_start(out=d, in_=s), wsem)
            else:
                p.dma("sy", lambda e, d=buf_ap[:, tt:tt + n, :],
                      s=scr[:, rt0 + tt:rt0 + tt + n, col0:col0 + cols]:
                      e.dma_start(out=d, in_=s), wsem)
        if first:
            p.wait("sy", wsem, p.cnt[wsem])
            for tt in range(0, nt, step):
                n = min(step, nt - tt)
                p.dma("sy", lambda e, d=scr[:, rt0 + tt:rt0 + tt + n, col0:col0 + cols],
                      s=buf_ap[:, tt:tt + n, :]:
                      e.dma_start(out=d, in_=s), f"wb{par}")
            LASTWB[par] = p.cnt[f"wb{par}"]
        return (wsem, p.cnt[wsem])

    def loadc(dst_ap, src_ap, cast=False):
        (p.gq if cast else p.sy)(
            lambda e, d=dst_ap, s=src_ap: e.dma_start(out=d, in_=s))

    loadc(m0[:], D["m0"]); loadc(m4[:], D["m4"])
    loadc(ident[:], D["ident"]); loadc(identD[:], D["identD"])
    loadc(onescol[:], D["onescol"]); loadc(onek[:], D["onek"])
    loadc(cvals[:], D["cvals"])
    loadc(sel8[:], D["sel8"]); loadc(sel16[:], D["sel16"])
    loadc(bdones[:], D["bdones"]); loadc(rotP[:], D["rotP"])
    loadc(convw[:], D["convw"]); loadc(convb[:], D["convb"])
    loadc(dtb[:], D["dtb"]); loadc(Dp[:], D["Dp"])
    loadc(Aneg[:], D["Alog"])
    loadc(qs8[:], D["qs"]); loadc(ks1[:], D["ks"])
    loadc(lng[:], D["lng"]); loadc(lnb[:], D["lnb"]); loadc(b2s[:], D["b2"])
    for l in (0, 1):
        loadc(b1a[:, l, :], D[f"b1a{l}"]); loadc(b1g[:, l, :], D[f"b1g{l}"])
    loadc(dtW[:], D["w_dtT"], cast=True)
    loadc(xpW[:], D["w_xpT"].rearrange("(t p) c -> p t c", p=128), cast=True)
    p.bar()
    AnegF = Aneg[:].rearrange("p a b -> p (a b)")
    p.ac(lambda e: e.activation(out=AnegF, in_=AnegF, func=AF.Exp))
    p.ac(lambda e: e.activation(out=qs8[:], in_=qs8[:], func=AF.Copy, scale=cvals[:, 3:4]))
    p.dv(lambda e: e.memset(hcarry[:].rearrange("p a b -> p (a b)"), 0.0))
    p.dv(lambda e: e.memset(ccarry[:].rearrange("p a b -> p (a b)"), 0.0))
    p.dv(lambda e: e.memset(scarry[:].rearrange("p a b -> p (a b)"), 0.0))
    for l in (0, 1):
        p.dv(lambda e, l=l: e.memset(
            vtringT[l][:].rearrange("p a b c -> p (a b c)"), 0.0))
    p.bar()
    p.dv(lambda e: e.tensor_scalar_mul(out=AnegF, in0=AnegF, scalar1=-1.0))
    p.bar()

    def seg(k, n):
        return rows1[0:1, k * AC:k * AC + n]

    def ln_post(n, bank):
        p.ac(lambda e, n=n, bank=bank: e.activation(out=seg(0, n), in_=bank[0:1, 0:n],
                                         func=AF.Copy, scale=cvals[0:1, 2:3]))
        ac0 = p.inc("ac")
        p.wait("dv", "ac", ac0)
        p.dv(lambda e, n=n: e.tensor_mul(out=seg(1, n), in0=seg(0, n), in1=seg(0, n)))
        p.chain("dv")
        p.dv(lambda e, n=n, bank=bank: e.scalar_tensor_tensor(
            out=seg(2, n), in0=bank[0:1, n:2 * n], scalar=1.0 / DM,
            in1=seg(1, n), op0=ALU.mult, op1=ALU.subtract))
        dv0 = p.inc("dv")
        p.wait("ac", "dv", dv0)
        p.ac(lambda e, n=n: e.activation(out=seg(3, n), in_=seg(2, n),
                                         func=AF.Ln, bias=cvals[0:1, 0:1]))
        p.chain("ac")
        p.ac(lambda e, n=n: e.activation(out=seg(4, n), in_=seg(3, n),
                                         func=AF.Exp, scale=cvals[0:1, 1:2]))
        p.chain("ac")
        p.ac(lambda e, n=n: e.copy(out=rowsb1[0:1, 0:n], in_=seg(0, n)))
        p.chain("ac")
        p.ac(lambda e, n=n: e.copy(out=rowsb1[0:1, n:2 * n], in_=seg(4, n)))
        ac1 = p.inc("ac")
        p.wait("pe", "ac", ac1)
        if n >= 512:
            mm(bank[:, 0:n], onescol[:], rowsb1[0:1, 0:n], start=True)
            mm(bank[:, n:2 * n], onescol[:], rowsb1[0:1, n:2 * n], start=True)
        else:
            mm(bank[:, 0:2 * n], onescol[:], rowsb1[0:1, 0:2 * n], start=True)
        p.bar()

    def ln_stats(src_ap_fn, n, bank, s1=None, s2=None):
        if s1 is None:
            s1 = sqb
        if s2 is None:
            s2 = sqb2
        for t in range(NPT):
            half = t % 2
            p.ac(lambda e, t=t, n=n, half=half: e.copy(
                out=s1[:, half, 0:n], in_=src_ap_fn(t)))
            p.chain("ac")
            p.ac(lambda e, t=t, n=n, half=half: e.activation(
                out=s2[:, half, 0:n], in_=src_ap_fn(t), func=AF.Square))
            lnac = p.inc("ac")
            p.wait("pe", "ac", lnac)
            mm(bank[0:1, 0:n], onek[:], s1[:, half, 0:n], start=(t == 0),
               stop=(t == NPT - 1 and n >= 512))
            mm(bank[0:1, n:2 * n], onek[:], s2[:, half, 0:n],
               start=(t == 0 and n >= 512), stop=(t == NPT - 1))
            if t < NPT - 1:
                pef = p.inc("pe")
                p.wait("ac", "pe", pef)
        p.bar()
        ln_post(n, bank)

    for c in range(NCHRUN):
        c0 = c * AC
        for t in range(NPT):
            p.sy(lambda e, t=t, c0=c0: e.dma_start(out=px[:, t, :],
                                                   in_=xT_r[:, t, c0:c0 + AC]))
        p.bar()

        for hh in range(2):
            h0 = hh * HC
            lnscrA = ar16[:, 2048:2560].rearrange("p (a b) -> p a b", a=2)
            lnscrB = ar16[:, 2560:3072].rearrange("p (a b) -> p a b", a=2)
            ln_stats(lambda t, h0=h0: px[:, t, h0:h0 + HC], HC, pb2,
                     s1=lnscrA, s2=lnscrB)
            for t in range(NPT):
                p.dv(lambda e, t=t, h0=h0: e.tensor_sub(
                    out=tf[:, 0, 0:HC], in0=px[:, t, h0:h0 + HC], in1=pb2[:, 0:HC]))
                p.chain("dv")
                p.dv(lambda e, t=t: e.tensor_mul(out=lnout[:, t, :],
                                                 in0=tf[:, 0, 0:HC],
                                                 in1=pb2[:, HC:2 * HC]))
                p.chain("dv")
            p.bar()

            p.dv(lambda e: e.tensor_copy(xinner[:, :, 1:4], ccarry[:, :, 1:4]))
            p.bar()

            def ip_slab(w):
                return wbuf[:, w % 2, 0:NPT * 256].rearrange("p (k c) -> p k c", k=NPT)

            ipld = {}
            ippe = {}
            ipac = {-3: 0, -2: 0, -1: 0}
            def ip_load(w):
                p.wait("sy", "pe", ippe.get(w - 2, 0))
                p.wait("gq", "pe", ippe.get(w - 2, 0))
                ipld[w] = wload(ip_slab(w), "w_inT", DM, 256 * w, 256, par=w % 2)
            ip_load(0)
            ip_load(1)
            for w in range(16):
                p.wait("pe", ipld[w][0], ipld[w][1])
                p.wait("pe", "ac", ipac.get(w - 3, 0))
                slab = ip_slab(w)
                for kt in range(NPT):
                    for oi in range(2):
                        mm(BANKS[2 * (w % 3) + oi][:, 0:HC],
                           slab[:, kt, 128 * oi:128 * (oi + 1)],
                           lnout[:, kt, :], start=(kt == 0),
                           stop=(kt == NPT - 1))
                ippe[w] = p.inc("pe")
                if w + 2 < 16:
                    ip_load(w + 2)
                p.wait("ac", "pe", ippe[w])
                for oi in range(2):
                    ot = 2 * w + oi
                    bk = BANKS[2 * (w % 3) + oi]
                    if ot < 16:
                        p.ac(lambda e, b=bk, d=ot: e.copy(
                            out=xinner[:, d, 4:4 + HC], in_=b[:, 0:HC]))
                    else:
                        p.ac(lambda e, b=bk, d=ot - 16: e.activation(
                            out=zsil[:, d, :], in_=b[:, 0:HC], func=AF.Silu))
                ipac[w] = p.inc("ac")
            p.bar()
            p.dv(lambda e: e.tensor_copy(ccarry[:, :, 1:4],
                                         xinner[:, :, 1 + HC:4 + HC]))
            p.bar()

            def op_slab(w):
                return wbuf[:, w % 2, 0:16 * 128].rearrange("p (k c) -> p k c", k=16)
            if hh == 1:
                p.wait("sy", "pe", p.cnt["pe"])
                p.wait("gq", "pe", p.cnt["pe"])
                opld = {0: wload(op_slab(0), "w_outT", DI, 0, 128, par=0),
                        1: wload(op_slab(1), "w_outT", DI, 128, 128, par=1)}

            cvld = {}
            cvpe = {}
            cvac = {-2: 0, -1: 0}
            def cv_load(t):
                p.wait("sy", "pe", cvpe.get(t - 2, 0))
                p.wait("gq", "pe", cvpe.get(t - 2, 0))
                cvld[t] = wload(cbuf[:, t % 2, :, :], "convdiag", 16 * 4 * 128, 0,
                                128, rt0=4 * t, rtn=4, par=t % 2)
            cv_load(0)
            cv_load(1)
            for t in range(16):
                p.wait("pe", cvld[t][0], cvld[t][1])
                p.wait("pe", "ac", cvac[t - 2])
                for k in range(4):
                    mm(BANKS[t % 2][:, 0:HC], cbuf[:, t % 2, k, :],
                       xinner[:, t, 1 + k:1 + k + HC], start=(k == 0), stop=(k == 3))
                cvpe[t] = p.inc("pe")
                if t + 2 < 16:
                    cv_load(t + 2)
                p.wait("ac", "pe", cvpe[t])
                p.ac(lambda e, t=t, b=BANKS[t % 2]: e.activation(
                    out=xconv[:, t, :], in_=b[:, 0:HC], func=AF.Silu,
                    bias=convb[:, t:t + 1]))
                cvac[t] = p.inc("ac")
            p.bar()

            for kt in range(16):
                mm(pb0[0:64, 0:HC], xpW[:, kt, 0:64], xconv[:, kt, :], start=(kt == 0), stop=(kt == 15))
                mm(pb1[0:16, 0:HC], xpW[:, kt, 64:80], xconv[:, kt, :], start=(kt == 0), stop=(kt == 15))
                mm(pb2[0:16, 0:HC], xpW[:, kt, 80:96], xconv[:, kt, :], start=(kt == 0), stop=(kt == 15))
            p.bar()
            p.ac(lambda e: e.copy(out=dlr[:, :], in_=pb0[0:DTR, 0:HC]))
            p.ac(lambda e: e.copy(out=bcrowB[:, :], in_=pb1[0:16, 0:HC]))
            p.ac(lambda e: e.copy(out=bcrowC[:, :], in_=pb2[0:16, 0:HC]))
            p.bar()
            for src_, dst in ((bcrowB, Bb), (bcrowC, Cb)):
                for n4 in range(4):
                    for k4 in range(4):
                        n = 4 * n4 + k4
                        mm(psA0[:, 256 * k4:256 * k4 + HC],
                           sel16[:, 128 * n:128 * (n + 1)], src_[:, :], start=True)
                    p.bar()
                    p.ac(lambda e, n4=n4, dst=dst: e.copy(
                        out=dst[:, 4 * n4:4 * n4 + 4, :],
                        in_=psA0[:].rearrange("p (a b) -> p a b", a=4)))
                    p.bar()

            mm(pb0[:, 0:HC], dtW[:, 0:128], dlr[:, :], start=True)
            p.bar()
            pe_dt0 = p.cnt["pe"]
            def inj_slot(w):
                bank = (psA0, psA1)[w // 4]
                return bank[:, 256 * (w % 4):256 * (w % 4) + HC]
            if hh == 1:
                injpe = {}
                def inj_load(w):
                    p.wait("sy", "pe", injpe.get(w - 2, 0))
                    p.wait("gq", "pe", injpe.get(w - 2, 0))
                    opld[w] = wload(op_slab(w), "w_outT", DI, 128 * w, 128,
                                    par=w % 2)
                for w in range(8):
                    p.wait("pe", opld[w][0], opld[w][1])
                    slab = op_slab(w)
                    for kt in range(16):
                        mm(inj_slot(w), slab[:, kt, 0:128], yg[:, kt, :],
                           start=(kt == 0 and w % 2 == 0),
                           stop=(kt == 15 and w % 2 == 1))
                    injpe[w] = p.inc("pe")
                    if w + 2 < 8:
                        inj_load(w + 2)
                p.wait("sy", "pe", injpe[6])
                p.wait("gq", "pe", injpe[6])
                opld2 = {0: wload(op_slab(0), "w_outT", DI, 0, 128, par=0)}
                p.wait("sy", "pe", injpe[7])
                p.wait("gq", "pe", injpe[7])
                opld2[1] = wload(op_slab(1), "w_outT", DI, 128, 128, par=1)
            acdtt = {}
            acfin = {}
            dvtts = {}
            dvfin = {}
            pefin = {0: pe_dt0}
            for t in range(16):
                dAb = dABUF[t % 2]
                dBx = dBxP[t % 2]
                pbank = (pb0, pb1)[t % 2]
                if t < 15:
                    if t >= 1:
                        p.wait("pe", "ac", acdtt[t - 1])
                    mm((pb0, pb1)[(t + 1) % 2][:, 0:HC],
                       dtW[:, 128 * (t + 1):128 * (t + 2)], dlr[:, :], start=True)
                    pefin[t + 1] = p.inc("pe")
                p.wait("ac", "pe", pefin[t])
                if t >= 2:
                    p.wait("ac", "dv", dvtts[t - 2])
                if t >= 1:
                    p.wait("ac", "ac", acfin[t - 1])
                p.ac(lambda e, t=t, b=pbank, dAb=dAb: e.activation(
                    out=dAb[:, 0, :], in_=b[:, 0:HC], func=AF.Exp,
                    bias=dtb[:, t:t + 1]))
                p.chain("ac")
                p.ac(lambda e, t=t, dAb=dAb: e.activation(
                    out=dtt[:, t % 2, :], in_=dAb[:, 0, :], func=AF.Ln,
                    bias=cvals[:, 5:6]))
                acdtt[t] = p.inc("ac")
                p.wait("ac", "ac", acdtt[t])
                for n in range(16):
                    p.ac(lambda e, t=t, n=n, dAb=dAb: e.activation(
                        out=dAb[:, n, :], in_=dtt[:, t % 2, :], func=AF.Exp,
                        scale=Aneg[:, t, n:n + 1]))
                acfin[t] = p.inc("ac")
                p.wait("dv", "ac", acdtt[t])
                if t >= 1:
                    p.wait("dv", "dv", dvfin[t - 1])
                p.dv(lambda e, t=t: e.tensor_mul(out=wt[:, :], in0=dtt[:, t % 2, :],
                                                 in1=xconv[:, t, :]))
                p.chain("dv")
                p.dv(lambda e, dBx=dBx: e.tensor_tensor(
                    out=dBx[:],
                    in0=wt[:, :].unsqueeze(1).broadcast_to([128, 16, HC]),
                    in1=Bb[:], op=ALU.mult))
                p.chain("dv")
                p.wait("dv", "ac", acfin[t])
                p.dv(lambda e, t=t, dAb=dAb: e.tensor_mul(out=yt[:, 0:16], in0=dAb[:, :, 0],
                                                 in1=hcarry[:, t, :]))
                p.chain("dv")
                p.dv(lambda e, dBx=dBx: e.tensor_add(out=dBx[:, :, 0], in0=dBx[:, :, 0],
                                            in1=yt[:, 0:16]))
                p.chain("dv")
                p.dv(lambda e, dAb=dAb: e.memset(dAb[:, :, 0], 0.0))
                p.chain("dv")
                p.dv(lambda e, dAb=dAb, dBx=dBx: e.tensor_tensor_scan(
                    out=dBx[:].rearrange("p n l -> p (n l)"),
                    data0=dAb[:].rearrange("p n l -> p (n l)"),
                    data1=dBx[:].rearrange("p n l -> p (n l)"),
                    initial=0.0, op0=ALU.mult, op1=ALU.add))
                dvtts[t] = p.inc("dv")
                p.wait("dv", "dv", dvtts[t])
                p.dv(lambda e, t=t, dBx=dBx: e.tensor_copy(hcarry[:, t, :], dBx[:, :, HC - 1]))
                p.chain("dv")
                p.dv(lambda e, dBx=dBx: e.tensor_tensor(
                    out=dBx[:], in0=dBx[:], in1=Cb[:], op=ALU.mult))
                p.chain("dv")
                p.dv(lambda e, dBx=dBx: e.tensor_add(
                    out=yah[:], in0=dBx[:, 0:8, :], in1=dBx[:, 8:16, :]))
                p.chain("dv")
                p.dv(lambda e: e.tensor_add(out=tt1[:], in0=yah[:, 0:4, :],
                                            in1=yah[:, 4:8, :]))
                p.chain("dv")
                p.dv(lambda e: e.tensor_add(out=tt2[:], in0=tt1[:, 0:2, :],
                                            in1=tt1[:, 2:4, :]))
                p.chain("dv")
                p.dv(lambda e: e.tensor_add(out=yt[:, :], in0=tt2[:, 0, :],
                                            in1=tt2[:, 1, :]))
                p.chain("dv")
                p.dv(lambda e, t=t: e.scalar_tensor_tensor(
                    out=yt[:, :], in0=xconv[:, t, :], scalar=Dp[:, t:t + 1],
                    in1=yt[:, :], op0=ALU.mult, op1=ALU.add))
                p.chain("dv")
                if hh == 1 and t == 0:
                    p.wait("dv", "pe", injpe[7])
                p.dv(lambda e, t=t: e.tensor_mul(out=yg[:, t, :], in0=yt[:, :],
                                                 in1=zsil[:, t, :]))
                dvfin[t] = p.inc("dv")
            p.bar()

            if hh == 0:
                continue
            for w in range(8):
                p.dv(lambda e, w=w: e.tensor_add(
                    out=px[:, w, 0:HC], in0=px[:, w, 0:HC], in1=inj_slot(w)))
                p.chain("dv")
            p.bar()

            oppe = {}
            opdv = {-2: 0, -1: 0}
            lnpe = {}
            def op_load(w):
                p.wait("sy", "pe", oppe.get(w - 2, 0))
                p.wait("gq", "pe", oppe.get(w - 2, 0))
                opld2[w] = wload(op_slab(w), "w_outT", DI, 128 * w, 128, par=w % 2)
            ls1 = atno[:, 0:2, :]
            ls2 = atno[:, 2:4, :]
            def ln2_stats(j):
                p.wait("ac", "dv", opdv[j])
                p.wait("ac", "pe", lnpe.get(j - 2, 0))
                p.ac(lambda e, j=j: e.copy(out=ls1[:, j % 2, :],
                                           in_=px[:, j, :]))
                p.chain("ac")
                p.ac(lambda e, j=j: e.activation(out=ls2[:, j % 2, :],
                                                 in_=px[:, j, :],
                                                 func=AF.Square))
                lnac = p.inc("ac")
                p.wait("pe", "ac", lnac)
                mm(psA1[0:1, 0:AC], onek[:], ls1[:, j % 2, :],
                   start=(j == 0), stop=(j == 7))
                mm(psA1[0:1, AC:2 * AC], onek[:], ls2[:, j % 2, :],
                   start=(j == 0), stop=(j == 7))
                lnpe[j] = p.inc("pe")
            for w in range(8):
                p.wait("pe", opld2[w][0], opld2[w][1])
                p.wait("pe", "dv", opdv[w - 2])
                slab = op_slab(w)
                for kt in range(16):
                    mm(BANKS[w % 2][:, 0:HC], slab[:, kt, 0:128],
                       yg[:, kt, :], start=(kt == 0), stop=(kt == 15))
                oppe[w] = p.inc("pe")
                if w + 2 < 8:
                    op_load(w + 2)
                p.wait("dv", "pe", oppe[w])
                p.dv(lambda e, w=w, h0=h0, b=BANKS[w % 2]: e.tensor_add(
                    out=px[:, w, h0:h0 + HC], in0=px[:, w, h0:h0 + HC],
                    in1=b[:, 0:HC]))
                opdv[w] = p.inc("dv")
                if STAGE != 1 and w >= 1:
                    ln2_stats(w - 1)
            if STAGE != 1:
                ln2_stats(7)
            p.bar()

        if STAGE == 1:
            for t in range(NPT):
                p.sy(lambda e, t=t, c0=c0: e.dma_start(out=out_r[:, t, c0:c0 + AC],
                                                       in_=px[:, t, :]))
            p.bar()
            continue

        def wq_slab(w):
            return wbuf[:, w % 2, 0:NPT * 256].rearrange("p (k c) -> p k c", k=NPT)
        p.wait("sy", "pe", p.cnt["pe"])
        p.wait("gq", "pe", p.cnt["pe"])
        qld_h = {0: wload(wq_slab(0), "wqkv0", DM, 0, 256, par=0),
                 1: wload(wq_slab(1), "wqkv0", DM, 256, 256, par=1)}

        ln_post(AC, psA1)
        dvn = 0
        for t in range(NPT):
            p.dv(lambda e, t=t: e.tensor_sub(out=tf[:, 0, :], in0=px[:, t, :],
                                             in1=psA1[:, 0:AC]))
            p.chain("dv")
            p.dv(lambda e, t=t: e.tensor_mul(out=pxi[:, t, 1:1 + AC],
                                             in0=tf[:, 0, :], in1=psA1[:, AC:2 * AC]))
            dvn = p.inc("dv")
            p.wait("dv", "dv", dvn)
            p.wait("ac", "dv", dvn)
            p.ac(lambda e, t=t: e.copy(out=pxib[:, t, 1:1 + AC],
                                       in_=pxi[:, t, 1:1 + AC]))
        p.bar()
        for i, nm in enumerate(("rcq", "rsq", "rck", "rsk")):
            p.sy(lambda e, i=i, nm=nm, c0=c0: e.dma_start(
                out=rott[:, i, :], in_=D[nm][:, c0:c0 + AC]))
        p.bar()

        nlayers = 1 if STAGE in (2, 3) else 2
        for l in range(nlayers):
            ring = c % 2
            prev_ring = (c + 1) % 2
            kring = kringT[l]
            vtring = vtringT[l]

            st_ = 2 * l
            p.dv(lambda e, st_=st_: e.tensor_copy(pxib[:, :, 0], scarry[:, st_, :]))
            p.chain("dv")
            p.dv(lambda e, st_=st_: e.tensor_copy(scarry[:, st_, :], pxib[:, :, AC]))
            p.bar()

            def qkv_rhs(kt):
                return pxib[:, kt, 1:1 + AC] if kt < NPT // 2 else pxib[:, kt, 0:AC]

            def l2norm_rotary(isq, l=l, kring=kring, ring=ring):
                qac = {}
                qpe = {}
                for t in range(4):
                    if t >= 2:
                        p.wait("ac", "pe", qpe[t - 2])
                    p.ac(lambda e, t=t: e.activation(out=sqb[:, t % 2, :],
                                                     in_=qraw[:, t, :], func=AF.Square))
                    qac[t] = p.inc("ac")
                    p.wait("pe", "ac", qac[t])
                    mm(pb0[0:8, 0:AC], bdones[:, t, :], sqb[:, t % 2, :],
                       start=(t == 0), stop=(t == 3))
                    qpe[t] = p.inc("pe")
                p.wait("ac", "pe", qpe[3])
                p.ac(lambda e: e.activation(out=nrm[:, 0, :], in_=pb0[0:8, 0:AC],
                                            func=AF.Ln))
                p.chain("ac")
                p.ac(lambda e: e.activation(out=nrm[:, 1, :], in_=nrm[:, 0, :],
                                            func=AF.Exp, scale=cvals[0:8, 1:2]))
                nac = p.inc("ac")
                p.wait("dv", "ac", nac)
                p.dv(lambda e: e.tensor_scalar_min(out=nrm[:, 1, :], in0=nrm[:, 1, :],
                                                   scalar1=1e12))
                ndv = p.inc("dv")
                p.wait("ac", "dv", ndv)
                p.ac(lambda e: e.copy(out=nrmb[:, :], in_=nrm[:, 1, :]))
                nac = p.inc("ac")
                sc = qs8 if isq else ks1
                ct, stt_ = (0, 1) if isq else (2, 3)
                bcd = {}
                rpd = {}
                sdv = {}
                rdv = {}
                p.wait("pe", "ac", nac)
                for t in range(4):
                    bk = (pb1, pb2)[t % 2]
                    if t >= 2:
                        p.wait("pe", "dv", rdv[t - 2])
                    mm(bk[:, 0:AC], sel8[:, 128 * t:128 * (t + 1)], nrmb[:, :],
                       start=True)
                    bcd[t] = p.inc("pe")
                    p.wait("dv", "pe", bcd[t])
                    p.dv(lambda e, t=t, sc=sc, b=bk, l=l: e.scalar_tensor_tensor(
                        out=q2b[:, t, :], in0=qraw[:, t, :], scalar=sc[:, l:l + 1],
                        in1=b[:, 0:AC], op0=ALU.mult, op1=ALU.mult))
                    sdv[t] = p.inc("dv")
                    p.wait("pe", "dv", sdv[t])
                    mm(bk[:, 0:AC], rotP[:], q2b[:, t, :], start=True)
                    rpd[t] = p.inc("pe")
                    p.dv(lambda e, t=t, ct=ct: e.tensor_mul(
                        out=tf[:, 0, :], in0=q2b[:, t, :], in1=rott[:, ct, :]))
                    p.chain("dv")
                    p.wait("dv", "pe", rpd[t])
                    p.dv(lambda e, t=t, stt_=stt_, b=bk: e.tensor_mul(
                        out=tf[:, 1, :], in0=b[:, 0:AC], in1=rott[:, stt_, :]))
                    rdv[t] = p.inc("dv")
                    p.wait("dv", "dv", rdv[t])
                    if isq:
                        p.dv(lambda e, t=t: e.tensor_add(
                            out=q3b[:, t, :], in0=tf[:, 0, :], in1=tf[:, 1, :]))
                    else:
                        p.dv(lambda e, t=t, ring=ring, kring=kring: e.tensor_add(
                            out=kring[:, t, ring * AC:ring * AC + AC],
                            in0=tf[:, 0, :], in1=tf[:, 1, :]))
                    p.chain("dv")
                p.bar()

            qld = {}
            qpe_ = {}
            qac_ = {-2: 0, -1: 0}
            def q_load(w):
                p.wait("sy", "pe", qpe_.get(w - 2, 0))
                p.wait("gq", "pe", qpe_.get(w - 2, 0))
                qld[w] = wload(wq_slab(w), f"wqkv{l}", DM, 256 * w, 256, par=w % 2)
            if l == 0:
                qld[0], qld[1] = qld_h[0], qld_h[1]
            else:
                q_load(0)
                q_load(1)
            for w in range(6):
                p.wait("pe", qld[w][0], qld[w][1])
                p.wait("pe", "ac", qac_.get(w - 3, 0))
                slab = wq_slab(w)
                for kt in range(NPT):
                    for oi in range(2):
                        mm(BANKS[2 * (w % 3) + oi],
                           slab[:, kt, 128 * oi:128 * (oi + 1)],
                           qkv_rhs(kt), start=(kt == 0), stop=(kt == NPT - 1))
                qpe_[w] = p.inc("pe")
                if w + 2 < 6:
                    q_load(w + 2)
                p.wait("ac", "pe", qpe_[w])
                for oi in range(2):
                    ot = 2 * w + oi
                    bk = BANKS[2 * (w % 3) + oi]
                    if ot < 8:
                        p.ac(lambda e, ot=ot, b=bk: e.copy(
                            out=qraw[:, ot % 4, :], in_=b[:, 0:AC]))
                    else:
                        p.ac(lambda e, ot=ot, b=bk: e.copy(
                            out=vbf[:, ot % 4, :], in_=b[:, 0:AC]))
                qac_[w] = p.inc("ac")
                if w == 1:
                    p.bar()
                    l2norm_rotary(True)
                elif w == 3:
                    p.bar()
                    l2norm_rotary(False)
            p.bar()

            for hd in range(HEADS):
                pt_, off = hd // 2, (hd % 2) * 64
                for jt in range(4):
                    p.pe(lambda e, pt_=pt_, off=off, jt=jt: e.transpose(
                        out=pbt[:, 64 * jt:64 * jt + 64],
                        in_=vbf[off:off + 64, pt_, 128 * jt:128 * (jt + 1)],
                        identity=identD[off:off + 64, :]))
                p.bar()
                p.ac(lambda e, hd=hd, ring=ring, vtring=vtring: e.copy(
                    out=vtring[:, hd, 4 * ring:4 * ring + 4, :].rearrange(
                        "p a b -> p (a b)"),
                    in_=pbt[:, 0:256]))
                p.bar()

            items = [(hd, ib) for hd in range(HEADS) for ib in range(4)]
            SIMR = (psA0, psA1)
            pesim = {}; dvmax = {}; acexp = {}; dvren = {}
            petr = {}; acev = {}; peav = {}; acat = {-1: 0}
            jts_of = lambda ib: (list(range(ib, ib + 5)) if c > 0
                                 else list(range(4, 5 + ib)))

            def emit_sim(i):
                hd, ib = items[i]
                pt_, off = hd // 2, (hd % 2) * 64
                sim = SIMR[i % 2]
                jts = jts_of(ib)
                if i >= 2:
                    p.wait("pe", "ac", acexp[i - 2])
                qblk = q3b[off:off + 64, pt_, 128 * ib:128 * (ib + 1)]
                for k, jt in enumerate(jts):
                    r = ring if jt >= 4 else prev_ring
                    colbase = r * AC + 128 * (jt % 4)
                    mm(sim[:, 128 * k:128 * (k + 1)], qblk,
                       kring[off:off + 64, pt_, colbase:colbase + 128], start=True)
                pesim[i] = p.inc("pe")

            emit_sim(0)
            emit_sim(1)
            nI = len(items)
            for i in range(nI + 1):
                if i < nI:
                    hd, ib = items[i]
                    pt_, off = hd // 2, (hd % 2) * 64
                    sim = SIMR[i % 2]
                    attn = attnP[i % 2]
                    sA = sAB[:, 4 * (i % 2):4 * (i % 2) + 4]
                    nj = len(jts_of(ib))
                    p.wait("dv", "pe", pesim[i])
                    if i >= 2:
                        p.wait("dv", "ac", acexp[i - 2])
                    if c > 0:
                        p.dv(lambda e, sim=sim: e.tensor_tensor(
                            out=sim[:, 0:128], in0=sim[:, 0:128], in1=m0[:],
                            op=ALU.add))
                        p.chain("dv")
                    p.dv(lambda e, sim=sim, nj=nj: e.tensor_tensor(
                        out=sim[:, 128 * (nj - 1):128 * nj],
                        in0=sim[:, 128 * (nj - 1):128 * nj], in1=m4[:], op=ALU.add))
                    p.chain("dv")
                    p.dv(lambda e, sim=sim, nj=nj, sA=sA: e.reduce_max(
                        out=sA[:, 0:1], in_=sim[:, 0:128 * nj], axis=AX.X,
                        negate=True))
                    dvmax[i] = p.inc("dv")
                if i >= 1:
                    j = i - 1
                    hdj, ibj = items[j]
                    ptj, offj = hdj // 2, (hdj % 2) * 64
                    attnj = attnP[j % 2]
                    attnTj = attnTP[j % 2]
                    sAj = sAB[:, 4 * (j % 2):4 * (j % 2) + 4]
                    njj = len(jts_of(ibj))
                    jtsj = jts_of(ibj)
                    p.wait("dv", "ac", acexp[j])
                    p.dv(lambda e, sA=sAj: e.reciprocal(out=sA[:, 2:3], in_=sA[:, 1:2]))
                    p.chain("dv")
                    p.dv(lambda e, nj=njj, attn=attnj, sA=sAj: e.tensor_scalar_mul(
                        out=attn[:, 0:128 * nj], in0=attn[:, 0:128 * nj],
                        scalar1=sA[:, 2:3]))
                    dvren[j] = p.inc("dv")
                    p.wait("pe", "dv", dvren[j])
                    if j >= 1:
                        p.wait("pe", "ac", acev[j - 1])
                    for k in range(njj):
                        p.pe(lambda e, k=k, attn=attnj: e.transpose(
                            out=pbt[:, 128 * k:128 * (k + 1)],
                            in_=attn[:, 128 * k:128 * (k + 1)], identity=ident[:]))
                    petr[j] = p.inc("pe")
                    p.wait("ac", "pe", petr[j])
                    if j >= 2:
                        p.wait("ac", "pe", peav[j - 2])
                    p.ac(lambda e, nj=njj, attnT=attnTj: e.copy(
                        out=attnT[:, 0:128 * nj], in_=pbt[:, 0:128 * nj]))
                    acev[j] = p.inc("ac")
                    p.wait("pe", "ac", acev[j])
                    p.wait("pe", "ac", acat[j - 1])
                    for k, jt in enumerate(jtsj):
                        slot = (ring if jt >= 4 else prev_ring) * 4 + (jt % 4)
                        mm(pb2[offj:offj + 64, 0:128], vtring[:, hdj, slot, :],
                           attnTj[:, 128 * k:128 * (k + 1)], start=(k == 0),
                           stop=(k == njj - 1), tile_position=(0, offj))
                    peav[j] = p.inc("pe")
                    p.wait("ac", "pe", peav[j])
                    p.ac(lambda e, off=offj, pt_=ptj, ib=ibj: e.copy(
                        out=atno[off:off + 64, pt_, 128 * ib:128 * (ib + 1)],
                        in_=pb2[off:off + 64, 0:128]))
                    acat[j] = p.inc("ac")
                if i < nI:
                    p.wait("ac", "dv", dvmax[i])
                    if i >= 2:
                        p.wait("ac", "pe", petr[i - 2])
                    p.ac(lambda e, sim=sim, nj=nj, attn=attn, sA=sA: e.activation(
                        out=attn[:, 0:128 * nj], in_=sim[:, 0:128 * nj], func=AF.Exp,
                        bias=sA[:, 0:1], accum_out=sA[:, 1:2]))
                    acexp[i] = p.inc("ac")
                    if i + 2 < nI:
                        emit_sim(i + 2)
            p.bar()

            def wo_slab(w):
                return wbuf[:, w % 2, 0:4 * 256].rearrange("p (k c) -> p k c", k=4)

            wold = {}
            wope = {}
            wodv = {-2: 0, -1: 0}
            def wo_load(w):
                p.wait("sy", "pe", wope.get(w - 2, 0))
                p.wait("gq", "pe", wope.get(w - 2, 0))
                wold[w] = wload(wo_slab(w), f"wo{l}", INNER, 256 * w, 256, par=w % 2)
            wo_load(0)
            wo_load(1)
            for w in range(4):
                p.wait("pe", wold[w][0], wold[w][1])
                p.wait("pe", "dv", wodv[w - 2])
                slab = wo_slab(w)
                for kt in range(4):
                    for oi in range(2):
                        mm(BANKS[2 * (w % 2) + oi],
                           slab[:, kt, 128 * oi:128 * (oi + 1)],
                           atno[:, kt, :], start=(kt == 0), stop=(kt == 3))
                wope[w] = p.inc("pe")
                if w + 2 < 4:
                    wo_load(w + 2)
                p.wait("dv", "pe", wope[w])
                p.dv(lambda e, b=BANKS[2 * (w % 2)], ot=2 * w: e.tensor_add(
                    out=pxi[:, ot, 1:1 + AC], in0=pxi[:, ot, 1:1 + AC],
                    in1=b[:, 0:AC]))
                p.chain("dv")
                p.dv(lambda e, b=BANKS[2 * (w % 2) + 1], ot=2 * w + 1: e.tensor_add(
                    out=pxi[:, ot, 1:1 + AC], in0=pxi[:, ot, 1:1 + AC],
                    in1=b[:, 0:AC]))
                wodv[w] = p.inc("dv")
            p.bar()
            for ot in range(NPT):
                p.ac(lambda e, ot=ot: e.copy(out=pxib[:, ot, 1:1 + AC],
                                             in_=pxi[:, ot, 1:1 + AC]))
            p.bar()
            if STAGE == 2:
                break

            def w1_slab(w):
                return wbuf[:, w % 2, 0:NPT * 256].rearrange("p (k c) -> p k c", k=NPT)
            p.wait("sy", "pe", p.cnt["pe"])
            p.wait("gq", "pe", p.cnt["pe"])
            w1ld = {0: wload(w1_slab(0), f"w1p{l}", DM, 0, 256, par=0),
                    1: wload(w1_slab(1), f"w1p{l}", DM, 256, 256, par=1)}

            st_ = 2 * l + 1
            p.dv(lambda e, st_=st_: e.tensor_copy(pxib[:, :, 0], scarry[:, st_, :]))
            p.chain("dv")
            p.dv(lambda e, st_=st_: e.tensor_copy(scarry[:, st_, :], pxib[:, :, AC]))
            p.bar()

            def sh_ap(t):
                return pxib[:, t, 1:1 + AC] if t < NPT // 2 else pxib[:, t, 0:AC]

            ln_stats(sh_ap, AC, psA0)
            for t in range(NPT):
                p.dv(lambda e, t=t: e.tensor_sub(out=tf[:, 0, :], in0=sh_ap(t),
                                                 in1=psA0[:, 0:AC]))
                p.chain("dv")
                p.dv(lambda e, t=t: e.tensor_mul(out=tf[:, 1, :], in0=tf[:, 0, :],
                                                 in1=psA0[:, AC:2 * AC]))
                p.chain("dv")
                p.dv(lambda e, t=t, l=l: e.tensor_scalar(
                    out=ln2o[:, t, :], in0=tf[:, 1, :], scalar1=lng[:, l, t:t + 1],
                    scalar2=lnb[:, l, t:t + 1], op0=ALU.mult, op1=ALU.add))
                p.chain("dv")
            p.bar()

            w1pe = {}
            w1ac = {}
            w1dv = {-3: 0, -2: 0, -1: 0}
            def w1_load(w):
                p.wait("sy", "pe", w1pe.get(w - 2, 0))
                p.wait("gq", "pe", w1pe.get(w - 2, 0))
                w1ld[w] = wload(w1_slab(w), f"w1p{l}", DM, 256 * w, 256, par=w % 2)
            for w in range(NHT):
                p.wait("pe", w1ld[w][0], w1ld[w][1])
                p.wait("pe", "dv", w1dv.get(w - 3, 0))
                slab = w1_slab(w)
                for kt in range(NPT):
                    for oi in range(2):
                        mm(BANKS[2 * (w % 3) + oi],
                           slab[:, kt, 128 * oi:128 * (oi + 1)],
                           ln2o[:, kt, :], start=(kt == 0), stop=(kt == NPT - 1))
                w1pe[w] = p.inc("pe")
                if w + 2 < NHT:
                    w1_load(w + 2)
                p.wait("ac", "pe", w1pe[w])
                p.wait("ac", "dv", w1dv.get(w - 2, 0))
                gb_ = BANKS[2 * (w % 3) + 1]
                ab_ = BANKS[2 * (w % 3)]
                p.ac(lambda e, g_=gb_, w=w, l=l: e.activation(
                    out=gel[:, w % 2, :], in_=g_[:, 0:AC], func=AF.Gelu,
                    bias=b1g[:, l, w:w + 1]))
                w1ac[w] = p.inc("ac")
                p.wait("dv", "ac", w1ac[w])
                p.dv(lambda e, a_=ab_, w=w, l=l: e.scalar_tensor_tensor(
                    out=gated[:, w, :], in0=a_[:, 0:AC], scalar=b1a[:, l, w:w + 1],
                    in1=gel[:, w % 2, :], op0=ALU.add, op1=ALU.mult))
                w1dv[w] = p.inc("dv")
            p.bar()

            def w2_slab(i):
                return wbuf[:, i % 2, 0:11 * 128].rearrange("p (k c) -> p k c", k=11)

            seq2 = [(w, gpr) for w in range(8) for gpr in range(2)]
            w2ld = {}
            w2pe = {}
            w2dv = {-2: 0, -1: 0, 0: 0, 1: 0}
            def w2_load(i):
                w_, g_ = seq2[i]
                p.wait("sy", "pe", w2pe.get(i - 2, 0))
                p.wait("gq", "pe", w2pe.get(i - 2, 0))
                w2ld[i] = wload(w2_slab(i), f"w2p{l}", HIPD_ if False else HIDP,
                                128 * w_, 128, rt0=11 * g_, rtn=11, par=i % 2)
            w2_load(0)
            w2_load(1)
            for i in range(16):
                w, gpr = seq2[i]
                p.wait("pe", w2ld[i][0], w2ld[i][1])
                p.wait("pe", "dv", w2dv[i - 2])
                slab = w2_slab(i)
                for kk in range(11):
                    kt = 11 * gpr + kk
                    mm(BANKS[4 + w % 2], slab[:, kk, 0:128], gated[:, kt, :],
                       start=(kt == 0), stop=(kt == NHT - 1))
                w2pe[i] = p.inc("pe")
                if i + 2 < 16:
                    w2_load(i + 2)
                if gpr == 1:
                    p.wait("dv", "pe", w2pe[i])
                    p.dv(lambda e, w=w, l=l, b=BANKS[4 + w % 2]:
                         e.scalar_tensor_tensor(
                             out=pxi[:, w, 1:1 + AC], in0=b[:, 0:AC],
                             scalar=b2s[:, l, w:w + 1], in1=pxi[:, w, 1:1 + AC],
                             op0=ALU.add, op1=ALU.add))
                    w2dv[i] = p.inc("dv")
                else:
                    w2dv[i] = w2dv[i - 1]
            p.bar()
            if l == 0 and nlayers > 1:
                for w in range(NPT):
                    p.ac(lambda e, w=w: e.copy(out=pxib[:, w, 1:1 + AC],
                                               in_=pxi[:, w, 1:1 + AC]))
                p.bar()

        for t in range(NPT):
            p.dv(lambda e, t=t: e.tensor_add(out=px[:, t, :], in0=px[:, t, :],
                                             in1=pxi[:, t, 1:1 + AC]))
        p.bar()
        for t in range(NPT):
            p.sy(lambda e, t=t, c0=c0: e.dma_start(out=out_r[:, t, c0:c0 + AC],
                                                   in_=px[:, t, :]))
        p.bar()

    p.wait("sy", "sy", p.cnt["sy"])

    with nc.Block() as block:
        p.materialize(block)
    nc._keepalive_ctx = ctx
    return nc


def kernel(_ncores=NCORE, **inputs):
    g = _host_prep(inputs)
    x = inputs["L0_attn_x"]
    nc = build()
    in_maps = []
    for cix in range(_ncores):
        m = {"xT": np.ascontiguousarray(np.asarray(x[cix]).T)}
        m.update(g)
        in_maps.append(m)
    res = run_bass_kernel_spmd(nc, in_maps, core_ids=list(range(_ncores)))
    outs = [np.asarray(r["out"]).T for r in res.results]
    return np.ascontiguousarray(np.stack(outs, 0))
